# version 07
# speedup vs baseline: 1.5541x; 1.0412x over previous
import math
import numpy as np
import ml_dtypes
from contextlib import ExitStack

import concourse.bass as bass
import concourse.mybir as mybir
from concourse.bass_utils import run_bass_kernel_spmd

F32 = mybir.dt.float32
BF16 = mybir.dt.bfloat16
I32 = mybir.dt.int32
AF = mybir.ActivationFunctionType
ALU = mybir.AluOpType
AX = mybir.AxisListType

NCORES = 8
B_LOC = 4
T_SEQ = 2048
NTOK = B_LOC * T_SEQ
D = 1024
NCHUNK = 512
ENGINES = ("sync", "tensor", "vector", "scalar", "gpsimd")


class SemC:
    def __init__(self, h):
        self.h = h
        self.n = 0

    def inc(self, k=1):
        self.n += k
        return self.n


class KB:
    def __init__(self, nc, es):
        self.nc = nc
        self.es = es
        self.ops = {e: [] for e in ENGINES}
        self._semi = 0
        self.bar = self.sem("bar")
        self.onesf = None
        self.onescol = None
        self.fsem = {e: self.sem("fence_" + e) for e in ENGINES}

    def sem(self, name=None):
        if name is not None:
            self._semi += 1
            h = self.es.enter_context(self.nc.semaphore(name))
            return SemC(h)
        if not hasattr(self, "_pool"):
            self._pool = []
            self._pooli = 0
        if len(self._pool) < 24:
            self._semi += 1
            h = self.es.enter_context(self.nc.semaphore(f"gp{len(self._pool)}"))
            sc = SemC(h)
            self._pool.append(sc)
            self._pooli = len(self._pool)
            return sc
        sc = self._pool[self._pooli % 24]
        self._pooli += 1
        return sc

    def sb(self, name, shape, dtype, scope=None):
        return (scope or self.es).enter_context(self.nc.sbuf_tensor(name, shape, dtype))

    def psum(self, name, shape, dtype=F32, scope=None):
        return (scope or self.es).enter_context(self.nc.psum_tensor(name, shape, dtype))

    def scope(self):
        return ExitStack()

    def on(self, eng, fn):
        self.ops[eng].append(fn)

    def barrier(self):
        t1 = self.bar.n + len(ENGINES)
        t2 = t1 + 1
        self.bar.n = t2
        pool = list(getattr(self, "_pool", []))
        for e in ENGINES:
            def f(eng, t1=t1, t2=t2, e=e, pool=pool):
                if e == "vector":
                    eng.memset(self.junk[:, 0:4], 0.0).then_inc(self.bar.h, 1)
                elif e == "scalar":
                    eng.copy(self.junk[:, 4:8], self.junk[:, 0:4]).then_inc(self.bar.h, 1)
                elif e == "gpsimd":
                    eng.memset(self.junk[:, 8:12], 0.0).then_inc(self.bar.h, 1)
                else:
                    eng.sem_inc(self.bar.h, 1)
                eng.wait_ge(self.bar.h, t1)
                if e == "gpsimd":
                    eng.sem_inc(self.bar.h, 1)
                eng.wait_ge(self.bar.h, t2)
            self.on(e, f)

    def emit_setup(self):
        self.onesf = self.sb("g_onesf", [1, 128], F32)
        self.onescol = self.sb("g_onescol", [128, 1], F32)
        self.onesb = self.sb("g_onesb", [1, 128], BF16)
        self.junk = self.sb("g_junk", [128, 16], F32)
        s_su = self.sem("setup")

        def f(eng):
            eng.memset(self.onesf[:], 1.0)
            eng.memset(self.onesb[:], 1.0)
            eng.memset(self.onescol[:], 1.0).then_inc(s_su.h, 1)
        self.on("vector", f)
        s_su.inc(1)
        self.barrier()

    def finalize(self):
        with self.nc.Block() as block:
            for ename in ENGINES:
                deco = getattr(block, ename)
                lst = self.ops[ename]
                def body(eng, lst=lst):
                    for f in lst:
                        f(eng)
                deco(body)



def emit_weight_load(kb, w_dram, wq_sb, ktiles, mtiles, stage_sb, sem_ld, sem_cvt,
                     scale=None, src_dt=F32):
    nstage = stage_sb.shape[1] // 128
    idx = 0
    for kt in range(ktiles):
        for mt in range(mtiles):
            slot = idx % nstage
            need_cvt = max(0, idx - nstage + 1)
            base_ld = sem_ld.n

            def fl(eng, kt=kt, mt=mt, slot=slot, need_cvt=need_cvt):
                if need_cvt:
                    eng.wait_ge(sem_cvt.h, need_cvt)
                eng.dma_start(
                    out=stage_sb[:, slot * 128:(slot + 1) * 128],
                    in_=w_dram[kt * 128:(kt + 1) * 128, mt * 128:(mt + 1) * 128],
                ).then_inc(sem_ld.h, 16)
            kb.on("sync", fl)
            sem_ld.inc(16)
            ldv = sem_ld.n

            def fc(eng, idx=idx, slot=slot, ldv=ldv, kt=kt, mt=mt):
                eng.wait_ge(sem_ld.h, ldv)
                col = (kt * mtiles + mt) * 128
                if scale is not None:
                    eng.tensor_scalar(
                        wq_sb[:, col:col + 128],
                        stage_sb[:, slot * 128:(slot + 1) * 128],
                        float(scale), None, ALU.mult,
                    ).then_inc(sem_cvt.h, 1)
                else:
                    eng.tensor_copy(
                        wq_sb[:, col:col + 128],
                        stage_sb[:, slot * 128:(slot + 1) * 128],
                    ).then_inc(sem_cvt.h, 1)
            kb.on("vector", fc)
            sem_cvt.inc(1)
            idx += 1


def ceil_div(a, b):
    return (a + b - 1) // b



def shard_x(x):
    shards = []
    for c in range(NCORES):
        xs = x[c * B_LOC:(c + 1) * B_LOC]
        xs = xs.reshape(NTOK, D).T
        shards.append(np.ascontiguousarray(xs))
    return shards


def unshard_x(shards):
    out = np.empty((NCORES * B_LOC, T_SEQ, D), np.float32)
    for c in range(NCORES):
        out[c * B_LOC:(c + 1) * B_LOC] = shards[c].T.reshape(B_LOC, T_SEQ, D)
    return out


def hostT(a):
    return np.ascontiguousarray(np.asarray(a).T)


def emit_ln_phase(kb, bufs, x_dram, xn_dram, dim, gamma_sb=None, beta_sb=None,
                  rms=False, eps=1e-5, xn_f32_dram=None, nchunk=NCHUNK,
                  ntok=NTOK, res_dram=None, sum_out_dram=None):
    pt = dim // 128
    nchunks = ntok // nchunk
    xs = bufs["lnx"]
    sq = bufs["lnsq"]
    tmpx = bufs["lntmp"]
    xnb = bufs["lnxn"]
    xnf = bufs.get("lnxnf")
    rows = bufs["lnrows"]
    onesf = bufs["onesf"]
    onescol = bufs["onescol"]
    pstat = bufs["ps_stat"]
    pb1 = bufs["ps_b1"]
    pb2 = bufs["ps_b2"]
    s_ld = kb.sem(); s_sq = kb.sem(); s_sqc = kb.sem(); s_st = kb.sem()
    s_row = kb.sem(); s_bc = kb.sem(); s_xn = kb.sem(); s_out = kb.sem()
    s_res = kb.sem()
    b_sqc = s_sqc.n; b_out = s_out.n
    resb = bufs.get("lnres")
    sumb = bufs.get("lnsum")
    use_f32 = xn_f32_dram is not None
    nst = 16 * (1 + (1 if use_f32 else 0) + (1 if sum_out_dram is not None else 0))
    n = nchunk
    rmu = rows[0:1, 0:n]; rrstd = rows[0:1, n:2 * n]
    rmr = rows[0:1, 2 * n:3 * n]; rtmp = rows[0:1, 3 * n:4 * n]
    xn_done = []
    row_done = []

    for c in range(nchunks):
        slot = c % 2
        xsv = xs[:, slot * pt * nchunk:(slot + 1) * pt * nchunk]
        xnv = xnb[:, slot * pt * nchunk:(slot + 1) * pt * nchunk]
        xnfv = xnf[:, slot * pt * nchunk:(slot + 1) * pt * nchunk] if use_f32 else None

        need_out = b_out + nst * max(0, c - 1) if c >= 2 else 0

        def fl(eng, c=c, xsv=xsv, need_out=need_out):
            if need_out:
                eng.wait_ge(s_out.h, need_out)
            eng.dma_start(
                out=xsv.rearrange("p (t n) -> p t n", t=pt),
                in_=x_dram[:, c * nchunk:(c + 1) * nchunk].rearrange(
                    "(t p) n -> p t n", p=128),
            ).then_inc(s_ld.h, 16)
        kb.on("sync", fl)
        s_ld.inc(16)
        ldv = s_ld.n
        if resb is not None:
            rv = resb[:, slot * pt * nchunk:(slot + 1) * pt * nchunk]

            def flr(eng, c=c, rv=rv, need_out=need_out):
                if need_out:
                    eng.wait_ge(s_out.h, need_out)
                eng.dma_start(
                    out=rv.rearrange("p (t n) -> p t n", t=pt),
                    in_=res_dram[:, c * nchunk:(c + 1) * nchunk].rearrange(
                        "(t p) n -> p t n", p=128),
                ).then_inc(s_res.h, 16)
            kb.on("sync", flr)
            s_res.inc(16)
        else:
            rv = None
        resv = s_res.n

        for t in range(pt):
            sqslot = t % 2
            sqv_need = s_sqc.n - 1 + t
            def fsq(eng, t=t, xsv=xsv, ldv=ldv, sqslot=sqslot, c=c):
                if t == 0:
                    eng.wait_ge(s_ld.h, ldv)
                if t >= 2:
                    eng.wait_ge(s_sqc.h, b_sqc + (c * pt) + t - 1)
                elif c > 0:
                    eng.wait_ge(s_sqc.h, b_sqc + c * pt)
                eng.activation(sq[:, sqslot * n:(sqslot + 1) * n],
                               xsv[:, t * n:(t + 1) * n],
                               AF.Square).then_inc(s_sq.h, 1)
            kb.on("scalar", fsq)
            s_sq.inc(1)
            sqv = s_sq.n

            def fmm(eng, t=t, sqv=sqv, c=c, xsv=xsv, sqslot=sqslot):
                eng.wait_ge(s_sq.h, sqv)
                if t == 0 and c > 0:
                    eng.wait_ge(s_row.h, row_done[c - 1])
                last = (t == pt - 1)
                mm1 = eng.matmul(pstat[0:1, 0:n], onescol[:],
                                 sq[:, sqslot * n:(sqslot + 1) * n],
                                 start=(t == 0), stop=last)
                mm1.then_inc(s_sqc.h, 1)
                mm2 = eng.matmul(pstat[0:1, n:2 * n], onescol[:],
                                 xsv[:, t * n:(t + 1) * n],
                                 start=(t == 0), stop=last)
                if last:
                    mm2.then_inc(s_st.h, 1)
            kb.on("tensor", fmm)
            s_sqc.inc(1)
        s_st.inc(1)
        stv = s_st.n

        fs = kb.fsem["vector"]
        if not rms:
            f1 = fs.inc(); f2 = fs.inc(); f3 = fs.inc()
        else:
            f1 = f2 = f3 = 0

        def frow(eng, stv=stv, f1=f1, f2=f2, f3=f3):
            eng.wait_ge(s_st.h, stv)
            if rms:
                eng.tensor_scalar(rtmp, pstat[0:1, 0:n], 1.0 / dim, float(eps),
                                  ALU.mult, ALU.add).then_inc(s_row.h, 1)
            else:
                eng.tensor_scalar(rmu, pstat[0:1, n:2 * n], 1.0 / dim, None, ALU.mult)
                eng.tensor_scalar(rtmp, pstat[0:1, 0:n], 1.0 / dim, None, ALU.mult).then_inc(fs.h, 1)
                eng.wait_ge(fs.h, f1)
                eng.tensor_tensor(rmr, rmu, rmu, ALU.mult).then_inc(fs.h, 1)
                eng.wait_ge(fs.h, f2)
                eng.tensor_tensor(rtmp, rtmp, rmr, ALU.subtract).then_inc(fs.h, 1)
                eng.wait_ge(fs.h, f3)
                eng.tensor_scalar(rtmp, rtmp, 1.0, float(eps),
                                  ALU.mult, ALU.add).then_inc(s_row.h, 1)
        kb.on("vector", frow)
        s_row.inc(1)
        v0 = s_row.n

        def fsqrt(eng, v0=v0):
            eng.wait_ge(s_row.h, v0)
            eng.activation(rtmp, rtmp, AF.Sqrt).then_inc(s_row.h, 1)
        kb.on("scalar", fsqrt)
        s_row.inc(1)
        v1 = s_row.n

        f4 = fs.inc()

        def frec(eng, v1=v1, f4=f4):
            eng.wait_ge(s_row.h, v1)
            eng.reciprocal(rrstd, rtmp).then_inc(fs.h, 1)
            eng.wait_ge(fs.h, f4)
            if not rms:
                eng.tensor_tensor(rmr, rmu, rrstd, ALU.mult).then_inc(s_row.h, 1)
            else:
                eng.tensor_copy(rmr, rrstd).then_inc(s_row.h, 1)
        kb.on("vector", frec)
        s_row.inc(1)
        v2 = s_row.n
        row_done.append(v2)

        def fbc(eng, v2=v2, c=c):
            eng.wait_ge(s_row.h, v2)
            if c > 0:
                eng.wait_ge(s_xn.h, xn_done[c - 1])
            eng.matmul(pb1[:, :], onesf[:], rrstd, start=True, stop=True)
            m2 = eng.matmul(pb2[:, :], onesf[:], rmr, start=True, stop=True)
            m2.then_inc(s_bc.h, 1)
        kb.on("tensor", fbc)
        s_bc.inc(1)
        bcv = s_bc.n

        for t in range(pt):
            def fxn(eng, t=t, bcv=bcv, xsv=xsv, xnv=xnv, xnfv=xnfv, rv=rv,
                    resv=resv, need_out=need_out, slot=slot):
                if t == 0:
                    eng.wait_ge(s_bc.h, bcv)
                    if need_out:
                        eng.wait_ge(s_out.h, need_out)
                    if rv is not None:
                        eng.wait_ge(s_res.h, resv)
                src = xsv[:, t * n:(t + 1) * n]
                tgt_b = xnv[:, t * n:(t + 1) * n]
                work = xnfv[:, t * n:(t + 1) * n] if xnfv is not None else tmpx[:]
                eng.tensor_tensor(tmpx[:], src, pb1[:, :], ALU.mult)
                if rms:
                    if gamma_sb is not None:
                        ins = eng.tensor_scalar(work, tmpx[:], gamma_sb[:, t:t + 1],
                                                None, ALU.mult)
                    else:
                        ins = eng.tensor_copy(work, tmpx[:])
                else:
                    ins = eng.tensor_tensor(work, tmpx[:], pb2[:, :], ALU.subtract)
                    if gamma_sb is not None:
                        ins = eng.tensor_scalar(work, work, gamma_sb[:, t:t + 1],
                                                beta_sb[:, t:t + 1], ALU.mult, ALU.add)
                if rv is not None:
                    ins = eng.tensor_tensor(tgt_b, work, rv[:, t * n:(t + 1) * n],
                                            ALU.add)
                elif sumb is not None:
                    ins = eng.tensor_copy(tgt_b, work)
                    ins = eng.tensor_tensor(
                        sumb[:, (slot * pt + t) * n:(slot * pt + t + 1) * n],
                        work, src, ALU.add)
                else:
                    ins = eng.tensor_copy(tgt_b, work)
                ins.then_inc(s_xn.h, 1)
            kb.on("vector", fxn)
            s_xn.inc(1)
        xn_done.append(s_xn.n)
        xnn = s_xn.n

        def fst(eng, c=c, xnv=xnv, xnfv=xnfv, xnn=xnn, slot=slot):
            eng.wait_ge(s_xn.h, xnn)
            eng.dma_start(
                out=xn_dram[:, c * nchunk:(c + 1) * nchunk].rearrange(
                    "(t p) n -> p t n", p=128),
                in_=xnv.rearrange("p (t n) -> p t n", t=pt),
            ).then_inc(s_out.h, 16)
            if xnfv is not None:
                eng.dma_start(
                    out=xn_f32_dram[:, c * nchunk:(c + 1) * nchunk].rearrange(
                        "(t p) n -> p t n", p=128),
                    in_=xnfv.rearrange("p (t n) -> p t n", t=pt),
                ).then_inc(s_out.h, 16)
            if sumb is not None:
                eng.dma_start(
                    out=sum_out_dram[:, c * nchunk:(c + 1) * nchunk].rearrange(
                        "(t p) n -> p t n", p=128),
                    in_=sumb[:, slot * pt * nchunk:(slot + 1) * pt * nchunk
                             ].rearrange("p (t n) -> p t n", t=pt),
                ).then_inc(s_out.h, 16)
        kb.on("gpsimd", fst)
        s_out.inc(nst)

    outn = s_out.n

    def fdrain(eng, outn=outn):
        eng.wait_ge(s_out.h, outn)
    kb.on("gpsimd", fdrain)


def emit_proj_phase(kb, bufs, pfx, weights, x_dram, ktiles, out_dram, mtiles,
                    mode="copy_bf16", res_dram=None, w_ready=None,
                    nchunk=NCHUNK, ntok=NTOK, krows=128, bias_sb=None,
                    out2_dram=None, mul_dram=None, act_scale=1.0,
                    res_scale=1.0, res2_dram=None, mask_dram=None,
                    accum_store=False, sems=None):
    nchunks = ntok // nchunk
    n = nchunk
    xb = bufs[pfx + "x"]
    outb = bufs[pfx + "o"]
    tmp = bufs[pfx + "t"]
    resb = bufs.get(pfx + "r")
    pss = bufs[pfx + "ps"]
    nbank = len(pss)
    if sems is None:
        sems = [kb.sem() for _ in range(7)]
    s_ld, s_mm, s_ev, s_st, s_res, s_res2, s_msk = sems
    b_st = s_st.n
    nsets = len(weights)
    bank_free_at = [0] * nbank
    mm_idx = 0
    ev_idx = 0
    ev_ends = []

    for c in range(nchunks):
        slot = c % 2
        xv = xb[:, slot * ktiles * n:(slot + 1) * ktiles * n]
        ov = outb[:, slot * mtiles * n:(slot + 1) * mtiles * n]
        rv = resb[:, slot * mtiles * n:(slot + 1) * mtiles * n] if resb is not None else None

        nst_per = 16 * (2 if out2_dram is not None else 1)
        need_st = b_st + nst_per * (c - 1) if c >= 2 else 0

        def flx(eng, c=c, xv=xv, need_st=need_st):
            if need_st:
                eng.wait_ge(s_st.h, need_st)
            if krows == 128:
                eng.dma_start(
                    out=xv.rearrange("p (t n) -> p t n", t=ktiles),
                    in_=x_dram[:, c * n:(c + 1) * n].rearrange("(t p) n -> p t n", p=128),
                ).then_inc(s_ld.h, 16)
            else:
                eng.dma_start(out=xv[0:krows, 0:n],
                              in_=x_dram[0:krows, c * n:(c + 1) * n]).then_inc(s_ld.h, 16)
        kb.on("sync", flx)
        s_ld.inc(16)
        if mul_dram is not None:
            mb = bufs[pfx + "m"]
            mv = mb[:, slot * mtiles * n:(slot + 1) * mtiles * n]

            def flm(eng, c=c, mv=mv, need_st=need_st):
                if need_st:
                    eng.wait_ge(s_st.h, need_st)
                eng.dma_start(
                    out=mv.rearrange("p (t n) -> p t n", t=mtiles),
                    in_=mul_dram[:, c * n:(c + 1) * n].rearrange("(t p) n -> p t n", p=128),
                ).then_inc(s_res.h, 16)
            kb.on("sync", flm)
            s_res.inc(16)
        else:
            mv = None
        if rv is not None:
            def flr(eng, c=c, rv=rv, need_st=need_st):
                if need_st:
                    eng.wait_ge(s_st.h, need_st)
                eng.dma_start(
                    out=rv.rearrange("p (t n) -> p t n", t=mtiles),
                    in_=res_dram[:, c * n:(c + 1) * n].rearrange("(t p) n -> p t n", p=128),
                ).then_inc(s_res.h, 16)
            kb.on("sync", flr)
            s_res.inc(16)
        ldv = s_ld.n
        resv = s_res.n
        if res2_dram is not None:
            r2b = bufs[pfx + "r2"]
            r2v = r2b[:, slot * mtiles * n:(slot + 1) * mtiles * n]

            def flr2(eng, c=c, r2v=r2v, need_st=need_st):
                if need_st:
                    eng.wait_ge(s_st.h, need_st)
                eng.dma_start(
                    out=r2v.rearrange("p (t n) -> p t n", t=mtiles),
                    in_=res2_dram[:, c * n:(c + 1) * n].rearrange("(t p) n -> p t n", p=128),
                ).then_inc(s_res2.h, 16)
            kb.on("sync", flr2)
            s_res2.inc(16)
        else:
            r2v = None
        res2v = s_res2.n
        if mask_dram is not None:
            mrow = bufs[pfx + "mr"]

            def flmk(eng, c=c, slot=slot, need_st=need_st):
                if need_st:
                    eng.wait_ge(s_st.h, need_st)
                eng.dma_start(out=mrow[0:1, slot * n:(slot + 1) * n],
                              in_=mask_dram[0:1, c * n:(c + 1) * n]).then_inc(s_msk.h, 16)
            kb.on("sync", flmk)
            s_msk.inc(16)
        mskv = s_msk.n

        for mt in range(mtiles):
            banks = []
            for si in range(nsets):
                bank = (mm_idx * nsets + si) % nbank
                banks.append(bank)
            need_ev = max(bank_free_at[b] for b in banks)

            def fmm(eng, c=c, mt=mt, banks=banks, need_ev=need_ev, ldv=ldv,
                    xv=xv, mskv=mskv, slot=slot):
                if mt == 0:
                    eng.wait_ge(s_ld.h, ldv)
                    if c == 0 and w_ready is not None:
                        eng.wait_ge(w_ready[0].h, w_ready[1])
                    if mask_dram is not None:
                        eng.wait_ge(s_msk.h, mskv)
                        if c >= 1:
                            eng.wait_ge(s_ev.h, ev_ends[c - 1])
                        mrow = bufs[pfx + "mr"]
                        eng.matmul(bufs[pfx + "psm"][:, :], kb.onesf[:],
                                   mrow[0:1, slot * n:(slot + 1) * n],
                                   start=True, stop=True)
                if need_ev:
                    eng.wait_ge(s_ev.h, need_ev)
                for si in range(nsets):
                    w = weights[si]
                    ps = pss[banks[si]]
                    for kt in range(ktiles):
                        kr = krows if kt == ktiles - 1 else 128
                        ins = eng.matmul(
                            ps[:, :],
                            w[0:kr, (kt * mtiles + mt) * 128:(kt * mtiles + mt) * 128 + 128],
                            xv[0:kr, kt * n:(kt + 1) * n],
                            start=(kt == 0), stop=(kt == ktiles - 1))
                    ins.then_inc(s_mm.h, 1)
            kb.on("tensor", fmm)
            s_mm.inc(nsets)
            mmv = s_mm.n
            mm_idx += 1

            otile = ov[:, mt * n:(mt + 1) * n]
            if mode == "silu_mul":
                tslot = mm_idx % 2
                tv = tmp[:, tslot * n:(tslot + 1) * n]

                def fev1(eng, mmv=mmv, b0=banks[0], tv=tv):
                    eng.wait_ge(s_mm.h, mmv)
                    eng.activation(tv, pss[b0][:, :], AF.Silu,
                                   scale=float(act_scale)).then_inc(s_ev.h, 1)
                kb.on("scalar", fev1)
                s_ev.inc(1)
                bank_free_at[banks[0]] = s_ev.n
                ev1 = s_ev.n

                def fev2(eng, ev1=ev1, b1=banks[1], tv=tv, otile=otile,
                         mt=mt, need_st=need_st):
                    eng.wait_ge(s_ev.h, ev1)
                    if mt == 0 and need_st:
                        eng.wait_ge(s_st.h, need_st)
                    eng.tensor_tensor(otile, tv, pss[b1][:, :], ALU.mult).then_inc(s_ev.h, 1)
                kb.on("vector", fev2)
                s_ev.inc(1)
                bank_free_at[banks[1]] = s_ev.n
            elif mode == "residual2":
                o2b2 = bufs[pfx + "o2"]
                o2t = o2b2[:, slot * mtiles * n:(slot + 1) * mtiles * n][:, mt * n:(mt + 1) * n]

                def fev(eng, mmv=mmv, b0=banks[0], otile=otile, rv=rv, mt=mt,
                        resv=resv, r2v=r2v, res2v=res2v, o2t=o2t, need_st=need_st):
                    eng.wait_ge(s_mm.h, mmv)
                    eng.wait_ge(s_res.h, resv)
                    eng.wait_ge(s_res2.h, res2v)
                    if mt == 0 and need_st:
                        eng.wait_ge(s_st.h, need_st)
                    eng.tensor_tensor(otile, rv[:, mt * n:(mt + 1) * n],
                                      pss[b0][:, :], ALU.add)
                    eng.tensor_tensor(o2t, r2v[:, mt * n:(mt + 1) * n],
                                      pss[b0][:, :], ALU.add).then_inc(s_ev.h, 1)
                kb.on("vector", fev)
                s_ev.inc(1)
                bank_free_at[banks[0]] = s_ev.n
            elif mode == "relu_bias":
                def fev(eng, mmv=mmv, b0=banks[0], otile=otile, mt=mt,
                        need_st=need_st):
                    eng.wait_ge(s_mm.h, mmv)
                    if mt == 0 and need_st:
                        eng.wait_ge(s_st.h, need_st)
                    eng.activation(otile, pss[b0][:, :], AF.Relu,
                                   bias=bias_sb[:, mt:mt + 1]).then_inc(s_ev.h, 1)
                kb.on("scalar", fev)
                s_ev.inc(1)
                bank_free_at[banks[0]] = s_ev.n
            elif mode == "moe_accum":
                fvx = kb.fsem["vector"]
                fz1 = fvx.inc()

                def fev(eng, mmv=mmv, b0=banks[0], otile=otile, mt=mt,
                        need_st=need_st, fz1=fz1):
                    eng.wait_ge(s_mm.h, mmv)
                    if mt == 0 and need_st:
                        eng.wait_ge(s_st.h, need_st)
                    eng.tensor_scalar(otile, pss[b0][:, :], bias_sb[:, mt:mt + 1],
                                      None, ALU.add).then_inc(fvx.h, 1)
                    eng.wait_ge(fvx.h, fz1)
                    eng.tensor_tensor(otile, otile, bufs[pfx + "psm"][:, :],
                                      ALU.mult).then_inc(s_ev.h, 1)
                kb.on("vector", fev)
                s_ev.inc(1)
                bank_free_at[banks[0]] = s_ev.n
            elif mode == "residual":
                def fev(eng, mmv=mmv, b0=banks[0], otile=otile, rv=rv, mt=mt,
                        resv=resv, need_st=need_st):
                    eng.wait_ge(s_mm.h, mmv)
                    eng.wait_ge(s_res.h, resv)
                    if mt == 0 and need_st:
                        eng.wait_ge(s_st.h, need_st)
                    if res_scale != 1.0:
                        eng.scalar_tensor_tensor(otile, pss[b0][:, :],
                                                 float(res_scale),
                                                 rv[:, mt * n:(mt + 1) * n],
                                                 ALU.mult, ALU.add).then_inc(s_ev.h, 1)
                    else:
                        eng.tensor_tensor(otile, rv[:, mt * n:(mt + 1) * n],
                                          pss[b0][:, :], ALU.add).then_inc(s_ev.h, 1)
                kb.on("vector", fev)
                s_ev.inc(1)
                bank_free_at[banks[0]] = s_ev.n
            elif mode == "softplus_bias":
                ltmp = bufs[pfx + "l"]
                tslot = mm_idx % 2
                lv = ltmp[:, tslot * n:(tslot + 1) * n]
                tv = tmp[:, tslot * n:(tslot + 1) * n]
                o2b = bufs.get(pfx + "o2")
                o2tile = (o2b[:, slot * mtiles * n:(slot + 1) * mtiles * n]
                          [:, mt * n:(mt + 1) * n]) if o2b is not None else None

                def fv1(eng, mmv=mmv, b0=banks[0], otile=otile, mt=mt,
                        resv=resv, need_st=need_st):
                    eng.wait_ge(s_mm.h, mmv)
                    if mt == 0 and need_st:
                        eng.wait_ge(s_st.h, need_st)
                    eng.tensor_scalar(otile, pss[b0][:, :], bias_sb[:, mt:mt + 1],
                                      None, ALU.add).then_inc(s_ev.h, 1)
                kb.on("vector", fv1)
                s_ev.inc(1)
                bank_free_at[banks[0]] = s_ev.n
                v1 = s_ev.n
                fsc = kb.fsem["scalar"]
                fa1 = fsc.inc(); fa2 = fsc.inc()

                def fa(eng, v1=v1, otile=otile, lv=lv, fa1=fa1, fa2=fa2):
                    eng.wait_ge(s_ev.h, v1)
                    eng.activation(lv, otile, AF.Abs).then_inc(fsc.h, 1)
                    eng.wait_ge(fsc.h, fa1)
                    eng.activation(lv, lv, AF.Exp, scale=-1.0).then_inc(fsc.h, 1)
                    eng.wait_ge(fsc.h, fa2)
                    eng.activation(lv, lv, AF.Ln, bias=1.0).then_inc(s_ev.h, 1)
                kb.on("scalar", fa)
                s_ev.inc(1)
                a1 = s_ev.n
                fvc = kb.fsem["vector"]
                fb1 = fvc.inc(); fb2 = fvc.inc()

                def fv2(eng, a1=a1, otile=otile, lv=lv, tv=tv, o2tile=o2tile,
                        mv=mv, mt=mt, resv=resv, fb1=fb1, fb2=fb2):
                    eng.wait_ge(s_ev.h, a1)
                    if o2tile is not None and mt == 0:
                        eng.wait_ge(s_res.h, resv)
                    eng.tensor_scalar_max(tv, otile, 0.0).then_inc(fvc.h, 1)
                    eng.wait_ge(fvc.h, fb1)
                    ins = eng.tensor_tensor(otile, tv, lv, ALU.add)
                    if o2tile is not None:
                        ins.then_inc(fvc.h, 1)
                        eng.wait_ge(fvc.h, fb2)
                        ins = eng.tensor_tensor(o2tile, otile,
                                                mv[:, mt * n:(mt + 1) * n], ALU.mult)
                    else:
                        fvc.n -= 0
                    ins.then_inc(s_ev.h, 1)
                kb.on("vector", fv2)
                s_ev.inc(1)
            else:
                func = {"copy_bf16": AF.Copy, "silu_bf16": AF.Silu,
                        "copy_f32": AF.Copy,
                        "sigmoid_bf16": AF.Sigmoid}[mode]

                def fev(eng, mmv=mmv, b0=banks[0], otile=otile, func=func,
                        mt=mt, need_st=need_st):
                    eng.wait_ge(s_mm.h, mmv)
                    if mt == 0 and need_st:
                        eng.wait_ge(s_st.h, need_st)
                    eng.activation(otile, pss[b0][:, :], func).then_inc(s_ev.h, 1)
                kb.on("scalar", fev)
                s_ev.inc(1)
                bank_free_at[banks[0]] = s_ev.n
        evv = s_ev.n
        ev_ends.append(evv)

        o2b = bufs.get(pfx + "o2")
        o2v = (o2b[:, slot * mtiles * n:(slot + 1) * mtiles * n]
               if o2b is not None else None)

        def fst(eng, c=c, ov=ov, evv=evv, o2v=o2v):
            eng.wait_ge(s_ev.h, evv)
            if accum_store:
                eng.dma_start(
                    out=out_dram[:, c * n:(c + 1) * n].rearrange("(t p) n -> p t n", p=128),
                    in_=ov.rearrange("p (t n) -> p t n", t=mtiles),
                    accum_op=ALU.add,
                ).then_inc(s_st.h, 16)
            else:
                eng.dma_start(
                    out=out_dram[:, c * n:(c + 1) * n].rearrange("(t p) n -> p t n", p=128),
                    in_=ov.rearrange("p (t n) -> p t n", t=mtiles),
                ).then_inc(s_st.h, 16)
            if o2v is not None:
                eng.dma_start(
                    out=out2_dram[:, c * n:(c + 1) * n].rearrange("(t p) n -> p t n", p=128),
                    in_=o2v.rearrange("p (t n) -> p t n", t=mtiles),
                ).then_inc(s_st.h, 16)
        kb.on("gpsimd", fst)
        s_st.inc(16 if out2_dram is None else 32)

    stn = s_st.n

    def fdrain(eng, stn=stn):
        eng.wait_ge(s_st.h, stn)
    kb.on("gpsimd", fdrain)


DEQ = 1.0 / 127.0


def make_ln_bufs(kb, sc, dim, with_f32=False, nchunk=NCHUNK, pfx="ln",
                 out_dtype=BF16, with_res=False, with_sum=False):
    pt = dim // 128
    b = {
        "lnx": kb.sb(pfx + "x", [128, 2 * pt * nchunk], F32, sc),
        "lnsq": kb.sb(pfx + "sq", [128, 2 * nchunk], F32, sc),
        "lntmp": kb.sb(pfx + "tmp", [128, nchunk], F32, sc),
        "lnxn": kb.sb(pfx + "xn", [128, 2 * pt * nchunk], out_dtype, sc),
        "lnrows": kb.sb(pfx + "rows", [1, 4 * nchunk], F32, sc),
        "onesf": kb.onesf, "onescol": kb.onescol,
        "ps_stat": kb.psum(pfx + "ps_stat", [128, 2 * nchunk], F32, sc),
        "ps_b1": kb.psum(pfx + "ps_b1", [128, nchunk], F32, sc),
        "ps_b2": kb.psum(pfx + "ps_b2", [128, nchunk], F32, sc),
    }
    if with_f32:
        b["lnxnf"] = kb.sb(pfx + "xnf", [128, 2 * pt * nchunk], F32, sc)
    if with_res:
        b["lnres"] = kb.sb(pfx + "res", [128, 2 * pt * nchunk], F32, sc)
    if with_sum:
        b["lnsum"] = kb.sb(pfx + "sum", [128, 2 * pt * nchunk], F32, sc)
    return b


def emit_block_l0ffn(kb, x_in, x_out, wgT, wuT, wdT, xn_bf, h_bf):
    nc = kb.nc
    sc1 = kb.scope()
    wg_sb = kb.sb("l0f_wg", [128, 8 * 24 * 128], BF16, sc1)
    wu_sb = kb.sb("l0f_wu", [128, 8 * 24 * 128], BF16, sc1)
    wstage = kb.sb("l0f_wst", [128, 8 * 128], I32, sc1)
    s_wld = kb.sem(); s_wcvt = kb.sem()
    emit_weight_load_g(kb, wgT, wg_sb, 8, 24, wstage, s_wld, s_wcvt)
    emit_weight_load_g(kb, wuT, wu_sb, 8, 24, wstage, s_wld, s_wcvt)
    w_ready = (s_wcvt, s_wcvt.n)

    sc2 = kb.scope()
    lnb = make_ln_bufs(kb, sc2, D, pfx="l0f_ln")
    emit_ln_phase(kb, lnb, x_in, xn_bf, D, eps=1e-5)
    sc2.close()
    kb.barrier()

    sc3 = kb.scope()
    gub = {
        "gux": kb.sb("l0f_gx", [128, 2 * 8 * NCHUNK], BF16, sc3),
        "guo": kb.sb("l0f_go", [128, 2 * 24 * NCHUNK], BF16, sc3),
        "gut": kb.sb("l0f_gt", [128, 2 * NCHUNK], F32, sc3),
        "gups": [kb.psum(f"l0f_gps{i}", [128, NCHUNK], F32, sc3) for i in range(4)],
    }
    emit_proj_phase(kb, gub, "gu", [wg_sb, wu_sb], xn_bf, 8, h_bf, 24,
                    mode="silu_mul", w_ready=w_ready, act_scale=DEQ)
    sc3.close()
    sc1.close()
    kb.barrier()

    sc4 = kb.scope()
    wd_sb = kb.sb("l0f_wd", [128, 24 * 8 * 128], BF16, sc4)
    wstage2 = kb.sb("l0f_wst2", [128, 8 * 128], I32, sc4)
    s_wld2 = kb.sem(); s_wcvt2 = kb.sem()
    emit_weight_load_g(kb, wdT, wd_sb, 24, 8, wstage2, s_wld2, s_wcvt2)
    dnb = {
        "dnx": kb.sb("l0f_dx", [128, 2 * 24 * NCHUNK], BF16, sc4),
        "dno": kb.sb("l0f_do", [128, 2 * 8 * NCHUNK], F32, sc4),
        "dnt": kb.sb("l0f_dt", [128, 2 * NCHUNK], F32, sc4),
        "dnr": kb.sb("l0f_dr", [128, 2 * 8 * NCHUNK], F32, sc4),
        "dnps": [kb.psum(f"l0f_dps{i}", [128, NCHUNK], F32, sc4) for i in range(2)],
    }
    emit_proj_phase(kb, dnb, "dn", [wd_sb], h_bf, 24, x_out, 8,
                    mode="residual", res_dram=x_in, w_ready=(s_wcvt2, s_wcvt2.n),
                    res_scale=DEQ * DEQ)
    sc4.close()
    kb.barrier()


def emit_weight_load_g(kb, w_dram, wq_sb, ktiles, mtiles, stage_sb, sem_ld,
                       sem_cvt, scale=None, krows=128):
    assert scale is None
    for kt in range(ktiles):
        kr = krows if kt == ktiles - 1 else 128

        def fl(eng, kt=kt, kr=kr):
            eng.dma_start(
                out=wq_sb[0:kr, kt * mtiles * 128:(kt + 1) * mtiles * 128],
                in_=w_dram[kt * 128:kt * 128 + kr, 0:mtiles * 128],
            ).then_inc(sem_cvt.h, 16)
        kb.on("gpsimd", fl)
        sem_cvt.inc(16)


def emit_l0_tokproj(kb, bufs, xn_bf, wv_sb, wk_sb, wb_sb, v_tm, k_tm, beta_tm,
                    w_ready):
    ntiles = NTOK // 128
    xb = bufs["tpx"]
    vo = bufs["tpv"]
    ko = bufs["tpk"]
    kf = bufs["tpkf"]
    ksq = bufs["tpksq"]
    bo = bufs["tpb"]
    rn = bufs["tprn"]
    psv = bufs["tp_psv"]
    psk = bufs["tp_psk"]
    psb = bufs["tp_psb"]
    s_ld = kb.sem(); s_mm = kb.sem(); s_ev = kb.sem(); s_st = kb.sem()
    b_st = s_st.n; b_ev = s_ev.n

    for tt in range(ntiles):
        slot = tt % 2

        def flx(eng, tt=tt, slot=slot):
            if tt >= 2:
                eng.wait_ge(s_st.h, b_st + 48 * (tt - 1))
            eng.dma_start(
                out=xb[:, slot * 1024:(slot + 1) * 1024].rearrange(
                    "p (k c) -> p k c", k=8),
                in_=xn_bf[:, tt * 128:(tt + 1) * 128].rearrange(
                    "(k p) c -> p k c", p=128),
            ).then_inc(s_ld.h, 16)
        kb.on("sync", flx)
        s_ld.inc(16)
        ldv = s_ld.n

        def fmm(eng, tt=tt, slot=slot, ldv=ldv):
            eng.wait_ge(s_ld.h, ldv)
            if tt == 0:
                eng.wait_ge(w_ready[0].h, w_ready[1])
            else:
                eng.wait_ge(s_ev.h, b_ev + 4 * (tt - 1) + 1)
            for kt in range(8):
                st = xb[:, slot * 1024 + kt * 128: slot * 1024 + (kt + 1) * 128]
                b0 = (kt == 0); b1 = (kt == 7)
                eng.matmul(psv[:, 0:512], st, wv_sb[:, kt * 1024:kt * 1024 + 512],
                           start=b0, stop=b1)
                eng.matmul(psv[:, 512:1024], st, wv_sb[:, kt * 1024 + 512:(kt + 1) * 1024],
                           start=b0, stop=b1)
                eng.matmul(psk[:, 0:512], st, wk_sb[:, kt * 1024:kt * 1024 + 512],
                           start=b0, stop=b1)
                eng.matmul(psk[:, 512:1024], st, wk_sb[:, kt * 1024 + 512:(kt + 1) * 1024],
                           start=b0, stop=b1)
                ins = eng.matmul(psb[:, 0:8], st, wb_sb[:, kt * 8:(kt + 1) * 8],
                                 start=b0, stop=b1)
            ins.then_inc(s_mm.h, 1)
        kb.on("tensor", fmm)
        s_mm.inc(1)
        mmv = s_mm.n

        vv = vo[:, slot * 1024:(slot + 1) * 1024]
        kv = ko[:, slot * 1024:(slot + 1) * 1024]
        bv = bo[:, slot * 8:(slot + 1) * 8]
        kfv = kf[:, slot * 1024:(slot + 1) * 1024]
        ksqv = ksq[:, slot * 1024:(slot + 1) * 1024]
        r0 = rn[:, slot * 24:slot * 24 + 8]
        r1 = rn[:, slot * 24 + 8:slot * 24 + 16]
        r2 = rn[:, slot * 24 + 16:slot * 24 + 24]

        def fsc(eng, mmv=mmv, vv=vv, bv=bv, kfv=kfv, ksqv=ksqv, tt=tt):
            eng.wait_ge(s_mm.h, mmv)
            if tt >= 2:
                eng.wait_ge(s_ev.h, b_ev + 4 * (tt - 1))
                eng.wait_ge(s_st.h, b_st + 48 * (tt - 1))
            eng.activation(vv[:, 0:512], psv[:, 0:512], AF.Silu)
            eng.activation(vv[:, 512:1024], psv[:, 512:1024], AF.Silu)
            eng.activation(bv, psb[:, 0:8], AF.Sigmoid)
            eng.copy(kfv, psk[:, :])
            eng.activation(ksqv, psk[:, :], AF.Square).then_inc(s_ev.h, 1)
        kb.on("scalar", fsc)
        s_ev.inc(1)
        evv = s_ev.n

        def fve(eng, evv=evv, ksqv=ksqv, r0=r0):
            eng.wait_ge(s_ev.h, evv)
            eng.tensor_reduce(r0, ksqv.rearrange("p (h c) -> p h c", h=8),
                              AX.X, ALU.add).then_inc(s_ev.h, 1)
        kb.on("vector", fve)
        s_ev.inc(1)
        ev2 = s_ev.n

        def fsc2(eng, ev2=ev2, r0=r0, r1=r1):
            eng.wait_ge(s_ev.h, ev2)
            eng.activation(r1, r0, AF.Sqrt).then_inc(s_ev.h, 1)
        kb.on("scalar", fsc2)
        s_ev.inc(1)
        ev3 = s_ev.n

        fs = kb.fsem["vector"]
        f1 = fs.inc(); f2 = fs.inc()

        def fve2(eng, ev3=ev3, kv=kv, kfv=kfv, r1=r1, r2=r2, f1=f1, f2=f2):
            eng.wait_ge(s_ev.h, ev3)
            eng.tensor_scalar_max(r1, r1, 1e-12).then_inc(fs.h, 1)
            eng.wait_ge(fs.h, f1)
            eng.reciprocal(r2, r1).then_inc(fs.h, 1)
            eng.wait_ge(fs.h, f2)
            for h in range(8):
                ins = eng.tensor_scalar(kv[:, h * 128:(h + 1) * 128],
                                        kfv[:, h * 128:(h + 1) * 128],
                                        r2[:, h:h + 1], None, ALU.mult)
            ins.then_inc(s_ev.h, 1)
        kb.on("vector", fve2)
        s_ev.inc(1)
        ev4 = s_ev.n

        def fst(eng, tt=tt, vv=vv, kv=kv, bv=bv, ev4=ev4):
            eng.wait_ge(s_ev.h, ev4)
            eng.dma_start(out=v_tm[tt * 128:(tt + 1) * 128, :], in_=vv).then_inc(s_st.h, 16)
            eng.dma_start(out=k_tm[tt * 128:(tt + 1) * 128, :], in_=kv).then_inc(s_st.h, 16)
            eng.dma_start(out=beta_tm[tt * 128:(tt + 1) * 128, :], in_=bv).then_inc(s_st.h, 16)
        kb.on("gpsimd", fst)
        s_st.inc(48)

    stn = s_st.n

    def fdrain(eng, stn=stn):
        eng.wait_ge(s_st.h, stn)
    kb.on("gpsimd", fdrain)

def emit_l0_scan(kb, bufs, q_dm, k_tm, v_tm, beta_tm, o_dm):
    H = 8
    W = 1024
    qt = bufs["sqt"]; ktm = bufs["sktm"]; vtm = bufs["svtm"]; bt = bufs["sbt"]
    KTD = bufs["sKTD"]; Bw = bufs["sB"]; BTw = bufs["sBT"]
    Pb = bufs["sP"]; Qb = bufs["sQ"]; MTb = bufs["sMT"]
    RHS = bufs["sRHS"]; Uw = bufs["sU"]; LTw = bufs["sLT"]
    o1w = bufs["so1"]; Bf = bufs["sBf"]
    Sf = bufs["sSf"]; Sb = bufs["sSb"]; Ow = bufs["sO"]
    idb8 = bufs["idb8"]; mneg8 = bufs["mneg8"]; mu8 = bufs["mu8"]; idb = bufs["idb"]
    pT = bufs["s_pT"]; pA = bufs["s_pA"]; pB = bufs["s_pB"]; pC = bufs["s_pC"]
    s_ld = kb.sem(); s_mm = kb.sem(); s_dn = kb.sem(); s_st = kb.sem()
    b_st = s_st.n

    NCH = T_SEQ // 128

    def mmstage(emit_mms, wait_dn):
        def f(eng, wait_dn=wait_dn, emit_mms=emit_mms):
            if wait_dn:
                eng.wait_ge(s_dn.h, wait_dn)
            ins = emit_mms(eng)
            ins.then_inc(s_mm.h, 1)
        kb.on("tensor", f)
        s_mm.inc(1)
        return s_mm.n

    def evstage(engine, emit_ev, wait_mm, extra_dn=0):
        def f(eng, wait_mm=wait_mm, emit_ev=emit_ev, extra_dn=extra_dn):
            if wait_mm:
                eng.wait_ge(s_mm.h, wait_mm)
            if extra_dn:
                eng.wait_ge(s_dn.h, extra_dn)
            ins = emit_ev(eng)
            ins.then_inc(s_dn.h, 1)
        kb.on(engine, f)
        s_dn.inc(1)
        return s_dn.n

    chunk_done = []
    for b in range(B_LOC):
        def fz(eng, b=b):
            eng.memset(Sf[:], 0.0)
            eng.memset(Sb[:], 0.0).then_inc(s_dn.h, 1)
        kb.on("vector", fz)
        s_dn.inc(1)
        s_after_reset = s_dn.n

        for ch in range(NCH):
            ci = b * NCH + ch
            slot = ci % 2
            col0 = b * T_SEQ + ch * 128
            prev_done = s_dn.n

            guard = chunk_done[ci - 2] if ci >= 2 else 0

            def fld(eng, ci=ci, slot=slot, col0=col0, guard=guard):
                if guard:
                    eng.wait_ge(s_dn.h, guard)
                    eng.wait_ge(s_st.h, b_st + 16 * (ci - 1))
                eng.dma_start(out=qt[:, slot * W:(slot + 1) * W].rearrange(
                                  "p (h c) -> p h c", h=H),
                              in_=q_dm[:, col0:col0 + 128].rearrange(
                                  "(h p) c -> p h c", p=128)).then_inc(s_ld.h, 16)
                r0 = col0
                eng.dma_start(out=ktm[:, slot * W:(slot + 1) * W],
                              in_=k_tm[r0:r0 + 128, :]).then_inc(s_ld.h, 16)
                eng.dma_start(out=vtm[:, slot * W:(slot + 1) * W],
                              in_=v_tm[r0:r0 + 128, :]).then_inc(s_ld.h, 16)
                eng.dma_start(out=bt[:, slot * 8:(slot + 1) * 8],
                              in_=beta_tm[r0:r0 + 128, :]).then_inc(s_ld.h, 16)
            kb.on("sync", fld)
            s_ld.inc(64)
            ldv = s_ld.n
            qv = qt[:, slot * W:(slot + 1) * W]
            kv = ktm[:, slot * W:(slot + 1) * W]
            vv = vtm[:, slot * W:(slot + 1) * W]
            bv = bt[:, slot * 8:(slot + 1) * 8]

            def mm1(eng, kv=kv, ldv=ldv):
                eng.wait_ge(s_ld.h, ldv)
                for h in range(H):
                    ins = eng.transpose(pT[:, h * 128:(h + 1) * 128],
                                        kv[:, h * 128:(h + 1) * 128], idb[:])
                return ins
            m = mmstage(mm1, prev_done)
            d_ktd = evstage("scalar", lambda eng: eng.copy(KTD[:], pT[:, :]), m)

            def mm2(eng):
                for h in range(H):
                    ins = eng.matmul(pA[:, h * 128:(h + 1) * 128],
                                     KTD[:, h * 128:(h + 1) * 128],
                                     KTD[:, h * 128:(h + 1) * 128],
                                     start=True, stop=True)
                return ins
            m = mmstage(mm2, d_ktd)

            fs = kb.fsem["vector"]
            fB = fs.inc()

            def evB(eng, bv=bv, fB=fB):
                for h in range(H):
                    ins = eng.tensor_scalar(Bf[:, h * 128:(h + 1) * 128],
                                            pA[:, h * 128:(h + 1) * 128],
                                            bv[:, h:h + 1], None, ALU.mult)
                ins.then_inc(fs.h, 1)
                eng.wait_ge(fs.h, fB)
                return eng.tensor_tensor(Bw[:], Bf[:], mneg8[:], ALU.mult)
            d_B = evstage("vector", evB, m)

            def mm3(eng):
                for h in range(H):
                    ins = eng.transpose(pT[:, h * 128:(h + 1) * 128],
                                        Bw[:, h * 128:(h + 1) * 128], idb[:])
                return ins
            m = mmstage(mm3, d_B)
            d_bt = evstage("scalar", lambda eng: eng.copy(BTw[:], pT[:, :]), m)
            d_mt = evstage("vector",
                           lambda eng: eng.tensor_tensor(MTb[0][:], idb8[:], BTw[:], ALU.add),
                           0, extra_dn=d_bt)

            Pcur, Qcur = Bw, BTw
            d_p, d_q = d_B, d_bt
            mtc = 0
            for j in range(1, 7):
                Pnew = Pb[j % 2]; Qnew = Qb[j % 2]

                def mmP(eng, Pcur=Pcur, Qcur=Qcur):
                    for h in range(H):
                        ins = eng.matmul(pB[:, h * 128:(h + 1) * 128],
                                         Qcur[:, h * 128:(h + 1) * 128],
                                         Pcur[:, h * 128:(h + 1) * 128],
                                         start=True, stop=True)
                    return ins
                m = mmstage(mmP, max(d_p, d_q))
                d_pn = evstage("scalar", lambda eng, Pnew=Pnew: eng.copy(Pnew[:], pB[:, :]), m)
                if j < 6:
                    def mmQ(eng, Pcur=Pcur, Qcur=Qcur):
                        for h in range(H):
                            ins = eng.matmul(pC[:, h * 128:(h + 1) * 128],
                                             Pcur[:, h * 128:(h + 1) * 128],
                                             Qcur[:, h * 128:(h + 1) * 128],
                                             start=True, stop=True)
                        return ins
                    m = mmstage(mmQ, max(d_p, d_q))
                    d_qn = evstage("scalar", lambda eng, Qnew=Qnew: eng.copy(Qnew[:], pC[:, :]), m)
                else:
                    d_qn = d_q

                MTold = MTb[mtc]; MTnew = MTb[1 - mtc]; mtc = 1 - mtc

                def mmM(eng, Pnew=Pnew, MTold=MTold):
                    for h in range(H):
                        ins = eng.matmul(pA[:, h * 128:(h + 1) * 128],
                                         Pnew[:, h * 128:(h + 1) * 128],
                                         MTold[:, h * 128:(h + 1) * 128],
                                         start=True, stop=True)
                    return ins
                m = mmstage(mmM, max(d_pn, d_mt))
                d_mt = evstage("vector",
                               lambda eng, MTnew=MTnew, MTold=MTold:
                               eng.tensor_tensor(MTnew[:], MTold[:], pA[:, :], ALU.add), m)
                Pcur, Qcur = Pnew, Qnew
                d_p, d_q = d_pn, d_qn

            MT = MTb[mtc]

            def mmKS(eng):
                for h in range(H):
                    ins = eng.matmul(pB[:, h * 128:(h + 1) * 128],
                                     KTD[:, h * 128:(h + 1) * 128],
                                     Sb[:, h * 128:(h + 1) * 128],
                                     start=True, stop=True)
                return ins
            state_dep = s_after_reset if ch == 0 else d_state
            m = mmstage(mmKS, max(state_dep, d_p))

            fR = kb.fsem["vector"].inc()

            def evRHS(eng, bv=bv, vv=vv, fR=fR):
                for h in range(H):
                    ins = eng.tensor_scalar(o1w[:, h * 128:(h + 1) * 128],
                                            pB[:, h * 128:(h + 1) * 128],
                                            bv[:, h:h + 1], None, ALU.mult)
                ins.then_inc(kb.fsem["vector"].h, 1)
                eng.wait_ge(kb.fsem["vector"].h, fR)
                for h in range(H):
                    ins = eng.scalar_tensor_tensor(
                        RHS[:, h * 128:(h + 1) * 128],
                        vv[:, h * 128:(h + 1) * 128],
                        bv[:, h:h + 1],
                        o1w[:, h * 128:(h + 1) * 128],
                        ALU.mult, ALU.subtract)
                return ins
            d_rhs = evstage("vector", evRHS, m)

            def mmU(eng, MT=MT):
                for h in range(H):
                    ins = eng.matmul(pC[:, h * 128:(h + 1) * 128],
                                     MT[:, h * 128:(h + 1) * 128],
                                     RHS[:, h * 128:(h + 1) * 128],
                                     start=True, stop=True)
                return ins
            m = mmstage(mmU, max(d_rhs, d_mt))
            d_u = evstage("scalar", lambda eng: eng.copy(Uw[:], pC[:, :]), m)

            def mmLT(eng, qv=qv):
                for h in range(H):
                    ins = eng.matmul(pA[:, h * 128:(h + 1) * 128],
                                     KTD[:, h * 128:(h + 1) * 128],
                                     qv[:, h * 128:(h + 1) * 128],
                                     start=True, stop=True)
                return ins
            m = mmstage(mmLT, d_mt)
            d_lt = evstage("vector",
                           lambda eng: eng.tensor_tensor(LTw[:], pA[:, :], mu8[:], ALU.mult), m)

            def mmO(eng, qv=qv):
                for h in range(H):
                    eng.matmul(pB[:, h * 128:(h + 1) * 128],
                               Sb[:, h * 128:(h + 1) * 128],
                               qv[:, h * 128:(h + 1) * 128],
                               start=True, stop=False)
                    ins = eng.matmul(pB[:, h * 128:(h + 1) * 128],
                                     Uw[:, h * 128:(h + 1) * 128],
                                     LTw[:, h * 128:(h + 1) * 128],
                                     start=False, stop=True)
                return ins
            m = mmstage(mmO, max(d_u, d_lt, d_rhs))
            oslot = ci % 2
            ov = Ow[:, oslot * W:(oslot + 1) * W]

            def evO(eng, ov=ov, ci=ci):
                if ci >= 2:
                    eng.wait_ge(s_st.h, b_st + 16 * (ci - 1))
                return eng.copy(ov, pB[:, :])
            d_o = evstage("scalar", evO, m)

            def fso(eng, ov=ov, col0=col0, d_o=d_o):
                eng.wait_ge(s_dn.h, d_o)
                eng.dma_start(out=o_dm[:, col0:col0 + 128].rearrange(
                                  "(h p) c -> p h c", p=128),
                              in_=ov.rearrange("p (h c) -> p h c", h=H)).then_inc(s_st.h, 16)
            kb.on("gpsimd", fso)
            s_st.inc(16)

            def mmS(eng, kv=kv):
                for h in range(H):
                    ins = eng.matmul(pC[:, h * 128:(h + 1) * 128],
                                     kv[:, h * 128:(h + 1) * 128],
                                     Uw[:, h * 128:(h + 1) * 128],
                                     start=True, stop=True)
                return ins
            m = mmstage(mmS, max(d_u, d_o))

            fS = kb.fsem["vector"].inc()

            def evS(eng, fS=fS):
                eng.tensor_tensor(Sf[:], Sf[:], pC[:, :], ALU.add).then_inc(
                    kb.fsem["vector"].h, 1)
                eng.wait_ge(kb.fsem["vector"].h, fS)
                return eng.tensor_copy(Sb[:], Sf[:])
            d_state = evstage("vector", evS, m)
            chunk_done.append(d_state)

    stn = s_st.n

    def fdrain(eng, stn=stn):
        eng.wait_ge(s_st.h, stn)
    kb.on("gpsimd", fdrain)

def emit_l0_wload_tok(kb, sc, WvD, WkD, WbD):
    wv_sb = kb.sb("l0a_wv", [128, 8 * 1024], BF16, sc)
    wk_sb = kb.sb("l0a_wk", [128, 8 * 1024], BF16, sc)
    wb_sb = kb.sb("l0a_wb", [128, 8 * 8], BF16, sc)
    stage = kb.sb("l0a_wst", [128, 2 * 1024], F32, sc)
    s_ld = kb.sem(); s_cvt = kb.sem()
    items = [(WvD, wv_sb, 1024), (WkD, wk_sb, 1024), (WbD, wb_sb, 8)]
    for (wd, wsb, ncol) in items:
        for kt in range(8):
            def fl(eng, wd=wd, wsb=wsb, kt=kt, ncol=ncol):
                eng.dma_start(out=wsb[:, kt * ncol:(kt + 1) * ncol],
                              in_=wd[kt * 128:(kt + 1) * 128, :]).then_inc(s_cvt.h, 16)
            kb.on("gpsimd", fl)
            s_cvt.inc(16)
    return wv_sb, wk_sb, wb_sb, (s_cvt, s_cvt.n)


def emit_const_load(kb, sc, cdram, shapes):
    out = {}
    s_ld = kb.sem("cld")
    off = 0
    tiles = {}
    for name, cols in shapes:
        t = kb.sb("c_" + name, [128, cols], BF16, sc)
        tiles[name] = (t, off, cols)
        off += cols
    for name, (t, o, cols) in tiles.items():
        def fl(eng, t=t, o=o, cols=cols):
            eng.dma_start(out=t[:], in_=cdram[:, o:o + cols]).then_inc(s_ld.h, 16)
        kb.on("sync", fl)
        s_ld.inc(16)
        out[name] = t
    return out, (s_ld, s_ld.n)


def emit_block_l0attn(kb, x_in, x_out, WqD, WkD, WvD, WbD, lnwD, lnbD, cdram,
                      xn_bf, q_dm, k_tm, v_tm, beta_tm, o_dm):
    scC = kb.scope()
    consts, c_ready = emit_const_load(kb, scC, cdram,
                                      [("idb", 128), ("idb8", 1024),
                                       ("mneg8", 1024), ("mu8", 1024)])
    lnw_sb = kb.sb("l0a_lnw", [128, 8], F32, scC)
    lnb_sb = kb.sb("l0a_lnb", [128, 8], F32, scC)
    s_p = kb.sem("l0a_sp")

    def flp(eng, cw=c_ready):
        eng.dma_start(out=lnw_sb[:], in_=lnwD[:, :]).then_inc(s_p.h, 16)
        eng.dma_start(out=lnb_sb[:], in_=lnbD[:, :]).then_inc(s_p.h, 16)
        eng.wait_ge(s_p.h, 32)
        eng.wait_ge(cw[0].h, cw[1])
    kb.on("sync", flp)
    s_p.inc(32)

    sc1 = kb.scope()
    lnb1 = make_ln_bufs(kb, sc1, D, pfx="l0a_ln")
    emit_ln_phase(kb, lnb1, x_in, xn_bf, D, eps=1e-5)
    sc1.close()
    kb.barrier()

    sc2 = kb.scope()
    wq_sb = kb.sb("l0a_wq", [128, 8 * 8 * 128], BF16, sc2)
    wstage = kb.sb("l0a_wqst", [128, 8 * 128], F32, sc2)
    s_wld = kb.sem(); s_wcvt = kb.sem()
    emit_weight_load_g(kb, WqD, wq_sb, 8, 8, wstage, s_wld, s_wcvt)
    qb = {
        "qx": kb.sb("l0a_qx", [128, 2 * 8 * NCHUNK], BF16, sc2),
        "qo": kb.sb("l0a_qo", [128, 2 * 8 * NCHUNK], BF16, sc2),
        "qt": kb.sb("l0a_qt", [128, 2 * NCHUNK], F32, sc2),
        "qps": [kb.psum(f"l0a_qps{i}", [128, NCHUNK], F32, sc2) for i in range(2)],
    }
    emit_proj_phase(kb, qb, "q", [wq_sb], xn_bf, 8, q_dm, 8,
                    mode="copy_bf16", w_ready=(s_wcvt, s_wcvt.n))
    sc2.close()
    kb.barrier()

    sc3 = kb.scope()
    wv_sb, wk_sb, wb_sb, w_ready = emit_l0_wload_tok(kb, sc3, WvD, WkD, WbD)
    tpb = {
        "tpx": kb.sb("l0a_tpx", [128, 2 * 1024], BF16, sc3),
        "tpv": kb.sb("l0a_tpv", [128, 2 * 1024], BF16, sc3),
        "tpk": kb.sb("l0a_tpk", [128, 2 * 1024], BF16, sc3),
        "tpkf": kb.sb("l0a_tpkf", [128, 2 * 1024], F32, sc3),
        "tpksq": kb.sb("l0a_tpksq", [128, 2 * 1024], F32, sc3),
        "tpb": kb.sb("l0a_tpb", [128, 2 * 8], F32, sc3),
        "tprn": kb.sb("l0a_tprn", [128, 2 * 24], F32, sc3),
        "tp_psv": kb.psum("l0a_psv", [128, 1024], F32, sc3),
        "tp_psk": kb.psum("l0a_psk", [128, 1024], F32, sc3),
        "tp_psb": kb.psum("l0a_psb", [128, 512], F32, sc3),
    }
    emit_l0_tokproj(kb, tpb, xn_bf, wv_sb, wk_sb, wb_sb, v_tm, k_tm, beta_tm,
                    w_ready)
    sc3.close()
    kb.barrier()

    sc4 = kb.scope()
    W = 1024
    sbufs = {
        "sqt": kb.sb("l0a_sqt", [128, 2 * W], BF16, sc4),
        "sktm": kb.sb("l0a_sktm", [128, 2 * W], BF16, sc4),
        "svtm": kb.sb("l0a_svtm", [128, 2 * W], BF16, sc4),
        "sbt": kb.sb("l0a_sbt", [128, 2 * 8], F32, sc4),
        "sKTD": kb.sb("l0a_sKTD", [128, W], BF16, sc4),
        "sB": kb.sb("l0a_sB", [128, W], BF16, sc4),
        "sBT": kb.sb("l0a_sBT", [128, W], BF16, sc4),
        "sP": [kb.sb(f"l0a_sP{i}", [128, W], BF16, sc4) for i in range(2)],
        "sQ": [kb.sb(f"l0a_sQ{i}", [128, W], BF16, sc4) for i in range(2)],
        "sMT": [kb.sb(f"l0a_sMT{i}", [128, W], BF16, sc4) for i in range(2)],
        "sRHS": kb.sb("l0a_sRHS", [128, W], BF16, sc4),
        "sU": kb.sb("l0a_sU", [128, W], BF16, sc4),
        "sLT": kb.sb("l0a_sLT", [128, W], BF16, sc4),
        "so1": kb.sb("l0a_so1", [128, W], F32, sc4),
        "sBf": kb.sb("l0a_sBf", [128, W], F32, sc4),
        "sSf": kb.sb("l0a_sSf", [128, W], F32, sc4),
        "sSb": kb.sb("l0a_sSb", [128, W], BF16, sc4),
        "sO": kb.sb("l0a_sO", [128, 2 * W], F32, sc4),
        "idb8": consts["idb8"], "mneg8": consts["mneg8"],
        "mu8": consts["mu8"], "idb": consts["idb"],
        "s_pT": kb.psum("l0a_pT", [128, W], BF16, sc4),
        "s_pA": kb.psum("l0a_pA", [128, W], F32, sc4),
        "s_pB": kb.psum("l0a_pB", [128, W], F32, sc4),
        "s_pC": kb.psum("l0a_pC", [128, W], F32, sc4),
    }
    emit_l0_scan(kb, sbufs, q_dm, k_tm, v_tm, beta_tm, o_dm)
    sc4.close()
    kb.barrier()

    sc5 = kb.scope()
    lnb2 = make_ln_bufs(kb, sc5, D, pfx="l0a_oln", out_dtype=F32, with_res=True)
    emit_ln_phase(kb, lnb2, o_dm, x_out, D, gamma_sb=lnw_sb, beta_sb=lnb_sb,
                  eps=1e-5, res_dram=x_in)
    sc5.close()
    scC.close()
    kb.barrier()


def l0attn_consts():
    idb = np.eye(128, dtype=np.float32)
    ml = np.tril(np.full((128, 128), -1.0, np.float32), -1)
    mu = np.triu(np.ones((128, 128), np.float32), 0)
    c = np.concatenate([idb] + [idb] * 8 + [ml] * 8 + [mu] * 8, axis=1)
    return c.astype(ml_dtypes.bfloat16)


def emit_mamba_conv(kb, bufs, xz_dm, xc_dm, convw_sb, convb_sb, w_ready):
    xs = bufs["cvx"]
    acc = bufs["cva"]
    oc = bufs["cvo"]
    s_ld = kb.sem(); s_ev = kb.sem(); s_ac = kb.sem(); s_st = kb.sem()
    b_st = s_st.n; b_ac = s_ac.n
    T = T_SEQ
    fvc = kb.fsem["vector"]
    idx = 0
    for dt in range(16):
        for b in range(B_LOC):
            slot = idx % 2
            xv = xs[:, slot * (3 + T):(slot + 1) * (3 + T)]
            av = acc[:, slot * T:(slot + 1) * T]
            ov = oc[:, slot * T:(slot + 1) * T]

            def fl(eng, xv=xv, dt=dt, b=b, idx=idx):
                if idx >= 2:
                    eng.wait_ge(s_st.h, b_st + 16 * (idx - 1))
                eng.dma_start(out=xv[:, 3:3 + T],
                              in_=xz_dm[dt * 128:(dt + 1) * 128,
                                        b * T:(b + 1) * T]).then_inc(s_ld.h, 16)
            kb.on("sync", fl)
            s_ld.inc(16)
            ldv = s_ld.n

            f1 = fvc.inc(); f2 = fvc.inc(); f3 = fvc.inc()

            def fc(eng, xv=xv, av=av, dt=dt, ldv=ldv, f1=f1, f2=f2, f3=f3,
                   idx=idx):
                eng.memset(xv[:, 0:3], 0.0)
                eng.wait_ge(s_ld.h, ldv)
                if idx == 0:
                    eng.wait_ge(w_ready[0].h, w_ready[1])
                if idx >= 2:
                    eng.wait_ge(s_ac.h, b_ac + idx - 1)
                eng.tensor_scalar(av, xv[:, 0:T], convw_sb[:, dt * 4:dt * 4 + 1],
                                  None, ALU.mult).then_inc(fvc.h, 1)
                eng.wait_ge(fvc.h, f1)
                eng.scalar_tensor_tensor(av, xv[:, 1:1 + T],
                                         convw_sb[:, dt * 4 + 1:dt * 4 + 2], av,
                                         ALU.mult, ALU.add).then_inc(fvc.h, 1)
                eng.wait_ge(fvc.h, f2)
                eng.scalar_tensor_tensor(av, xv[:, 2:2 + T],
                                         convw_sb[:, dt * 4 + 2:dt * 4 + 3], av,
                                         ALU.mult, ALU.add).then_inc(fvc.h, 1)
                eng.wait_ge(fvc.h, f3)
                eng.scalar_tensor_tensor(av, xv[:, 3:3 + T],
                                         convw_sb[:, dt * 4 + 3:dt * 4 + 4], av,
                                         ALU.mult, ALU.add).then_inc(s_ev.h, 1)
            kb.on("vector", fc)
            s_ev.inc(1)
            evv = s_ev.n

            def fs(eng, av=av, ov=ov, dt=dt, evv=evv, idx=idx):
                eng.wait_ge(s_ev.h, evv)
                if idx >= 2:
                    eng.wait_ge(s_st.h, b_st + 16 * (idx - 1))
                eng.activation(ov, av, AF.Silu,
                               bias=convb_sb[:, dt:dt + 1]).then_inc(s_ac.h, 1)
            kb.on("scalar", fs)
            s_ac.inc(1)
            acv = s_ac.n

            def fst(eng, ov=ov, dt=dt, b=b, acv=acv):
                eng.wait_ge(s_ac.h, acv)
                eng.dma_start(out=xc_dm[dt * 128:(dt + 1) * 128, b * T:(b + 1) * T],
                              in_=ov).then_inc(s_st.h, 16)
            kb.on("gpsimd", fst)
            s_st.inc(16)
            idx += 1
    stn = s_st.n

    def fdrain(eng, stn=stn):
        eng.wait_ge(s_st.h, stn)
    kb.on("gpsimd", fdrain)


def emit_mamba_scan(kb, bufs, xdbl_dm, dt_dm, dtx_dm, ys_dm, A_sb, a_ready):
    T = T_SEQ
    dtb = bufs["scdt"]
    dxb = bufs["scdx"]
    Bbc = bufs["scB"]
    Cbc = bufs["scC"]
    rowb = bufs["scrow"]
    dAb = bufs["scdA"]
    hb = bufs["sch"]
    bbf = bufs["scb"]
    yb = bufs["scy"]
    psb = bufs["sc_ps"]
    s_ld = kb.sem(); s_row = kb.sem(); s_bc = kb.sem(); s_bce = kb.sem()
    s_ev = kb.sem(); s_ac = kb.sem(); s_st = kb.sem()
    b_st = s_st.n; b_bce = s_bce.n
    fvc = kb.fsem["vector"]
    onesf = kb.onesf
    NC2 = T // 512
    idx = 0
    iter_done = []
    for b in range(B_LOC):
        bbc_consumed = iter_done[-1] if iter_done else 0
        bc_base = s_bc.n
        bce_base = s_bce.n
        for j in range(32):
            tgt = Bbc if j < 16 else Cbc
            nn = j % 16
            rslot = j % 2

            def flrow(eng, j=j, b=b, rslot=rslot, bc_base=bc_base):
                if j >= 2:
                    eng.wait_ge(s_bc.h, bc_base + (j - 1) * NC2)
                eng.dma_start(out=rowb[0:1, rslot * T:(rslot + 1) * T],
                              in_=xdbl_dm[64 + j:65 + j, b * T:(b + 1) * T]
                              ).then_inc(s_row.h, 16)
            kb.on("sync", flrow)
            s_row.inc(16)
            rowv = s_row.n

            def fbc(eng, j=j, rslot=rslot, rowv=rowv, b=b,
                    bbc_consumed=bbc_consumed):
                eng.wait_ge(s_row.h, rowv)
                if j == 0:
                    if b == 0:
                        eng.wait_ge(a_ready[0].h, a_ready[1])
                    if bbc_consumed:
                        eng.wait_ge(s_ev.h, bbc_consumed)
                for cc in range(NC2):
                    if j * NC2 + cc >= 2:
                        eng.wait_ge(s_bce.h, b_bce + (32 * NC2) * b + j * NC2 + cc - 1)
                    eng.matmul(psb[cc % 2][:, :], kb.onesb[:],
                               rowb[0:1, rslot * T + cc * 512:rslot * T + (cc + 1) * 512],
                               start=True, stop=True).then_inc(s_bc.h, 1)
            kb.on("tensor", fbc)
            s_bc.inc(NC2)
            bcv = s_bc.n

            def fbe(eng, j=j, nn=nn, tgt=tgt, bcv=bcv):
                for cc in range(NC2):
                    eng.wait_ge(s_bc.h, bcv - NC2 + cc + 1)
                    eng.copy(tgt[:, nn * T + cc * 512:nn * T + (cc + 1) * 512],
                             psb[cc % 2][:, :]).then_inc(s_bce.h, 1)
            kb.on("scalar", fbe)
            s_bce.inc(NC2)
        bc_done = s_bce.n

        for dt in range(16):
            slot = idx % 2
            dtv = dtb[:, slot * T:(slot + 1) * T]
            dxv = dxb[:, slot * T:(slot + 1) * T]
            yv = yb[:, slot * T:(slot + 1) * T]

            def fld(eng, dt=dt, b=b, slot=slot, idx=idx):
                if idx >= 2:
                    eng.wait_ge(s_st.h, b_st + 16 * (idx - 1))
                eng.dma_start(out=dtb[:, slot * T:(slot + 1) * T],
                              in_=dt_dm[dt * 128:(dt + 1) * 128, b * T:(b + 1) * T]
                              ).then_inc(s_ld.h, 16)
                eng.dma_start(out=dxb[:, slot * T:(slot + 1) * T],
                              in_=dtx_dm[dt * 128:(dt + 1) * 128, b * T:(b + 1) * T]
                              ).then_inc(s_ld.h, 16)
            kb.on("sync", fld)
            s_ld.inc(32)
            ldv = s_ld.n

            for nn in range(16):
                g = idx * 16 + nn
                nslot = g % 2
                dAv = dAb[:, nslot * T:(nslot + 1) * T]
                bv = bbf[:, nslot * T:(nslot + 1) * T]
                hv = hb[:, nslot * T:(nslot + 1) * T]
                g_done = iter_done[g - 2] if g >= 2 else 0

                def fda(eng, dAv=dAv, dtv=dtv, dt=dt, nn=nn, ldv=ldv,
                        g_done=g_done):
                    eng.wait_ge(s_ld.h, ldv)
                    if g_done:
                        eng.wait_ge(s_ev.h, g_done)
                    eng.activation(dAv, dtv, AF.Exp,
                                   scale=A_sb[:, dt * 16 + nn:dt * 16 + nn + 1]
                                   ).then_inc(s_ac.h, 1)
                kb.on("scalar", fda)
                s_ac.inc(1)
                acv = s_ac.n

                def fsc(eng, nn=nn, dt=dt, dAv=dAv, bv=bv, hv=hv, dxv=dxv,
                        yv=yv, acv=acv, g_done=g_done, bce_base=bce_base,
                        idx=idx):
                    eng.wait_ge(s_ac.h, acv)
                    if dt == 0:
                        eng.wait_ge(s_bce.h, bce_base + (17 + nn) * NC2)
                    if nn == 0:
                        if idx >= 2:
                            eng.wait_ge(s_st.h, b_st + 16 * (idx - 1))
                    if g_done:
                        eng.wait_ge(s_ev.h, g_done)
                    eng.tensor_tensor(bv, dxv, Bbc[:, nn * T:(nn + 1) * T],
                                      ALU.mult)
                    eng.tensor_tensor_scan(hv, dAv, bv, 0.0, ALU.mult,
                                           ALU.add)
                    eng.tensor_tensor(hv, hv, Cbc[:, nn * T:(nn + 1) * T],
                                      ALU.mult)
                    if nn == 0:
                        ins = eng.tensor_copy(yv, hv)
                    else:
                        ins = eng.tensor_tensor(yv, yv, hv, ALU.add)
                    ins.then_inc(s_ev.h, 1)
                kb.on("vector", fsc)
                s_ev.inc(1)
                iter_done.append(s_ev.n)
            yv_done = iter_done[-1]

            def fst(eng, dt=dt, b=b, yv=yv, yv_done=yv_done):
                eng.wait_ge(s_ev.h, yv_done)
                eng.dma_start(out=ys_dm[dt * 128:(dt + 1) * 128, b * T:(b + 1) * T],
                              in_=yv).then_inc(s_st.h, 16)
            kb.on("gpsimd", fst)
            s_st.inc(16)
            idx += 1
    stn = s_st.n

    def fdrain(eng, stn=stn):
        eng.wait_ge(s_st.h, stn)
    kb.on("gpsimd", fdrain)

def emit_mamba_gate(kb, bufs, ys_dm, xc_dm, xz_dm, y2_dm, D_sb, d_ready):
    ysb = bufs["mgy"]; xcb = bufs["mgx"]; zb = bufs["mgz"]; szb = bufs["mgs"]
    ob = bufs["mgo"]
    n = NCHUNK
    s_ld = kb.sem(); s_ac = kb.sem(); s_ev = kb.sem(); s_st = kb.sem()
    b_st = s_st.n; b_ev = s_ev.n
    fvc = kb.fsem["vector"]
    nch = NTOK // n
    for c in range(nch):
        slot = c % 2
        ysv = ysb[:, slot * 16 * n:(slot + 1) * 16 * n]
        xcv = xcb[:, slot * 16 * n:(slot + 1) * 16 * n]
        zv = zb[:, slot * 16 * n:(slot + 1) * 16 * n]
        szv = szb[:, slot * 16 * n:(slot + 1) * 16 * n]
        ov = ob[:, slot * 16 * n:(slot + 1) * 16 * n]

        def fl(eng, c=c, ysv=ysv, xcv=xcv, zv=zv):
            if c >= 2:
                eng.wait_ge(s_st.h, b_st + 16 * (c - 1))
            eng.dma_start(out=ysv.rearrange("p (t n) -> p t n", t=16),
                          in_=ys_dm[:, c * n:(c + 1) * n].rearrange(
                              "(t p) n -> p t n", p=128)).then_inc(s_ld.h, 16)
            eng.dma_start(out=xcv.rearrange("p (t n) -> p t n", t=16),
                          in_=xc_dm[:, c * n:(c + 1) * n].rearrange(
                              "(t p) n -> p t n", p=128)).then_inc(s_ld.h, 16)
            eng.dma_start(out=zv.rearrange("p (t n) -> p t n", t=16),
                          in_=xz_dm[2048:4096, c * n:(c + 1) * n].rearrange(
                              "(t p) n -> p t n", p=128)).then_inc(s_ld.h, 16)
        kb.on("sync", fl)
        s_ld.inc(48)
        ldv = s_ld.n

        def fs(eng, c=c, zv=zv, szv=szv, ldv=ldv):
            eng.wait_ge(s_ld.h, ldv)
            if c == 0:
                eng.wait_ge(d_ready[0].h, d_ready[1])
            if c >= 2:
                eng.wait_ge(s_ev.h, b_ev + c - 1)
            eng.activation(szv, zv, AF.Silu).then_inc(s_ac.h, 1)
        kb.on("scalar", fs)
        s_ac.inc(1)
        acv = s_ac.n

        fg = [fvc.inc() for _ in range(16)]

        def fv(eng, c=c, ysv=ysv, xcv=xcv, szv=szv, ov=ov, acv=acv, fg=fg):
            eng.wait_ge(s_ac.h, acv)
            if c >= 2:
                eng.wait_ge(s_st.h, b_st + 16 * (c - 1))
            for t in range(16):
                eng.scalar_tensor_tensor(
                    ov[:, t * n:(t + 1) * n], xcv[:, t * n:(t + 1) * n],
                    D_sb[:, t:t + 1], ysv[:, t * n:(t + 1) * n],
                    ALU.mult, ALU.add).then_inc(fvc.h, 1)
                eng.wait_ge(fvc.h, fg[t])
                ins = eng.tensor_tensor(ov[:, t * n:(t + 1) * n],
                                        ov[:, t * n:(t + 1) * n],
                                        szv[:, t * n:(t + 1) * n], ALU.mult)
            ins.then_inc(s_ev.h, 1)
        kb.on("vector", fv)
        s_ev.inc(1)
        evv = s_ev.n

        def fst(eng, c=c, ov=ov, evv=evv):
            eng.wait_ge(s_ev.h, evv)
            eng.dma_start(out=y2_dm[:, c * n:(c + 1) * n].rearrange(
                              "(t p) n -> p t n", p=128),
                          in_=ov.rearrange("p (t n) -> p t n", t=16)
                          ).then_inc(s_st.h, 16)
        kb.on("gpsimd", fst)
        s_st.inc(16)
    stn = s_st.n

    def fdrain(eng, stn=stn):
        eng.wait_ge(s_st.h, stn)
    kb.on("gpsimd", fdrain)


def emit_block_l1attn(kb, x_in, x_out, inpT, xprojT, dtprojT, outprojT,
                      lng_D, lnb_D, convw_D, convb_D, dtb_D, Alog_D, Dp_D,
                      xn_bf, xz_dm, xc_dm, xdbl_dm, dt_dm, dtx_dm, ys_dm, y2_dm,
                      xn1_f32, xres):
    nc = kb.nc
    scP = kb.scope()
    lng_sb = kb.sb("m_lng", [128, 8], F32, scP)
    lnbt_sb = kb.sb("m_lnb", [128, 8], F32, scP)
    convw_sb = kb.sb("m_cw", [128, 64], F32, scP)
    convb_sb = kb.sb("m_cb", [128, 16], F32, scP)
    dtb_sb = kb.sb("m_dtb", [128, 16], F32, scP)
    Alog_sb = kb.sb("m_alog", [128, 256], F32, scP)
    A_sb = kb.sb("m_A", [128, 256], F32, scP)
    D_sb = kb.sb("m_D", [128, 16], F32, scP)
    s_p = kb.sem("m_sp")

    def flp(eng):
        eng.dma_start(out=lng_sb[:], in_=lng_D[:, :]).then_inc(s_p.h, 16)
        eng.dma_start(out=lnbt_sb[:], in_=lnb_D[:, :]).then_inc(s_p.h, 16)
        eng.dma_start(out=convw_sb[:], in_=convw_D[:, :]).then_inc(s_p.h, 16)
        eng.dma_start(out=convb_sb[:], in_=convb_D[:, :]).then_inc(s_p.h, 16)
        eng.dma_start(out=dtb_sb[:], in_=dtb_D[:, :]).then_inc(s_p.h, 16)
        eng.dma_start(out=Alog_sb[:], in_=Alog_D[:, :]).then_inc(s_p.h, 16)
        eng.dma_start(out=D_sb[:], in_=Dp_D[:, :]).then_inc(s_p.h, 16)
        eng.wait_ge(s_p.h, 112)
    kb.on("sync", flp)
    s_p.inc(112)
    pv = s_p.n
    fsc = kb.fsem["scalar"]
    fA = fsc.inc()

    def fprep(eng, pv=pv, fA=fA):
        eng.wait_ge(s_p.h, pv)
        eng.activation(A_sb[:], Alog_sb[:], AF.Exp).then_inc(fsc.h, 1)
        eng.wait_ge(fsc.h, fA)
        eng.mul(A_sb[:], A_sb[:], -1.0).then_inc(s_p.h, 1)
    kb.on("scalar", fprep)
    s_p.inc(1)
    a_ready = (s_p, s_p.n)

    sc1 = kb.scope()
    lnb1 = make_ln_bufs(kb, sc1, D, pfx="m_ln1", with_f32=True, with_sum=True)
    emit_ln_phase(kb, lnb1, x_in, xn_bf, D, eps=1e-5, xn_f32_dram=xn1_f32,
                  sum_out_dram=xres)
    sc1.close()
    kb.barrier()
    sc1b = kb.scope()
    lnb1b = make_ln_bufs(kb, sc1b, D, pfx="m_ln2")
    emit_ln_phase(kb, lnb1b, xn1_f32, xn_bf, D, gamma_sb=lng_sb, beta_sb=lnbt_sb,
                  eps=1e-5)
    sc1b.close()
    kb.barrier()

    sc2 = kb.scope()
    wi_sb = kb.sb("m_wi", [128, 8 * 32 * 128], BF16, sc2)
    wst = kb.sb("m_wist", [128, 8 * 128], F32, sc2)
    s_wld = kb.sem(); s_wcvt = kb.sem()
    emit_weight_load_g(kb, inpT, wi_sb, 8, 32, wst, s_wld, s_wcvt)
    ipb = {
        "ipx": kb.sb("m_ipx", [128, 2 * 8 * NCHUNK], BF16, sc2),
        "ipo": kb.sb("m_ipo", [128, 2 * 32 * NCHUNK], BF16, sc2),
        "ipt": kb.sb("m_ipt", [128, 2 * NCHUNK], F32, sc2),
        "ipps": [kb.psum(f"m_ipps{i}", [128, NCHUNK], F32, sc2) for i in range(2)],
    }
    emit_proj_phase(kb, ipb, "ip", [wi_sb], xn_bf, 8, xz_dm, 32,
                    mode="copy_bf16", w_ready=(s_wcvt, s_wcvt.n))
    sc2.close()
    kb.barrier()

    sc3 = kb.scope()
    cvb = {
        "cvx": kb.sb("m_cvx", [128, 2 * (3 + T_SEQ)], BF16, sc3),
        "cva": kb.sb("m_cva", [128, 2 * T_SEQ], F32, sc3),
        "cvo": kb.sb("m_cvo", [128, 2 * T_SEQ], BF16, sc3),
    }
    emit_mamba_conv(kb, cvb, xz_dm, xc_dm, convw_sb, convb_sb, (s_p, pv))
    sc3.close()
    kb.barrier()

    sc4 = kb.scope()
    wx_sb = kb.sb("m_wx", [128, 16 * 1 * 128], BF16, sc4)
    wst2 = kb.sb("m_wxst", [128, 4 * 128], F32, sc4)
    s_w2 = kb.sem(); s_wc2 = kb.sem()
    emit_weight_load_g(kb, xprojT, wx_sb, 16, 1, wst2, s_w2, s_wc2)
    xdb = {
        "xdx": kb.sb("m_xdx", [128, 2 * 16 * NCHUNK], BF16, sc4),
        "xdo": kb.sb("m_xdo", [128, 2 * 1 * NCHUNK], BF16, sc4),
        "xdt": kb.sb("m_xdt", [128, 2 * NCHUNK], F32, sc4),
        "xdps": [kb.psum(f"m_xdps{i}", [128, NCHUNK], F32, sc4) for i in range(2)],
    }
    emit_proj_phase(kb, xdb, "xd", [wx_sb], xc_dm, 16, xdbl_dm, 1,
                    mode="copy_bf16", w_ready=(s_wc2, s_wc2.n))
    sc4.close()
    kb.barrier()

    sc5 = kb.scope()
    wdt_sb = kb.sb("m_wdt", [128, 1 * 16 * 128], BF16, sc5)
    wst3 = kb.sb("m_wdtst", [128, 2 * 128], F32, sc5)
    s_w3 = kb.sem(); s_wc3 = kb.sem()
    emit_weight_load_g(kb, dtprojT, wdt_sb, 1, 16, wst3, s_w3, s_wc3, krows=64)
    dtbf = {
        "dtx": kb.sb("m_dtx", [128, 2 * 1 * NCHUNK], BF16, sc5),
        "dto": kb.sb("m_dto", [128, 2 * 16 * NCHUNK], F32, sc5),
        "dtt": kb.sb("m_dtt", [128, 2 * NCHUNK], F32, sc5),
        "dtl": kb.sb("m_dtl", [128, 2 * NCHUNK], F32, sc5),
        "dtm": kb.sb("m_dtm", [128, 2 * 16 * NCHUNK], BF16, sc5),
        "dto2": kb.sb("m_dto2", [128, 2 * 16 * NCHUNK], BF16, sc5),
        "dtps": [kb.psum(f"m_dtps{i}", [128, NCHUNK], F32, sc5) for i in range(4)],
    }
    emit_proj_phase(kb, dtbf, "dt", [wdt_sb], xdbl_dm, 1, dt_dm, 16,
                    mode="softplus_bias", w_ready=(s_wc3, s_wc3.n), krows=64,
                    bias_sb=dtb_sb, out2_dram=dtx_dm, mul_dram=xc_dm)
    sc5.close()
    kb.barrier()

    sc6 = kb.scope()
    scb = {
        "scdt": kb.sb("m_scdt", [128, 2 * T_SEQ], F32, sc6),
        "scdx": kb.sb("m_scdx", [128, 2 * T_SEQ], BF16, sc6),
        "scB": kb.sb("m_scB", [128, 16 * T_SEQ], BF16, sc6),
        "scC": kb.sb("m_scC", [128, 16 * T_SEQ], BF16, sc6),
        "scrow": kb.sb("m_scrow", [1, 2 * T_SEQ], BF16, sc6),
        "scdA": kb.sb("m_scdA", [128, 2 * T_SEQ], BF16, sc6),
        "sch": kb.sb("m_sch", [128, 2 * T_SEQ], BF16, sc6),
        "scb": kb.sb("m_scb", [128, 2 * T_SEQ], BF16, sc6),
        "scy": kb.sb("m_scy", [128, 2 * T_SEQ], BF16, sc6),
        "sc_ps": [kb.psum(f"m_scps{i}", [128, 512], F32, sc6) for i in range(2)],
    }
    emit_mamba_scan(kb, scb, xdbl_dm, dt_dm, dtx_dm, ys_dm, A_sb, a_ready)
    sc6.close()
    kb.barrier()

    sc7 = kb.scope()
    mgb = {
        "mgy": kb.sb("m_mgy", [128, 2 * 16 * NCHUNK], BF16, sc7),
        "mgx": kb.sb("m_mgx", [128, 2 * 16 * NCHUNK], BF16, sc7),
        "mgz": kb.sb("m_mgz", [128, 2 * 16 * NCHUNK], BF16, sc7),
        "mgs": kb.sb("m_mgs", [128, 2 * 16 * NCHUNK], BF16, sc7),
        "mgo": kb.sb("m_mgo", [128, 2 * 16 * NCHUNK], BF16, sc7),
    }
    emit_mamba_gate(kb, mgb, ys_dm, xc_dm, xz_dm, y2_dm, D_sb, (s_p, pv))
    sc7.close()
    kb.barrier()

    sc8 = kb.scope()
    wo_sb = kb.sb("m_wo", [128, 16 * 8 * 128], BF16, sc8)
    wst4 = kb.sb("m_wost", [128, 4 * 128], F32, sc8)
    s_w4 = kb.sem(); s_wc4 = kb.sem()
    emit_weight_load_g(kb, outprojT, wo_sb, 16, 8, wst4, s_w4, s_wc4)
    opb = {
        "opx": kb.sb("m_opx", [128, 2 * 16 * NCHUNK], BF16, sc8),
        "opo": kb.sb("m_opo", [128, 2 * 8 * NCHUNK], F32, sc8),
        "opt": kb.sb("m_opt", [128, 2 * NCHUNK], F32, sc8),
        "opr": kb.sb("m_opr", [128, 2 * 8 * NCHUNK], F32, sc8),
        "opps": [kb.psum(f"m_opps{i}", [128, NCHUNK], F32, sc8) for i in range(2)],
    }
    emit_proj_phase(kb, opb, "op", [wo_sb], y2_dm, 16, x_out, 8,
                    mode="residual", res_dram=xres, w_ready=(s_wc4, s_wc4.n))
    sc8.close()
    scP.close()
    kb.barrier()


def emit_rope_phase(kb, bufs, x_dm, cos_sb, sin_sb, psw_sb, w_ready):
    T = T_SEQ
    xb = bufs["rpx"]; t1 = bufs["rpa"]; t2 = bufs["rps"]; ob = bufs["rpo"]
    psw = bufs["rp_ps"]
    s_ld = kb.sem(); s_mm = kb.sem(); s_ev = kb.sem(); s_st = kb.sem()
    b_st = s_st.n
    fvc = kb.fsem["vector"]
    idx = 0
    NC2 = T // 512
    for dt in range(8):
        for b in range(B_LOC):
            slot = idx % 2
            xv = xb[:, slot * T:(slot + 1) * T]
            ov = ob[:, slot * T:(slot + 1) * T]

            def fl(eng, dt=dt, b=b, xv=xv, idx=idx):
                if idx >= 2:
                    eng.wait_ge(s_st.h, b_st + 16 * (idx - 1))
                eng.dma_start(out=xv, in_=x_dm[dt * 128:(dt + 1) * 128,
                                               b * T:(b + 1) * T]).then_inc(s_ld.h, 16)
            kb.on("sync", fl)
            s_ld.inc(16)
            ldv = s_ld.n

            for cc in range(NC2):
                bank = cc % 2
                wdn = ev_hist[-2] if False else 0

                def fmm(eng, xv=xv, cc=cc, bank=bank, ldv=ldv, idx=idx,
                        wdn=(s_ev.n - 1 if (idx * NC2 + cc) >= 2 else 0)):
                    if cc == 0:
                        eng.wait_ge(s_ld.h, ldv)
                        if idx == 0:
                            eng.wait_ge(w_ready[0].h, w_ready[1])
                    if wdn:
                        eng.wait_ge(s_ev.h, wdn)
                    return eng.matmul(psw[bank][:, :], psw_sb[:],
                                      xv[:, cc * 512:(cc + 1) * 512],
                                      start=True, stop=True).then_inc(s_mm.h, 1)
                kb.on("tensor", fmm)
                s_mm.inc(1)
                mmv = s_mm.n

                f1 = fvc.inc(); f2 = fvc.inc()

                def fv(eng, xv=xv, ov=ov, cc=cc, bank=bank, b=b, mmv=mmv,
                       idx=idx, f1=f1, f2=f2):
                    eng.wait_ge(s_mm.h, mmv)
                    if idx >= 2 and cc == 0:
                        eng.wait_ge(s_st.h, b_st + 16 * (idx - 1))
                    c0 = cc * 512
                    eng.tensor_tensor(t1[:, 0:512], xv[:, c0:c0 + 512],
                                      cos_sb[:, c0:c0 + 512], ALU.mult).then_inc(fvc.h, 1)
                    eng.tensor_tensor(t2[:, 0:512], psw[bank][:, :],
                                      sin_sb[:, c0:c0 + 512], ALU.mult).then_inc(fvc.h, 1)
                    eng.wait_ge(fvc.h, f2)
                    return eng.tensor_tensor(ov[:, c0:c0 + 512], t1[:, 0:512],
                                             t2[:, 0:512], ALU.add).then_inc(s_ev.h, 1)
                kb.on("vector", fv)
                s_ev.inc(1)
            evv = s_ev.n

            def fst(eng, dt=dt, b=b, ov=ov, evv=evv):
                eng.wait_ge(s_ev.h, evv)
                eng.dma_start(out=x_dm[dt * 128:(dt + 1) * 128, b * T:(b + 1) * T],
                              in_=ov).then_inc(s_st.h, 16)
            kb.on("gpsimd", fst)
            s_st.inc(16)
            idx += 1
    stn = s_st.n

    def fdrain(eng, stn=stn):
        eng.wait_ge(s_st.h, stn)
    kb.on("gpsimd", fdrain)


def emit_mla_attn(kb, bufs, q_dm, k_dm, vt_tm, attn_dm):
    T = T_SEQ
    H = 16
    qb = bufs["maq"]; kbuf = bufs["mak"]; vb = bufs["mav"]
    expb = bufs["maxp"]
    osb = bufs["mao"]
    ones64 = bufs["ones64"]
    outb = bufs["maob"]
    stat = bufs["mast"]
    recb = bufs["marc"]
    rec0 = bufs["marc0"]
    onescolb = bufs["onescolb"]
    cmask_s = bufs["cmask_s"]; cmask_b = bufs["cmask_b"]
    psS = bufs["ma_psS"]
    psO = bufs["ma_psO"]
    psB = bufs["ma_psB"]
    idq = bufs["idq"]
    s_ld = kb.sem(); s_mm = kb.sem(); s_st = kb.sem(); s_lv = kb.sem()
    s_dv = kb.sem(); s_ds = kb.sem()
    fvc = kb.fsem["vector"]
    NQT = T // 128
    inst = 0
    inst_done = []
    st_count = []
    ZERO = ()
    last_psS = [ZERO, ZERO, ZERO, ZERO]
    last_psO = [ZERO, ZERO]
    last_psB = [ZERO, ZERO]
    last_osb = [ZERO, ZERO]
    pending = [None]
    last_stat = [ZERO, ZERO]

    def wdeps(eng, deps):
        dv = 0; ds = 0
        for (tag, val) in deps:
            if tag == "v":
                dv = max(dv, val)
            else:
                ds = max(ds, val)
        if dv:
            eng.wait_ge(s_dv.h, dv)
        if ds:
            eng.wait_ge(s_ds.h, ds)

    def mmstage(emit_mms, deps=(), wait_ld=0, wait_lv=0):
        deps = tuple(deps)

        def f(eng, deps=deps, wait_ld=wait_ld, wait_lv=wait_lv,
              emit_mms=emit_mms):
            if wait_ld:
                eng.wait_ge(s_ld.h, wait_ld)
            if wait_lv:
                eng.wait_ge(s_lv.h, wait_lv)
            wdeps(eng, deps)
            emit_mms(eng).then_inc(s_mm.h, 1)
        kb.on("tensor", f)
        s_mm.inc(1)
        return s_mm.n

    def evstage(engine, emit_ev, wait_mm=0, deps=(), wait_st=0):
        deps = tuple(deps)
        sem = s_dv if engine == "vector" else s_ds

        def f(eng, wait_mm=wait_mm, emit_ev=emit_ev, deps=deps,
              wait_st=wait_st, sem=sem):
            if wait_mm:
                eng.wait_ge(s_mm.h, wait_mm)
            wdeps(eng, deps)
            if wait_st:
                eng.wait_ge(s_st.h, wait_st)
            emit_ev(eng).then_inc(sem.h, 1)
        kb.on(engine, f)
        sem.inc(1)
        return (("v" if engine == "vector" else "s"), sem.n)

    def fsetup(eng):
        eng.memset(ones64[:], 1.0)
        eng.memset(onescolb[:], 1.0).then_inc(s_dv.h, 1)
    kb.on("vector", fsetup)
    s_dv.inc(1)

    for b in range(B_LOC):
        for h in range(H):
            slot = inst % 2
            qv = qb[0:64, slot * T:(slot + 1) * T]
            kv = kbuf[0:64, slot * T:(slot + 1) * T]
            vv = vb[:, slot * 1072:slot * 1072 + 1072]
            ov = outb[:, slot * T:(slot + 1) * T]
            in_guard = inst_done[inst - 2] if inst >= 2 else ()
            r0 = h * 64

            def fld(eng, b=b, r0=r0, qv=qv, kv=kv, in_guard=in_guard):
                wdeps(eng, in_guard)
                eng.dma_start(out=qv, in_=q_dm[r0:r0 + 64, b * T:(b + 1) * T]
                              ).then_inc(s_ld.h, 16)
                eng.dma_start(out=kv, in_=k_dm[r0:r0 + 64, b * T:(b + 1) * T]
                              ).then_inc(s_ld.h, 16)
            kb.on("sync", fld)
            s_ld.inc(32)

            def fldv(eng, b=b, r0=r0, vv=vv, in_guard=in_guard):
                wdeps(eng, in_guard)
                eng.memset(vv, 1.0).then_inc(s_lv.h, 1)
                eng.dma_start(
                    out=vv.rearrange("p (kt c) -> p kt c", kt=16)[:, :, 0:64],
                    in_=vt_tm[b * T:(b + 1) * T, r0:r0 + 64].rearrange(
                        "(kt p) c -> p kt c", p=128),
                ).then_inc(s_lv.h, 16)
            kb.on("gpsimd", fldv)
            s_lv.inc(17)
            lvv = s_lv.n
            ldv = s_ld.n


            for qc in range(4):
                nkt = (qc + 1) * 4
                ob_ = qc % 2
                psOc = psO[ob_]
                exp_deps = []
                kt = 0
                while kt < nkt:
                    pair = [kt, kt + 1]
                    bp = (0, 1) if (kt // 2) % 2 == 0 else (2, 3)

                    def mmS(eng, qc=qc, pair=pair, bp=bp, qv=qv, kv=kv):
                        for j, k in enumerate(pair):
                            ins = eng.matmul(psS[bp[j]][:, :],
                                             kv[:, k * 128:(k + 1) * 128],
                                             qv[:, qc * 512:(qc + 1) * 512],
                                             start=True, stop=True)
                        return ins
                    dd = list(last_psS[bp[0]]) + list(last_psS[bp[1]])
                    if qc == 0 and kt == 0:
                        dd += list(in_guard)
                    m = mmstage(mmS, deps=dd,
                                wait_ld=(ldv if qc == 0 and kt == 0 else 0))

                    diag0 = pair[0] >= qc * 4
                    if diag0:
                        def evmask(eng, qc=qc, pair=pair, bp=bp):
                            for j, k in enumerate(pair):
                                soff = 128 + (k % 2) * 512
                                v = k - qc * 4
                                ins = eng.tensor_tensor(
                                    stat[:, soff:soff + 512], psS[bp[j]][:, :],
                                    cmask_b[:, v * 512:(v + 1) * 512], ALU.add)
                            return eng.tensor_copy(stat[:, 1665:1666],
                                                   stat[:, 128:129])
                        d_msk = evstage("vector", evmask, wait_mm=m,
                                        deps=list(last_stat[0]) + list(last_stat[1]))

                        def evexp(eng, pair=pair):
                            for k in pair:
                                soff = 128 + (k % 2) * 512
                                ins = eng.activation(
                                    expb[:, k * 512:(k + 1) * 512],
                                    stat[:, soff:soff + 512], AF.Exp, scale=0.125)
                            return ins
                        d_e = evstage("scalar", evexp, deps=(d_msk,))
                        last_stat[0] = (d_e,)
                        last_stat[1] = (d_e,)
                        last_psS[bp[0]] = (d_msk,)
                        last_psS[bp[1]] = (d_msk,)
                        exp_deps.append(d_e)
                    else:
                        def evexp(eng, pair=pair, bp=bp):
                            for j, k in enumerate(pair):
                                ins = eng.activation(
                                    expb[:, k * 512:(k + 1) * 512],
                                    psS[bp[j]][:, :], AF.Exp, scale=0.125)
                            return ins
                        d_e = evstage("scalar", evexp, wait_mm=m)
                        last_psS[bp[0]] = (d_e,)
                        last_psS[bp[1]] = (d_e,)
                        exp_deps.append(d_e)
                    kt += 2

                if pending[0] is not None:
                    pending[0]()
                    pending[0] = None

                def mmV(eng, nkt=nkt, vv=vv, psOc=psOc):
                    for kt in range(nkt):
                        ins = eng.matmul(psOc[0:65, :], vv[:, kt * 67:kt * 67 + 65],
                                         expb[:, kt * 512:(kt + 1) * 512],
                                         start=(kt == 0), stop=(kt == nkt - 1))
                    return ins
                m = mmstage(mmV, deps=list(exp_deps) + list(last_psO[ob_]),
                            wait_lv=lvv)

                def evrec(eng, psOc=psOc):
                    eng.reciprocal(rec0[0:1, :], psOc[64:65, :])
                    return eng.memset(kb.junk[0:1, 12:13], 0.0)
                d_rec = evstage("vector", evrec, wait_mm=m)
                osv = osb[:, ob_ * 512:(ob_ + 1) * 512]
                d_oc = evstage("scalar",
                               lambda eng, psOc=psOc, osv=osv: eng.copy(
                                   osv[0:64, :], psOc[0:64, :]),
                               wait_mm=m, deps=last_osb[ob_])
                last_psO[ob_] = (d_oc, d_rec)

                def mktail(qc=qc, ob_=ob_, d_rec=d_rec, d_oc=d_oc, osv=osv,
                           ov=ov, inst=inst, b=b, r0=r0):
                    def flush():
                        m2 = mmstage(
                            lambda eng, ob_=ob_: eng.matmul(
                                psB[ob_][0:64, :], kb.onesf[0:1, 0:64],
                                rec0[0:1, :], start=True, stop=True),
                            deps=list(last_psB[ob_]) + [d_rec])

                        def evO(eng, qc=qc, ov=ov, inst=inst, ob_=ob_,
                                osv=osv):
                            if inst >= 2 and qc == 0:
                                eng.wait_ge(s_st.h, st_count[inst - 2])
                            return eng.tensor_tensor(
                                ov[0:64, qc * 512:(qc + 1) * 512],
                                osv[0:64, :], psB[ob_][0:64, :], ALU.mult)
                        dO = evstage("vector", evO, wait_mm=m2, deps=(d_oc,))
                        last_psB[ob_] = (dO,)
                        last_osb[ob_] = (dO,)
                        if qc == 3:
                            dnv = (dO, d_oc)
                            inst_done.append(dnv)

                            def fso(eng, b=b, r0=r0, ov=ov, dnv=dnv):
                                wdeps(eng, dnv)
                                eng.dma_start(
                                    out=attn_dm[r0:r0 + 64, b * T:(b + 1) * T],
                                    in_=ov[0:64, :]).then_inc(s_st.h, 16)
                            kb.on("gpsimd", fso)
                            s_st.inc(16)
                            st_count.append(s_st.n)
                    return flush
                pending[0] = mktail()

            inst += 1
    if pending[0] is not None:
        pending[0]()
        pending[0] = None
    stn = s_st.n

    def fdrain(eng, stn=stn):
        eng.wait_ge(s_st.h, stn)
    kb.on("gpsimd", fdrain)


def emit_tokmajor_proj(kb, bufs, pfx, x_dm, w_sb, ktiles, mcols, out_tm,
                       w_ready):
    ntiles = NTOK // 128
    xb = bufs[pfx + "x"]
    ob = bufs[pfx + "o"]
    pss = bufs[pfx + "ps"]
    s_ld = kb.sem(); s_mm = kb.sem(); s_ev = kb.sem(); s_st = kb.sem()
    b_st = s_st.n; b_ev = s_ev.n
    nc2 = ceil_div(mcols, 512)
    for tt in range(ntiles):
        slot = tt % 2

        def flx(eng, tt=tt, slot=slot):
            if tt >= 2:
                eng.wait_ge(s_st.h, b_st + 16 * (tt - 1))
            eng.dma_start(
                out=xb[:, slot * ktiles * 128:(slot + 1) * ktiles * 128
                       ].rearrange("p (k c) -> p k c", k=ktiles),
                in_=x_dm[:, tt * 128:(tt + 1) * 128].rearrange(
                    "(k p) c -> p k c", p=128),
            ).then_inc(s_ld.h, 16)
        kb.on("sync", flx)
        s_ld.inc(16)
        ldv = s_ld.n

        def fmm(eng, tt=tt, slot=slot, ldv=ldv):
            eng.wait_ge(s_ld.h, ldv)
            if tt == 0:
                eng.wait_ge(w_ready[0].h, w_ready[1])
            else:
                eng.wait_ge(s_ev.h, b_ev + tt)
            for kt in range(ktiles):
                st = xb[:, slot * ktiles * 128 + kt * 128:
                        slot * ktiles * 128 + (kt + 1) * 128]
                for cc in range(nc2):
                    c0 = cc * 512; c1 = min(mcols, c0 + 512)
                    ins = eng.matmul(pss[:, c0:c1], st,
                                     w_sb[:, kt * mcols + c0:kt * mcols + c1],
                                     start=(kt == 0), stop=(kt == ktiles - 1))
            ins.then_inc(s_mm.h, 1)
        kb.on("tensor", fmm)
        s_mm.inc(1)
        mmv = s_mm.n
        ov = ob[:, slot * mcols:(slot + 1) * mcols]

        def fev(eng, mmv=mmv, ov=ov, tt=tt):
            eng.wait_ge(s_mm.h, mmv)
            if tt >= 2:
                eng.wait_ge(s_st.h, b_st + 16 * (tt - 1))
            eng.activation(ov, pss[:, :], AF.Copy).then_inc(s_ev.h, 1)
        kb.on("scalar", fev)
        s_ev.inc(1)
        evv = s_ev.n

        def fst(eng, tt=tt, ov=ov, evv=evv):
            eng.wait_ge(s_ev.h, evv)
            eng.dma_start(out=out_tm[tt * 128:(tt + 1) * 128, :],
                          in_=ov).then_inc(s_st.h, 16)
        kb.on("gpsimd", fst)
        s_st.inc(16)
    stn = s_st.n

    def fdrain(eng, stn=stn):
        eng.wait_ge(s_st.h, stn)
    kb.on("gpsimd", fdrain)


def emit_gating(kb, bufs, xn2_f32, gwT_sb, gb_row, mask_dm, idq, w_ready):
    ntiles = NTOK // 128
    xb = bufs["gtx"]
    lg = bufs["gtl"]
    mro = bufs["gtm"]
    psL = bufs["gt_psL"]
    psT = bufs["gt_psT"]
    s_ld = kb.sem(); s_mm = kb.sem(); s_dn = kb.sem(); s_st = kb.sem()
    b_st = s_st.n
    fvc = kb.fsem["vector"]
    for tt in range(ntiles):
        slot = tt % 2

        def flx(eng, tt=tt, slot=slot):
            if tt >= 2:
                eng.wait_ge(s_st.h, b_st + 64 * (tt - 1))
            eng.dma_start(
                out=xb[:, slot * 1024:(slot + 1) * 1024].rearrange(
                    "p (k c) -> p k c", k=8),
                in_=xn2_f32[:, tt * 128:(tt + 1) * 128].rearrange(
                    "(k p) c -> p k c", p=128),
            ).then_inc(s_ld.h, 16)
        kb.on("sync", flx)
        s_ld.inc(16)
        ldv = s_ld.n

        def fmm(eng, tt=tt, slot=slot, ldv=ldv, wdn=(s_dn.n)):
            eng.wait_ge(s_ld.h, ldv)
            if tt == 0:
                eng.wait_ge(w_ready[0].h, w_ready[1])
            if wdn:
                eng.wait_ge(s_dn.h, wdn)
            for kt in range(8):
                st = xb[:, slot * 1024 + kt * 128:slot * 1024 + (kt + 1) * 128]
                eng.matmul(psL[:, 0:4], st, gwT_sb[:, kt * 4:(kt + 1) * 4],
                           start=(kt == 0), stop=False)
            ins = eng.matmul(psL[:, 0:4], kb.onesf[:], gb_row[:, :],
                             start=False, stop=True)
            ins.then_inc(s_mm.h, 1)
        kb.on("tensor", fmm)
        s_mm.inc(1)
        mmv = s_mm.n

        fs = [fvc.inc() for _ in range(6)]

        def fdv(eng, mmv=mmv, fs=fs):
            eng.wait_ge(s_mm.h, mmv)
            eng.tensor_copy(lg[:, 0:4], psL[:, 0:4]).then_inc(fvc.h, 1)
            eng.wait_ge(fvc.h, fs[0])
            eng.tensor_reduce(lg[:, 4:5], lg[:, 0:4], AX.X, ALU.max).then_inc(fvc.h, 1)
            eng.wait_ge(fvc.h, fs[1])
            eng.tensor_tensor(lg[:, 5:9], lg[:, 0:4],
                              lg[:, 4:5].broadcast_to([128, 4]) if False else
                              lg[:, 4:5], ALU.is_ge) if False else None
            for e in range(4):
                eng.tensor_tensor(lg[:, 5 + e:6 + e], lg[:, e:e + 1],
                                  lg[:, 4:5], ALU.is_ge)
            ins = eng.tensor_copy(lg[:, 9:10], lg[:, 5:6])
            ins.then_inc(fvc.h, 1)
            eng.wait_ge(fvc.h, fs[2])
            eng.tensor_scalar(lg[:, 13:14], lg[:, 9:10], -1.0, 1.0,
                              ALU.mult, ALU.add).then_inc(fvc.h, 1)
            eng.wait_ge(fvc.h, fs[3])
            eng.tensor_tensor(lg[:, 10:11], lg[:, 6:7], lg[:, 13:14], ALU.mult)
            eng.tensor_tensor(lg[:, 14:15], lg[:, 9:10], lg[:, 10:11], ALU.add).then_inc(fvc.h, 1)
            eng.wait_ge(fvc.h, fs[4])
            eng.tensor_scalar(lg[:, 13:14], lg[:, 14:15], -1.0, 1.0,
                              ALU.mult, ALU.add)
            eng.tensor_tensor(lg[:, 11:12], lg[:, 7:8], lg[:, 13:14], ALU.mult)
            eng.tensor_tensor(lg[:, 14:15], lg[:, 14:15], lg[:, 11:12], ALU.add).then_inc(fvc.h, 1)
            eng.wait_ge(fvc.h, fs[5])
            eng.tensor_scalar(lg[:, 13:14], lg[:, 14:15], -1.0, 1.0,
                              ALU.mult, ALU.add)
            ins = eng.tensor_tensor(lg[:, 12:13], lg[:, 8:9], lg[:, 13:14],
                                    ALU.mult)
            return ins
        d_m = None
        def fdvw(eng, fdv=fdv):
            fdv(eng).then_inc(s_dn.h, 1)
        kb.on("vector", fdvw)
        s_dn.inc(1)
        d_m = s_dn.n

        def fmt(eng, d_m=d_m, wst=(s_mm.n)):
            eng.wait_ge(s_dn.h, d_m)
            for e in range(4):
                eng.transpose(psT[0:1, e * 128:(e + 1) * 128],
                              lg[:, 9 + e:10 + e], idq[:]).then_inc(s_mm.h, 1)
        kb.on("tensor", fmt)
        s_mm.inc(4)
        mtv = s_mm.n

        def fme(eng, mtv=mtv, slot=slot, tt=tt):
            eng.wait_ge(s_mm.h, mtv)
            if tt >= 2:
                eng.wait_ge(s_st.h, b_st + 64 * (tt - 1))
            ins = eng.copy(mro[0:1, slot * 512:slot * 512 + 512],
                           psT[0:1, 0:512])
            ins.then_inc(s_dn.h, 1)
        kb.on("scalar", fme)
        s_dn.inc(1)
        dv2 = s_dn.n

        def fms(eng, tt=tt, slot=slot, dv2=dv2):
            eng.wait_ge(s_dn.h, dv2)
            for e in range(4):
                eng.dma_start(out=mask_dm[e:e + 1, tt * 128:(tt + 1) * 128],
                              in_=mro[0:1, slot * 512 + e * 128:
                                      slot * 512 + (e + 1) * 128]
                              ).then_inc(s_st.h, 16)
        kb.on("gpsimd", fms)
        s_st.inc(64)
    stn = s_st.n

    def fdrain(eng, stn=stn):
        eng.wait_ge(s_st.h, stn)
    kb.on("gpsimd", fdrain)

def emit_block_l1ffn(kb, x_in, x_out, WqT, WdkvT, WukT, WuvT, WoT, gwT_D,
                     gb_D, W1T, b1_D, W2T, b2_D, n1_D, n2_D, lng_D, lnb_D,
                     ropec_D, ropes_D, cmask_D, idq_D, psw_D,
                     xn_bf, xp_f32, xres0, q_dm, k_dm, lat_f32, latn_bf,
                     vt_tm, attn_dm, xc2_f32, xn2_bf, xn2_f32, mask_dm,
                     he_bf, stop_after=99):
    scP = kb.scope()
    n1_sb = kb.sb("f_n1", [128, 8], F32, scP)
    n2_sb = kb.sb("f_n2", [128, 8], F32, scP)
    lng_sb = kb.sb("f_lng", [128, 4], F32, scP)
    lnb_sb = kb.sb("f_lnb", [128, 4], F32, scP)
    gwT_sb = kb.sb("f_gw", [128, 32], F32, scP)
    gb_sb = kb.sb("f_gb", [1, 4], F32, scP)
    b1_sb = kb.sb("f_b1", [128, 128], F32, scP)
    b2_sb = kb.sb("f_b2", [128, 32], F32, scP)
    cosb = kb.sb("f_cos", [128, T_SEQ], BF16, scP)
    sinb = kb.sb("f_sin", [128, T_SEQ], BF16, scP)
    pswb = kb.sb("f_psw", [128, 128], BF16, scP)
    cmsk = kb.sb("f_cm", [128, 2 * 2048], F32, scP)
    idq = kb.sb("f_idq", [128, 128], F32, scP)
    s_p = kb.sem("f_sp")

    def flp(eng):
        eng.dma_start(out=n1_sb[:], in_=n1_D[:, :]).then_inc(s_p.h, 16)
        eng.dma_start(out=n2_sb[:], in_=n2_D[:, :]).then_inc(s_p.h, 16)
        eng.dma_start(out=lng_sb[:], in_=lng_D[:, :]).then_inc(s_p.h, 16)
        eng.dma_start(out=lnb_sb[:], in_=lnb_D[:, :]).then_inc(s_p.h, 16)
        eng.dma_start(out=gwT_sb[:], in_=gwT_D[:, :]).then_inc(s_p.h, 16)
        eng.dma_start(out=gb_sb[:], in_=gb_D[:, :]).then_inc(s_p.h, 16)
        eng.dma_start(out=b1_sb[:], in_=b1_D[:, :]).then_inc(s_p.h, 16)
        eng.dma_start(out=b2_sb[:], in_=b2_D[:, :]).then_inc(s_p.h, 16)
        eng.dma_start(out=cosb[:], in_=ropec_D[:, :]).then_inc(s_p.h, 16)
        eng.dma_start(out=sinb[:], in_=ropes_D[:, :]).then_inc(s_p.h, 16)
        eng.dma_start(out=cmsk[:], in_=cmask_D[:, :]).then_inc(s_p.h, 16)
        eng.dma_start(out=idq[:], in_=idq_D[:, :]).then_inc(s_p.h, 16)
        eng.dma_start(out=pswb[:], in_=psw_D[:, :]).then_inc(s_p.h, 16)
        eng.wait_ge(s_p.h, 16 * 13)
    kb.on("sync", flp)
    s_p.inc(16 * 13)
    p_ready = (s_p, s_p.n)

    sc = kb.scope()
    lnb1 = make_ln_bufs(kb, sc, D, pfx="f_ln1", with_f32=True, with_sum=True)
    emit_ln_phase(kb, lnb1, x_in, xn_bf, D, eps=1e-5, xn_f32_dram=xp_f32,
                  sum_out_dram=xres0)
    sc.close()
    kb.barrier()
    if stop_after <= 1:
        return

    sc = kb.scope()
    lnr = make_ln_bufs(kb, sc, D, pfx="f_rms1")
    emit_ln_phase(kb, lnr, xp_f32, xn_bf, D, gamma_sb=n1_sb, rms=True, eps=1e-6)
    sc.close()
    kb.barrier()
    if stop_after <= 2:
        return

    sc = kb.scope()
    wq_sb = kb.sb("f_wq", [128, 8 * 8 * 128], BF16, sc)
    wl_sb = kb.sb("f_wl", [128, 8 * 4 * 128], BF16, sc)
    sA = kb.sem(); sB = kb.sem()
    emit_weight_load_g(kb, WqT, wq_sb, 8, 8, None, sA, sA)
    emit_weight_load_g(kb, WdkvT, wl_sb, 8, 4, None, sA, sA)
    qb = {
        "qx": kb.sb("f_qx", [128, 2 * 8 * NCHUNK], BF16, sc),
        "qo": kb.sb("f_qo", [128, 2 * 8 * NCHUNK], BF16, sc),
        "qt": kb.sb("f_qt", [128, 2 * NCHUNK], F32, sc),
        "qps": [kb.psum(f"f_qps{i}", [128, NCHUNK], F32, sc) for i in range(2)],
    }
    emit_proj_phase(kb, qb, "q", [wq_sb], xn_bf, 8, q_dm, 8,
                    mode="copy_bf16", w_ready=(sA, sA.n))
    lb = {
        "lx": kb.sb("f_lx", [128, 2 * 8 * NCHUNK], BF16, sc),
        "lo": kb.sb("f_lo", [128, 2 * 4 * NCHUNK], F32, sc),
        "lt": kb.sb("f_lt", [128, 2 * NCHUNK], F32, sc),
        "lps": [kb.psum(f"f_lps{i}", [128, NCHUNK], F32, sc) for i in range(2)],
    }
    emit_proj_phase(kb, lb, "l", [wl_sb], xn_bf, 8, lat_f32, 4,
                    mode="copy_f32", w_ready=(sA, sA.n))
    sc.close()
    kb.barrier()
    if stop_after <= 3:
        return

    sc = kb.scope()
    lnl = make_ln_bufs(kb, sc, 512, pfx="f_lnl")
    emit_ln_phase(kb, lnl, lat_f32, latn_bf, 512, gamma_sb=lng_sb,
                  beta_sb=lnb_sb, eps=1e-5)
    sc.close()
    kb.barrier()
    if stop_after <= 4:
        return

    sc = kb.scope()
    wk_sb = kb.sb("f_wk", [128, 4 * 8 * 128], BF16, sc)
    wv_sb = kb.sb("f_wv", [128, 4 * 1024], BF16, sc)
    sC = kb.sem()
    emit_weight_load_g(kb, WukT, wk_sb, 4, 8, None, sC, sC)
    for kt in range(4):
        def flw(eng, kt=kt):
            eng.dma_start(out=wv_sb[:, kt * 1024:(kt + 1) * 1024],
                          in_=WuvT[kt * 128:(kt + 1) * 128, :]).then_inc(sC.h, 16)
        kb.on("gpsimd", flw)
        sC.inc(16)
    kpb = {
        "kx": kb.sb("f_kx", [128, 2 * 4 * NCHUNK], BF16, sc),
        "ko": kb.sb("f_ko", [128, 2 * 8 * NCHUNK], BF16, sc),
        "kt": kb.sb("f_kt", [128, 2 * NCHUNK], F32, sc),
        "kps": [kb.psum(f"f_kps{i}", [128, NCHUNK], F32, sc) for i in range(2)],
    }
    emit_proj_phase(kb, kpb, "k", [wk_sb], latn_bf, 4, k_dm, 8,
                    mode="copy_bf16", w_ready=(sC, sC.n))
    vtb = {
        "vtx": kb.sb("f_vtx", [128, 2 * 4 * 128], BF16, sc),
        "vto": kb.sb("f_vto", [128, 2 * 1024], BF16, sc),
        "vtps": kb.psum("f_vtps", [128, 1024], F32, sc),
    }
    emit_tokmajor_proj(kb, vtb, "vt", latn_bf, wv_sb, 4, 1024, vt_tm,
                       (sC, sC.n))
    sc.close()
    kb.barrier()
    if stop_after <= 5:
        return

    sc = kb.scope()
    rpb = {
        "rpx": kb.sb("f_rpx", [128, 2 * T_SEQ], BF16, sc),
        "rpa": kb.sb("f_rpa", [128, 512], F32, sc),
        "rps": kb.sb("f_rps", [128, 512], F32, sc),
        "rpo": kb.sb("f_rpo", [128, 2 * T_SEQ], BF16, sc),
        "rp_ps": [kb.psum(f"f_rpps{i}", [128, 512], F32, sc) for i in range(2)],
    }
    emit_rope_phase(kb, rpb, q_dm, cosb, sinb, pswb, p_ready)
    kb.barrier()
    if stop_after <= 6:
        return
    emit_rope_phase(kb, rpb, k_dm, cosb, sinb, pswb, p_ready)
    sc.close()
    kb.barrier()
    if stop_after <= 7:
        return

    sc = kb.scope()
    mab = {
        "maq": kb.sb("f_maq", [64, 2 * T_SEQ], BF16, sc),
        "mak": kb.sb("f_mak", [64, 2 * T_SEQ], BF16, sc),
        "mav": kb.sb("f_mav", [128, 2 * 1072], BF16, sc),
        "mam1": kb.sb("f_mam1", [1, T_SEQ], BF16, sc),
        "maxp": kb.sb("f_maxp", [128, 16 * 512], BF16, sc),
        "mao": kb.sb("f_mao", [128, 1024], F32, sc),
        "maob": kb.sb("f_maob", [64, 2 * T_SEQ], BF16, sc),
        "mast": kb.sb("f_mast", [128, 1824], F32, sc),
        "marc": kb.sb("f_marc", [128, 512], F32, sc),
        "marc0": kb.sb("f_marc0", [1, 512], F32, sc),
        "onescolb": kb.sb("f_onescolb", [128, 1], BF16, sc),
        "ones64": kb.sb("f_ones64", [128, 64], F32, sc),
        "cmask_s": cmsk[:, 0:2048],
        "cmask_b": cmsk[:, 2048:4096],
        "ma_psS": [kb.psum(f"f_psS{i}", [128, 512], F32, sc) for i in range(4)],
        "ma_psO": [kb.psum(f"f_psO{i}", [128, 512], F32, sc) for i in range(2)],
        "ma_psB": [kb.psum(f"f_psB{i}", [128, 512], F32, sc) for i in range(2)],
        "idq": idq,
    }
    emit_mla_attn(kb, mab, q_dm, k_dm, vt_tm, attn_dm)
    sc.close()
    kb.barrier()
    if stop_after <= 8:
        return

    sc = kb.scope()
    wo_sb = kb.sb("f_wo", [128, 8 * 8 * 128], BF16, sc)
    sD = kb.sem()
    emit_weight_load_g(kb, WoT, wo_sb, 8, 8, None, sD, sD)
    opb = {
        "wox": kb.sb("f_wox", [128, 2 * 8 * NCHUNK], BF16, sc),
        "woo": kb.sb("f_woo", [128, 2 * 8 * NCHUNK], F32, sc),
        "wot": kb.sb("f_wot", [128, 2 * NCHUNK], F32, sc),
        "wor": kb.sb("f_wor", [128, 2 * 8 * NCHUNK], F32, sc),
        "wor2": kb.sb("f_wor2", [128, 2 * 8 * NCHUNK], F32, sc),
        "woo2": kb.sb("f_woo2", [128, 2 * 8 * NCHUNK], F32, sc),
        "wops": [kb.psum(f"f_wops{i}", [128, NCHUNK], F32, sc) for i in range(2)],
    }
    emit_proj_phase(kb, opb, "wo", [wo_sb], attn_dm, 8, xc2_f32, 8,
                    mode="residual2", res_dram=xp_f32, res2_dram=xres0,
                    out2_dram=x_out, w_ready=(sD, sD.n))
    sc.close()
    kb.barrier()
    if stop_after <= 9:
        return

    sc = kb.scope()
    lnr2 = make_ln_bufs(kb, sc, D, pfx="f_rms2", with_f32=True)
    emit_ln_phase(kb, lnr2, xc2_f32, xn2_bf, D, gamma_sb=n2_sb, rms=True,
                  eps=1e-6, xn_f32_dram=xn2_f32)
    sc.close()
    kb.barrier()
    if stop_after <= 10:
        return

    sc = kb.scope()
    gtb = {
        "gtx": kb.sb("f_gtx", [128, 2 * 1024], F32, sc),
        "gtl": kb.sb("f_gtl", [128, 16], F32, sc),
        "gtm": kb.sb("f_gtm", [1, 2 * 512], F32, sc),
        "gt_psL": kb.psum("f_gpsL", [128, 512], F32, sc),
        "gt_psT": kb.psum("f_gpsT", [128, 512], F32, sc),
    }
    emit_gating(kb, gtb, xn2_f32, gwT_sb, gb_sb, mask_dm, idq, p_ready)
    sc.close()
    kb.barrier()
    if stop_after <= 11:
        return

    esems1 = [kb.sem() for _ in range(7)]
    esems2 = [kb.sem() for _ in range(7)]
    for e in range(4):
        sc = kb.scope()
        w1_sb = kb.sb(f"f_w1_{e}", [128, 8 * 32 * 128], BF16, sc)
        sE = kb.sem()
        emit_weight_load_g(kb, W1T[e], w1_sb, 8, 32, None, sE, sE)
        heb = {
            "hex": kb.sb(f"f_hex{e}", [128, 2 * 8 * NCHUNK], BF16, sc),
            "heo": kb.sb(f"f_heo{e}", [128, 2 * 32 * NCHUNK], BF16, sc),
            "het": kb.sb(f"f_het{e}", [128, 2 * NCHUNK], F32, sc),
            "heps": [kb.psum(f"f_heps{e}_{i}", [128, NCHUNK], F32, sc) for i in range(2)],
        }
        emit_proj_phase(kb, heb, "he", [w1_sb], xn2_bf, 8, he_bf, 32,
                        mode="relu_bias", bias_sb=b1_sb[:, e * 32:(e + 1) * 32],
                        w_ready=(sE, sE.n), sems=esems1)
        sc.close()
        kb.barrier()
        if stop_after <= 12:
            return
        sc = kb.scope()
        w2_sb = kb.sb(f"f_w2_{e}", [128, 32 * 8 * 128], BF16, sc)
        sF = kb.sem()
        emit_weight_load_g(kb, W2T[e], w2_sb, 32, 8, None, sF, sF)
        zb = {
            "zx": kb.sb(f"f_zx{e}", [128, 2 * 32 * NCHUNK], BF16, sc),
            "zo": kb.sb(f"f_zo{e}", [128, 2 * 8 * NCHUNK], F32, sc),
            "zt": kb.sb(f"f_zt{e}", [128, 2 * NCHUNK], F32, sc),
            "zmr": kb.sb(f"f_zmr{e}", [1, 2 * NCHUNK], F32, sc),
            "zpsm": kb.psum(f"f_zpsm{e}", [128, NCHUNK], F32, sc),
            "zps": [kb.psum(f"f_zps{e}_{i}", [128, NCHUNK], F32, sc) for i in range(2)],
        }
        emit_proj_phase(kb, zb, "z", [w2_sb], he_bf, 32, x_out, 8,
                        mode="moe_accum", bias_sb=b2_sb[:, e * 8:(e + 1) * 8],
                        mask_dram=mask_dm[e:e + 1, :], accum_store=True,
                        w_ready=(sF, sF.n), sems=esems2)
        sc.close()
        kb.barrier()
        if stop_after <= 13:
            return
    scP.close()


def l1ffn_consts():
    t = np.arange(T_SEQ, dtype=np.float64)
    inv = 1.0 / 10000.0 ** (np.arange(32, dtype=np.float64) / 32.0)
    ang = inv[:, None] * t[None, :]
    cos = np.tile(np.cos(ang), (4, 1)).astype(ml_dtypes.bfloat16)
    sgn = np.tile(np.sin(ang), (4, 1))
    sin = np.concatenate([np.concatenate([-sgn[0:32], sgn[32:64]], 0)] * 2,
                         0).astype(ml_dtypes.bfloat16)
    psw = np.zeros((128, 128), np.float32)
    for i in range(128):
        j = i + 32 if (i % 64) < 32 else i - 32
        psw[j, i] = 1.0
    psw = psw.astype(ml_dtypes.bfloat16)
    r = np.arange(128)[:, None]
    j = np.arange(512)[None, :]
    cs = []
    cb = []
    for v in range(4):
        cs.append(np.where(j <= v * 128 + r, 0.0, -1e9).astype(np.float32))
        cb.append(np.where(j >= r + v * 128, 0.0, -1e9).astype(np.float32))
    cmask = np.concatenate(cs + cb, axis=1)
    idq = np.eye(128, dtype=np.float32)
    return cos, sin, cmask, idq, psw


def _decl_x(nc, name, kind=None):
    if kind:
        return nc.dram_tensor(name, [1024, NTOK], F32, kind=kind)
    return nc.dram_tensor(name, [1024, NTOK], F32)


def build_l0attn():
    nc = bass.Bass("TRN2", target_bir_lowering=False)
    DT = nc.dram_tensor
    x_in = DT("x_in", [1024, NTOK], F32, kind="ExternalInput")
    WqD = DT("WqD", [1024, 1024], F32, kind="ExternalInput")
    WkD = DT("WkD", [1024, 1024], F32, kind="ExternalInput")
    WvD = DT("WvD", [1024, 1024], F32, kind="ExternalInput")
    WbD = DT("WbD", [1024, 8], F32, kind="ExternalInput")
    lnwD = DT("lnwD", [128, 8], F32, kind="ExternalInput")
    lnbD = DT("lnbD", [128, 8], F32, kind="ExternalInput")
    cD = DT("cD", [128, 128 + 3 * 1024], BF16, kind="ExternalInput")
    x_out = DT("x_out", [1024, NTOK], F32, kind="ExternalOutput")
    xn_bf = DT("xn_bf", [1024, NTOK], BF16)
    q_dm = DT("q_dm", [1024, NTOK], BF16)
    k_tm = DT("k_tm", [NTOK, 1024], BF16)
    v_tm = DT("v_tm", [NTOK, 1024], BF16)
    beta_tm = DT("beta_tm", [NTOK, 8], F32)
    o_dm = DT("o_dm", [1024, NTOK], F32)
    es = ExitStack()
    kb = KB(nc, es)
    kb.emit_setup()
    emit_block_l0attn(kb, x_in, x_out, WqD, WkD, WvD, WbD, lnwD, lnbD, cD,
                      xn_bf, q_dm, k_tm, v_tm, beta_tm, o_dm)
    kb.finalize()
    es.close()
    return nc


def build_l0ffn():
    nc = bass.Bass("TRN2", target_bir_lowering=False)
    DT = nc.dram_tensor
    x_in = DT("x_in", [1024, NTOK], F32, kind="ExternalInput")
    wgT = DT("wgT", [1024, 3072], I32, kind="ExternalInput")
    wuT = DT("wuT", [1024, 3072], I32, kind="ExternalInput")
    wdT = DT("wdT", [3072, 1024], I32, kind="ExternalInput")
    x_out = DT("x_out", [1024, NTOK], F32, kind="ExternalOutput")
    xn_bf = DT("xn_bf", [1024, NTOK], BF16)
    h_bf = DT("h_bf", [3072, NTOK], BF16)
    es = ExitStack()
    kb = KB(nc, es)
    kb.emit_setup()
    emit_block_l0ffn(kb, x_in, x_out, wgT, wuT, wdT, xn_bf, h_bf)
    kb.finalize()
    es.close()
    return nc


def build_l1attn():
    nc = bass.Bass("TRN2", target_bir_lowering=False)
    DT = nc.dram_tensor
    x_in = DT("x_in", [1024, NTOK], F32, kind="ExternalInput")
    inpT = DT("inpT", [1024, 4096], F32, kind="ExternalInput")
    xprojT = DT("xprojT", [2048, 128], F32, kind="ExternalInput")
    dtprojT = DT("dtprojT", [64, 2048], F32, kind="ExternalInput")
    outprojT = DT("outprojT", [2048, 1024], F32, kind="ExternalInput")
    lng_D = DT("lng_D", [128, 8], F32, kind="ExternalInput")
    lnb_D = DT("lnb_D", [128, 8], F32, kind="ExternalInput")
    convw_D = DT("convw_D", [128, 64], F32, kind="ExternalInput")
    convb_D = DT("convb_D", [128, 16], F32, kind="ExternalInput")
    dtb_D = DT("dtb_D", [128, 16], F32, kind="ExternalInput")
    Alog_D = DT("Alog_D", [128, 256], F32, kind="ExternalInput")
    Dp_D = DT("Dp_D", [128, 16], F32, kind="ExternalInput")
    x_out = DT("x_out", [1024, NTOK], F32, kind="ExternalOutput")
    xn_bf = DT("xn_bf", [1024, NTOK], BF16)
    xz_dm = DT("xz_dm", [4096, NTOK], BF16)
    xc_dm = DT("xc_dm", [2048, NTOK], BF16)
    xdbl_dm = DT("xdbl_dm", [128, NTOK], BF16)
    dt_dm = DT("dt_dm", [2048, NTOK], F32)
    dtx_dm = DT("dtx_dm", [2048, NTOK], BF16)
    ys_dm = DT("ys_dm", [2048, NTOK], BF16)
    y2_dm = DT("y2_dm", [2048, NTOK], BF16)
    xn1_f32 = DT("xn1_f32", [1024, NTOK], F32)
    xres = DT("xres", [1024, NTOK], F32)
    es = ExitStack()
    kb = KB(nc, es)
    kb.emit_setup()
    emit_block_l1attn(kb, x_in, x_out, inpT, xprojT, dtprojT, outprojT,
                      lng_D, lnb_D, convw_D, convb_D, dtb_D, Alog_D, Dp_D,
                      xn_bf, xz_dm, xc_dm, xdbl_dm, dt_dm, dtx_dm, ys_dm,
                      y2_dm, xn1_f32, xres)
    kb.finalize()
    es.close()
    return nc


def build_l1ffn():
    nc = bass.Bass("TRN2", target_bir_lowering=False)
    DT = nc.dram_tensor
    x_in = DT("x_in", [1024, NTOK], F32, kind="ExternalInput")
    WqT_D = DT("WqT", [1024, 1024], F32, kind="ExternalInput")
    WdkvT_D = DT("WdkvT", [1024, 512], F32, kind="ExternalInput")
    WukT_D = DT("WukT", [512, 1024], F32, kind="ExternalInput")
    WuvT_D = DT("WuvT", [512, 1024], F32, kind="ExternalInput")
    WoT_D = DT("WoT", [1024, 1024], F32, kind="ExternalInput")
    gwT_D = DT("gwT", [128, 32], F32, kind="ExternalInput")
    gb_D = DT("gb", [1, 4], F32, kind="ExternalInput")
    W1T_D = DT("W1T", [4, 1024, 4096], F32, kind="ExternalInput")
    b1_D = DT("b1D", [128, 128], F32, kind="ExternalInput")
    W2T_D = DT("W2T", [4, 4096, 1024], F32, kind="ExternalInput")
    b2_D = DT("b2D", [128, 32], F32, kind="ExternalInput")
    n1_D = DT("n1D", [128, 8], F32, kind="ExternalInput")
    n2_D = DT("n2D", [128, 8], F32, kind="ExternalInput")
    lng_D = DT("lngD", [128, 4], F32, kind="ExternalInput")
    lnb_D = DT("lnbD", [128, 4], F32, kind="ExternalInput")
    ropec_D = DT("ropecD", [128, 2048], BF16, kind="ExternalInput")
    ropes_D = DT("ropesD", [128, 2048], BF16, kind="ExternalInput")
    psw_D = DT("pswD", [128, 128], BF16, kind="ExternalInput")
    cmask_D = DT("cmaskD", [128, 4096], F32, kind="ExternalInput")
    idq_D = DT("idqD", [128, 128], F32, kind="ExternalInput")
    x_out = DT("x_out", [1024, NTOK], F32, kind="ExternalOutput")
    xn_bf = DT("xn_bf", [1024, NTOK], BF16)
    xp_f32 = DT("xp_f32", [1024, NTOK], F32)
    xres0 = DT("xres0", [1024, NTOK], F32)
    q_dm = DT("q_dm", [1024, NTOK], BF16)
    k_dm = DT("k_dm", [1024, NTOK], BF16)
    lat_f32 = DT("lat_f32", [512, NTOK], F32)
    latn_bf = DT("latn_bf", [512, NTOK], BF16)
    vt_tm = DT("vt_tm", [NTOK, 1024], BF16)
    attn_dm = DT("attn_dm", [1024, NTOK], BF16)
    xc2_f32 = DT("xc2_f32", [1024, NTOK], F32)
    xn2_bf = DT("xn2_bf", [1024, NTOK], BF16)
    xn2_f32 = DT("xn2_f32", [1024, NTOK], F32)
    mask_dm = DT("mask_dm", [4, NTOK], F32)
    he_bf = DT("he_bf", [4096, NTOK], BF16)
    es = ExitStack()
    kb = KB(nc, es)
    kb.emit_setup()
    emit_block_l1ffn(kb, x_in, x_out, WqT_D, WdkvT_D, WukT_D, WuvT_D, WoT_D,
                     gwT_D, gb_D, W1T_D, b1_D, W2T_D, b2_D, n1_D, n2_D,
                     lng_D, lnb_D, ropec_D, ropes_D, cmask_D, idq_D, psw_D,
                     xn_bf, xp_f32, xres0, q_dm, k_dm, lat_f32, latn_bf,
                     vt_tm, attn_dm, xc2_f32, xn2_bf, xn2_f32, mask_dm, he_bf)
    kb.finalize()
    es.close()
    return nc


def _colmaj(a, tiles):
    return np.ascontiguousarray(np.asarray(a, np.float32).reshape(tiles, 128).T)


FUSED = True


def _common_dicts(L0_attn_Wq, L0_attn_Wk, L0_attn_Wv, L0_attn_Wb,
                  L0_attn_ln_w, L0_attn_ln_b, L0_ffn_w_gate, L0_ffn_w_up,
                  L0_ffn_w_down, L1_attn_ln_gamma, L1_attn_ln_beta,
                  L1_attn_in_proj_w, L1_attn_conv_w, L1_attn_conv_b,
                  L1_attn_x_proj_w, L1_attn_dt_proj_w, L1_attn_dt_proj_b,
                  L1_attn_A_log, L1_attn_D_param, L1_attn_out_proj_w,
                  L1_ffn_norm1_w, L1_ffn_norm2_w, L1_ffn_Wq, L1_ffn_Wdkv,
                  L1_ffn_Wuk, L1_ffn_Wuv, L1_ffn_Wo, L1_ffn_ln_g,
                  L1_ffn_ln_b, L1_ffn_gate_w, L1_ffn_gate_b, L1_ffn_W1,
                  L1_ffn_b1, L1_ffn_W2, L1_ffn_b2):
    f32 = lambda a: np.ascontiguousarray(np.asarray(a, np.float32))
    c1 = {"WqD": f32(L0_attn_Wq), "WkD": f32(L0_attn_Wk),
          "WvD": f32(L0_attn_Wv), "WbD": f32(L0_attn_Wb),
          "lnwD": _colmaj(L0_attn_ln_w, 8), "lnbD": _colmaj(L0_attn_ln_b, 8),
          "cD": l0attn_consts()}
    c2 = {"wgT": np.ascontiguousarray(np.asarray(L0_ffn_w_gate).T),
          "wuT": np.ascontiguousarray(np.asarray(L0_ffn_w_up).T),
          "wdT": np.ascontiguousarray(np.asarray(L0_ffn_w_down).T)}
    xpp = np.zeros((2048, 128), np.float32)
    xpp[:, 0:96] = np.asarray(L1_attn_x_proj_w, np.float32).T
    c3 = {
        "inpT": np.ascontiguousarray(np.asarray(L1_attn_in_proj_w, np.float32).T),
        "xprojT": xpp,
        "dtprojT": np.ascontiguousarray(np.asarray(L1_attn_dt_proj_w, np.float32).T),
        "outprojT": np.ascontiguousarray(np.asarray(L1_attn_out_proj_w, np.float32).T),
        "lng_D": _colmaj(L1_attn_ln_gamma, 8), "lnb_D": _colmaj(L1_attn_ln_beta, 8),
        "convw_D": np.ascontiguousarray(np.asarray(L1_attn_conv_w, np.float32)
                                        .reshape(16, 128, 4).transpose(1, 0, 2)
                                        .reshape(128, 64)),
        "convb_D": _colmaj(L1_attn_conv_b, 16),
        "dtb_D": _colmaj(L1_attn_dt_proj_b, 16),
        "Alog_D": np.ascontiguousarray(np.asarray(L1_attn_A_log, np.float32)
                                       .reshape(16, 128, 16).transpose(1, 0, 2)
                                       .reshape(128, 256)),
        "Dp_D": _colmaj(L1_attn_D_param, 16),
    }
    cos, sin, cmask, idq, psw = l1ffn_consts()
    c4 = {
        "WqT": np.ascontiguousarray(np.asarray(L1_ffn_Wq, np.float32).T),
        "WdkvT": np.ascontiguousarray(np.asarray(L1_ffn_Wdkv, np.float32).T),
        "WukT": np.ascontiguousarray(np.asarray(L1_ffn_Wuk, np.float32).T),
        "WuvT": np.ascontiguousarray(np.asarray(L1_ffn_Wuv, np.float32).T),
        "WoT": np.ascontiguousarray(np.asarray(L1_ffn_Wo, np.float32).T),
        "gwT": np.ascontiguousarray(np.asarray(L1_ffn_gate_w, np.float32).T
                                    .reshape(8, 128, 4).transpose(1, 0, 2)
                                    .reshape(128, 32)),
        "gb": np.asarray(L1_ffn_gate_b, np.float32).reshape(1, 4),
        "W1T": np.ascontiguousarray(np.asarray(L1_ffn_W1, np.float32)
                                    .transpose(0, 2, 1)),
        "b1D": np.ascontiguousarray(np.concatenate(
            [np.asarray(L1_ffn_b1, np.float32)[e].reshape(32, 128).T
             for e in range(4)], 1)),
        "W2T": np.ascontiguousarray(np.asarray(L1_ffn_W2, np.float32)
                                    .transpose(0, 2, 1)),
        "b2D": np.ascontiguousarray(np.concatenate(
            [np.asarray(L1_ffn_b2, np.float32)[e].reshape(8, 128).T
             for e in range(4)], 1)),
        "n1D": _colmaj(L1_ffn_norm1_w, 8), "n2D": _colmaj(L1_ffn_norm2_w, 8),
        "lngD": _colmaj(L1_ffn_ln_g, 4), "lnbD": _colmaj(L1_ffn_ln_b, 4),
        "ropecD": cos, "ropesD": sin, "cmaskD": cmask, "idqD": idq, "pswD": psw,
    }
    return c1, c2, c3, c4


def kernel(L0_attn_x, L0_attn_Wq, L0_attn_Wk, L0_attn_Wv, L0_attn_Wb,
           L0_attn_ln_w, L0_attn_ln_b, L0_ffn_w_gate, L0_ffn_w_up,
           L0_ffn_w_down, L1_attn_ln_gamma, L1_attn_ln_beta,
           L1_attn_in_proj_w, L1_attn_conv_w, L1_attn_conv_b,
           L1_attn_x_proj_w, L1_attn_dt_proj_w, L1_attn_dt_proj_b,
           L1_attn_A_log, L1_attn_D_param, L1_attn_out_proj_w,
           L1_ffn_norm1_w, L1_ffn_norm2_w, L1_ffn_Wq, L1_ffn_Wdkv,
           L1_ffn_Wuk, L1_ffn_Wuv, L1_ffn_Wo, L1_ffn_ln_g, L1_ffn_ln_b,
           L1_ffn_gate_w, L1_ffn_gate_b, L1_ffn_W1, L1_ffn_b1, L1_ffn_W2,
           L1_ffn_b2):
    shards = shard_x(np.asarray(L0_attn_x, np.float32))
    c1, c2, c3, c4 = _common_dicts(
        L0_attn_Wq, L0_attn_Wk, L0_attn_Wv, L0_attn_Wb, L0_attn_ln_w,
        L0_attn_ln_b, L0_ffn_w_gate, L0_ffn_w_up, L0_ffn_w_down,
        L1_attn_ln_gamma, L1_attn_ln_beta, L1_attn_in_proj_w, L1_attn_conv_w,
        L1_attn_conv_b, L1_attn_x_proj_w, L1_attn_dt_proj_w, L1_attn_dt_proj_b,
        L1_attn_A_log, L1_attn_D_param, L1_attn_out_proj_w, L1_ffn_norm1_w,
        L1_ffn_norm2_w, L1_ffn_Wq, L1_ffn_Wdkv, L1_ffn_Wuk, L1_ffn_Wuv,
        L1_ffn_Wo, L1_ffn_ln_g, L1_ffn_ln_b, L1_ffn_gate_w, L1_ffn_gate_b,
        L1_ffn_W1, L1_ffn_b1, L1_ffn_W2, L1_ffn_b2)
    if FUSED:
        c4f = dict(c4)
        c4f["flngD"] = c4f.pop("lngD")
        c4f["flnbD"] = c4f.pop("lnbD")
        merged = {}
        for c in (c1, c2, c3, c4f):
            merged.update(c)
        nc = build_fused()
        in_maps = [dict(merged, x_in=shards[ci]) for ci in range(NCORES)]
        res = run_bass_kernel_spmd(nc, in_maps, core_ids=list(range(NCORES)))
        shards = [r["x_out"] for r in res.results]
        return unshard_x(shards)
    nc = build_l0attn()
    in_maps = [dict(c1, x_in=shards[c]) for c in range(NCORES)]
    res = run_bass_kernel_spmd(nc, in_maps, core_ids=list(range(NCORES)))
    shards = [r["x_out"] for r in res.results]
    nc = build_l0ffn()
    in_maps = [dict(c2, x_in=shards[c]) for c in range(NCORES)]
    res = run_bass_kernel_spmd(nc, in_maps, core_ids=list(range(NCORES)))
    shards = [r["x_out"] for r in res.results]
    nc = build_l1attn()
    in_maps = [dict(c3, x_in=shards[c]) for c in range(NCORES)]
    res = run_bass_kernel_spmd(nc, in_maps, core_ids=list(range(NCORES)))
    shards = [r["x_out"] for r in res.results]
    nc = build_l1ffn()
    in_maps = [dict(c4, x_in=shards[c]) for c in range(NCORES)]
    res = run_bass_kernel_spmd(nc, in_maps, core_ids=list(range(NCORES)))
    shards = [r["x_out"] for r in res.results]
    return unshard_x(shards)



def build_fused():
    nc = bass.Bass("TRN2", target_bir_lowering=False)
    DT = nc.dram_tensor
    x_in = DT("x_in", [1024, NTOK], F32, kind="ExternalInput")
    x_out = DT("x_out", [1024, NTOK], F32, kind="ExternalOutput")
    x1 = DT("x1", [1024, NTOK], F32)
    x2 = DT("x2", [1024, NTOK], F32)
    x3 = DT("x3", [1024, NTOK], F32)
    WqD = DT("WqD", [1024, 1024], F32, kind="ExternalInput")
    WkD = DT("WkD", [1024, 1024], F32, kind="ExternalInput")
    WvD = DT("WvD", [1024, 1024], F32, kind="ExternalInput")
    WbD = DT("WbD", [1024, 8], F32, kind="ExternalInput")
    lnwD = DT("lnwD", [128, 8], F32, kind="ExternalInput")
    lnbD = DT("lnbD", [128, 8], F32, kind="ExternalInput")
    cD = DT("cD", [128, 128 + 3 * 1024], BF16, kind="ExternalInput")
    a_xn = DT("dr_a_xn", [1024, NTOK], BF16)
    a_q = DT("dr_a_q", [1024, NTOK], BF16)
    a_ktm = DT("dr_a_ktm", [NTOK, 1024], BF16)
    a_vtm = DT("dr_a_vtm", [NTOK, 1024], BF16)
    a_beta = DT("dr_a_beta", [NTOK, 8], F32)
    a_o = DT("dr_a_o", [1024, NTOK], F32)
    wgT = DT("wgT", [1024, 3072], I32, kind="ExternalInput")
    wuT = DT("wuT", [1024, 3072], I32, kind="ExternalInput")
    wdT = DT("wdT", [3072, 1024], I32, kind="ExternalInput")
    f_xn = DT("dr_f_xn", [1024, NTOK], BF16)
    f_h = DT("dr_f_h", [3072, NTOK], BF16)
    inpT = DT("inpT", [1024, 4096], F32, kind="ExternalInput")
    xprojT = DT("xprojT", [2048, 128], F32, kind="ExternalInput")
    dtprojT = DT("dtprojT", [64, 2048], F32, kind="ExternalInput")
    outprojT = DT("outprojT", [2048, 1024], F32, kind="ExternalInput")
    lng_D = DT("lng_D", [128, 8], F32, kind="ExternalInput")
    lnb_D = DT("lnb_D", [128, 8], F32, kind="ExternalInput")
    convw_D = DT("convw_D", [128, 64], F32, kind="ExternalInput")
    convb_D = DT("convb_D", [128, 16], F32, kind="ExternalInput")
    dtb_D = DT("dtb_D", [128, 16], F32, kind="ExternalInput")
    Alog_D = DT("Alog_D", [128, 256], F32, kind="ExternalInput")
    Dp_D = DT("Dp_D", [128, 16], F32, kind="ExternalInput")
    m_xn = DT("dr_m_xn", [1024, NTOK], BF16)
    m_xz = DT("dr_m_xz", [4096, NTOK], BF16)
    m_xc = DT("dr_m_xc", [2048, NTOK], BF16)
    m_xdbl = DT("dr_m_xdbl", [128, NTOK], BF16)
    m_dt = DT("dr_m_dt", [2048, NTOK], F32)
    m_dtx = DT("dr_m_dtx", [2048, NTOK], BF16)
    m_ys = DT("dr_m_ys", [2048, NTOK], BF16)
    m_y2 = DT("dr_m_y2", [2048, NTOK], BF16)
    m_xn1 = DT("dr_m_xn1", [1024, NTOK], F32)
    m_xres = DT("dr_m_xres", [1024, NTOK], F32)
    WqT_D = DT("WqT", [1024, 1024], F32, kind="ExternalInput")
    WdkvT_D = DT("WdkvT", [1024, 512], F32, kind="ExternalInput")
    WukT_D = DT("WukT", [512, 1024], F32, kind="ExternalInput")
    WuvT_D = DT("WuvT", [512, 1024], F32, kind="ExternalInput")
    WoT_D = DT("WoT", [1024, 1024], F32, kind="ExternalInput")
    gwT_D = DT("gwT", [128, 32], F32, kind="ExternalInput")
    gb_D = DT("gb", [1, 4], F32, kind="ExternalInput")
    W1T_D = DT("W1T", [4, 1024, 4096], F32, kind="ExternalInput")
    b1_D = DT("b1D", [128, 128], F32, kind="ExternalInput")
    W2T_D = DT("W2T", [4, 4096, 1024], F32, kind="ExternalInput")
    b2_D = DT("b2D", [128, 32], F32, kind="ExternalInput")
    n1_D = DT("n1D", [128, 8], F32, kind="ExternalInput")
    n2_D = DT("n2D", [128, 8], F32, kind="ExternalInput")
    flng_D = DT("flngD", [128, 4], F32, kind="ExternalInput")
    flnb_D = DT("flnbD", [128, 4], F32, kind="ExternalInput")
    ropec_D = DT("ropecD", [128, 2048], BF16, kind="ExternalInput")
    ropes_D = DT("ropesD", [128, 2048], BF16, kind="ExternalInput")
    psw_D = DT("pswD", [128, 128], BF16, kind="ExternalInput")
    cmask_D = DT("cmaskD", [128, 4096], F32, kind="ExternalInput")
    idq_D = DT("idqD", [128, 128], F32, kind="ExternalInput")
    g_xn = DT("dr_g_xn", [1024, NTOK], BF16)
    g_xp = DT("dr_g_xp", [1024, NTOK], F32)
    g_xres0 = DT("dr_g_xres0", [1024, NTOK], F32)
    g_q = DT("dr_g_q", [1024, NTOK], BF16)
    g_k = DT("dr_g_k", [1024, NTOK], BF16)
    g_lat = DT("dr_g_lat", [512, NTOK], F32)
    g_latn = DT("dr_g_latn", [512, NTOK], BF16)
    g_vt = DT("dr_g_vt", [NTOK, 1024], BF16)
    g_attn = DT("dr_g_attn", [1024, NTOK], BF16)
    g_xc2 = DT("dr_g_xc2", [1024, NTOK], F32)
    g_xn2 = DT("dr_g_xn2", [1024, NTOK], BF16)
    g_xn2f = DT("dr_g_xn2f", [1024, NTOK], F32)
    g_mask = DT("dr_g_mask", [4, NTOK], F32)
    g_he = DT("dr_g_he", [4096, NTOK], BF16)

    es = ExitStack()
    kb = KB(nc, es)
    kb.emit_setup()
    emit_block_l0attn(kb, x_in, x1, WqD, WkD, WvD, WbD, lnwD, lnbD, cD,
                      a_xn, a_q, a_ktm, a_vtm, a_beta, a_o)
    kb.barrier()
    emit_block_l0ffn(kb, x1, x2, wgT, wuT, wdT, f_xn, f_h)
    kb.barrier()
    emit_block_l1attn(kb, x2, x3, inpT, xprojT, dtprojT, outprojT,
                      lng_D, lnb_D, convw_D, convb_D, dtb_D, Alog_D, Dp_D,
                      m_xn, m_xz, m_xc, m_xdbl, m_dt, m_dtx, m_ys,
                      m_y2, m_xn1, m_xres)
    kb.barrier()
    emit_block_l1ffn(kb, x3, x_out, WqT_D, WdkvT_D, WukT_D, WuvT_D, WoT_D,
                     gwT_D, gb_D, W1T_D, b1_D, W2T_D, b2_D, n1_D, n2_D,
                     flng_D, flnb_D, ropec_D, ropes_D, cmask_D, idq_D, psw_D,
                     g_xn, g_xp, g_xres0, g_q, g_k, g_lat, g_latn,
                     g_vt, g_attn, g_xc2, g_xn2, g_xn2f, g_mask, g_he)
    kb.finalize()
    es.close()
    return nc


def kernel_fused(inputs_common_by_launch, shards):
    nc = build_fused()
    c1, c2, c3, c4 = inputs_common_by_launch
    common = {}
    for c in (c1, c2, c3, c4):
        common.update(c)
    in_maps = [dict(common, x_in=shards[cix]) for cix in range(NCORES)]
    res = run_bass_kernel_spmd(nc, in_maps, core_ids=list(range(NCORES)))
    return [r["x_out"] for r in res.results]


# version 12
# speedup vs baseline: 1.5544x; 1.0002x over previous
import math
import numpy as np
import ml_dtypes
from contextlib import ExitStack

import concourse.bass as bass
import concourse.mybir as mybir
from concourse.bass_utils import run_bass_kernel_spmd

F32 = mybir.dt.float32
BF16 = mybir.dt.bfloat16
I32 = mybir.dt.int32
AF = mybir.ActivationFunctionType
ALU = mybir.AluOpType
AX = mybir.AxisListType

NCORES = 8
B_LOC = 4
T_SEQ = 2048
NTOK = B_LOC * T_SEQ
D = 1024
NCHUNK = 512
ENGINES = ("sync", "tensor", "vector", "scalar", "gpsimd")


class SemC:
    def __init__(self, h):
        self.h = h
        self.n = 0

    def inc(self, k=1):
        self.n += k
        return self.n


class KB:
    def __init__(self, nc, es):
        self.nc = nc
        self.es = es
        self.ops = {e: [] for e in ENGINES}
        self._semi = 0
        self.bar = self.sem("bar")
        self.onesf = None
        self.onescol = None
        self.fsem = {e: self.sem("fence_" + e) for e in ENGINES}

    def sem(self, name=None):
        if name is not None:
            self._semi += 1
            h = self.es.enter_context(self.nc.semaphore(name))
            return SemC(h)
        if not hasattr(self, "_pool"):
            self._pool = []
            self._pooli = 0
        if len(self._pool) < 24:
            self._semi += 1
            h = self.es.enter_context(self.nc.semaphore(f"gp{len(self._pool)}"))
            sc = SemC(h)
            self._pool.append(sc)
            self._pooli = len(self._pool)
            return sc
        sc = self._pool[self._pooli % 24]
        self._pooli += 1
        return sc

    def sb(self, name, shape, dtype, scope=None):
        return (scope or self.es).enter_context(self.nc.sbuf_tensor(name, shape, dtype))

    def psum(self, name, shape, dtype=F32, scope=None):
        return (scope or self.es).enter_context(self.nc.psum_tensor(name, shape, dtype))

    def scope(self):
        return ExitStack()

    def on(self, eng, fn):
        self.ops[eng].append(fn)

    def barrier(self):
        t1 = self.bar.n + len(ENGINES)
        t2 = t1 + 1
        self.bar.n = t2
        pool = list(getattr(self, "_pool", []))
        for e in ENGINES:
            def f(eng, t1=t1, t2=t2, e=e, pool=pool):
                if e == "vector":
                    eng.memset(self.junk[:, 0:4], 0.0).then_inc(self.bar.h, 1)
                elif e == "scalar":
                    eng.copy(self.junk[:, 4:8], self.junk[:, 0:4]).then_inc(self.bar.h, 1)
                elif e == "gpsimd":
                    eng.memset(self.junk[:, 8:12], 0.0).then_inc(self.bar.h, 1)
                else:
                    eng.sem_inc(self.bar.h, 1)
                eng.wait_ge(self.bar.h, t1)
                if e == "gpsimd":
                    eng.sem_inc(self.bar.h, 1)
                eng.wait_ge(self.bar.h, t2)
            self.on(e, f)

    def emit_setup(self):
        self.onesf = self.sb("g_onesf", [1, 128], F32)
        self.onescol = self.sb("g_onescol", [128, 1], F32)
        self.onesb = self.sb("g_onesb", [1, 128], BF16)
        self.junk = self.sb("g_junk", [128, 16], F32)
        s_su = self.sem("setup")

        def f(eng):
            eng.memset(self.onesf[:], 1.0)
            eng.memset(self.onesb[:], 1.0)
            eng.memset(self.onescol[:], 1.0).then_inc(s_su.h, 1)
        self.on("vector", f)
        s_su.inc(1)
        self.barrier()

    def finalize(self):
        with self.nc.Block() as block:
            for ename in ENGINES:
                deco = getattr(block, ename)
                lst = self.ops[ename]
                def body(eng, lst=lst):
                    for f in lst:
                        f(eng)
                deco(body)



def emit_weight_load(kb, w_dram, wq_sb, ktiles, mtiles, stage_sb, sem_ld, sem_cvt,
                     scale=None, src_dt=F32):
    nstage = stage_sb.shape[1] // 128
    idx = 0
    for kt in range(ktiles):
        for mt in range(mtiles):
            slot = idx % nstage
            need_cvt = max(0, idx - nstage + 1)
            base_ld = sem_ld.n

            def fl(eng, kt=kt, mt=mt, slot=slot, need_cvt=need_cvt):
                if need_cvt:
                    eng.wait_ge(sem_cvt.h, need_cvt)
                eng.dma_start(
                    out=stage_sb[:, slot * 128:(slot + 1) * 128],
                    in_=w_dram[kt * 128:(kt + 1) * 128, mt * 128:(mt + 1) * 128],
                ).then_inc(sem_ld.h, 16)
            kb.on("sync", fl)
            sem_ld.inc(16)
            ldv = sem_ld.n

            def fc(eng, idx=idx, slot=slot, ldv=ldv, kt=kt, mt=mt):
                eng.wait_ge(sem_ld.h, ldv)
                col = (kt * mtiles + mt) * 128
                if scale is not None:
                    eng.tensor_scalar(
                        wq_sb[:, col:col + 128],
                        stage_sb[:, slot * 128:(slot + 1) * 128],
                        float(scale), None, ALU.mult,
                    ).then_inc(sem_cvt.h, 1)
                else:
                    eng.tensor_copy(
                        wq_sb[:, col:col + 128],
                        stage_sb[:, slot * 128:(slot + 1) * 128],
                    ).then_inc(sem_cvt.h, 1)
            kb.on("vector", fc)
            sem_cvt.inc(1)
            idx += 1


def ceil_div(a, b):
    return (a + b - 1) // b



def shard_x(x):
    shards = []
    for c in range(NCORES):
        xs = x[c * B_LOC:(c + 1) * B_LOC]
        xs = xs.reshape(NTOK, D).T
        shards.append(np.ascontiguousarray(xs))
    return shards


def unshard_x(shards):
    out = np.empty((NCORES * B_LOC, T_SEQ, D), np.float32)
    for c in range(NCORES):
        out[c * B_LOC:(c + 1) * B_LOC] = shards[c].T.reshape(B_LOC, T_SEQ, D)
    return out


def hostT(a):
    return np.ascontiguousarray(np.asarray(a).T)


def emit_ln_phase(kb, bufs, x_dram, xn_dram, dim, gamma_sb=None, beta_sb=None,
                  rms=False, eps=1e-5, xn_f32_dram=None, nchunk=NCHUNK,
                  ntok=NTOK, res_dram=None, sum_out_dram=None):
    pt = dim // 128
    nchunks = ntok // nchunk
    xs = bufs["lnx"]
    sq = bufs["lnsq"]
    tmpx = bufs["lntmp"]
    xnb = bufs["lnxn"]
    xnf = bufs.get("lnxnf")
    rows = bufs["lnrows"]
    onesf = bufs["onesf"]
    onescol = bufs["onescol"]
    pstat = bufs["ps_stat"]
    pb1 = bufs["ps_b1"]
    pb2 = bufs["ps_b2"]
    s_ld = kb.sem(); s_sq = kb.sem(); s_sqc = kb.sem(); s_st = kb.sem()
    s_row = kb.sem(); s_bc = kb.sem(); s_xn = kb.sem(); s_out = kb.sem()
    s_res = kb.sem()
    b_sqc = s_sqc.n; b_out = s_out.n
    resb = bufs.get("lnres")
    sumb = bufs.get("lnsum")
    use_f32 = xn_f32_dram is not None
    nst = 16 * (1 + (1 if use_f32 else 0) + (1 if sum_out_dram is not None else 0))
    n = nchunk
    rmu = rows[0:1, 0:n]; rrstd = rows[0:1, n:2 * n]
    rmr = rows[0:1, 2 * n:3 * n]; rtmp = rows[0:1, 3 * n:4 * n]
    xn_done = []
    row_done = []

    for c in range(nchunks):
        slot = c % 2
        xsv = xs[:, slot * pt * nchunk:(slot + 1) * pt * nchunk]
        xnv = xnb[:, slot * pt * nchunk:(slot + 1) * pt * nchunk]
        xnfv = xnf[:, slot * pt * nchunk:(slot + 1) * pt * nchunk] if use_f32 else None

        need_out = b_out + nst * max(0, c - 1) if c >= 2 else 0

        def fl(eng, c=c, xsv=xsv, need_out=need_out):
            if need_out:
                eng.wait_ge(s_out.h, need_out)
            eng.dma_start(
                out=xsv.rearrange("p (t n) -> p t n", t=pt),
                in_=x_dram[:, c * nchunk:(c + 1) * nchunk].rearrange(
                    "(t p) n -> p t n", p=128),
            ).then_inc(s_ld.h, 16)
        kb.on("sync", fl)
        s_ld.inc(16)
        ldv = s_ld.n
        if resb is not None:
            rv = resb[:, slot * pt * nchunk:(slot + 1) * pt * nchunk]

            def flr(eng, c=c, rv=rv, need_out=need_out):
                if need_out:
                    eng.wait_ge(s_out.h, need_out)
                eng.dma_start(
                    out=rv.rearrange("p (t n) -> p t n", t=pt),
                    in_=res_dram[:, c * nchunk:(c + 1) * nchunk].rearrange(
                        "(t p) n -> p t n", p=128),
                ).then_inc(s_res.h, 16)
            kb.on("sync", flr)
            s_res.inc(16)
        else:
            rv = None
        resv = s_res.n

        for t in range(pt):
            sqslot = t % 2
            sqv_need = s_sqc.n - 1 + t
            def fsq(eng, t=t, xsv=xsv, ldv=ldv, sqslot=sqslot, c=c):
                if t == 0:
                    eng.wait_ge(s_ld.h, ldv)
                if t >= 2:
                    eng.wait_ge(s_sqc.h, b_sqc + (c * pt) + t - 1)
                elif c > 0:
                    eng.wait_ge(s_sqc.h, b_sqc + c * pt)
                eng.activation(sq[:, sqslot * n:(sqslot + 1) * n],
                               xsv[:, t * n:(t + 1) * n],
                               AF.Square).then_inc(s_sq.h, 1)
            kb.on("scalar", fsq)
            s_sq.inc(1)
            sqv = s_sq.n

            def fmm(eng, t=t, sqv=sqv, c=c, xsv=xsv, sqslot=sqslot):
                eng.wait_ge(s_sq.h, sqv)
                if t == 0 and c > 0:
                    eng.wait_ge(s_row.h, row_done[c - 1])
                last = (t == pt - 1)
                mm1 = eng.matmul(pstat[0:1, 0:n], onescol[:],
                                 sq[:, sqslot * n:(sqslot + 1) * n],
                                 start=(t == 0), stop=last)
                mm1.then_inc(s_sqc.h, 1)
                mm2 = eng.matmul(pstat[0:1, n:2 * n], onescol[:],
                                 xsv[:, t * n:(t + 1) * n],
                                 start=(t == 0), stop=last)
                if last:
                    mm2.then_inc(s_st.h, 1)
            kb.on("tensor", fmm)
            s_sqc.inc(1)
        s_st.inc(1)
        stv = s_st.n

        fs = kb.fsem["vector"]
        if not rms:
            f1 = fs.inc(); f2 = fs.inc(); f3 = fs.inc()
        else:
            f1 = f2 = f3 = 0

        def frow(eng, stv=stv, f1=f1, f2=f2, f3=f3):
            eng.wait_ge(s_st.h, stv)
            if rms:
                eng.tensor_scalar(rtmp, pstat[0:1, 0:n], 1.0 / dim, float(eps),
                                  ALU.mult, ALU.add).then_inc(s_row.h, 1)
            else:
                eng.tensor_scalar(rmu, pstat[0:1, n:2 * n], 1.0 / dim, None, ALU.mult)
                eng.tensor_scalar(rtmp, pstat[0:1, 0:n], 1.0 / dim, None, ALU.mult).then_inc(fs.h, 1)
                eng.wait_ge(fs.h, f1)
                eng.tensor_tensor(rmr, rmu, rmu, ALU.mult).then_inc(fs.h, 1)
                eng.wait_ge(fs.h, f2)
                eng.tensor_tensor(rtmp, rtmp, rmr, ALU.subtract).then_inc(fs.h, 1)
                eng.wait_ge(fs.h, f3)
                eng.tensor_scalar(rtmp, rtmp, 1.0, float(eps),
                                  ALU.mult, ALU.add).then_inc(s_row.h, 1)
        kb.on("vector", frow)
        s_row.inc(1)
        v0 = s_row.n

        def fsqrt(eng, v0=v0):
            eng.wait_ge(s_row.h, v0)
            eng.activation(rtmp, rtmp, AF.Sqrt).then_inc(s_row.h, 1)
        kb.on("scalar", fsqrt)
        s_row.inc(1)
        v1 = s_row.n

        f4 = fs.inc()

        def frec(eng, v1=v1, f4=f4):
            eng.wait_ge(s_row.h, v1)
            eng.reciprocal(rrstd, rtmp).then_inc(fs.h, 1)
            eng.wait_ge(fs.h, f4)
            if not rms:
                eng.tensor_tensor(rmr, rmu, rrstd, ALU.mult).then_inc(s_row.h, 1)
            else:
                eng.tensor_copy(rmr, rrstd).then_inc(s_row.h, 1)
        kb.on("vector", frec)
        s_row.inc(1)
        v2 = s_row.n
        row_done.append(v2)

        def fbc(eng, v2=v2, c=c):
            eng.wait_ge(s_row.h, v2)
            if c > 0:
                eng.wait_ge(s_xn.h, xn_done[c - 1])
            eng.matmul(pb1[:, :], onesf[:], rrstd, start=True, stop=True)
            m2 = eng.matmul(pb2[:, :], onesf[:], rmr, start=True, stop=True)
            m2.then_inc(s_bc.h, 1)
        kb.on("tensor", fbc)
        s_bc.inc(1)
        bcv = s_bc.n

        for t in range(pt):
            def fxn(eng, t=t, bcv=bcv, xsv=xsv, xnv=xnv, xnfv=xnfv, rv=rv,
                    resv=resv, need_out=need_out, slot=slot):
                if t == 0:
                    eng.wait_ge(s_bc.h, bcv)
                    if need_out:
                        eng.wait_ge(s_out.h, need_out)
                    if rv is not None:
                        eng.wait_ge(s_res.h, resv)
                src = xsv[:, t * n:(t + 1) * n]
                tgt_b = xnv[:, t * n:(t + 1) * n]
                work = xnfv[:, t * n:(t + 1) * n] if xnfv is not None else tmpx[:]
                eng.tensor_tensor(tmpx[:], src, pb1[:, :], ALU.mult)
                if rms:
                    if gamma_sb is not None:
                        ins = eng.tensor_scalar(work, tmpx[:], gamma_sb[:, t:t + 1],
                                                None, ALU.mult)
                    else:
                        ins = eng.tensor_copy(work, tmpx[:])
                else:
                    ins = eng.tensor_tensor(work, tmpx[:], pb2[:, :], ALU.subtract)
                    if gamma_sb is not None:
                        ins = eng.tensor_scalar(work, work, gamma_sb[:, t:t + 1],
                                                beta_sb[:, t:t + 1], ALU.mult, ALU.add)
                if rv is not None:
                    ins = eng.tensor_tensor(tgt_b, work, rv[:, t * n:(t + 1) * n],
                                            ALU.add)
                elif sumb is not None:
                    ins = eng.tensor_copy(tgt_b, work)
                    ins = eng.tensor_tensor(
                        sumb[:, (slot * pt + t) * n:(slot * pt + t + 1) * n],
                        work, src, ALU.add)
                else:
                    ins = eng.tensor_copy(tgt_b, work)
                ins.then_inc(s_xn.h, 1)
            kb.on("vector", fxn)
            s_xn.inc(1)
        xn_done.append(s_xn.n)
        xnn = s_xn.n

        def fst(eng, c=c, xnv=xnv, xnfv=xnfv, xnn=xnn, slot=slot):
            eng.wait_ge(s_xn.h, xnn)
            eng.dma_start(
                out=xn_dram[:, c * nchunk:(c + 1) * nchunk].rearrange(
                    "(t p) n -> p t n", p=128),
                in_=xnv.rearrange("p (t n) -> p t n", t=pt),
            ).then_inc(s_out.h, 16)
            if xnfv is not None:
                eng.dma_start(
                    out=xn_f32_dram[:, c * nchunk:(c + 1) * nchunk].rearrange(
                        "(t p) n -> p t n", p=128),
                    in_=xnfv.rearrange("p (t n) -> p t n", t=pt),
                ).then_inc(s_out.h, 16)
            if sumb is not None:
                eng.dma_start(
                    out=sum_out_dram[:, c * nchunk:(c + 1) * nchunk].rearrange(
                        "(t p) n -> p t n", p=128),
                    in_=sumb[:, slot * pt * nchunk:(slot + 1) * pt * nchunk
                             ].rearrange("p (t n) -> p t n", t=pt),
                ).then_inc(s_out.h, 16)
        kb.on("gpsimd", fst)
        s_out.inc(nst)

    outn = s_out.n

    def fdrain(eng, outn=outn):
        eng.wait_ge(s_out.h, outn)
    kb.on("gpsimd", fdrain)


def emit_proj_phase(kb, bufs, pfx, weights, x_dram, ktiles, out_dram, mtiles,
                    mode="copy_bf16", res_dram=None, w_ready=None,
                    nchunk=NCHUNK, ntok=NTOK, krows=128, bias_sb=None,
                    out2_dram=None, mul_dram=None, act_scale=1.0,
                    res_scale=1.0, res2_dram=None, mask_dram=None,
                    accum_store=False, sems=None):
    nchunks = ntok // nchunk
    n = nchunk
    xb = bufs[pfx + "x"]
    outb = bufs[pfx + "o"]
    tmp = bufs[pfx + "t"]
    resb = bufs.get(pfx + "r")
    pss = bufs[pfx + "ps"]
    nbank = len(pss)
    if sems is None:
        sems = [kb.sem() for _ in range(7)]
    s_ld, s_mm, s_ev, s_st, s_res, s_res2, s_msk = sems
    b_st = s_st.n
    nsets = len(weights)
    bank_free_at = [0] * nbank
    mm_idx = 0
    ev_idx = 0
    ev_ends = []

    for c in range(nchunks):
        slot = c % 2
        xv = xb[:, slot * ktiles * n:(slot + 1) * ktiles * n]
        ov = outb[:, slot * mtiles * n:(slot + 1) * mtiles * n]
        rv = resb[:, slot * mtiles * n:(slot + 1) * mtiles * n] if resb is not None else None

        nst_per = 16 * (2 if out2_dram is not None else 1)
        need_st = b_st + nst_per * (c - 1) if c >= 2 else 0

        def flx(eng, c=c, xv=xv, need_st=need_st):
            if need_st:
                eng.wait_ge(s_st.h, need_st)
            if krows == 128:
                eng.dma_start(
                    out=xv.rearrange("p (t n) -> p t n", t=ktiles),
                    in_=x_dram[:, c * n:(c + 1) * n].rearrange("(t p) n -> p t n", p=128),
                ).then_inc(s_ld.h, 16)
            else:
                eng.dma_start(out=xv[0:krows, 0:n],
                              in_=x_dram[0:krows, c * n:(c + 1) * n]).then_inc(s_ld.h, 16)
        kb.on("sync", flx)
        s_ld.inc(16)
        if mul_dram is not None:
            mb = bufs[pfx + "m"]
            mv = mb[:, slot * mtiles * n:(slot + 1) * mtiles * n]

            def flm(eng, c=c, mv=mv, need_st=need_st):
                if need_st:
                    eng.wait_ge(s_st.h, need_st)
                eng.dma_start(
                    out=mv.rearrange("p (t n) -> p t n", t=mtiles),
                    in_=mul_dram[:, c * n:(c + 1) * n].rearrange("(t p) n -> p t n", p=128),
                ).then_inc(s_res.h, 16)
            kb.on("sync", flm)
            s_res.inc(16)
        else:
            mv = None
        if rv is not None:
            def flr(eng, c=c, rv=rv, need_st=need_st):
                if need_st:
                    eng.wait_ge(s_st.h, need_st)
                eng.dma_start(
                    out=rv.rearrange("p (t n) -> p t n", t=mtiles),
                    in_=res_dram[:, c * n:(c + 1) * n].rearrange("(t p) n -> p t n", p=128),
                ).then_inc(s_res.h, 16)
            kb.on("sync", flr)
            s_res.inc(16)
        ldv = s_ld.n
        resv = s_res.n
        if res2_dram is not None:
            r2b = bufs[pfx + "r2"]
            r2v = r2b[:, slot * mtiles * n:(slot + 1) * mtiles * n]

            def flr2(eng, c=c, r2v=r2v, need_st=need_st):
                if need_st:
                    eng.wait_ge(s_st.h, need_st)
                eng.dma_start(
                    out=r2v.rearrange("p (t n) -> p t n", t=mtiles),
                    in_=res2_dram[:, c * n:(c + 1) * n].rearrange("(t p) n -> p t n", p=128),
                ).then_inc(s_res2.h, 16)
            kb.on("sync", flr2)
            s_res2.inc(16)
        else:
            r2v = None
        res2v = s_res2.n
        if mask_dram is not None:
            mrow = bufs[pfx + "mr"]

            def flmk(eng, c=c, slot=slot, need_st=need_st):
                if need_st:
                    eng.wait_ge(s_st.h, need_st)
                eng.dma_start(out=mrow[0:1, slot * n:(slot + 1) * n],
                              in_=mask_dram[0:1, c * n:(c + 1) * n]).then_inc(s_msk.h, 16)
            kb.on("sync", flmk)
            s_msk.inc(16)
        mskv = s_msk.n

        for mt in range(mtiles):
            banks = []
            for si in range(nsets):
                bank = (mm_idx * nsets + si) % nbank
                banks.append(bank)
            need_ev = max(bank_free_at[b] for b in banks)

            def fmm(eng, c=c, mt=mt, banks=banks, need_ev=need_ev, ldv=ldv,
                    xv=xv, mskv=mskv, slot=slot):
                if mt == 0:
                    eng.wait_ge(s_ld.h, ldv)
                    if c == 0 and w_ready is not None:
                        eng.wait_ge(w_ready[0].h, w_ready[1])
                    if mask_dram is not None:
                        eng.wait_ge(s_msk.h, mskv)
                        if c >= 1:
                            eng.wait_ge(s_ev.h, ev_ends[c - 1])
                        mrow = bufs[pfx + "mr"]
                        eng.matmul(bufs[pfx + "psm"][:, :], kb.onesf[:],
                                   mrow[0:1, slot * n:(slot + 1) * n],
                                   start=True, stop=True)
                if need_ev:
                    eng.wait_ge(s_ev.h, need_ev)
                for si in range(nsets):
                    w = weights[si]
                    ps = pss[banks[si]]
                    for kt in range(ktiles):
                        kr = krows if kt == ktiles - 1 else 128
                        ins = eng.matmul(
                            ps[:, :],
                            w[0:kr, (kt * mtiles + mt) * 128:(kt * mtiles + mt) * 128 + 128],
                            xv[0:kr, kt * n:(kt + 1) * n],
                            start=(kt == 0), stop=(kt == ktiles - 1))
                    ins.then_inc(s_mm.h, 1)
            kb.on("tensor", fmm)
            s_mm.inc(nsets)
            mmv = s_mm.n
            mm_idx += 1

            otile = ov[:, mt * n:(mt + 1) * n]
            if mode == "silu_mul":
                tslot = mm_idx % 2
                tv = tmp[:, tslot * n:(tslot + 1) * n]

                def fev1(eng, mmv=mmv, b0=banks[0], tv=tv):
                    eng.wait_ge(s_mm.h, mmv)
                    eng.activation(tv, pss[b0][:, :], AF.Silu,
                                   scale=float(act_scale)).then_inc(s_ev.h, 1)
                kb.on("scalar", fev1)
                s_ev.inc(1)
                bank_free_at[banks[0]] = s_ev.n
                ev1 = s_ev.n

                def fev2(eng, ev1=ev1, b1=banks[1], tv=tv, otile=otile,
                         mt=mt, need_st=need_st):
                    eng.wait_ge(s_ev.h, ev1)
                    if mt == 0 and need_st:
                        eng.wait_ge(s_st.h, need_st)
                    eng.tensor_tensor(otile, tv, pss[b1][:, :], ALU.mult).then_inc(s_ev.h, 1)
                kb.on("vector", fev2)
                s_ev.inc(1)
                bank_free_at[banks[1]] = s_ev.n
            elif mode == "residual2":
                o2b2 = bufs[pfx + "o2"]
                o2t = o2b2[:, slot * mtiles * n:(slot + 1) * mtiles * n][:, mt * n:(mt + 1) * n]

                def fev(eng, mmv=mmv, b0=banks[0], otile=otile, rv=rv, mt=mt,
                        resv=resv, r2v=r2v, res2v=res2v, o2t=o2t, need_st=need_st):
                    eng.wait_ge(s_mm.h, mmv)
                    eng.wait_ge(s_res.h, resv)
                    eng.wait_ge(s_res2.h, res2v)
                    if mt == 0 and need_st:
                        eng.wait_ge(s_st.h, need_st)
                    eng.tensor_tensor(otile, rv[:, mt * n:(mt + 1) * n],
                                      pss[b0][:, :], ALU.add)
                    eng.tensor_tensor(o2t, r2v[:, mt * n:(mt + 1) * n],
                                      pss[b0][:, :], ALU.add).then_inc(s_ev.h, 1)
                kb.on("vector", fev)
                s_ev.inc(1)
                bank_free_at[banks[0]] = s_ev.n
            elif mode == "relu_bias":
                def fev(eng, mmv=mmv, b0=banks[0], otile=otile, mt=mt,
                        need_st=need_st):
                    eng.wait_ge(s_mm.h, mmv)
                    if mt == 0 and need_st:
                        eng.wait_ge(s_st.h, need_st)
                    eng.activation(otile, pss[b0][:, :], AF.Relu,
                                   bias=bias_sb[:, mt:mt + 1]).then_inc(s_ev.h, 1)
                kb.on("scalar", fev)
                s_ev.inc(1)
                bank_free_at[banks[0]] = s_ev.n
            elif mode == "moe_accum":
                fvx = kb.fsem["vector"]
                fz1 = fvx.inc()

                def fev(eng, mmv=mmv, b0=banks[0], otile=otile, mt=mt,
                        need_st=need_st, fz1=fz1):
                    eng.wait_ge(s_mm.h, mmv)
                    if mt == 0 and need_st:
                        eng.wait_ge(s_st.h, need_st)
                    eng.tensor_scalar(otile, pss[b0][:, :], bias_sb[:, mt:mt + 1],
                                      None, ALU.add).then_inc(fvx.h, 1)
                    eng.wait_ge(fvx.h, fz1)
                    eng.tensor_tensor(otile, otile, bufs[pfx + "psm"][:, :],
                                      ALU.mult).then_inc(s_ev.h, 1)
                kb.on("vector", fev)
                s_ev.inc(1)
                bank_free_at[banks[0]] = s_ev.n
            elif mode == "residual":
                def fev(eng, mmv=mmv, b0=banks[0], otile=otile, rv=rv, mt=mt,
                        resv=resv, need_st=need_st):
                    eng.wait_ge(s_mm.h, mmv)
                    eng.wait_ge(s_res.h, resv)
                    if mt == 0 and need_st:
                        eng.wait_ge(s_st.h, need_st)
                    if res_scale != 1.0:
                        eng.scalar_tensor_tensor(otile, pss[b0][:, :],
                                                 float(res_scale),
                                                 rv[:, mt * n:(mt + 1) * n],
                                                 ALU.mult, ALU.add).then_inc(s_ev.h, 1)
                    else:
                        eng.tensor_tensor(otile, rv[:, mt * n:(mt + 1) * n],
                                          pss[b0][:, :], ALU.add).then_inc(s_ev.h, 1)
                kb.on("vector", fev)
                s_ev.inc(1)
                bank_free_at[banks[0]] = s_ev.n
            elif mode == "softplus_bias":
                ltmp = bufs[pfx + "l"]
                tslot = mm_idx % 2
                lv = ltmp[:, tslot * n:(tslot + 1) * n]
                tv = tmp[:, tslot * n:(tslot + 1) * n]
                o2b = bufs.get(pfx + "o2")
                o2tile = (o2b[:, slot * mtiles * n:(slot + 1) * mtiles * n]
                          [:, mt * n:(mt + 1) * n]) if o2b is not None else None

                def fv1(eng, mmv=mmv, b0=banks[0], otile=otile, mt=mt,
                        resv=resv, need_st=need_st):
                    eng.wait_ge(s_mm.h, mmv)
                    if mt == 0 and need_st:
                        eng.wait_ge(s_st.h, need_st)
                    eng.tensor_scalar(otile, pss[b0][:, :], bias_sb[:, mt:mt + 1],
                                      None, ALU.add).then_inc(s_ev.h, 1)
                kb.on("vector", fv1)
                s_ev.inc(1)
                bank_free_at[banks[0]] = s_ev.n
                v1 = s_ev.n
                fsc = kb.fsem["scalar"]
                fa1 = fsc.inc(); fa2 = fsc.inc()

                def fa(eng, v1=v1, otile=otile, lv=lv, fa1=fa1, fa2=fa2):
                    eng.wait_ge(s_ev.h, v1)
                    eng.activation(lv, otile, AF.Abs).then_inc(fsc.h, 1)
                    eng.wait_ge(fsc.h, fa1)
                    eng.activation(lv, lv, AF.Exp, scale=-1.0).then_inc(fsc.h, 1)
                    eng.wait_ge(fsc.h, fa2)
                    eng.activation(lv, lv, AF.Ln, bias=1.0).then_inc(s_ev.h, 1)
                kb.on("scalar", fa)
                s_ev.inc(1)
                a1 = s_ev.n
                fvc = kb.fsem["vector"]
                fb1 = fvc.inc(); fb2 = fvc.inc()

                def fv2(eng, a1=a1, otile=otile, lv=lv, tv=tv, o2tile=o2tile,
                        mv=mv, mt=mt, resv=resv, fb1=fb1, fb2=fb2):
                    eng.wait_ge(s_ev.h, a1)
                    if o2tile is not None and mt == 0:
                        eng.wait_ge(s_res.h, resv)
                    eng.tensor_scalar_max(tv, otile, 0.0).then_inc(fvc.h, 1)
                    eng.wait_ge(fvc.h, fb1)
                    ins = eng.tensor_tensor(otile, tv, lv, ALU.add)
                    if o2tile is not None:
                        ins.then_inc(fvc.h, 1)
                        eng.wait_ge(fvc.h, fb2)
                        ins = eng.tensor_tensor(o2tile, otile,
                                                mv[:, mt * n:(mt + 1) * n], ALU.mult)
                    else:
                        fvc.n -= 0
                    ins.then_inc(s_ev.h, 1)
                kb.on("vector", fv2)
                s_ev.inc(1)
            else:
                func = {"copy_bf16": AF.Copy, "silu_bf16": AF.Silu,
                        "copy_f32": AF.Copy,
                        "sigmoid_bf16": AF.Sigmoid}[mode]

                def fev(eng, mmv=mmv, b0=banks[0], otile=otile, func=func,
                        mt=mt, need_st=need_st):
                    eng.wait_ge(s_mm.h, mmv)
                    if mt == 0 and need_st:
                        eng.wait_ge(s_st.h, need_st)
                    eng.activation(otile, pss[b0][:, :], func).then_inc(s_ev.h, 1)
                kb.on("scalar", fev)
                s_ev.inc(1)
                bank_free_at[banks[0]] = s_ev.n
        evv = s_ev.n
        ev_ends.append(evv)

        o2b = bufs.get(pfx + "o2")
        o2v = (o2b[:, slot * mtiles * n:(slot + 1) * mtiles * n]
               if o2b is not None else None)

        def fst(eng, c=c, ov=ov, evv=evv, o2v=o2v):
            eng.wait_ge(s_ev.h, evv)
            if accum_store:
                eng.dma_start(
                    out=out_dram[:, c * n:(c + 1) * n].rearrange("(t p) n -> p t n", p=128),
                    in_=ov.rearrange("p (t n) -> p t n", t=mtiles),
                    accum_op=ALU.add,
                ).then_inc(s_st.h, 16)
            else:
                eng.dma_start(
                    out=out_dram[:, c * n:(c + 1) * n].rearrange("(t p) n -> p t n", p=128),
                    in_=ov.rearrange("p (t n) -> p t n", t=mtiles),
                ).then_inc(s_st.h, 16)
            if o2v is not None:
                eng.dma_start(
                    out=out2_dram[:, c * n:(c + 1) * n].rearrange("(t p) n -> p t n", p=128),
                    in_=o2v.rearrange("p (t n) -> p t n", t=mtiles),
                ).then_inc(s_st.h, 16)
        kb.on("gpsimd", fst)
        s_st.inc(16 if out2_dram is None else 32)

    stn = s_st.n

    def fdrain(eng, stn=stn):
        eng.wait_ge(s_st.h, stn)
    kb.on("gpsimd", fdrain)


DEQ = 1.0 / 127.0


def make_ln_bufs(kb, sc, dim, with_f32=False, nchunk=NCHUNK, pfx="ln",
                 out_dtype=BF16, with_res=False, with_sum=False):
    pt = dim // 128
    b = {
        "lnx": kb.sb(pfx + "x", [128, 2 * pt * nchunk], F32, sc),
        "lnsq": kb.sb(pfx + "sq", [128, 2 * nchunk], F32, sc),
        "lntmp": kb.sb(pfx + "tmp", [128, nchunk], F32, sc),
        "lnxn": kb.sb(pfx + "xn", [128, 2 * pt * nchunk], out_dtype, sc),
        "lnrows": kb.sb(pfx + "rows", [1, 4 * nchunk], F32, sc),
        "onesf": kb.onesf, "onescol": kb.onescol,
        "ps_stat": kb.psum(pfx + "ps_stat", [128, 2 * nchunk], F32, sc),
        "ps_b1": kb.psum(pfx + "ps_b1", [128, nchunk], F32, sc),
        "ps_b2": kb.psum(pfx + "ps_b2", [128, nchunk], F32, sc),
    }
    if with_f32:
        b["lnxnf"] = kb.sb(pfx + "xnf", [128, 2 * pt * nchunk], F32, sc)
    if with_res:
        b["lnres"] = kb.sb(pfx + "res", [128, 2 * pt * nchunk], F32, sc)
    if with_sum:
        b["lnsum"] = kb.sb(pfx + "sum", [128, 2 * pt * nchunk], F32, sc)
    return b


def emit_block_l0ffn(kb, x_in, x_out, wgT, wuT, wdT, xn_bf, h_bf):
    nc = kb.nc
    sc1 = kb.scope()
    wg_sb = kb.sb("l0f_wg", [128, 8 * 24 * 128], BF16, sc1)
    wu_sb = kb.sb("l0f_wu", [128, 8 * 24 * 128], BF16, sc1)
    wstage = kb.sb("l0f_wst", [128, 8 * 128], I32, sc1)
    s_wld = kb.sem(); s_wcvt = kb.sem()
    emit_weight_load_g(kb, wgT, wg_sb, 8, 24, wstage, s_wld, s_wcvt)
    emit_weight_load_g(kb, wuT, wu_sb, 8, 24, wstage, s_wld, s_wcvt)
    w_ready = (s_wcvt, s_wcvt.n)

    sc2 = kb.scope()
    lnb = make_ln_bufs(kb, sc2, D, pfx="l0f_ln")
    emit_ln_phase(kb, lnb, x_in, xn_bf, D, eps=1e-5)
    sc2.close()
    kb.barrier()

    sc3 = kb.scope()
    gub = {
        "gux": kb.sb("l0f_gx", [128, 2 * 8 * NCHUNK], BF16, sc3),
        "guo": kb.sb("l0f_go", [128, 2 * 24 * NCHUNK], BF16, sc3),
        "gut": kb.sb("l0f_gt", [128, 2 * NCHUNK], F32, sc3),
        "gups": [kb.psum(f"l0f_gps{i}", [128, NCHUNK], F32, sc3) for i in range(4)],
    }
    emit_proj_phase(kb, gub, "gu", [wg_sb, wu_sb], xn_bf, 8, h_bf, 24,
                    mode="silu_mul", w_ready=w_ready, act_scale=DEQ)
    sc3.close()
    sc1.close()
    kb.barrier()

    sc4 = kb.scope()
    wd_sb = kb.sb("l0f_wd", [128, 24 * 8 * 128], BF16, sc4)
    wstage2 = kb.sb("l0f_wst2", [128, 8 * 128], I32, sc4)
    s_wld2 = kb.sem(); s_wcvt2 = kb.sem()
    emit_weight_load_g(kb, wdT, wd_sb, 24, 8, wstage2, s_wld2, s_wcvt2)
    dnb = {
        "dnx": kb.sb("l0f_dx", [128, 2 * 24 * NCHUNK], BF16, sc4),
        "dno": kb.sb("l0f_do", [128, 2 * 8 * NCHUNK], F32, sc4),
        "dnt": kb.sb("l0f_dt", [128, 2 * NCHUNK], F32, sc4),
        "dnr": kb.sb("l0f_dr", [128, 2 * 8 * NCHUNK], F32, sc4),
        "dnps": [kb.psum(f"l0f_dps{i}", [128, NCHUNK], F32, sc4) for i in range(2)],
    }
    emit_proj_phase(kb, dnb, "dn", [wd_sb], h_bf, 24, x_out, 8,
                    mode="residual", res_dram=x_in, w_ready=(s_wcvt2, s_wcvt2.n),
                    res_scale=DEQ * DEQ)
    sc4.close()
    kb.barrier()


def emit_weight_load_g(kb, w_dram, wq_sb, ktiles, mtiles, stage_sb, sem_ld,
                       sem_cvt, scale=None, krows=128):
    assert scale is None
    for kt in range(ktiles):
        kr = krows if kt == ktiles - 1 else 128

        def fl(eng, kt=kt, kr=kr):
            eng.dma_start(
                out=wq_sb[0:kr, kt * mtiles * 128:(kt + 1) * mtiles * 128],
                in_=w_dram[kt * 128:kt * 128 + kr, 0:mtiles * 128],
            ).then_inc(sem_cvt.h, 16)
        kb.on("gpsimd", fl)
        sem_cvt.inc(16)


def emit_l0_tokproj(kb, bufs, xn_bf, wv_sb, wk_sb, wb_sb, v_tm, k_tm, beta_tm,
                    w_ready):
    ntiles = NTOK // 128
    xb = bufs["tpx"]
    vo = bufs["tpv"]
    ko = bufs["tpk"]
    kf = bufs["tpkf"]
    ksq = bufs["tpksq"]
    bo = bufs["tpb"]
    rn = bufs["tprn"]
    psv = bufs["tp_psv"]
    psk = bufs["tp_psk"]
    psb = bufs["tp_psb"]
    s_ld = kb.sem(); s_mm = kb.sem(); s_ev = kb.sem(); s_st = kb.sem()
    b_st = s_st.n; b_ev = s_ev.n

    for tt in range(ntiles):
        slot = tt % 2

        def flx(eng, tt=tt, slot=slot):
            if tt >= 2:
                eng.wait_ge(s_st.h, b_st + 48 * (tt - 1))
            eng.dma_start(
                out=xb[:, slot * 1024:(slot + 1) * 1024].rearrange(
                    "p (k c) -> p k c", k=8),
                in_=xn_bf[:, tt * 128:(tt + 1) * 128].rearrange(
                    "(k p) c -> p k c", p=128),
            ).then_inc(s_ld.h, 16)
        kb.on("sync", flx)
        s_ld.inc(16)
        ldv = s_ld.n

        def fmm(eng, tt=tt, slot=slot, ldv=ldv):
            eng.wait_ge(s_ld.h, ldv)
            if tt == 0:
                eng.wait_ge(w_ready[0].h, w_ready[1])
            else:
                eng.wait_ge(s_ev.h, b_ev + 4 * (tt - 1) + 1)
            for kt in range(8):
                st = xb[:, slot * 1024 + kt * 128: slot * 1024 + (kt + 1) * 128]
                b0 = (kt == 0); b1 = (kt == 7)
                eng.matmul(psv[:, 0:512], st, wv_sb[:, kt * 1024:kt * 1024 + 512],
                           start=b0, stop=b1)
                eng.matmul(psv[:, 512:1024], st, wv_sb[:, kt * 1024 + 512:(kt + 1) * 1024],
                           start=b0, stop=b1)
                eng.matmul(psk[:, 0:512], st, wk_sb[:, kt * 1024:kt * 1024 + 512],
                           start=b0, stop=b1)
                eng.matmul(psk[:, 512:1024], st, wk_sb[:, kt * 1024 + 512:(kt + 1) * 1024],
                           start=b0, stop=b1)
                ins = eng.matmul(psb[:, 0:8], st, wb_sb[:, kt * 8:(kt + 1) * 8],
                                 start=b0, stop=b1)
            ins.then_inc(s_mm.h, 1)
        kb.on("tensor", fmm)
        s_mm.inc(1)
        mmv = s_mm.n

        vv = vo[:, slot * 1024:(slot + 1) * 1024]
        kv = ko[:, slot * 1024:(slot + 1) * 1024]
        bv = bo[:, slot * 8:(slot + 1) * 8]
        kfv = kf[:, slot * 1024:(slot + 1) * 1024]
        ksqv = ksq[:, slot * 1024:(slot + 1) * 1024]
        r0 = rn[:, slot * 24:slot * 24 + 8]
        r1 = rn[:, slot * 24 + 8:slot * 24 + 16]
        r2 = rn[:, slot * 24 + 16:slot * 24 + 24]

        def fsc(eng, mmv=mmv, vv=vv, bv=bv, kfv=kfv, ksqv=ksqv, tt=tt):
            eng.wait_ge(s_mm.h, mmv)
            if tt >= 2:
                eng.wait_ge(s_ev.h, b_ev + 4 * (tt - 1))
                eng.wait_ge(s_st.h, b_st + 48 * (tt - 1))
            eng.activation(vv[:, 0:512], psv[:, 0:512], AF.Silu)
            eng.activation(vv[:, 512:1024], psv[:, 512:1024], AF.Silu)
            eng.activation(bv, psb[:, 0:8], AF.Sigmoid)
            eng.copy(kfv, psk[:, :])
            eng.activation(ksqv, psk[:, :], AF.Square).then_inc(s_ev.h, 1)
        kb.on("scalar", fsc)
        s_ev.inc(1)
        evv = s_ev.n

        def fve(eng, evv=evv, ksqv=ksqv, r0=r0):
            eng.wait_ge(s_ev.h, evv)
            eng.tensor_reduce(r0, ksqv.rearrange("p (h c) -> p h c", h=8),
                              AX.X, ALU.add).then_inc(s_ev.h, 1)
        kb.on("vector", fve)
        s_ev.inc(1)
        ev2 = s_ev.n

        def fsc2(eng, ev2=ev2, r0=r0, r1=r1):
            eng.wait_ge(s_ev.h, ev2)
            eng.activation(r1, r0, AF.Sqrt).then_inc(s_ev.h, 1)
        kb.on("scalar", fsc2)
        s_ev.inc(1)
        ev3 = s_ev.n

        fs = kb.fsem["vector"]
        f1 = fs.inc(); f2 = fs.inc()

        def fve2(eng, ev3=ev3, kv=kv, kfv=kfv, r1=r1, r2=r2, f1=f1, f2=f2):
            eng.wait_ge(s_ev.h, ev3)
            eng.tensor_scalar_max(r1, r1, 1e-12).then_inc(fs.h, 1)
            eng.wait_ge(fs.h, f1)
            eng.reciprocal(r2, r1).then_inc(fs.h, 1)
            eng.wait_ge(fs.h, f2)
            for h in range(8):
                ins = eng.tensor_scalar(kv[:, h * 128:(h + 1) * 128],
                                        kfv[:, h * 128:(h + 1) * 128],
                                        r2[:, h:h + 1], None, ALU.mult)
            ins.then_inc(s_ev.h, 1)
        kb.on("vector", fve2)
        s_ev.inc(1)
        ev4 = s_ev.n

        def fst(eng, tt=tt, vv=vv, kv=kv, bv=bv, ev4=ev4):
            eng.wait_ge(s_ev.h, ev4)
            eng.dma_start(out=v_tm[tt * 128:(tt + 1) * 128, :], in_=vv).then_inc(s_st.h, 16)
            eng.dma_start(out=k_tm[tt * 128:(tt + 1) * 128, :], in_=kv).then_inc(s_st.h, 16)
            eng.dma_start(out=beta_tm[tt * 128:(tt + 1) * 128, :], in_=bv).then_inc(s_st.h, 16)
        kb.on("gpsimd", fst)
        s_st.inc(48)

    stn = s_st.n

    def fdrain(eng, stn=stn):
        eng.wait_ge(s_st.h, stn)
    kb.on("gpsimd", fdrain)

def emit_l0_scan(kb, bufs, q_dm, k_tm, v_tm, beta_tm, o_dm):
    H = 8
    W = 1024
    qt = bufs["sqt"]; ktm = bufs["sktm"]; vtm = bufs["svtm"]; bt = bufs["sbt"]
    KTD = bufs["sKTD"]; Bw = bufs["sB"]; BTw = bufs["sBT"]
    Pb = bufs["sP"]; Qb = bufs["sQ"]; MTb = bufs["sMT"]
    RHS = bufs["sRHS"]; Uw = bufs["sU"]; LTw = bufs["sLT"]
    o1w = bufs["so1"]; Bf = bufs["sBf"]
    Sf = bufs["sSf"]; Sb = bufs["sSb"]; Ow = bufs["sO"]
    idb8 = bufs["idb8"]; mneg8 = bufs["mneg8"]; mu8 = bufs["mu8"]; idb = bufs["idb"]
    pT = bufs["s_pT"]; pA = bufs["s_pA"]; pB = bufs["s_pB"]; pC = bufs["s_pC"]
    s_ld = kb.sem(); s_mm = kb.sem(); s_dn = kb.sem(); s_st = kb.sem()
    b_st = s_st.n

    NCH = T_SEQ // 128

    def mmstage(emit_mms, wait_dn):
        def f(eng, wait_dn=wait_dn, emit_mms=emit_mms):
            if wait_dn:
                eng.wait_ge(s_dn.h, wait_dn)
            ins = emit_mms(eng)
            ins.then_inc(s_mm.h, 1)
        kb.on("tensor", f)
        s_mm.inc(1)
        return s_mm.n

    def evstage(engine, emit_ev, wait_mm, extra_dn=0):
        def f(eng, wait_mm=wait_mm, emit_ev=emit_ev, extra_dn=extra_dn):
            if wait_mm:
                eng.wait_ge(s_mm.h, wait_mm)
            if extra_dn:
                eng.wait_ge(s_dn.h, extra_dn)
            ins = emit_ev(eng)
            ins.then_inc(s_dn.h, 1)
        kb.on(engine, f)
        s_dn.inc(1)
        return s_dn.n

    chunk_done = []
    for b in range(B_LOC):
        def fz(eng, b=b):
            eng.memset(Sf[:], 0.0)
            eng.memset(Sb[:], 0.0).then_inc(s_dn.h, 1)
        kb.on("vector", fz)
        s_dn.inc(1)
        s_after_reset = s_dn.n

        for ch in range(NCH):
            ci = b * NCH + ch
            slot = ci % 2
            col0 = b * T_SEQ + ch * 128
            prev_done = s_dn.n

            guard = chunk_done[ci - 2] if ci >= 2 else 0

            def fld(eng, ci=ci, slot=slot, col0=col0, guard=guard):
                if guard:
                    eng.wait_ge(s_dn.h, guard)
                    eng.wait_ge(s_st.h, b_st + 16 * (ci - 1))
                eng.dma_start(out=qt[:, slot * W:(slot + 1) * W].rearrange(
                                  "p (h c) -> p h c", h=H),
                              in_=q_dm[:, col0:col0 + 128].rearrange(
                                  "(h p) c -> p h c", p=128)).then_inc(s_ld.h, 16)
                r0 = col0
                eng.dma_start(out=ktm[:, slot * W:(slot + 1) * W],
                              in_=k_tm[r0:r0 + 128, :]).then_inc(s_ld.h, 16)
                eng.dma_start(out=vtm[:, slot * W:(slot + 1) * W],
                              in_=v_tm[r0:r0 + 128, :]).then_inc(s_ld.h, 16)
                eng.dma_start(out=bt[:, slot * 8:(slot + 1) * 8],
                              in_=beta_tm[r0:r0 + 128, :]).then_inc(s_ld.h, 16)
            kb.on("sync", fld)
            s_ld.inc(64)
            ldv = s_ld.n
            qv = qt[:, slot * W:(slot + 1) * W]
            kv = ktm[:, slot * W:(slot + 1) * W]
            vv = vtm[:, slot * W:(slot + 1) * W]
            bv = bt[:, slot * 8:(slot + 1) * 8]

            def mm1(eng, kv=kv, ldv=ldv):
                eng.wait_ge(s_ld.h, ldv)
                for h in range(H):
                    ins = eng.transpose(pT[:, h * 128:(h + 1) * 128],
                                        kv[:, h * 128:(h + 1) * 128], idb[:])
                return ins
            m = mmstage(mm1, prev_done)
            d_ktd = evstage("scalar", lambda eng: eng.copy(KTD[:], pT[:, :]), m)

            def mm2(eng):
                for h in range(H):
                    ins = eng.matmul(pA[:, h * 128:(h + 1) * 128],
                                     KTD[:, h * 128:(h + 1) * 128],
                                     KTD[:, h * 128:(h + 1) * 128],
                                     start=True, stop=True)
                return ins
            m = mmstage(mm2, d_ktd)

            fs = kb.fsem["vector"]
            fB = fs.inc()

            def evB(eng, bv=bv, fB=fB):
                for h in range(H):
                    ins = eng.tensor_scalar(Bf[:, h * 128:(h + 1) * 128],
                                            pA[:, h * 128:(h + 1) * 128],
                                            bv[:, h:h + 1], None, ALU.mult)
                ins.then_inc(fs.h, 1)
                eng.wait_ge(fs.h, fB)
                return eng.tensor_tensor(Bw[:], Bf[:], mneg8[:], ALU.mult)
            d_B = evstage("vector", evB, m)

            def mm3(eng):
                for h in range(H):
                    ins = eng.transpose(pT[:, h * 128:(h + 1) * 128],
                                        Bw[:, h * 128:(h + 1) * 128], idb[:])
                return ins
            m = mmstage(mm3, d_B)
            d_bt = evstage("scalar", lambda eng: eng.copy(BTw[:], pT[:, :]), m)
            d_mt = evstage("vector",
                           lambda eng: eng.tensor_tensor(MTb[0][:], idb8[:], BTw[:], ALU.add),
                           0, extra_dn=d_bt)

            Pcur, Qcur = Bw, BTw
            d_p, d_q = d_B, d_bt
            mtc = 0
            for j in range(1, 7):
                Pnew = Pb[j % 2]; Qnew = Qb[j % 2]

                def mmP(eng, Pcur=Pcur, Qcur=Qcur):
                    for h in range(H):
                        ins = eng.matmul(pB[:, h * 128:(h + 1) * 128],
                                         Qcur[:, h * 128:(h + 1) * 128],
                                         Pcur[:, h * 128:(h + 1) * 128],
                                         start=True, stop=True)
                    return ins
                m = mmstage(mmP, max(d_p, d_q))
                d_pn = evstage("scalar", lambda eng, Pnew=Pnew: eng.copy(Pnew[:], pB[:, :]), m)
                if j < 6:
                    def mmQ(eng, Pcur=Pcur, Qcur=Qcur):
                        for h in range(H):
                            ins = eng.matmul(pC[:, h * 128:(h + 1) * 128],
                                             Pcur[:, h * 128:(h + 1) * 128],
                                             Qcur[:, h * 128:(h + 1) * 128],
                                             start=True, stop=True)
                        return ins
                    m = mmstage(mmQ, max(d_p, d_q))
                    d_qn = evstage("scalar", lambda eng, Qnew=Qnew: eng.copy(Qnew[:], pC[:, :]), m)
                else:
                    d_qn = d_q

                MTold = MTb[mtc]; MTnew = MTb[1 - mtc]; mtc = 1 - mtc

                def mmM(eng, Pnew=Pnew, MTold=MTold):
                    for h in range(H):
                        ins = eng.matmul(pA[:, h * 128:(h + 1) * 128],
                                         Pnew[:, h * 128:(h + 1) * 128],
                                         MTold[:, h * 128:(h + 1) * 128],
                                         start=True, stop=True)
                    return ins
                m = mmstage(mmM, max(d_pn, d_mt))
                d_mt = evstage("vector",
                               lambda eng, MTnew=MTnew, MTold=MTold:
                               eng.tensor_tensor(MTnew[:], MTold[:], pA[:, :], ALU.add), m)
                Pcur, Qcur = Pnew, Qnew
                d_p, d_q = d_pn, d_qn

            MT = MTb[mtc]

            def mmKS(eng):
                for h in range(H):
                    ins = eng.matmul(pB[:, h * 128:(h + 1) * 128],
                                     KTD[:, h * 128:(h + 1) * 128],
                                     Sb[:, h * 128:(h + 1) * 128],
                                     start=True, stop=True)
                return ins
            state_dep = s_after_reset if ch == 0 else d_state
            m = mmstage(mmKS, max(state_dep, d_p))

            fR = kb.fsem["vector"].inc()

            def evRHS(eng, bv=bv, vv=vv, fR=fR):
                for h in range(H):
                    ins = eng.tensor_scalar(o1w[:, h * 128:(h + 1) * 128],
                                            pB[:, h * 128:(h + 1) * 128],
                                            bv[:, h:h + 1], None, ALU.mult)
                ins.then_inc(kb.fsem["vector"].h, 1)
                eng.wait_ge(kb.fsem["vector"].h, fR)
                for h in range(H):
                    ins = eng.scalar_tensor_tensor(
                        RHS[:, h * 128:(h + 1) * 128],
                        vv[:, h * 128:(h + 1) * 128],
                        bv[:, h:h + 1],
                        o1w[:, h * 128:(h + 1) * 128],
                        ALU.mult, ALU.subtract)
                return ins
            d_rhs = evstage("vector", evRHS, m)

            def mmU(eng, MT=MT):
                for h in range(H):
                    ins = eng.matmul(pC[:, h * 128:(h + 1) * 128],
                                     MT[:, h * 128:(h + 1) * 128],
                                     RHS[:, h * 128:(h + 1) * 128],
                                     start=True, stop=True)
                return ins
            m = mmstage(mmU, max(d_rhs, d_mt))
            d_u = evstage("scalar", lambda eng: eng.copy(Uw[:], pC[:, :]), m)

            def mmLT(eng, qv=qv):
                for h in range(H):
                    ins = eng.matmul(pA[:, h * 128:(h + 1) * 128],
                                     KTD[:, h * 128:(h + 1) * 128],
                                     qv[:, h * 128:(h + 1) * 128],
                                     start=True, stop=True)
                return ins
            m = mmstage(mmLT, d_mt)
            d_lt = evstage("vector",
                           lambda eng: eng.tensor_tensor(LTw[:], pA[:, :], mu8[:], ALU.mult), m)

            def mmO(eng, qv=qv):
                for h in range(H):
                    eng.matmul(pB[:, h * 128:(h + 1) * 128],
                               Sb[:, h * 128:(h + 1) * 128],
                               qv[:, h * 128:(h + 1) * 128],
                               start=True, stop=False)
                    ins = eng.matmul(pB[:, h * 128:(h + 1) * 128],
                                     Uw[:, h * 128:(h + 1) * 128],
                                     LTw[:, h * 128:(h + 1) * 128],
                                     start=False, stop=True)
                return ins
            m = mmstage(mmO, max(d_u, d_lt, d_rhs))
            oslot = ci % 2
            ov = Ow[:, oslot * W:(oslot + 1) * W]

            def evO(eng, ov=ov, ci=ci):
                if ci >= 2:
                    eng.wait_ge(s_st.h, b_st + 16 * (ci - 1))
                return eng.copy(ov, pB[:, :])
            d_o = evstage("scalar", evO, m)

            def fso(eng, ov=ov, col0=col0, d_o=d_o):
                eng.wait_ge(s_dn.h, d_o)
                eng.dma_start(out=o_dm[:, col0:col0 + 128].rearrange(
                                  "(h p) c -> p h c", p=128),
                              in_=ov.rearrange("p (h c) -> p h c", h=H)).then_inc(s_st.h, 16)
            kb.on("gpsimd", fso)
            s_st.inc(16)

            def mmS(eng, kv=kv):
                for h in range(H):
                    ins = eng.matmul(pC[:, h * 128:(h + 1) * 128],
                                     kv[:, h * 128:(h + 1) * 128],
                                     Uw[:, h * 128:(h + 1) * 128],
                                     start=True, stop=True)
                return ins
            m = mmstage(mmS, max(d_u, d_o))

            fS = kb.fsem["vector"].inc()

            def evS(eng, fS=fS):
                eng.tensor_tensor(Sf[:], Sf[:], pC[:, :], ALU.add).then_inc(
                    kb.fsem["vector"].h, 1)
                eng.wait_ge(kb.fsem["vector"].h, fS)
                return eng.tensor_copy(Sb[:], Sf[:])
            d_state = evstage("vector", evS, m)
            chunk_done.append(d_state)

    stn = s_st.n

    def fdrain(eng, stn=stn):
        eng.wait_ge(s_st.h, stn)
    kb.on("gpsimd", fdrain)

def emit_l0_wload_tok(kb, sc, WvD, WkD, WbD):
    wv_sb = kb.sb("l0a_wv", [128, 8 * 1024], BF16, sc)
    wk_sb = kb.sb("l0a_wk", [128, 8 * 1024], BF16, sc)
    wb_sb = kb.sb("l0a_wb", [128, 8 * 8], BF16, sc)
    stage = kb.sb("l0a_wst", [128, 2 * 1024], F32, sc)
    s_ld = kb.sem(); s_cvt = kb.sem()
    items = [(WvD, wv_sb, 1024), (WkD, wk_sb, 1024), (WbD, wb_sb, 8)]
    for (wd, wsb, ncol) in items:
        for kt in range(8):
            def fl(eng, wd=wd, wsb=wsb, kt=kt, ncol=ncol):
                eng.dma_start(out=wsb[:, kt * ncol:(kt + 1) * ncol],
                              in_=wd[kt * 128:(kt + 1) * 128, :]).then_inc(s_cvt.h, 16)
            kb.on("gpsimd", fl)
            s_cvt.inc(16)
    return wv_sb, wk_sb, wb_sb, (s_cvt, s_cvt.n)


def emit_const_load(kb, sc, cdram, shapes):
    out = {}
    s_ld = kb.sem("cld")
    off = 0
    tiles = {}
    for name, cols in shapes:
        t = kb.sb("c_" + name, [128, cols], BF16, sc)
        tiles[name] = (t, off, cols)
        off += cols
    for name, (t, o, cols) in tiles.items():
        def fl(eng, t=t, o=o, cols=cols):
            eng.dma_start(out=t[:], in_=cdram[:, o:o + cols]).then_inc(s_ld.h, 16)
        kb.on("sync", fl)
        s_ld.inc(16)
        out[name] = t
    return out, (s_ld, s_ld.n)


def emit_block_l0attn(kb, x_in, x_out, WqD, WkD, WvD, WbD, lnwD, lnbD, cdram,
                      xn_bf, q_dm, k_tm, v_tm, beta_tm, o_dm):
    scC = kb.scope()
    consts, c_ready = emit_const_load(kb, scC, cdram,
                                      [("idb", 128), ("idb8", 1024),
                                       ("mneg8", 1024), ("mu8", 1024)])
    lnw_sb = kb.sb("l0a_lnw", [128, 8], F32, scC)
    lnb_sb = kb.sb("l0a_lnb", [128, 8], F32, scC)
    s_p = kb.sem("l0a_sp")

    def flp(eng, cw=c_ready):
        eng.dma_start(out=lnw_sb[:], in_=lnwD[:, :]).then_inc(s_p.h, 16)
        eng.dma_start(out=lnb_sb[:], in_=lnbD[:, :]).then_inc(s_p.h, 16)
        eng.wait_ge(s_p.h, 32)
        eng.wait_ge(cw[0].h, cw[1])
    kb.on("sync", flp)
    s_p.inc(32)

    sc1 = kb.scope()
    lnb1 = make_ln_bufs(kb, sc1, D, pfx="l0a_ln")
    emit_ln_phase(kb, lnb1, x_in, xn_bf, D, eps=1e-5)
    sc1.close()
    kb.barrier()

    sc2 = kb.scope()
    wq_sb = kb.sb("l0a_wq", [128, 8 * 8 * 128], BF16, sc2)
    wstage = kb.sb("l0a_wqst", [128, 8 * 128], F32, sc2)
    s_wld = kb.sem(); s_wcvt = kb.sem()
    emit_weight_load_g(kb, WqD, wq_sb, 8, 8, wstage, s_wld, s_wcvt)
    qb = {
        "qx": kb.sb("l0a_qx", [128, 2 * 8 * NCHUNK], BF16, sc2),
        "qo": kb.sb("l0a_qo", [128, 2 * 8 * NCHUNK], BF16, sc2),
        "qt": kb.sb("l0a_qt", [128, 2 * NCHUNK], F32, sc2),
        "qps": [kb.psum(f"l0a_qps{i}", [128, NCHUNK], F32, sc2) for i in range(2)],
    }
    emit_proj_phase(kb, qb, "q", [wq_sb], xn_bf, 8, q_dm, 8,
                    mode="copy_bf16", w_ready=(s_wcvt, s_wcvt.n))
    sc2.close()
    kb.barrier()

    sc3 = kb.scope()
    wv_sb, wk_sb, wb_sb, w_ready = emit_l0_wload_tok(kb, sc3, WvD, WkD, WbD)
    tpb = {
        "tpx": kb.sb("l0a_tpx", [128, 2 * 1024], BF16, sc3),
        "tpv": kb.sb("l0a_tpv", [128, 2 * 1024], BF16, sc3),
        "tpk": kb.sb("l0a_tpk", [128, 2 * 1024], BF16, sc3),
        "tpkf": kb.sb("l0a_tpkf", [128, 2 * 1024], F32, sc3),
        "tpksq": kb.sb("l0a_tpksq", [128, 2 * 1024], F32, sc3),
        "tpb": kb.sb("l0a_tpb", [128, 2 * 8], F32, sc3),
        "tprn": kb.sb("l0a_tprn", [128, 2 * 24], F32, sc3),
        "tp_psv": kb.psum("l0a_psv", [128, 1024], F32, sc3),
        "tp_psk": kb.psum("l0a_psk", [128, 1024], F32, sc3),
        "tp_psb": kb.psum("l0a_psb", [128, 512], F32, sc3),
    }
    emit_l0_tokproj(kb, tpb, xn_bf, wv_sb, wk_sb, wb_sb, v_tm, k_tm, beta_tm,
                    w_ready)
    sc3.close()
    kb.barrier()

    sc4 = kb.scope()
    W = 1024
    sbufs = {
        "sqt": kb.sb("l0a_sqt", [128, 2 * W], BF16, sc4),
        "sktm": kb.sb("l0a_sktm", [128, 2 * W], BF16, sc4),
        "svtm": kb.sb("l0a_svtm", [128, 2 * W], BF16, sc4),
        "sbt": kb.sb("l0a_sbt", [128, 2 * 8], F32, sc4),
        "sKTD": kb.sb("l0a_sKTD", [128, W], BF16, sc4),
        "sB": kb.sb("l0a_sB", [128, W], BF16, sc4),
        "sBT": kb.sb("l0a_sBT", [128, W], BF16, sc4),
        "sP": [kb.sb(f"l0a_sP{i}", [128, W], BF16, sc4) for i in range(2)],
        "sQ": [kb.sb(f"l0a_sQ{i}", [128, W], BF16, sc4) for i in range(2)],
        "sMT": [kb.sb(f"l0a_sMT{i}", [128, W], BF16, sc4) for i in range(2)],
        "sRHS": kb.sb("l0a_sRHS", [128, W], BF16, sc4),
        "sU": kb.sb("l0a_sU", [128, W], BF16, sc4),
        "sLT": kb.sb("l0a_sLT", [128, W], BF16, sc4),
        "so1": kb.sb("l0a_so1", [128, W], F32, sc4),
        "sBf": kb.sb("l0a_sBf", [128, W], F32, sc4),
        "sSf": kb.sb("l0a_sSf", [128, W], F32, sc4),
        "sSb": kb.sb("l0a_sSb", [128, W], BF16, sc4),
        "sO": kb.sb("l0a_sO", [128, 2 * W], F32, sc4),
        "idb8": consts["idb8"], "mneg8": consts["mneg8"],
        "mu8": consts["mu8"], "idb": consts["idb"],
        "s_pT": kb.psum("l0a_pT", [128, W], BF16, sc4),
        "s_pA": kb.psum("l0a_pA", [128, W], F32, sc4),
        "s_pB": kb.psum("l0a_pB", [128, W], F32, sc4),
        "s_pC": kb.psum("l0a_pC", [128, W], F32, sc4),
    }
    emit_l0_scan(kb, sbufs, q_dm, k_tm, v_tm, beta_tm, o_dm)
    sc4.close()
    kb.barrier()

    sc5 = kb.scope()
    lnb2 = make_ln_bufs(kb, sc5, D, pfx="l0a_oln", out_dtype=F32, with_res=True)
    emit_ln_phase(kb, lnb2, o_dm, x_out, D, gamma_sb=lnw_sb, beta_sb=lnb_sb,
                  eps=1e-5, res_dram=x_in)
    sc5.close()
    scC.close()
    kb.barrier()


def l0attn_consts():
    idb = np.eye(128, dtype=np.float32)
    ml = np.tril(np.full((128, 128), -1.0, np.float32), -1)
    mu = np.triu(np.ones((128, 128), np.float32), 0)
    c = np.concatenate([idb] + [idb] * 8 + [ml] * 8 + [mu] * 8, axis=1)
    return c.astype(ml_dtypes.bfloat16)


def emit_mamba_conv(kb, bufs, xz_dm, xc_dm, convw_sb, convb_sb, w_ready):
    xs = bufs["cvx"]
    acc = bufs["cva"]
    oc = bufs["cvo"]
    s_ld = kb.sem(); s_ev = kb.sem(); s_ac = kb.sem(); s_st = kb.sem()
    b_st = s_st.n; b_ac = s_ac.n
    T = T_SEQ
    fvc = kb.fsem["vector"]
    idx = 0
    for dt in range(16):
        for b in range(B_LOC):
            slot = idx % 2
            xv = xs[:, slot * (3 + T):(slot + 1) * (3 + T)]
            av = acc[:, slot * T:(slot + 1) * T]
            ov = oc[:, slot * T:(slot + 1) * T]

            def fl(eng, xv=xv, dt=dt, b=b, idx=idx):
                if idx >= 2:
                    eng.wait_ge(s_st.h, b_st + 16 * (idx - 1))
                eng.dma_start(out=xv[:, 3:3 + T],
                              in_=xz_dm[dt * 128:(dt + 1) * 128,
                                        b * T:(b + 1) * T]).then_inc(s_ld.h, 16)
            kb.on("sync", fl)
            s_ld.inc(16)
            ldv = s_ld.n

            f1 = fvc.inc(); f2 = fvc.inc(); f3 = fvc.inc()

            def fc(eng, xv=xv, av=av, dt=dt, ldv=ldv, f1=f1, f2=f2, f3=f3,
                   idx=idx):
                eng.memset(xv[:, 0:3], 0.0)
                eng.wait_ge(s_ld.h, ldv)
                if idx == 0:
                    eng.wait_ge(w_ready[0].h, w_ready[1])
                if idx >= 2:
                    eng.wait_ge(s_ac.h, b_ac + idx - 1)
                eng.tensor_scalar(av, xv[:, 0:T], convw_sb[:, dt * 4:dt * 4 + 1],
                                  None, ALU.mult).then_inc(fvc.h, 1)
                eng.wait_ge(fvc.h, f1)
                eng.scalar_tensor_tensor(av, xv[:, 1:1 + T],
                                         convw_sb[:, dt * 4 + 1:dt * 4 + 2], av,
                                         ALU.mult, ALU.add).then_inc(fvc.h, 1)
                eng.wait_ge(fvc.h, f2)
                eng.scalar_tensor_tensor(av, xv[:, 2:2 + T],
                                         convw_sb[:, dt * 4 + 2:dt * 4 + 3], av,
                                         ALU.mult, ALU.add).then_inc(fvc.h, 1)
                eng.wait_ge(fvc.h, f3)
                eng.scalar_tensor_tensor(av, xv[:, 3:3 + T],
                                         convw_sb[:, dt * 4 + 3:dt * 4 + 4], av,
                                         ALU.mult, ALU.add).then_inc(s_ev.h, 1)
            kb.on("vector", fc)
            s_ev.inc(1)
            evv = s_ev.n

            def fs(eng, av=av, ov=ov, dt=dt, evv=evv, idx=idx):
                eng.wait_ge(s_ev.h, evv)
                if idx >= 2:
                    eng.wait_ge(s_st.h, b_st + 16 * (idx - 1))
                eng.activation(ov, av, AF.Silu,
                               bias=convb_sb[:, dt:dt + 1]).then_inc(s_ac.h, 1)
            kb.on("scalar", fs)
            s_ac.inc(1)
            acv = s_ac.n

            def fst(eng, ov=ov, dt=dt, b=b, acv=acv):
                eng.wait_ge(s_ac.h, acv)
                eng.dma_start(out=xc_dm[dt * 128:(dt + 1) * 128, b * T:(b + 1) * T],
                              in_=ov).then_inc(s_st.h, 16)
            kb.on("gpsimd", fst)
            s_st.inc(16)
            idx += 1
    stn = s_st.n

    def fdrain(eng, stn=stn):
        eng.wait_ge(s_st.h, stn)
    kb.on("gpsimd", fdrain)


def emit_mamba_scan(kb, bufs, xdbl_dm, dt_dm, dtx_dm, ys_dm, A_sb, a_ready):
    T = T_SEQ
    dtb = bufs["scdt"]
    dxb = bufs["scdx"]
    Bbc = bufs["scB"]
    Cbc = bufs["scC"]
    rowb = bufs["scrow"]
    dAb = bufs["scdA"]
    hb = bufs["sch"]
    bbf = bufs["scb"]
    yb = bufs["scy"]
    psb = bufs["sc_ps"]
    s_ld = kb.sem(); s_row = kb.sem(); s_bc = kb.sem(); s_bce = kb.sem()
    s_ev = kb.sem(); s_ac = kb.sem(); s_st = kb.sem()
    b_st = s_st.n; b_bce = s_bce.n
    fvc = kb.fsem["vector"]
    onesf = kb.onesf
    NC2 = T // 512
    idx = 0
    iter_done = []
    for b in range(B_LOC):
        bbc_consumed = iter_done[-1] if iter_done else 0
        bc_base = s_bc.n
        bce_base = s_bce.n
        for j in range(32):
            tgt = Bbc if j < 16 else Cbc
            nn = j % 16
            rslot = j % 2

            def flrow(eng, j=j, b=b, rslot=rslot, bc_base=bc_base):
                if j >= 2:
                    eng.wait_ge(s_bc.h, bc_base + (j - 1) * NC2)
                eng.dma_start(out=rowb[0:1, rslot * T:(rslot + 1) * T],
                              in_=xdbl_dm[64 + j:65 + j, b * T:(b + 1) * T]
                              ).then_inc(s_row.h, 16)
            kb.on("sync", flrow)
            s_row.inc(16)
            rowv = s_row.n

            row_free = (iter_done[(b * 16 - 1) * 16 + (j % 16)]
                        if b > 0 else 0)

            def fbc(eng, j=j, rslot=rslot, rowv=rowv, b=b,
                    row_free=row_free):
                eng.wait_ge(s_row.h, rowv)
                if j == 0 and b == 0:
                    eng.wait_ge(a_ready[0].h, a_ready[1])
                if row_free:
                    eng.wait_ge(s_ev.h, row_free)
                for cc in range(NC2):
                    if j * NC2 + cc >= 2:
                        eng.wait_ge(s_bce.h, b_bce + (32 * NC2) * b + j * NC2 + cc - 1)
                    eng.matmul(psb[cc % 2][:, :], kb.onesb[:],
                               rowb[0:1, rslot * T + cc * 512:rslot * T + (cc + 1) * 512],
                               start=True, stop=True).then_inc(s_bc.h, 1)
            kb.on("tensor", fbc)
            s_bc.inc(NC2)
            bcv = s_bc.n

            def fbe(eng, j=j, nn=nn, tgt=tgt, bcv=bcv):
                for cc in range(NC2):
                    eng.wait_ge(s_bc.h, bcv - NC2 + cc + 1)
                    eng.copy(tgt[:, nn * T + cc * 512:nn * T + (cc + 1) * 512],
                             psb[cc % 2][:, :]).then_inc(s_bce.h, 1)
            kb.on("scalar", fbe)
            s_bce.inc(NC2)
        bc_done = s_bce.n

        for dt in range(16):
            slot = idx % 2
            dtv = dtb[:, slot * T:(slot + 1) * T]
            dxv = dxb[:, slot * T:(slot + 1) * T]
            yv = yb[:, slot * T:(slot + 1) * T]

            def fld(eng, dt=dt, b=b, slot=slot, idx=idx):
                if idx >= 2:
                    eng.wait_ge(s_st.h, b_st + 16 * (idx - 1))
                eng.dma_start(out=dtb[:, slot * T:(slot + 1) * T],
                              in_=dt_dm[dt * 128:(dt + 1) * 128, b * T:(b + 1) * T]
                              ).then_inc(s_ld.h, 16)
                eng.dma_start(out=dxb[:, slot * T:(slot + 1) * T],
                              in_=dtx_dm[dt * 128:(dt + 1) * 128, b * T:(b + 1) * T]
                              ).then_inc(s_ld.h, 16)
            kb.on("sync", fld)
            s_ld.inc(32)
            ldv = s_ld.n

            for nn in range(16):
                g = idx * 16 + nn
                nslot = g % 2
                dAv = dAb[:, nslot * T:(nslot + 1) * T]
                bv = bbf[:, nslot * T:(nslot + 1) * T]
                hv = hb[:, nslot * T:(nslot + 1) * T]
                g_done = iter_done[g - 2] if g >= 2 else 0

                def fda(eng, dAv=dAv, dtv=dtv, dt=dt, nn=nn, ldv=ldv,
                        g_done=g_done):
                    eng.wait_ge(s_ld.h, ldv)
                    if g_done:
                        eng.wait_ge(s_ev.h, g_done)
                    eng.activation(dAv, dtv, AF.Exp,
                                   scale=A_sb[:, dt * 16 + nn:dt * 16 + nn + 1]
                                   ).then_inc(s_ac.h, 1)
                kb.on("scalar", fda)
                s_ac.inc(1)
                acv = s_ac.n

                def fsc(eng, nn=nn, dt=dt, dAv=dAv, bv=bv, hv=hv, dxv=dxv,
                        yv=yv, acv=acv, g_done=g_done, bce_base=bce_base,
                        idx=idx):
                    eng.wait_ge(s_ac.h, acv)
                    if dt == 0:
                        eng.wait_ge(s_bce.h, bce_base + (17 + nn) * NC2)
                    if nn == 0:
                        if idx >= 2:
                            eng.wait_ge(s_st.h, b_st + 16 * (idx - 1))
                    if g_done:
                        eng.wait_ge(s_ev.h, g_done)
                    eng.tensor_tensor(bv, dxv, Bbc[:, nn * T:(nn + 1) * T],
                                      ALU.mult)
                    eng.tensor_tensor_scan(hv, dAv, bv, 0.0, ALU.mult,
                                           ALU.add)
                    eng.tensor_tensor(hv, hv, Cbc[:, nn * T:(nn + 1) * T],
                                      ALU.mult)
                    if nn == 0:
                        ins = eng.tensor_copy(yv, hv)
                    else:
                        ins = eng.tensor_tensor(yv, yv, hv, ALU.add)
                    ins.then_inc(s_ev.h, 1)
                kb.on("vector", fsc)
                s_ev.inc(1)
                iter_done.append(s_ev.n)
            yv_done = iter_done[-1]

            def fst(eng, dt=dt, b=b, yv=yv, yv_done=yv_done):
                eng.wait_ge(s_ev.h, yv_done)
                eng.dma_start(out=ys_dm[dt * 128:(dt + 1) * 128, b * T:(b + 1) * T],
                              in_=yv).then_inc(s_st.h, 16)
            kb.on("gpsimd", fst)
            s_st.inc(16)
            idx += 1
    stn = s_st.n

    def fdrain(eng, stn=stn):
        eng.wait_ge(s_st.h, stn)
    kb.on("gpsimd", fdrain)

def emit_mamba_gate(kb, bufs, ys_dm, xc_dm, xz_dm, y2_dm, D_sb, d_ready):
    ysb = bufs["mgy"]; xcb = bufs["mgx"]; zb = bufs["mgz"]; szb = bufs["mgs"]
    ob = bufs["mgo"]
    n = NCHUNK
    s_ld = kb.sem(); s_ac = kb.sem(); s_ev = kb.sem(); s_st = kb.sem()
    b_st = s_st.n; b_ev = s_ev.n
    fvc = kb.fsem["vector"]
    nch = NTOK // n
    for c in range(nch):
        slot = c % 2
        ysv = ysb[:, slot * 16 * n:(slot + 1) * 16 * n]
        xcv = xcb[:, slot * 16 * n:(slot + 1) * 16 * n]
        zv = zb[:, slot * 16 * n:(slot + 1) * 16 * n]
        szv = szb[:, slot * 16 * n:(slot + 1) * 16 * n]
        ov = ob[:, slot * 16 * n:(slot + 1) * 16 * n]

        def fl(eng, c=c, ysv=ysv, xcv=xcv, zv=zv):
            if c >= 2:
                eng.wait_ge(s_st.h, b_st + 16 * (c - 1))
            eng.dma_start(out=ysv.rearrange("p (t n) -> p t n", t=16),
                          in_=ys_dm[:, c * n:(c + 1) * n].rearrange(
                              "(t p) n -> p t n", p=128)).then_inc(s_ld.h, 16)
            eng.dma_start(out=xcv.rearrange("p (t n) -> p t n", t=16),
                          in_=xc_dm[:, c * n:(c + 1) * n].rearrange(
                              "(t p) n -> p t n", p=128)).then_inc(s_ld.h, 16)
            eng.dma_start(out=zv.rearrange("p (t n) -> p t n", t=16),
                          in_=xz_dm[2048:4096, c * n:(c + 1) * n].rearrange(
                              "(t p) n -> p t n", p=128)).then_inc(s_ld.h, 16)
        kb.on("sync", fl)
        s_ld.inc(48)
        ldv = s_ld.n

        def fs(eng, c=c, zv=zv, szv=szv, ldv=ldv):
            eng.wait_ge(s_ld.h, ldv)
            if c == 0:
                eng.wait_ge(d_ready[0].h, d_ready[1])
            if c >= 2:
                eng.wait_ge(s_ev.h, b_ev + c - 1)
            eng.activation(szv, zv, AF.Silu).then_inc(s_ac.h, 1)
        kb.on("scalar", fs)
        s_ac.inc(1)
        acv = s_ac.n

        fg = [fvc.inc() for _ in range(16)]

        def fv(eng, c=c, ysv=ysv, xcv=xcv, szv=szv, ov=ov, acv=acv, fg=fg):
            eng.wait_ge(s_ac.h, acv)
            if c >= 2:
                eng.wait_ge(s_st.h, b_st + 16 * (c - 1))
            for t in range(16):
                eng.scalar_tensor_tensor(
                    ov[:, t * n:(t + 1) * n], xcv[:, t * n:(t + 1) * n],
                    D_sb[:, t:t + 1], ysv[:, t * n:(t + 1) * n],
                    ALU.mult, ALU.add).then_inc(fvc.h, 1)
                eng.wait_ge(fvc.h, fg[t])
                ins = eng.tensor_tensor(ov[:, t * n:(t + 1) * n],
                                        ov[:, t * n:(t + 1) * n],
                                        szv[:, t * n:(t + 1) * n], ALU.mult)
            ins.then_inc(s_ev.h, 1)
        kb.on("vector", fv)
        s_ev.inc(1)
        evv = s_ev.n

        def fst(eng, c=c, ov=ov, evv=evv):
            eng.wait_ge(s_ev.h, evv)
            eng.dma_start(out=y2_dm[:, c * n:(c + 1) * n].rearrange(
                              "(t p) n -> p t n", p=128),
                          in_=ov.rearrange("p (t n) -> p t n", t=16)
                          ).then_inc(s_st.h, 16)
        kb.on("gpsimd", fst)
        s_st.inc(16)
    stn = s_st.n

    def fdrain(eng, stn=stn):
        eng.wait_ge(s_st.h, stn)
    kb.on("gpsimd", fdrain)


def emit_block_l1attn(kb, x_in, x_out, inpT, xprojT, dtprojT, outprojT,
                      lng_D, lnb_D, convw_D, convb_D, dtb_D, Alog_D, Dp_D,
                      xn_bf, xz_dm, xc_dm, xdbl_dm, dt_dm, dtx_dm, ys_dm, y2_dm,
                      xn1_f32, xres):
    nc = kb.nc
    scP = kb.scope()
    lng_sb = kb.sb("m_lng", [128, 8], F32, scP)
    lnbt_sb = kb.sb("m_lnb", [128, 8], F32, scP)
    convw_sb = kb.sb("m_cw", [128, 64], F32, scP)
    convb_sb = kb.sb("m_cb", [128, 16], F32, scP)
    dtb_sb = kb.sb("m_dtb", [128, 16], F32, scP)
    Alog_sb = kb.sb("m_alog", [128, 256], F32, scP)
    A_sb = kb.sb("m_A", [128, 256], F32, scP)
    D_sb = kb.sb("m_D", [128, 16], F32, scP)
    s_p = kb.sem("m_sp")

    def flp(eng):
        eng.dma_start(out=lng_sb[:], in_=lng_D[:, :]).then_inc(s_p.h, 16)
        eng.dma_start(out=lnbt_sb[:], in_=lnb_D[:, :]).then_inc(s_p.h, 16)
        eng.dma_start(out=convw_sb[:], in_=convw_D[:, :]).then_inc(s_p.h, 16)
        eng.dma_start(out=convb_sb[:], in_=convb_D[:, :]).then_inc(s_p.h, 16)
        eng.dma_start(out=dtb_sb[:], in_=dtb_D[:, :]).then_inc(s_p.h, 16)
        eng.dma_start(out=Alog_sb[:], in_=Alog_D[:, :]).then_inc(s_p.h, 16)
        eng.dma_start(out=D_sb[:], in_=Dp_D[:, :]).then_inc(s_p.h, 16)
        eng.wait_ge(s_p.h, 112)
    kb.on("sync", flp)
    s_p.inc(112)
    pv = s_p.n
    fsc = kb.fsem["scalar"]
    fA = fsc.inc()

    def fprep(eng, pv=pv, fA=fA):
        eng.wait_ge(s_p.h, pv)
        eng.activation(A_sb[:], Alog_sb[:], AF.Exp).then_inc(fsc.h, 1)
        eng.wait_ge(fsc.h, fA)
        eng.mul(A_sb[:], A_sb[:], -1.0).then_inc(s_p.h, 1)
    kb.on("scalar", fprep)
    s_p.inc(1)
    a_ready = (s_p, s_p.n)

    sc1 = kb.scope()
    lnb1 = make_ln_bufs(kb, sc1, D, pfx="m_ln1", with_f32=True, with_sum=True)
    emit_ln_phase(kb, lnb1, x_in, xn_bf, D, eps=1e-5, xn_f32_dram=xn1_f32,
                  sum_out_dram=xres)
    sc1.close()
    kb.barrier()
    sc1b = kb.scope()
    lnb1b = make_ln_bufs(kb, sc1b, D, pfx="m_ln2")
    emit_ln_phase(kb, lnb1b, xn1_f32, xn_bf, D, gamma_sb=lng_sb, beta_sb=lnbt_sb,
                  eps=1e-5)
    sc1b.close()
    kb.barrier()

    sc2 = kb.scope()
    wi_sb = kb.sb("m_wi", [128, 8 * 32 * 128], BF16, sc2)
    wst = kb.sb("m_wist", [128, 8 * 128], F32, sc2)
    s_wld = kb.sem(); s_wcvt = kb.sem()
    emit_weight_load_g(kb, inpT, wi_sb, 8, 32, wst, s_wld, s_wcvt)
    ipb = {
        "ipx": kb.sb("m_ipx", [128, 2 * 8 * NCHUNK], BF16, sc2),
        "ipo": kb.sb("m_ipo", [128, 2 * 32 * NCHUNK], BF16, sc2),
        "ipt": kb.sb("m_ipt", [128, 2 * NCHUNK], F32, sc2),
        "ipps": [kb.psum(f"m_ipps{i}", [128, NCHUNK], F32, sc2) for i in range(2)],
    }
    emit_proj_phase(kb, ipb, "ip", [wi_sb], xn_bf, 8, xz_dm, 32,
                    mode="copy_bf16", w_ready=(s_wcvt, s_wcvt.n))
    sc2.close()
    kb.barrier()

    sc3 = kb.scope()
    cvb = {
        "cvx": kb.sb("m_cvx", [128, 2 * (3 + T_SEQ)], BF16, sc3),
        "cva": kb.sb("m_cva", [128, 2 * T_SEQ], F32, sc3),
        "cvo": kb.sb("m_cvo", [128, 2 * T_SEQ], BF16, sc3),
    }
    emit_mamba_conv(kb, cvb, xz_dm, xc_dm, convw_sb, convb_sb, (s_p, pv))
    sc3.close()
    kb.barrier()

    sc4 = kb.scope()
    wx_sb = kb.sb("m_wx", [128, 16 * 1 * 128], BF16, sc4)
    wst2 = kb.sb("m_wxst", [128, 4 * 128], F32, sc4)
    s_w2 = kb.sem(); s_wc2 = kb.sem()
    emit_weight_load_g(kb, xprojT, wx_sb, 16, 1, wst2, s_w2, s_wc2)
    xdb = {
        "xdx": kb.sb("m_xdx", [128, 2 * 16 * NCHUNK], BF16, sc4),
        "xdo": kb.sb("m_xdo", [128, 2 * 1 * NCHUNK], BF16, sc4),
        "xdt": kb.sb("m_xdt", [128, 2 * NCHUNK], F32, sc4),
        "xdps": [kb.psum(f"m_xdps{i}", [128, NCHUNK], F32, sc4) for i in range(2)],
    }
    emit_proj_phase(kb, xdb, "xd", [wx_sb], xc_dm, 16, xdbl_dm, 1,
                    mode="copy_bf16", w_ready=(s_wc2, s_wc2.n))
    sc4.close()
    kb.barrier()

    sc5 = kb.scope()
    wdt_sb = kb.sb("m_wdt", [128, 1 * 16 * 128], BF16, sc5)
    wst3 = kb.sb("m_wdtst", [128, 2 * 128], F32, sc5)
    s_w3 = kb.sem(); s_wc3 = kb.sem()
    emit_weight_load_g(kb, dtprojT, wdt_sb, 1, 16, wst3, s_w3, s_wc3, krows=64)
    dtbf = {
        "dtx": kb.sb("m_dtx", [128, 2 * 1 * NCHUNK], BF16, sc5),
        "dto": kb.sb("m_dto", [128, 2 * 16 * NCHUNK], F32, sc5),
        "dtt": kb.sb("m_dtt", [128, 2 * NCHUNK], F32, sc5),
        "dtl": kb.sb("m_dtl", [128, 2 * NCHUNK], F32, sc5),
        "dtm": kb.sb("m_dtm", [128, 2 * 16 * NCHUNK], BF16, sc5),
        "dto2": kb.sb("m_dto2", [128, 2 * 16 * NCHUNK], BF16, sc5),
        "dtps": [kb.psum(f"m_dtps{i}", [128, NCHUNK], F32, sc5) for i in range(4)],
    }
    emit_proj_phase(kb, dtbf, "dt", [wdt_sb], xdbl_dm, 1, dt_dm, 16,
                    mode="softplus_bias", w_ready=(s_wc3, s_wc3.n), krows=64,
                    bias_sb=dtb_sb, out2_dram=dtx_dm, mul_dram=xc_dm)
    sc5.close()
    kb.barrier()

    sc6 = kb.scope()
    scb = {
        "scdt": kb.sb("m_scdt", [128, 2 * T_SEQ], F32, sc6),
        "scdx": kb.sb("m_scdx", [128, 2 * T_SEQ], BF16, sc6),
        "scB": kb.sb("m_scB", [128, 16 * T_SEQ], BF16, sc6),
        "scC": kb.sb("m_scC", [128, 16 * T_SEQ], BF16, sc6),
        "scrow": kb.sb("m_scrow", [1, 2 * T_SEQ], BF16, sc6),
        "scdA": kb.sb("m_scdA", [128, 2 * T_SEQ], BF16, sc6),
        "sch": kb.sb("m_sch", [128, 2 * T_SEQ], BF16, sc6),
        "scb": kb.sb("m_scb", [128, 2 * T_SEQ], BF16, sc6),
        "scy": kb.sb("m_scy", [128, 2 * T_SEQ], BF16, sc6),
        "sc_ps": [kb.psum(f"m_scps{i}", [128, 512], F32, sc6) for i in range(2)],
    }
    emit_mamba_scan(kb, scb, xdbl_dm, dt_dm, dtx_dm, ys_dm, A_sb, a_ready)
    sc6.close()
    kb.barrier()

    sc7 = kb.scope()
    mgb = {
        "mgy": kb.sb("m_mgy", [128, 2 * 16 * NCHUNK], BF16, sc7),
        "mgx": kb.sb("m_mgx", [128, 2 * 16 * NCHUNK], BF16, sc7),
        "mgz": kb.sb("m_mgz", [128, 2 * 16 * NCHUNK], BF16, sc7),
        "mgs": kb.sb("m_mgs", [128, 2 * 16 * NCHUNK], BF16, sc7),
        "mgo": kb.sb("m_mgo", [128, 2 * 16 * NCHUNK], BF16, sc7),
    }
    emit_mamba_gate(kb, mgb, ys_dm, xc_dm, xz_dm, y2_dm, D_sb, (s_p, pv))
    sc7.close()
    kb.barrier()

    sc8 = kb.scope()
    wo_sb = kb.sb("m_wo", [128, 16 * 8 * 128], BF16, sc8)
    wst4 = kb.sb("m_wost", [128, 4 * 128], F32, sc8)
    s_w4 = kb.sem(); s_wc4 = kb.sem()
    emit_weight_load_g(kb, outprojT, wo_sb, 16, 8, wst4, s_w4, s_wc4)
    opb = {
        "opx": kb.sb("m_opx", [128, 2 * 16 * NCHUNK], BF16, sc8),
        "opo": kb.sb("m_opo", [128, 2 * 8 * NCHUNK], F32, sc8),
        "opt": kb.sb("m_opt", [128, 2 * NCHUNK], F32, sc8),
        "opr": kb.sb("m_opr", [128, 2 * 8 * NCHUNK], F32, sc8),
        "opps": [kb.psum(f"m_opps{i}", [128, NCHUNK], F32, sc8) for i in range(2)],
    }
    emit_proj_phase(kb, opb, "op", [wo_sb], y2_dm, 16, x_out, 8,
                    mode="residual", res_dram=xres, w_ready=(s_wc4, s_wc4.n))
    sc8.close()
    scP.close()
    kb.barrier()


def emit_rope_phase(kb, bufs, x_dm, cos_sb, sin_sb, psw_sb, w_ready):
    T = T_SEQ
    xb = bufs["rpx"]; t1 = bufs["rpa"]; t2 = bufs["rps"]; ob = bufs["rpo"]
    psw = bufs["rp_ps"]
    s_ld = kb.sem(); s_mm = kb.sem(); s_ev = kb.sem(); s_st = kb.sem()
    b_st = s_st.n
    fvc = kb.fsem["vector"]
    idx = 0
    NC2 = T // 512
    for dt in range(8):
        for b in range(B_LOC):
            slot = idx % 2
            xv = xb[:, slot * T:(slot + 1) * T]
            ov = ob[:, slot * T:(slot + 1) * T]

            def fl(eng, dt=dt, b=b, xv=xv, idx=idx):
                if idx >= 2:
                    eng.wait_ge(s_st.h, b_st + 16 * (idx - 1))
                eng.dma_start(out=xv, in_=x_dm[dt * 128:(dt + 1) * 128,
                                               b * T:(b + 1) * T]).then_inc(s_ld.h, 16)
            kb.on("sync", fl)
            s_ld.inc(16)
            ldv = s_ld.n

            for cc in range(NC2):
                bank = cc % 2
                wdn = ev_hist[-2] if False else 0

                def fmm(eng, xv=xv, cc=cc, bank=bank, ldv=ldv, idx=idx,
                        wdn=(s_ev.n - 1 if (idx * NC2 + cc) >= 2 else 0)):
                    if cc == 0:
                        eng.wait_ge(s_ld.h, ldv)
                        if idx == 0:
                            eng.wait_ge(w_ready[0].h, w_ready[1])
                    if wdn:
                        eng.wait_ge(s_ev.h, wdn)
                    return eng.matmul(psw[bank][:, :], psw_sb[:],
                                      xv[:, cc * 512:(cc + 1) * 512],
                                      start=True, stop=True).then_inc(s_mm.h, 1)
                kb.on("tensor", fmm)
                s_mm.inc(1)
                mmv = s_mm.n

                f1 = fvc.inc(); f2 = fvc.inc()

                def fv(eng, xv=xv, ov=ov, cc=cc, bank=bank, b=b, mmv=mmv,
                       idx=idx, f1=f1, f2=f2):
                    eng.wait_ge(s_mm.h, mmv)
                    if idx >= 2 and cc == 0:
                        eng.wait_ge(s_st.h, b_st + 16 * (idx - 1))
                    c0 = cc * 512
                    eng.tensor_tensor(t1[:, 0:512], xv[:, c0:c0 + 512],
                                      cos_sb[:, c0:c0 + 512], ALU.mult).then_inc(fvc.h, 1)
                    eng.tensor_tensor(t2[:, 0:512], psw[bank][:, :],
                                      sin_sb[:, c0:c0 + 512], ALU.mult).then_inc(fvc.h, 1)
                    eng.wait_ge(fvc.h, f2)
                    return eng.tensor_tensor(ov[:, c0:c0 + 512], t1[:, 0:512],
                                             t2[:, 0:512], ALU.add).then_inc(s_ev.h, 1)
                kb.on("vector", fv)
                s_ev.inc(1)
            evv = s_ev.n

            def fst(eng, dt=dt, b=b, ov=ov, evv=evv):
                eng.wait_ge(s_ev.h, evv)
                eng.dma_start(out=x_dm[dt * 128:(dt + 1) * 128, b * T:(b + 1) * T],
                              in_=ov).then_inc(s_st.h, 16)
            kb.on("gpsimd", fst)
            s_st.inc(16)
            idx += 1
    stn = s_st.n

    def fdrain(eng, stn=stn):
        eng.wait_ge(s_st.h, stn)
    kb.on("gpsimd", fdrain)


def emit_mla_attn(kb, bufs, q_dm, k_dm, vt_tm, attn_dm):
    T = T_SEQ
    H = 16
    qb = bufs["maq"]; kbuf = bufs["mak"]; vb = bufs["mav"]
    expb = bufs["maxp"]
    osb = bufs["mao"]
    ones64 = bufs["ones64"]
    outb = bufs["maob"]
    stat = bufs["mast"]
    recb = bufs["marc"]
    rec0 = bufs["marc0"]
    onescolb = bufs["onescolb"]
    cmask_s = bufs["cmask_s"]; cmask_b = bufs["cmask_b"]
    psS = bufs["ma_psS"]
    psO = bufs["ma_psO"]
    psB = bufs["ma_psB"]
    idq = bufs["idq"]
    s_ld = kb.sem(); s_mm = kb.sem(); s_st = kb.sem(); s_lv = kb.sem()
    s_dv = kb.sem(); s_ds = kb.sem()
    fvc = kb.fsem["vector"]
    NQT = T // 128
    inst = 0
    inst_done = []
    st_count = []
    ZERO = ()
    last_psS = [ZERO, ZERO, ZERO, ZERO]
    last_psO = [ZERO, ZERO]
    last_psB = [ZERO, ZERO]
    last_osb = [ZERO, ZERO]
    pending = [None]
    last_stat = [ZERO, ZERO]

    def wdeps(eng, deps):
        dv = 0; ds = 0
        for (tag, val) in deps:
            if tag == "v":
                dv = max(dv, val)
            else:
                ds = max(ds, val)
        if dv:
            eng.wait_ge(s_dv.h, dv)
        if ds:
            eng.wait_ge(s_ds.h, ds)

    def mmstage(emit_mms, deps=(), wait_ld=0, wait_lv=0):
        deps = tuple(deps)

        def f(eng, deps=deps, wait_ld=wait_ld, wait_lv=wait_lv,
              emit_mms=emit_mms):
            if wait_ld:
                eng.wait_ge(s_ld.h, wait_ld)
            if wait_lv:
                eng.wait_ge(s_lv.h, wait_lv)
            wdeps(eng, deps)
            emit_mms(eng).then_inc(s_mm.h, 1)
        kb.on("tensor", f)
        s_mm.inc(1)
        return s_mm.n

    def evstage(engine, emit_ev, wait_mm=0, deps=(), wait_st=0):
        deps = tuple(deps)
        sem = s_dv if engine == "vector" else s_ds

        def f(eng, wait_mm=wait_mm, emit_ev=emit_ev, deps=deps,
              wait_st=wait_st, sem=sem):
            if wait_mm:
                eng.wait_ge(s_mm.h, wait_mm)
            wdeps(eng, deps)
            if wait_st:
                eng.wait_ge(s_st.h, wait_st)
            emit_ev(eng).then_inc(sem.h, 1)
        kb.on(engine, f)
        sem.inc(1)
        return (("v" if engine == "vector" else "s"), sem.n)

    def fsetup(eng):
        eng.memset(ones64[:], 1.0)
        eng.memset(onescolb[:], 1.0).then_inc(s_dv.h, 1)
    kb.on("vector", fsetup)
    s_dv.inc(1)

    for b in range(B_LOC):
        for h in range(H):
            slot = inst % 2
            qv = qb[0:64, slot * T:(slot + 1) * T]
            kv = kbuf[0:64, slot * T:(slot + 1) * T]
            vv = vb[:, slot * 1072:slot * 1072 + 1072]
            ov = outb[:, slot * T:(slot + 1) * T]
            in_guard = inst_done[inst - 2] if inst >= 2 else ()
            r0 = h * 64

            def fld(eng, b=b, r0=r0, qv=qv, kv=kv, in_guard=in_guard):
                wdeps(eng, in_guard)
                eng.dma_start(out=qv, in_=q_dm[r0:r0 + 64, b * T:(b + 1) * T]
                              ).then_inc(s_ld.h, 16)
                eng.dma_start(out=kv, in_=k_dm[r0:r0 + 64, b * T:(b + 1) * T]
                              ).then_inc(s_ld.h, 16)
            kb.on("sync", fld)
            s_ld.inc(32)

            def fldv(eng, b=b, r0=r0, vv=vv, in_guard=in_guard):
                wdeps(eng, in_guard)
                eng.memset(vv, 1.0).then_inc(s_lv.h, 1)
                eng.dma_start(
                    out=vv.rearrange("p (kt c) -> p kt c", kt=16)[:, :, 0:64],
                    in_=vt_tm[b * T:(b + 1) * T, r0:r0 + 64].rearrange(
                        "(kt p) c -> p kt c", p=128),
                ).then_inc(s_lv.h, 16)
            kb.on("gpsimd", fldv)
            s_lv.inc(17)
            lvv = s_lv.n
            ldv = s_ld.n


            for qc in range(4):
                nkt = (qc + 1) * 4
                ob_ = qc % 2
                psOc = psO[ob_]
                exp_deps = []
                kt = 0
                while kt < nkt:
                    pair = [kt, kt + 1]
                    bp = (0, 1) if (kt // 2) % 2 == 0 else (2, 3)

                    def mmS(eng, qc=qc, pair=pair, bp=bp, qv=qv, kv=kv):
                        for j, k in enumerate(pair):
                            ins = eng.matmul(psS[bp[j]][:, :],
                                             kv[:, k * 128:(k + 1) * 128],
                                             qv[:, qc * 512:(qc + 1) * 512],
                                             start=True, stop=True)
                        return ins
                    dd = list(last_psS[bp[0]]) + list(last_psS[bp[1]])
                    if qc == 0 and kt == 0:
                        dd += list(in_guard)
                    m = mmstage(mmS, deps=dd,
                                wait_ld=(ldv if qc == 0 and kt == 0 else 0))

                    diag0 = pair[0] >= qc * 4
                    if diag0:
                        def evmask(eng, qc=qc, pair=pair, bp=bp):
                            for j, k in enumerate(pair):
                                soff = 128 + (k % 2) * 512
                                v = k - qc * 4
                                ins = eng.tensor_tensor(
                                    stat[:, soff:soff + 512], psS[bp[j]][:, :],
                                    cmask_b[:, v * 512:(v + 1) * 512], ALU.add)
                            return eng.tensor_copy(stat[:, 1665:1666],
                                                   stat[:, 128:129])
                        d_msk = evstage("vector", evmask, wait_mm=m,
                                        deps=list(last_stat[0]) + list(last_stat[1]))

                        def evexp(eng, pair=pair):
                            for k in pair:
                                soff = 128 + (k % 2) * 512
                                ins = eng.activation(
                                    expb[:, k * 512:(k + 1) * 512],
                                    stat[:, soff:soff + 512], AF.Exp, scale=0.125)
                            return ins
                        d_e = evstage("scalar", evexp, deps=(d_msk,))
                        last_stat[0] = (d_e,)
                        last_stat[1] = (d_e,)
                        last_psS[bp[0]] = (d_msk,)
                        last_psS[bp[1]] = (d_msk,)
                        exp_deps.append(d_e)
                    else:
                        def evexp(eng, pair=pair, bp=bp):
                            for j, k in enumerate(pair):
                                ins = eng.activation(
                                    expb[:, k * 512:(k + 1) * 512],
                                    psS[bp[j]][:, :], AF.Exp, scale=0.125)
                            return ins
                        d_e = evstage("scalar", evexp, wait_mm=m)
                        last_psS[bp[0]] = (d_e,)
                        last_psS[bp[1]] = (d_e,)
                        exp_deps.append(d_e)
                    kt += 2

                if pending[0] is not None:
                    pending[0]()
                    pending[0] = None

                def mmV(eng, nkt=nkt, vv=vv, psOc=psOc):
                    for kt in range(nkt):
                        ins = eng.matmul(psOc[0:65, :], vv[:, kt * 67:kt * 67 + 65],
                                         expb[:, kt * 512:(kt + 1) * 512],
                                         start=(kt == 0), stop=(kt == nkt - 1))
                    return ins
                m = mmstage(mmV, deps=list(exp_deps) + list(last_psO[ob_]),
                            wait_lv=lvv)

                def evrec(eng, psOc=psOc):
                    eng.reciprocal(rec0[0:1, :], psOc[64:65, :])
                    return eng.memset(kb.junk[0:1, 12:13], 0.0)
                d_rec = evstage("vector", evrec, wait_mm=m)
                osv = osb[:, ob_ * 512:(ob_ + 1) * 512]
                d_oc = evstage("scalar",
                               lambda eng, psOc=psOc, osv=osv: eng.copy(
                                   osv[0:64, :], psOc[0:64, :]),
                               wait_mm=m, deps=last_osb[ob_])
                last_psO[ob_] = (d_oc, d_rec)

                def mktail(qc=qc, ob_=ob_, d_rec=d_rec, d_oc=d_oc, osv=osv,
                           ov=ov, inst=inst, b=b, r0=r0):
                    def flush():
                        m2 = mmstage(
                            lambda eng, ob_=ob_: eng.matmul(
                                psB[ob_][0:64, :], kb.onesf[0:1, 0:64],
                                rec0[0:1, :], start=True, stop=True),
                            deps=list(last_psB[ob_]) + [d_rec])

                        def evO(eng, qc=qc, ov=ov, inst=inst, ob_=ob_,
                                osv=osv):
                            if inst >= 2 and qc == 0:
                                eng.wait_ge(s_st.h, st_count[inst - 2])
                            return eng.tensor_tensor(
                                ov[0:64, qc * 512:(qc + 1) * 512],
                                osv[0:64, :], psB[ob_][0:64, :], ALU.mult)
                        dO = evstage("vector", evO, wait_mm=m2, deps=(d_oc,))
                        last_psB[ob_] = (dO,)
                        last_osb[ob_] = (dO,)
                        if qc == 3:
                            dnv = (dO, d_oc)
                            inst_done.append(dnv)

                            def fso(eng, b=b, r0=r0, ov=ov, dnv=dnv):
                                wdeps(eng, dnv)
                                eng.dma_start(
                                    out=attn_dm[r0:r0 + 64, b * T:(b + 1) * T],
                                    in_=ov[0:64, :]).then_inc(s_st.h, 16)
                            kb.on("gpsimd", fso)
                            s_st.inc(16)
                            st_count.append(s_st.n)
                    return flush
                pending[0] = mktail()

            inst += 1
    if pending[0] is not None:
        pending[0]()
        pending[0] = None
    stn = s_st.n

    def fdrain(eng, stn=stn):
        eng.wait_ge(s_st.h, stn)
    kb.on("gpsimd", fdrain)


def emit_tokmajor_proj(kb, bufs, pfx, x_dm, w_sb, ktiles, mcols, out_tm,
                       w_ready):
    ntiles = NTOK // 128
    xb = bufs[pfx + "x"]
    ob = bufs[pfx + "o"]
    pss = bufs[pfx + "ps"]
    s_ld = kb.sem(); s_mm = kb.sem(); s_ev = kb.sem(); s_st = kb.sem()
    b_st = s_st.n; b_ev = s_ev.n
    nc2 = ceil_div(mcols, 512)
    for tt in range(ntiles):
        slot = tt % 2

        def flx(eng, tt=tt, slot=slot):
            if tt >= 2:
                eng.wait_ge(s_st.h, b_st + 16 * (tt - 1))
            eng.dma_start(
                out=xb[:, slot * ktiles * 128:(slot + 1) * ktiles * 128
                       ].rearrange("p (k c) -> p k c", k=ktiles),
                in_=x_dm[:, tt * 128:(tt + 1) * 128].rearrange(
                    "(k p) c -> p k c", p=128),
            ).then_inc(s_ld.h, 16)
        kb.on("sync", flx)
        s_ld.inc(16)
        ldv = s_ld.n

        def fmm(eng, tt=tt, slot=slot, ldv=ldv):
            eng.wait_ge(s_ld.h, ldv)
            if tt == 0:
                eng.wait_ge(w_ready[0].h, w_ready[1])
            else:
                eng.wait_ge(s_ev.h, b_ev + tt)
            for kt in range(ktiles):
                st = xb[:, slot * ktiles * 128 + kt * 128:
                        slot * ktiles * 128 + (kt + 1) * 128]
                for cc in range(nc2):
                    c0 = cc * 512; c1 = min(mcols, c0 + 512)
                    ins = eng.matmul(pss[:, c0:c1], st,
                                     w_sb[:, kt * mcols + c0:kt * mcols + c1],
                                     start=(kt == 0), stop=(kt == ktiles - 1))
            ins.then_inc(s_mm.h, 1)
        kb.on("tensor", fmm)
        s_mm.inc(1)
        mmv = s_mm.n
        ov = ob[:, slot * mcols:(slot + 1) * mcols]

        def fev(eng, mmv=mmv, ov=ov, tt=tt):
            eng.wait_ge(s_mm.h, mmv)
            if tt >= 2:
                eng.wait_ge(s_st.h, b_st + 16 * (tt - 1))
            eng.activation(ov, pss[:, :], AF.Copy).then_inc(s_ev.h, 1)
        kb.on("scalar", fev)
        s_ev.inc(1)
        evv = s_ev.n

        def fst(eng, tt=tt, ov=ov, evv=evv):
            eng.wait_ge(s_ev.h, evv)
            eng.dma_start(out=out_tm[tt * 128:(tt + 1) * 128, :],
                          in_=ov).then_inc(s_st.h, 16)
        kb.on("gpsimd", fst)
        s_st.inc(16)
    stn = s_st.n

    def fdrain(eng, stn=stn):
        eng.wait_ge(s_st.h, stn)
    kb.on("gpsimd", fdrain)


def emit_gating(kb, bufs, xn2_f32, gwT_sb, gb_row, mask_dm, idq, w_ready):
    ntiles = NTOK // 128
    xb = bufs["gtx"]
    lg = bufs["gtl"]
    mro = bufs["gtm"]
    psL = bufs["gt_psL"]
    psT = bufs["gt_psT"]
    s_ld = kb.sem(); s_mm = kb.sem(); s_dn = kb.sem(); s_st = kb.sem()
    b_st = s_st.n
    fvc = kb.fsem["vector"]
    for tt in range(ntiles):
        slot = tt % 2

        def flx(eng, tt=tt, slot=slot):
            if tt >= 2:
                eng.wait_ge(s_st.h, b_st + 64 * (tt - 1))
            eng.dma_start(
                out=xb[:, slot * 1024:(slot + 1) * 1024].rearrange(
                    "p (k c) -> p k c", k=8),
                in_=xn2_f32[:, tt * 128:(tt + 1) * 128].rearrange(
                    "(k p) c -> p k c", p=128),
            ).then_inc(s_ld.h, 16)
        kb.on("sync", flx)
        s_ld.inc(16)
        ldv = s_ld.n

        def fmm(eng, tt=tt, slot=slot, ldv=ldv, wdn=(s_dn.n)):
            eng.wait_ge(s_ld.h, ldv)
            if tt == 0:
                eng.wait_ge(w_ready[0].h, w_ready[1])
            if wdn:
                eng.wait_ge(s_dn.h, wdn)
            for kt in range(8):
                st = xb[:, slot * 1024 + kt * 128:slot * 1024 + (kt + 1) * 128]
                eng.matmul(psL[:, 0:4], st, gwT_sb[:, kt * 4:(kt + 1) * 4],
                           start=(kt == 0), stop=False)
            ins = eng.matmul(psL[:, 0:4], kb.onesf[:], gb_row[:, :],
                             start=False, stop=True)
            ins.then_inc(s_mm.h, 1)
        kb.on("tensor", fmm)
        s_mm.inc(1)
        mmv = s_mm.n

        fs = [fvc.inc() for _ in range(6)]

        def fdv(eng, mmv=mmv, fs=fs):
            eng.wait_ge(s_mm.h, mmv)
            eng.tensor_copy(lg[:, 0:4], psL[:, 0:4]).then_inc(fvc.h, 1)
            eng.wait_ge(fvc.h, fs[0])
            eng.tensor_reduce(lg[:, 4:5], lg[:, 0:4], AX.X, ALU.max).then_inc(fvc.h, 1)
            eng.wait_ge(fvc.h, fs[1])
            eng.tensor_tensor(lg[:, 5:9], lg[:, 0:4],
                              lg[:, 4:5].broadcast_to([128, 4]) if False else
                              lg[:, 4:5], ALU.is_ge) if False else None
            for e in range(4):
                eng.tensor_tensor(lg[:, 5 + e:6 + e], lg[:, e:e + 1],
                                  lg[:, 4:5], ALU.is_ge)
            ins = eng.tensor_copy(lg[:, 9:10], lg[:, 5:6])
            ins.then_inc(fvc.h, 1)
            eng.wait_ge(fvc.h, fs[2])
            eng.tensor_scalar(lg[:, 13:14], lg[:, 9:10], -1.0, 1.0,
                              ALU.mult, ALU.add).then_inc(fvc.h, 1)
            eng.wait_ge(fvc.h, fs[3])
            eng.tensor_tensor(lg[:, 10:11], lg[:, 6:7], lg[:, 13:14], ALU.mult)
            eng.tensor_tensor(lg[:, 14:15], lg[:, 9:10], lg[:, 10:11], ALU.add).then_inc(fvc.h, 1)
            eng.wait_ge(fvc.h, fs[4])
            eng.tensor_scalar(lg[:, 13:14], lg[:, 14:15], -1.0, 1.0,
                              ALU.mult, ALU.add)
            eng.tensor_tensor(lg[:, 11:12], lg[:, 7:8], lg[:, 13:14], ALU.mult)
            eng.tensor_tensor(lg[:, 14:15], lg[:, 14:15], lg[:, 11:12], ALU.add).then_inc(fvc.h, 1)
            eng.wait_ge(fvc.h, fs[5])
            eng.tensor_scalar(lg[:, 13:14], lg[:, 14:15], -1.0, 1.0,
                              ALU.mult, ALU.add)
            ins = eng.tensor_tensor(lg[:, 12:13], lg[:, 8:9], lg[:, 13:14],
                                    ALU.mult)
            return ins
        d_m = None
        def fdvw(eng, fdv=fdv):
            fdv(eng).then_inc(s_dn.h, 1)
        kb.on("vector", fdvw)
        s_dn.inc(1)
        d_m = s_dn.n

        def fmt(eng, d_m=d_m, wst=(s_mm.n)):
            eng.wait_ge(s_dn.h, d_m)
            for e in range(4):
                eng.transpose(psT[0:1, e * 128:(e + 1) * 128],
                              lg[:, 9 + e:10 + e], idq[:]).then_inc(s_mm.h, 1)
        kb.on("tensor", fmt)
        s_mm.inc(4)
        mtv = s_mm.n

        def fme(eng, mtv=mtv, slot=slot, tt=tt):
            eng.wait_ge(s_mm.h, mtv)
            if tt >= 2:
                eng.wait_ge(s_st.h, b_st + 64 * (tt - 1))
            ins = eng.copy(mro[0:1, slot * 512:slot * 512 + 512],
                           psT[0:1, 0:512])
            ins.then_inc(s_dn.h, 1)
        kb.on("scalar", fme)
        s_dn.inc(1)
        dv2 = s_dn.n

        def fms(eng, tt=tt, slot=slot, dv2=dv2):
            eng.wait_ge(s_dn.h, dv2)
            for e in range(4):
                eng.dma_start(out=mask_dm[e:e + 1, tt * 128:(tt + 1) * 128],
                              in_=mro[0:1, slot * 512 + e * 128:
                                      slot * 512 + (e + 1) * 128]
                              ).then_inc(s_st.h, 16)
        kb.on("gpsimd", fms)
        s_st.inc(64)
    stn = s_st.n

    def fdrain(eng, stn=stn):
        eng.wait_ge(s_st.h, stn)
    kb.on("gpsimd", fdrain)

def emit_block_l1ffn(kb, x_in, x_out, WqT, WdkvT, WukT, WuvT, WoT, gwT_D,
                     gb_D, W1T, b1_D, W2T, b2_D, n1_D, n2_D, lng_D, lnb_D,
                     ropec_D, ropes_D, cmask_D, idq_D, psw_D,
                     xn_bf, xp_f32, xres0, q_dm, k_dm, lat_f32, latn_bf,
                     vt_tm, attn_dm, xc2_f32, xn2_bf, xn2_f32, mask_dm,
                     he_bf, stop_after=99):
    scP = kb.scope()
    n1_sb = kb.sb("f_n1", [128, 8], F32, scP)
    n2_sb = kb.sb("f_n2", [128, 8], F32, scP)
    lng_sb = kb.sb("f_lng", [128, 4], F32, scP)
    lnb_sb = kb.sb("f_lnb", [128, 4], F32, scP)
    gwT_sb = kb.sb("f_gw", [128, 32], F32, scP)
    gb_sb = kb.sb("f_gb", [1, 4], F32, scP)
    b1_sb = kb.sb("f_b1", [128, 128], F32, scP)
    b2_sb = kb.sb("f_b2", [128, 32], F32, scP)
    cosb = kb.sb("f_cos", [128, T_SEQ], BF16, scP)
    sinb = kb.sb("f_sin", [128, T_SEQ], BF16, scP)
    pswb = kb.sb("f_psw", [128, 128], BF16, scP)
    cmsk = kb.sb("f_cm", [128, 2 * 2048], F32, scP)
    idq = kb.sb("f_idq", [128, 128], F32, scP)
    s_p = kb.sem("f_sp")

    def flp(eng):
        eng.dma_start(out=n1_sb[:], in_=n1_D[:, :]).then_inc(s_p.h, 16)
        eng.dma_start(out=n2_sb[:], in_=n2_D[:, :]).then_inc(s_p.h, 16)
        eng.dma_start(out=lng_sb[:], in_=lng_D[:, :]).then_inc(s_p.h, 16)
        eng.dma_start(out=lnb_sb[:], in_=lnb_D[:, :]).then_inc(s_p.h, 16)
        eng.dma_start(out=gwT_sb[:], in_=gwT_D[:, :]).then_inc(s_p.h, 16)
        eng.dma_start(out=gb_sb[:], in_=gb_D[:, :]).then_inc(s_p.h, 16)
        eng.dma_start(out=b1_sb[:], in_=b1_D[:, :]).then_inc(s_p.h, 16)
        eng.dma_start(out=b2_sb[:], in_=b2_D[:, :]).then_inc(s_p.h, 16)
        eng.dma_start(out=cosb[:], in_=ropec_D[:, :]).then_inc(s_p.h, 16)
        eng.dma_start(out=sinb[:], in_=ropes_D[:, :]).then_inc(s_p.h, 16)
        eng.dma_start(out=cmsk[:], in_=cmask_D[:, :]).then_inc(s_p.h, 16)
        eng.dma_start(out=idq[:], in_=idq_D[:, :]).then_inc(s_p.h, 16)
        eng.dma_start(out=pswb[:], in_=psw_D[:, :]).then_inc(s_p.h, 16)
        eng.wait_ge(s_p.h, 16 * 13)
    kb.on("sync", flp)
    s_p.inc(16 * 13)
    p_ready = (s_p, s_p.n)

    sc = kb.scope()
    lnb1 = make_ln_bufs(kb, sc, D, pfx="f_ln1", with_f32=True, with_sum=True)
    emit_ln_phase(kb, lnb1, x_in, xn_bf, D, eps=1e-5, xn_f32_dram=xp_f32,
                  sum_out_dram=xres0)
    sc.close()
    kb.barrier()
    if stop_after <= 1:
        return

    sc = kb.scope()
    lnr = make_ln_bufs(kb, sc, D, pfx="f_rms1")
    emit_ln_phase(kb, lnr, xp_f32, xn_bf, D, gamma_sb=n1_sb, rms=True, eps=1e-6)
    sc.close()
    kb.barrier()
    if stop_after <= 2:
        return

    sc = kb.scope()
    wq_sb = kb.sb("f_wq", [128, 8 * 8 * 128], BF16, sc)
    wl_sb = kb.sb("f_wl", [128, 8 * 4 * 128], BF16, sc)
    sA = kb.sem(); sB = kb.sem()
    emit_weight_load_g(kb, WqT, wq_sb, 8, 8, None, sA, sA)
    emit_weight_load_g(kb, WdkvT, wl_sb, 8, 4, None, sA, sA)
    qb = {
        "qx": kb.sb("f_qx", [128, 2 * 8 * NCHUNK], BF16, sc),
        "qo": kb.sb("f_qo", [128, 2 * 8 * NCHUNK], BF16, sc),
        "qt": kb.sb("f_qt", [128, 2 * NCHUNK], F32, sc),
        "qps": [kb.psum(f"f_qps{i}", [128, NCHUNK], F32, sc) for i in range(2)],
    }
    emit_proj_phase(kb, qb, "q", [wq_sb], xn_bf, 8, q_dm, 8,
                    mode="copy_bf16", w_ready=(sA, sA.n))
    lb = {
        "lx": kb.sb("f_lx", [128, 2 * 8 * NCHUNK], BF16, sc),
        "lo": kb.sb("f_lo", [128, 2 * 4 * NCHUNK], F32, sc),
        "lt": kb.sb("f_lt", [128, 2 * NCHUNK], F32, sc),
        "lps": [kb.psum(f"f_lps{i}", [128, NCHUNK], F32, sc) for i in range(2)],
    }
    emit_proj_phase(kb, lb, "l", [wl_sb], xn_bf, 8, lat_f32, 4,
                    mode="copy_f32", w_ready=(sA, sA.n))
    sc.close()
    kb.barrier()
    if stop_after <= 3:
        return

    sc = kb.scope()
    lnl = make_ln_bufs(kb, sc, 512, pfx="f_lnl")
    emit_ln_phase(kb, lnl, lat_f32, latn_bf, 512, gamma_sb=lng_sb,
                  beta_sb=lnb_sb, eps=1e-5)
    sc.close()
    kb.barrier()
    if stop_after <= 4:
        return

    sc = kb.scope()
    wk_sb = kb.sb("f_wk", [128, 4 * 8 * 128], BF16, sc)
    wv_sb = kb.sb("f_wv", [128, 4 * 1024], BF16, sc)
    sC = kb.sem()
    emit_weight_load_g(kb, WukT, wk_sb, 4, 8, None, sC, sC)
    for kt in range(4):
        def flw(eng, kt=kt):
            eng.dma_start(out=wv_sb[:, kt * 1024:(kt + 1) * 1024],
                          in_=WuvT[kt * 128:(kt + 1) * 128, :]).then_inc(sC.h, 16)
        kb.on("gpsimd", flw)
        sC.inc(16)
    kpb = {
        "kx": kb.sb("f_kx", [128, 2 * 4 * NCHUNK], BF16, sc),
        "ko": kb.sb("f_ko", [128, 2 * 8 * NCHUNK], BF16, sc),
        "kt": kb.sb("f_kt", [128, 2 * NCHUNK], F32, sc),
        "kps": [kb.psum(f"f_kps{i}", [128, NCHUNK], F32, sc) for i in range(2)],
    }
    emit_proj_phase(kb, kpb, "k", [wk_sb], latn_bf, 4, k_dm, 8,
                    mode="copy_bf16", w_ready=(sC, sC.n))
    vtb = {
        "vtx": kb.sb("f_vtx", [128, 2 * 4 * 128], BF16, sc),
        "vto": kb.sb("f_vto", [128, 2 * 1024], BF16, sc),
        "vtps": kb.psum("f_vtps", [128, 1024], F32, sc),
    }
    emit_tokmajor_proj(kb, vtb, "vt", latn_bf, wv_sb, 4, 1024, vt_tm,
                       (sC, sC.n))
    sc.close()
    kb.barrier()
    if stop_after <= 5:
        return

    sc = kb.scope()
    rpb = {
        "rpx": kb.sb("f_rpx", [128, 2 * T_SEQ], BF16, sc),
        "rpa": kb.sb("f_rpa", [128, 512], F32, sc),
        "rps": kb.sb("f_rps", [128, 512], F32, sc),
        "rpo": kb.sb("f_rpo", [128, 2 * T_SEQ], BF16, sc),
        "rp_ps": [kb.psum(f"f_rpps{i}", [128, 512], F32, sc) for i in range(2)],
    }
    emit_rope_phase(kb, rpb, q_dm, cosb, sinb, pswb, p_ready)
    kb.barrier()
    if stop_after <= 6:
        return
    emit_rope_phase(kb, rpb, k_dm, cosb, sinb, pswb, p_ready)
    sc.close()
    kb.barrier()
    if stop_after <= 7:
        return

    sc = kb.scope()
    mab = {
        "maq": kb.sb("f_maq", [64, 2 * T_SEQ], BF16, sc),
        "mak": kb.sb("f_mak", [64, 2 * T_SEQ], BF16, sc),
        "mav": kb.sb("f_mav", [128, 2 * 1072], BF16, sc),
        "mam1": kb.sb("f_mam1", [1, T_SEQ], BF16, sc),
        "maxp": kb.sb("f_maxp", [128, 16 * 512], BF16, sc),
        "mao": kb.sb("f_mao", [128, 1024], F32, sc),
        "maob": kb.sb("f_maob", [64, 2 * T_SEQ], BF16, sc),
        "mast": kb.sb("f_mast", [128, 1824], F32, sc),
        "marc": kb.sb("f_marc", [128, 512], F32, sc),
        "marc0": kb.sb("f_marc0", [1, 512], F32, sc),
        "onescolb": kb.sb("f_onescolb", [128, 1], BF16, sc),
        "ones64": kb.sb("f_ones64", [128, 64], F32, sc),
        "cmask_s": cmsk[:, 0:2048],
        "cmask_b": cmsk[:, 2048:4096],
        "ma_psS": [kb.psum(f"f_psS{i}", [128, 512], F32, sc) for i in range(4)],
        "ma_psO": [kb.psum(f"f_psO{i}", [128, 512], F32, sc) for i in range(2)],
        "ma_psB": [kb.psum(f"f_psB{i}", [128, 512], F32, sc) for i in range(2)],
        "idq": idq,
    }
    emit_mla_attn(kb, mab, q_dm, k_dm, vt_tm, attn_dm)
    sc.close()
    kb.barrier()
    if stop_after <= 8:
        return

    sc = kb.scope()
    wo_sb = kb.sb("f_wo", [128, 8 * 8 * 128], BF16, sc)
    sD = kb.sem()
    emit_weight_load_g(kb, WoT, wo_sb, 8, 8, None, sD, sD)
    opb = {
        "wox": kb.sb("f_wox", [128, 2 * 8 * NCHUNK], BF16, sc),
        "woo": kb.sb("f_woo", [128, 2 * 8 * NCHUNK], F32, sc),
        "wot": kb.sb("f_wot", [128, 2 * NCHUNK], F32, sc),
        "wor": kb.sb("f_wor", [128, 2 * 8 * NCHUNK], F32, sc),
        "wor2": kb.sb("f_wor2", [128, 2 * 8 * NCHUNK], F32, sc),
        "woo2": kb.sb("f_woo2", [128, 2 * 8 * NCHUNK], F32, sc),
        "wops": [kb.psum(f"f_wops{i}", [128, NCHUNK], F32, sc) for i in range(2)],
    }
    emit_proj_phase(kb, opb, "wo", [wo_sb], attn_dm, 8, xc2_f32, 8,
                    mode="residual2", res_dram=xp_f32, res2_dram=xres0,
                    out2_dram=x_out, w_ready=(sD, sD.n))
    sc.close()
    kb.barrier()
    if stop_after <= 9:
        return

    sc = kb.scope()
    lnr2 = make_ln_bufs(kb, sc, D, pfx="f_rms2", with_f32=True)
    emit_ln_phase(kb, lnr2, xc2_f32, xn2_bf, D, gamma_sb=n2_sb, rms=True,
                  eps=1e-6, xn_f32_dram=xn2_f32)
    sc.close()
    kb.barrier()
    if stop_after <= 10:
        return

    sc = kb.scope()
    gtb = {
        "gtx": kb.sb("f_gtx", [128, 2 * 1024], F32, sc),
        "gtl": kb.sb("f_gtl", [128, 16], F32, sc),
        "gtm": kb.sb("f_gtm", [1, 2 * 512], F32, sc),
        "gt_psL": kb.psum("f_gpsL", [128, 512], F32, sc),
        "gt_psT": kb.psum("f_gpsT", [128, 512], F32, sc),
    }
    emit_gating(kb, gtb, xn2_f32, gwT_sb, gb_sb, mask_dm, idq, p_ready)
    sc.close()
    kb.barrier()
    if stop_after <= 11:
        return

    esems1 = [kb.sem() for _ in range(7)]
    esems2 = [kb.sem() for _ in range(7)]
    for e in range(4):
        sc = kb.scope()
        w1_sb = kb.sb(f"f_w1_{e}", [128, 8 * 32 * 128], BF16, sc)
        sE = kb.sem()
        emit_weight_load_g(kb, W1T[e], w1_sb, 8, 32, None, sE, sE)
        heb = {
            "hex": kb.sb(f"f_hex{e}", [128, 2 * 8 * NCHUNK], BF16, sc),
            "heo": kb.sb(f"f_heo{e}", [128, 2 * 32 * NCHUNK], BF16, sc),
            "het": kb.sb(f"f_het{e}", [128, 2 * NCHUNK], F32, sc),
            "heps": [kb.psum(f"f_heps{e}_{i}", [128, NCHUNK], F32, sc) for i in range(2)],
        }
        emit_proj_phase(kb, heb, "he", [w1_sb], xn2_bf, 8, he_bf, 32,
                        mode="relu_bias", bias_sb=b1_sb[:, e * 32:(e + 1) * 32],
                        w_ready=(sE, sE.n), sems=esems1)
        sc.close()
        kb.barrier()
        if stop_after <= 12:
            return
        sc = kb.scope()
        w2_sb = kb.sb(f"f_w2_{e}", [128, 32 * 8 * 128], BF16, sc)
        sF = kb.sem()
        emit_weight_load_g(kb, W2T[e], w2_sb, 32, 8, None, sF, sF)
        zb = {
            "zx": kb.sb(f"f_zx{e}", [128, 2 * 32 * NCHUNK], BF16, sc),
            "zo": kb.sb(f"f_zo{e}", [128, 2 * 8 * NCHUNK], F32, sc),
            "zt": kb.sb(f"f_zt{e}", [128, 2 * NCHUNK], F32, sc),
            "zmr": kb.sb(f"f_zmr{e}", [1, 2 * NCHUNK], F32, sc),
            "zpsm": kb.psum(f"f_zpsm{e}", [128, NCHUNK], F32, sc),
            "zps": [kb.psum(f"f_zps{e}_{i}", [128, NCHUNK], F32, sc) for i in range(2)],
        }
        emit_proj_phase(kb, zb, "z", [w2_sb], he_bf, 32, x_out, 8,
                        mode="moe_accum", bias_sb=b2_sb[:, e * 8:(e + 1) * 8],
                        mask_dram=mask_dm[e:e + 1, :], accum_store=True,
                        w_ready=(sF, sF.n), sems=esems2)
        sc.close()
        kb.barrier()
        if stop_after <= 13:
            return
    scP.close()


def l1ffn_consts():
    t = np.arange(T_SEQ, dtype=np.float64)
    inv = 1.0 / 10000.0 ** (np.arange(32, dtype=np.float64) / 32.0)
    ang = inv[:, None] * t[None, :]
    cos = np.tile(np.cos(ang), (4, 1)).astype(ml_dtypes.bfloat16)
    sgn = np.tile(np.sin(ang), (4, 1))
    sin = np.concatenate([np.concatenate([-sgn[0:32], sgn[32:64]], 0)] * 2,
                         0).astype(ml_dtypes.bfloat16)
    psw = np.zeros((128, 128), np.float32)
    for i in range(128):
        j = i + 32 if (i % 64) < 32 else i - 32
        psw[j, i] = 1.0
    psw = psw.astype(ml_dtypes.bfloat16)
    r = np.arange(128)[:, None]
    j = np.arange(512)[None, :]
    cs = []
    cb = []
    for v in range(4):
        cs.append(np.where(j <= v * 128 + r, 0.0, -1e9).astype(np.float32))
        cb.append(np.where(j >= r + v * 128, 0.0, -1e9).astype(np.float32))
    cmask = np.concatenate(cs + cb, axis=1)
    idq = np.eye(128, dtype=np.float32)
    return cos, sin, cmask, idq, psw


def _decl_x(nc, name, kind=None):
    if kind:
        return nc.dram_tensor(name, [1024, NTOK], F32, kind=kind)
    return nc.dram_tensor(name, [1024, NTOK], F32)


def build_l0attn():
    nc = bass.Bass("TRN2", target_bir_lowering=False)
    DT = nc.dram_tensor
    x_in = DT("x_in", [1024, NTOK], F32, kind="ExternalInput")
    WqD = DT("WqD", [1024, 1024], F32, kind="ExternalInput")
    WkD = DT("WkD", [1024, 1024], F32, kind="ExternalInput")
    WvD = DT("WvD", [1024, 1024], F32, kind="ExternalInput")
    WbD = DT("WbD", [1024, 8], F32, kind="ExternalInput")
    lnwD = DT("lnwD", [128, 8], F32, kind="ExternalInput")
    lnbD = DT("lnbD", [128, 8], F32, kind="ExternalInput")
    cD = DT("cD", [128, 128 + 3 * 1024], BF16, kind="ExternalInput")
    x_out = DT("x_out", [1024, NTOK], F32, kind="ExternalOutput")
    xn_bf = DT("xn_bf", [1024, NTOK], BF16)
    q_dm = DT("q_dm", [1024, NTOK], BF16)
    k_tm = DT("k_tm", [NTOK, 1024], BF16)
    v_tm = DT("v_tm", [NTOK, 1024], BF16)
    beta_tm = DT("beta_tm", [NTOK, 8], F32)
    o_dm = DT("o_dm", [1024, NTOK], F32)
    es = ExitStack()
    kb = KB(nc, es)
    kb.emit_setup()
    emit_block_l0attn(kb, x_in, x_out, WqD, WkD, WvD, WbD, lnwD, lnbD, cD,
                      xn_bf, q_dm, k_tm, v_tm, beta_tm, o_dm)
    kb.finalize()
    es.close()
    return nc


def build_l0ffn():
    nc = bass.Bass("TRN2", target_bir_lowering=False)
    DT = nc.dram_tensor
    x_in = DT("x_in", [1024, NTOK], F32, kind="ExternalInput")
    wgT = DT("wgT", [1024, 3072], I32, kind="ExternalInput")
    wuT = DT("wuT", [1024, 3072], I32, kind="ExternalInput")
    wdT = DT("wdT", [3072, 1024], I32, kind="ExternalInput")
    x_out = DT("x_out", [1024, NTOK], F32, kind="ExternalOutput")
    xn_bf = DT("xn_bf", [1024, NTOK], BF16)
    h_bf = DT("h_bf", [3072, NTOK], BF16)
    es = ExitStack()
    kb = KB(nc, es)
    kb.emit_setup()
    emit_block_l0ffn(kb, x_in, x_out, wgT, wuT, wdT, xn_bf, h_bf)
    kb.finalize()
    es.close()
    return nc


def build_l1attn():
    nc = bass.Bass("TRN2", target_bir_lowering=False)
    DT = nc.dram_tensor
    x_in = DT("x_in", [1024, NTOK], F32, kind="ExternalInput")
    inpT = DT("inpT", [1024, 4096], F32, kind="ExternalInput")
    xprojT = DT("xprojT", [2048, 128], F32, kind="ExternalInput")
    dtprojT = DT("dtprojT", [64, 2048], F32, kind="ExternalInput")
    outprojT = DT("outprojT", [2048, 1024], F32, kind="ExternalInput")
    lng_D = DT("lng_D", [128, 8], F32, kind="ExternalInput")
    lnb_D = DT("lnb_D", [128, 8], F32, kind="ExternalInput")
    convw_D = DT("convw_D", [128, 64], F32, kind="ExternalInput")
    convb_D = DT("convb_D", [128, 16], F32, kind="ExternalInput")
    dtb_D = DT("dtb_D", [128, 16], F32, kind="ExternalInput")
    Alog_D = DT("Alog_D", [128, 256], F32, kind="ExternalInput")
    Dp_D = DT("Dp_D", [128, 16], F32, kind="ExternalInput")
    x_out = DT("x_out", [1024, NTOK], F32, kind="ExternalOutput")
    xn_bf = DT("xn_bf", [1024, NTOK], BF16)
    xz_dm = DT("xz_dm", [4096, NTOK], BF16)
    xc_dm = DT("xc_dm", [2048, NTOK], BF16)
    xdbl_dm = DT("xdbl_dm", [128, NTOK], BF16)
    dt_dm = DT("dt_dm", [2048, NTOK], F32)
    dtx_dm = DT("dtx_dm", [2048, NTOK], BF16)
    ys_dm = DT("ys_dm", [2048, NTOK], BF16)
    y2_dm = DT("y2_dm", [2048, NTOK], BF16)
    xn1_f32 = DT("xn1_f32", [1024, NTOK], F32)
    xres = DT("xres", [1024, NTOK], F32)
    es = ExitStack()
    kb = KB(nc, es)
    kb.emit_setup()
    emit_block_l1attn(kb, x_in, x_out, inpT, xprojT, dtprojT, outprojT,
                      lng_D, lnb_D, convw_D, convb_D, dtb_D, Alog_D, Dp_D,
                      xn_bf, xz_dm, xc_dm, xdbl_dm, dt_dm, dtx_dm, ys_dm,
                      y2_dm, xn1_f32, xres)
    kb.finalize()
    es.close()
    return nc


def build_l1ffn():
    nc = bass.Bass("TRN2", target_bir_lowering=False)
    DT = nc.dram_tensor
    x_in = DT("x_in", [1024, NTOK], F32, kind="ExternalInput")
    WqT_D = DT("WqT", [1024, 1024], F32, kind="ExternalInput")
    WdkvT_D = DT("WdkvT", [1024, 512], F32, kind="ExternalInput")
    WukT_D = DT("WukT", [512, 1024], F32, kind="ExternalInput")
    WuvT_D = DT("WuvT", [512, 1024], F32, kind="ExternalInput")
    WoT_D = DT("WoT", [1024, 1024], F32, kind="ExternalInput")
    gwT_D = DT("gwT", [128, 32], F32, kind="ExternalInput")
    gb_D = DT("gb", [1, 4], F32, kind="ExternalInput")
    W1T_D = DT("W1T", [4, 1024, 4096], F32, kind="ExternalInput")
    b1_D = DT("b1D", [128, 128], F32, kind="ExternalInput")
    W2T_D = DT("W2T", [4, 4096, 1024], F32, kind="ExternalInput")
    b2_D = DT("b2D", [128, 32], F32, kind="ExternalInput")
    n1_D = DT("n1D", [128, 8], F32, kind="ExternalInput")
    n2_D = DT("n2D", [128, 8], F32, kind="ExternalInput")
    lng_D = DT("lngD", [128, 4], F32, kind="ExternalInput")
    lnb_D = DT("lnbD", [128, 4], F32, kind="ExternalInput")
    ropec_D = DT("ropecD", [128, 2048], BF16, kind="ExternalInput")
    ropes_D = DT("ropesD", [128, 2048], BF16, kind="ExternalInput")
    psw_D = DT("pswD", [128, 128], BF16, kind="ExternalInput")
    cmask_D = DT("cmaskD", [128, 4096], F32, kind="ExternalInput")
    idq_D = DT("idqD", [128, 128], F32, kind="ExternalInput")
    x_out = DT("x_out", [1024, NTOK], F32, kind="ExternalOutput")
    xn_bf = DT("xn_bf", [1024, NTOK], BF16)
    xp_f32 = DT("xp_f32", [1024, NTOK], F32)
    xres0 = DT("xres0", [1024, NTOK], F32)
    q_dm = DT("q_dm", [1024, NTOK], BF16)
    k_dm = DT("k_dm", [1024, NTOK], BF16)
    lat_f32 = DT("lat_f32", [512, NTOK], F32)
    latn_bf = DT("latn_bf", [512, NTOK], BF16)
    vt_tm = DT("vt_tm", [NTOK, 1024], BF16)
    attn_dm = DT("attn_dm", [1024, NTOK], BF16)
    xc2_f32 = DT("xc2_f32", [1024, NTOK], F32)
    xn2_bf = DT("xn2_bf", [1024, NTOK], BF16)
    xn2_f32 = DT("xn2_f32", [1024, NTOK], F32)
    mask_dm = DT("mask_dm", [4, NTOK], F32)
    he_bf = DT("he_bf", [4096, NTOK], BF16)
    es = ExitStack()
    kb = KB(nc, es)
    kb.emit_setup()
    emit_block_l1ffn(kb, x_in, x_out, WqT_D, WdkvT_D, WukT_D, WuvT_D, WoT_D,
                     gwT_D, gb_D, W1T_D, b1_D, W2T_D, b2_D, n1_D, n2_D,
                     lng_D, lnb_D, ropec_D, ropes_D, cmask_D, idq_D, psw_D,
                     xn_bf, xp_f32, xres0, q_dm, k_dm, lat_f32, latn_bf,
                     vt_tm, attn_dm, xc2_f32, xn2_bf, xn2_f32, mask_dm, he_bf)
    kb.finalize()
    es.close()
    return nc


def _colmaj(a, tiles):
    return np.ascontiguousarray(np.asarray(a, np.float32).reshape(tiles, 128).T)


FUSED = True


def _common_dicts(L0_attn_Wq, L0_attn_Wk, L0_attn_Wv, L0_attn_Wb,
                  L0_attn_ln_w, L0_attn_ln_b, L0_ffn_w_gate, L0_ffn_w_up,
                  L0_ffn_w_down, L1_attn_ln_gamma, L1_attn_ln_beta,
                  L1_attn_in_proj_w, L1_attn_conv_w, L1_attn_conv_b,
                  L1_attn_x_proj_w, L1_attn_dt_proj_w, L1_attn_dt_proj_b,
                  L1_attn_A_log, L1_attn_D_param, L1_attn_out_proj_w,
                  L1_ffn_norm1_w, L1_ffn_norm2_w, L1_ffn_Wq, L1_ffn_Wdkv,
                  L1_ffn_Wuk, L1_ffn_Wuv, L1_ffn_Wo, L1_ffn_ln_g,
                  L1_ffn_ln_b, L1_ffn_gate_w, L1_ffn_gate_b, L1_ffn_W1,
                  L1_ffn_b1, L1_ffn_W2, L1_ffn_b2):
    f32 = lambda a: np.ascontiguousarray(np.asarray(a, np.float32))
    c1 = {"WqD": f32(L0_attn_Wq), "WkD": f32(L0_attn_Wk),
          "WvD": f32(L0_attn_Wv), "WbD": f32(L0_attn_Wb),
          "lnwD": _colmaj(L0_attn_ln_w, 8), "lnbD": _colmaj(L0_attn_ln_b, 8),
          "cD": l0attn_consts()}
    c2 = {"wgT": np.ascontiguousarray(np.asarray(L0_ffn_w_gate).T),
          "wuT": np.ascontiguousarray(np.asarray(L0_ffn_w_up).T),
          "wdT": np.ascontiguousarray(np.asarray(L0_ffn_w_down).T)}
    xpp = np.zeros((2048, 128), np.float32)
    xpp[:, 0:96] = np.asarray(L1_attn_x_proj_w, np.float32).T
    c3 = {
        "inpT": np.ascontiguousarray(np.asarray(L1_attn_in_proj_w, np.float32).T),
        "xprojT": xpp,
        "dtprojT": np.ascontiguousarray(np.asarray(L1_attn_dt_proj_w, np.float32).T),
        "outprojT": np.ascontiguousarray(np.asarray(L1_attn_out_proj_w, np.float32).T),
        "lng_D": _colmaj(L1_attn_ln_gamma, 8), "lnb_D": _colmaj(L1_attn_ln_beta, 8),
        "convw_D": np.ascontiguousarray(np.asarray(L1_attn_conv_w, np.float32)
                                        .reshape(16, 128, 4).transpose(1, 0, 2)
                                        .reshape(128, 64)),
        "convb_D": _colmaj(L1_attn_conv_b, 16),
        "dtb_D": _colmaj(L1_attn_dt_proj_b, 16),
        "Alog_D": np.ascontiguousarray(np.asarray(L1_attn_A_log, np.float32)
                                       .reshape(16, 128, 16).transpose(1, 0, 2)
                                       .reshape(128, 256)),
        "Dp_D": _colmaj(L1_attn_D_param, 16),
    }
    cos, sin, cmask, idq, psw = l1ffn_consts()
    c4 = {
        "WqT": np.ascontiguousarray(np.asarray(L1_ffn_Wq, np.float32).T),
        "WdkvT": np.ascontiguousarray(np.asarray(L1_ffn_Wdkv, np.float32).T),
        "WukT": np.ascontiguousarray(np.asarray(L1_ffn_Wuk, np.float32).T),
        "WuvT": np.ascontiguousarray(np.asarray(L1_ffn_Wuv, np.float32).T),
        "WoT": np.ascontiguousarray(np.asarray(L1_ffn_Wo, np.float32).T),
        "gwT": np.ascontiguousarray(np.asarray(L1_ffn_gate_w, np.float32).T
                                    .reshape(8, 128, 4).transpose(1, 0, 2)
                                    .reshape(128, 32)),
        "gb": np.asarray(L1_ffn_gate_b, np.float32).reshape(1, 4),
        "W1T": np.ascontiguousarray(np.asarray(L1_ffn_W1, np.float32)
                                    .transpose(0, 2, 1)),
        "b1D": np.ascontiguousarray(np.concatenate(
            [np.asarray(L1_ffn_b1, np.float32)[e].reshape(32, 128).T
             for e in range(4)], 1)),
        "W2T": np.ascontiguousarray(np.asarray(L1_ffn_W2, np.float32)
                                    .transpose(0, 2, 1)),
        "b2D": np.ascontiguousarray(np.concatenate(
            [np.asarray(L1_ffn_b2, np.float32)[e].reshape(8, 128).T
             for e in range(4)], 1)),
        "n1D": _colmaj(L1_ffn_norm1_w, 8), "n2D": _colmaj(L1_ffn_norm2_w, 8),
        "lngD": _colmaj(L1_ffn_ln_g, 4), "lnbD": _colmaj(L1_ffn_ln_b, 4),
        "ropecD": cos, "ropesD": sin, "cmaskD": cmask, "idqD": idq, "pswD": psw,
    }
    return c1, c2, c3, c4


def kernel(L0_attn_x, L0_attn_Wq, L0_attn_Wk, L0_attn_Wv, L0_attn_Wb,
           L0_attn_ln_w, L0_attn_ln_b, L0_ffn_w_gate, L0_ffn_w_up,
           L0_ffn_w_down, L1_attn_ln_gamma, L1_attn_ln_beta,
           L1_attn_in_proj_w, L1_attn_conv_w, L1_attn_conv_b,
           L1_attn_x_proj_w, L1_attn_dt_proj_w, L1_attn_dt_proj_b,
           L1_attn_A_log, L1_attn_D_param, L1_attn_out_proj_w,
           L1_ffn_norm1_w, L1_ffn_norm2_w, L1_ffn_Wq, L1_ffn_Wdkv,
           L1_ffn_Wuk, L1_ffn_Wuv, L1_ffn_Wo, L1_ffn_ln_g, L1_ffn_ln_b,
           L1_ffn_gate_w, L1_ffn_gate_b, L1_ffn_W1, L1_ffn_b1, L1_ffn_W2,
           L1_ffn_b2):
    shards = shard_x(np.asarray(L0_attn_x, np.float32))
    c1, c2, c3, c4 = _common_dicts(
        L0_attn_Wq, L0_attn_Wk, L0_attn_Wv, L0_attn_Wb, L0_attn_ln_w,
        L0_attn_ln_b, L0_ffn_w_gate, L0_ffn_w_up, L0_ffn_w_down,
        L1_attn_ln_gamma, L1_attn_ln_beta, L1_attn_in_proj_w, L1_attn_conv_w,
        L1_attn_conv_b, L1_attn_x_proj_w, L1_attn_dt_proj_w, L1_attn_dt_proj_b,
        L1_attn_A_log, L1_attn_D_param, L1_attn_out_proj_w, L1_ffn_norm1_w,
        L1_ffn_norm2_w, L1_ffn_Wq, L1_ffn_Wdkv, L1_ffn_Wuk, L1_ffn_Wuv,
        L1_ffn_Wo, L1_ffn_ln_g, L1_ffn_ln_b, L1_ffn_gate_w, L1_ffn_gate_b,
        L1_ffn_W1, L1_ffn_b1, L1_ffn_W2, L1_ffn_b2)
    if FUSED:
        c4f = dict(c4)
        c4f["flngD"] = c4f.pop("lngD")
        c4f["flnbD"] = c4f.pop("lnbD")
        merged = {}
        for c in (c1, c2, c3, c4f):
            merged.update(c)
        nc = build_fused()
        in_maps = [dict(merged, x_in=shards[ci]) for ci in range(NCORES)]
        res = run_bass_kernel_spmd(nc, in_maps, core_ids=list(range(NCORES)))
        shards = [r["x_out"] for r in res.results]
        return unshard_x(shards)
    nc = build_l0attn()
    in_maps = [dict(c1, x_in=shards[c]) for c in range(NCORES)]
    res = run_bass_kernel_spmd(nc, in_maps, core_ids=list(range(NCORES)))
    shards = [r["x_out"] for r in res.results]
    nc = build_l0ffn()
    in_maps = [dict(c2, x_in=shards[c]) for c in range(NCORES)]
    res = run_bass_kernel_spmd(nc, in_maps, core_ids=list(range(NCORES)))
    shards = [r["x_out"] for r in res.results]
    nc = build_l1attn()
    in_maps = [dict(c3, x_in=shards[c]) for c in range(NCORES)]
    res = run_bass_kernel_spmd(nc, in_maps, core_ids=list(range(NCORES)))
    shards = [r["x_out"] for r in res.results]
    nc = build_l1ffn()
    in_maps = [dict(c4, x_in=shards[c]) for c in range(NCORES)]
    res = run_bass_kernel_spmd(nc, in_maps, core_ids=list(range(NCORES)))
    shards = [r["x_out"] for r in res.results]
    return unshard_x(shards)



def build_fused():
    nc = bass.Bass("TRN2", target_bir_lowering=False)
    DT = nc.dram_tensor
    x_in = DT("x_in", [1024, NTOK], F32, kind="ExternalInput")
    x_out = DT("x_out", [1024, NTOK], F32, kind="ExternalOutput")
    x1 = DT("x1", [1024, NTOK], F32)
    x2 = DT("x2", [1024, NTOK], F32)
    x3 = DT("x3", [1024, NTOK], F32)
    WqD = DT("WqD", [1024, 1024], F32, kind="ExternalInput")
    WkD = DT("WkD", [1024, 1024], F32, kind="ExternalInput")
    WvD = DT("WvD", [1024, 1024], F32, kind="ExternalInput")
    WbD = DT("WbD", [1024, 8], F32, kind="ExternalInput")
    lnwD = DT("lnwD", [128, 8], F32, kind="ExternalInput")
    lnbD = DT("lnbD", [128, 8], F32, kind="ExternalInput")
    cD = DT("cD", [128, 128 + 3 * 1024], BF16, kind="ExternalInput")
    a_xn = DT("dr_a_xn", [1024, NTOK], BF16)
    a_q = DT("dr_a_q", [1024, NTOK], BF16)
    a_ktm = DT("dr_a_ktm", [NTOK, 1024], BF16)
    a_vtm = DT("dr_a_vtm", [NTOK, 1024], BF16)
    a_beta = DT("dr_a_beta", [NTOK, 8], F32)
    a_o = DT("dr_a_o", [1024, NTOK], F32)
    wgT = DT("wgT", [1024, 3072], I32, kind="ExternalInput")
    wuT = DT("wuT", [1024, 3072], I32, kind="ExternalInput")
    wdT = DT("wdT", [3072, 1024], I32, kind="ExternalInput")
    f_xn = DT("dr_f_xn", [1024, NTOK], BF16)
    f_h = DT("dr_f_h", [3072, NTOK], BF16)
    inpT = DT("inpT", [1024, 4096], F32, kind="ExternalInput")
    xprojT = DT("xprojT", [2048, 128], F32, kind="ExternalInput")
    dtprojT = DT("dtprojT", [64, 2048], F32, kind="ExternalInput")
    outprojT = DT("outprojT", [2048, 1024], F32, kind="ExternalInput")
    lng_D = DT("lng_D", [128, 8], F32, kind="ExternalInput")
    lnb_D = DT("lnb_D", [128, 8], F32, kind="ExternalInput")
    convw_D = DT("convw_D", [128, 64], F32, kind="ExternalInput")
    convb_D = DT("convb_D", [128, 16], F32, kind="ExternalInput")
    dtb_D = DT("dtb_D", [128, 16], F32, kind="ExternalInput")
    Alog_D = DT("Alog_D", [128, 256], F32, kind="ExternalInput")
    Dp_D = DT("Dp_D", [128, 16], F32, kind="ExternalInput")
    m_xn = DT("dr_m_xn", [1024, NTOK], BF16)
    m_xz = DT("dr_m_xz", [4096, NTOK], BF16)
    m_xc = DT("dr_m_xc", [2048, NTOK], BF16)
    m_xdbl = DT("dr_m_xdbl", [128, NTOK], BF16)
    m_dt = DT("dr_m_dt", [2048, NTOK], F32)
    m_dtx = DT("dr_m_dtx", [2048, NTOK], BF16)
    m_ys = DT("dr_m_ys", [2048, NTOK], BF16)
    m_y2 = DT("dr_m_y2", [2048, NTOK], BF16)
    m_xn1 = DT("dr_m_xn1", [1024, NTOK], F32)
    m_xres = DT("dr_m_xres", [1024, NTOK], F32)
    WqT_D = DT("WqT", [1024, 1024], F32, kind="ExternalInput")
    WdkvT_D = DT("WdkvT", [1024, 512], F32, kind="ExternalInput")
    WukT_D = DT("WukT", [512, 1024], F32, kind="ExternalInput")
    WuvT_D = DT("WuvT", [512, 1024], F32, kind="ExternalInput")
    WoT_D = DT("WoT", [1024, 1024], F32, kind="ExternalInput")
    gwT_D = DT("gwT", [128, 32], F32, kind="ExternalInput")
    gb_D = DT("gb", [1, 4], F32, kind="ExternalInput")
    W1T_D = DT("W1T", [4, 1024, 4096], F32, kind="ExternalInput")
    b1_D = DT("b1D", [128, 128], F32, kind="ExternalInput")
    W2T_D = DT("W2T", [4, 4096, 1024], F32, kind="ExternalInput")
    b2_D = DT("b2D", [128, 32], F32, kind="ExternalInput")
    n1_D = DT("n1D", [128, 8], F32, kind="ExternalInput")
    n2_D = DT("n2D", [128, 8], F32, kind="ExternalInput")
    flng_D = DT("flngD", [128, 4], F32, kind="ExternalInput")
    flnb_D = DT("flnbD", [128, 4], F32, kind="ExternalInput")
    ropec_D = DT("ropecD", [128, 2048], BF16, kind="ExternalInput")
    ropes_D = DT("ropesD", [128, 2048], BF16, kind="ExternalInput")
    psw_D = DT("pswD", [128, 128], BF16, kind="ExternalInput")
    cmask_D = DT("cmaskD", [128, 4096], F32, kind="ExternalInput")
    idq_D = DT("idqD", [128, 128], F32, kind="ExternalInput")
    g_xn = DT("dr_g_xn", [1024, NTOK], BF16)
    g_xp = DT("dr_g_xp", [1024, NTOK], F32)
    g_xres0 = DT("dr_g_xres0", [1024, NTOK], F32)
    g_q = DT("dr_g_q", [1024, NTOK], BF16)
    g_k = DT("dr_g_k", [1024, NTOK], BF16)
    g_lat = DT("dr_g_lat", [512, NTOK], F32)
    g_latn = DT("dr_g_latn", [512, NTOK], BF16)
    g_vt = DT("dr_g_vt", [NTOK, 1024], BF16)
    g_attn = DT("dr_g_attn", [1024, NTOK], BF16)
    g_xc2 = DT("dr_g_xc2", [1024, NTOK], F32)
    g_xn2 = DT("dr_g_xn2", [1024, NTOK], BF16)
    g_xn2f = DT("dr_g_xn2f", [1024, NTOK], F32)
    g_mask = DT("dr_g_mask", [4, NTOK], F32)
    g_he = DT("dr_g_he", [4096, NTOK], BF16)

    es = ExitStack()
    kb = KB(nc, es)
    kb.emit_setup()
    emit_block_l0attn(kb, x_in, x1, WqD, WkD, WvD, WbD, lnwD, lnbD, cD,
                      a_xn, a_q, a_ktm, a_vtm, a_beta, a_o)
    kb.barrier()
    emit_block_l0ffn(kb, x1, x2, wgT, wuT, wdT, f_xn, f_h)
    kb.barrier()
    emit_block_l1attn(kb, x2, x3, inpT, xprojT, dtprojT, outprojT,
                      lng_D, lnb_D, convw_D, convb_D, dtb_D, Alog_D, Dp_D,
                      m_xn, m_xz, m_xc, m_xdbl, m_dt, m_dtx, m_ys,
                      m_y2, m_xn1, m_xres)
    kb.barrier()
    emit_block_l1ffn(kb, x3, x_out, WqT_D, WdkvT_D, WukT_D, WuvT_D, WoT_D,
                     gwT_D, gb_D, W1T_D, b1_D, W2T_D, b2_D, n1_D, n2_D,
                     flng_D, flnb_D, ropec_D, ropes_D, cmask_D, idq_D, psw_D,
                     g_xn, g_xp, g_xres0, g_q, g_k, g_lat, g_latn,
                     g_vt, g_attn, g_xc2, g_xn2, g_xn2f, g_mask, g_he)
    kb.finalize()
    es.close()
    return nc


def kernel_fused(inputs_common_by_launch, shards):
    nc = build_fused()
    c1, c2, c3, c4 = inputs_common_by_launch
    common = {}
    for c in (c1, c2, c3, c4):
        common.update(c)
    in_maps = [dict(common, x_in=shards[cix]) for cix in range(NCORES)]
    res = run_bass_kernel_spmd(nc, in_maps, core_ids=list(range(NCORES)))
    return [r["x_out"] for r in res.results]


# version 13
# speedup vs baseline: 1.5917x; 1.0240x over previous
import math
import numpy as np
import ml_dtypes
from contextlib import ExitStack

import concourse.bass as bass
import concourse.mybir as mybir
from concourse.bass_utils import run_bass_kernel_spmd

F32 = mybir.dt.float32
BF16 = mybir.dt.bfloat16
I32 = mybir.dt.int32
AF = mybir.ActivationFunctionType
ALU = mybir.AluOpType
AX = mybir.AxisListType

NCORES = 8
B_LOC = 4
T_SEQ = 2048
NTOK = B_LOC * T_SEQ
D = 1024
NCHUNK = 512
ENGINES = ("sync", "tensor", "vector", "scalar", "gpsimd")


class SemC:
    def __init__(self, h):
        self.h = h
        self.n = 0

    def inc(self, k=1):
        self.n += k
        return self.n


class KB:
    def __init__(self, nc, es):
        self.nc = nc
        self.es = es
        self.ops = {e: [] for e in ENGINES}
        self._semi = 0
        self.bar = self.sem("bar")
        self.onesf = None
        self.onescol = None
        self.fsem = {e: self.sem("fence_" + e) for e in ENGINES}

    def sem(self, name=None):
        if name is not None:
            self._semi += 1
            h = self.es.enter_context(self.nc.semaphore(name))
            return SemC(h)
        if not hasattr(self, "_pool"):
            self._pool = []
            self._pooli = 0
        if len(self._pool) < 24:
            self._semi += 1
            h = self.es.enter_context(self.nc.semaphore(f"gp{len(self._pool)}"))
            sc = SemC(h)
            self._pool.append(sc)
            self._pooli = len(self._pool)
            return sc
        sc = self._pool[self._pooli % 24]
        self._pooli += 1
        return sc

    def sb(self, name, shape, dtype, scope=None):
        return (scope or self.es).enter_context(self.nc.sbuf_tensor(name, shape, dtype))

    def psum(self, name, shape, dtype=F32, scope=None):
        return (scope or self.es).enter_context(self.nc.psum_tensor(name, shape, dtype))

    def scope(self):
        return ExitStack()

    def on(self, eng, fn):
        self.ops[eng].append(fn)

    def barrier(self):
        t1 = self.bar.n + len(ENGINES)
        t2 = t1 + 1
        self.bar.n = t2
        pool = list(getattr(self, "_pool", []))
        for e in ENGINES:
            def f(eng, t1=t1, t2=t2, e=e, pool=pool):
                if e == "vector":
                    eng.memset(self.junk[:, 0:4], 0.0).then_inc(self.bar.h, 1)
                elif e == "scalar":
                    eng.copy(self.junk[:, 4:8], self.junk[:, 0:4]).then_inc(self.bar.h, 1)
                elif e == "gpsimd":
                    eng.memset(self.junk[:, 8:12], 0.0).then_inc(self.bar.h, 1)
                else:
                    eng.sem_inc(self.bar.h, 1)
                eng.wait_ge(self.bar.h, t1)
                if e == "gpsimd":
                    eng.sem_inc(self.bar.h, 1)
                eng.wait_ge(self.bar.h, t2)
            self.on(e, f)

    def emit_setup(self):
        self.onesf = self.sb("g_onesf", [1, 128], F32)
        self.onescol = self.sb("g_onescol", [128, 1], F32)
        self.onesb = self.sb("g_onesb", [1, 128], BF16)
        self.junk = self.sb("g_junk", [128, 16], F32)
        s_su = self.sem("setup")

        def f(eng):
            eng.memset(self.onesf[:], 1.0)
            eng.memset(self.onesb[:], 1.0)
            eng.memset(self.onescol[:], 1.0).then_inc(s_su.h, 1)
        self.on("vector", f)
        s_su.inc(1)
        self.barrier()

    def finalize(self):
        with self.nc.Block() as block:
            for ename in ENGINES:
                deco = getattr(block, ename)
                lst = self.ops[ename]
                def body(eng, lst=lst):
                    for f in lst:
                        f(eng)
                deco(body)



def emit_weight_load(kb, w_dram, wq_sb, ktiles, mtiles, stage_sb, sem_ld, sem_cvt,
                     scale=None, src_dt=F32):
    nstage = stage_sb.shape[1] // 128
    idx = 0
    for kt in range(ktiles):
        for mt in range(mtiles):
            slot = idx % nstage
            need_cvt = max(0, idx - nstage + 1)
            base_ld = sem_ld.n

            def fl(eng, kt=kt, mt=mt, slot=slot, need_cvt=need_cvt):
                if need_cvt:
                    eng.wait_ge(sem_cvt.h, need_cvt)
                eng.dma_start(
                    out=stage_sb[:, slot * 128:(slot + 1) * 128],
                    in_=w_dram[kt * 128:(kt + 1) * 128, mt * 128:(mt + 1) * 128],
                ).then_inc(sem_ld.h, 16)
            kb.on("sync", fl)
            sem_ld.inc(16)
            ldv = sem_ld.n

            def fc(eng, idx=idx, slot=slot, ldv=ldv, kt=kt, mt=mt):
                eng.wait_ge(sem_ld.h, ldv)
                col = (kt * mtiles + mt) * 128
                if scale is not None:
                    eng.tensor_scalar(
                        wq_sb[:, col:col + 128],
                        stage_sb[:, slot * 128:(slot + 1) * 128],
                        float(scale), None, ALU.mult,
                    ).then_inc(sem_cvt.h, 1)
                else:
                    eng.tensor_copy(
                        wq_sb[:, col:col + 128],
                        stage_sb[:, slot * 128:(slot + 1) * 128],
                    ).then_inc(sem_cvt.h, 1)
            kb.on("vector", fc)
            sem_cvt.inc(1)
            idx += 1


def ceil_div(a, b):
    return (a + b - 1) // b



def shard_x(x):
    shards = []
    for c in range(NCORES):
        xs = x[c * B_LOC:(c + 1) * B_LOC]
        xs = xs.reshape(NTOK, D).T
        shards.append(np.ascontiguousarray(xs))
    return shards


def unshard_x(shards):
    out = np.empty((NCORES * B_LOC, T_SEQ, D), np.float32)
    for c in range(NCORES):
        out[c * B_LOC:(c + 1) * B_LOC] = shards[c].T.reshape(B_LOC, T_SEQ, D)
    return out


def hostT(a):
    return np.ascontiguousarray(np.asarray(a).T)


def emit_ln_phase(kb, bufs, x_dram, xn_dram, dim, gamma_sb=None, beta_sb=None,
                  rms=False, eps=1e-5, xn_f32_dram=None, nchunk=NCHUNK,
                  ntok=NTOK, res_dram=None, sum_out_dram=None):
    pt = dim // 128
    nchunks = ntok // nchunk
    xs = bufs["lnx"]
    sq = bufs["lnsq"]
    tmpx = bufs["lntmp"]
    xnb = bufs["lnxn"]
    xnf = bufs.get("lnxnf")
    rows = bufs["lnrows"]
    onesf = bufs["onesf"]
    onescol = bufs["onescol"]
    pstat = bufs["ps_stat"]
    pb1 = bufs["ps_b1"]
    pb2 = bufs["ps_b2"]
    s_ld = kb.sem(); s_sq = kb.sem(); s_sqc = kb.sem(); s_st = kb.sem()
    s_row = kb.sem(); s_bc = kb.sem(); s_xn = kb.sem(); s_out = kb.sem()
    s_res = kb.sem()
    b_sqc = s_sqc.n; b_out = s_out.n
    resb = bufs.get("lnres")
    sumb = bufs.get("lnsum")
    use_f32 = xn_f32_dram is not None
    nst = 16 * (1 + (1 if use_f32 else 0) + (1 if sum_out_dram is not None else 0))
    n = nchunk
    rmu = rows[0:1, 0:n]; rrstd = rows[0:1, n:2 * n]
    rmr = rows[0:1, 2 * n:3 * n]; rtmp = rows[0:1, 3 * n:4 * n]
    xn_done = []
    row_done = []

    for c in range(nchunks):
        slot = c % 2
        xsv = xs[:, slot * pt * nchunk:(slot + 1) * pt * nchunk]
        xnv = xnb[:, slot * pt * nchunk:(slot + 1) * pt * nchunk]
        xnfv = xnf[:, slot * pt * nchunk:(slot + 1) * pt * nchunk] if use_f32 else None

        need_out = b_out + nst * max(0, c - 1) if c >= 2 else 0

        def fl(eng, c=c, xsv=xsv, need_out=need_out):
            if need_out:
                eng.wait_ge(s_out.h, need_out)
            eng.dma_start(
                out=xsv.rearrange("p (t n) -> p t n", t=pt),
                in_=x_dram[:, c * nchunk:(c + 1) * nchunk].rearrange(
                    "(t p) n -> p t n", p=128),
            ).then_inc(s_ld.h, 16)
        kb.on("sync", fl)
        s_ld.inc(16)
        ldv = s_ld.n
        if resb is not None:
            rv = resb[:, slot * pt * nchunk:(slot + 1) * pt * nchunk]

            def flr(eng, c=c, rv=rv, need_out=need_out):
                if need_out:
                    eng.wait_ge(s_out.h, need_out)
                eng.dma_start(
                    out=rv.rearrange("p (t n) -> p t n", t=pt),
                    in_=res_dram[:, c * nchunk:(c + 1) * nchunk].rearrange(
                        "(t p) n -> p t n", p=128),
                ).then_inc(s_res.h, 16)
            kb.on("sync", flr)
            s_res.inc(16)
        else:
            rv = None
        resv = s_res.n

        for t in range(pt):
            sqslot = t % 2
            sqv_need = s_sqc.n - 1 + t
            def fsq(eng, t=t, xsv=xsv, ldv=ldv, sqslot=sqslot, c=c):
                if t == 0:
                    eng.wait_ge(s_ld.h, ldv)
                if t >= 2:
                    eng.wait_ge(s_sqc.h, b_sqc + (c * pt) + t - 1)
                elif c > 0:
                    eng.wait_ge(s_sqc.h, b_sqc + c * pt)
                eng.activation(sq[:, sqslot * n:(sqslot + 1) * n],
                               xsv[:, t * n:(t + 1) * n],
                               AF.Square).then_inc(s_sq.h, 1)
            kb.on("scalar", fsq)
            s_sq.inc(1)
            sqv = s_sq.n

            def fmm(eng, t=t, sqv=sqv, c=c, xsv=xsv, sqslot=sqslot):
                eng.wait_ge(s_sq.h, sqv)
                if t == 0 and c > 0:
                    eng.wait_ge(s_row.h, row_done[c - 1])
                last = (t == pt - 1)
                mm1 = eng.matmul(pstat[0:1, 0:n], onescol[:],
                                 sq[:, sqslot * n:(sqslot + 1) * n],
                                 start=(t == 0), stop=last)
                mm1.then_inc(s_sqc.h, 1)
                mm2 = eng.matmul(pstat[0:1, n:2 * n], onescol[:],
                                 xsv[:, t * n:(t + 1) * n],
                                 start=(t == 0), stop=last)
                if last:
                    mm2.then_inc(s_st.h, 1)
            kb.on("tensor", fmm)
            s_sqc.inc(1)
        s_st.inc(1)
        stv = s_st.n

        fs = kb.fsem["vector"]
        if not rms:
            f1 = fs.inc(); f2 = fs.inc(); f3 = fs.inc()
        else:
            f1 = f2 = f3 = 0

        def frow(eng, stv=stv, f1=f1, f2=f2, f3=f3):
            eng.wait_ge(s_st.h, stv)
            if rms:
                eng.tensor_scalar(rtmp, pstat[0:1, 0:n], 1.0 / dim, float(eps),
                                  ALU.mult, ALU.add).then_inc(s_row.h, 1)
            else:
                eng.tensor_scalar(rmu, pstat[0:1, n:2 * n], 1.0 / dim, None, ALU.mult)
                eng.tensor_scalar(rtmp, pstat[0:1, 0:n], 1.0 / dim, None, ALU.mult).then_inc(fs.h, 1)
                eng.wait_ge(fs.h, f1)
                eng.tensor_tensor(rmr, rmu, rmu, ALU.mult).then_inc(fs.h, 1)
                eng.wait_ge(fs.h, f2)
                eng.tensor_tensor(rtmp, rtmp, rmr, ALU.subtract).then_inc(fs.h, 1)
                eng.wait_ge(fs.h, f3)
                eng.tensor_scalar(rtmp, rtmp, 1.0, float(eps),
                                  ALU.mult, ALU.add).then_inc(s_row.h, 1)
        kb.on("vector", frow)
        s_row.inc(1)
        v0 = s_row.n

        def fsqrt(eng, v0=v0):
            eng.wait_ge(s_row.h, v0)
            eng.activation(rtmp, rtmp, AF.Sqrt).then_inc(s_row.h, 1)
        kb.on("scalar", fsqrt)
        s_row.inc(1)
        v1 = s_row.n

        f4 = fs.inc()

        def frec(eng, v1=v1, f4=f4):
            eng.wait_ge(s_row.h, v1)
            eng.reciprocal(rrstd, rtmp).then_inc(fs.h, 1)
            eng.wait_ge(fs.h, f4)
            if not rms:
                eng.tensor_tensor(rmr, rmu, rrstd, ALU.mult).then_inc(s_row.h, 1)
            else:
                eng.tensor_copy(rmr, rrstd).then_inc(s_row.h, 1)
        kb.on("vector", frec)
        s_row.inc(1)
        v2 = s_row.n
        row_done.append(v2)

        def fbc(eng, v2=v2, c=c):
            eng.wait_ge(s_row.h, v2)
            if c > 0:
                eng.wait_ge(s_xn.h, xn_done[c - 1])
            eng.matmul(pb1[:, :], onesf[:], rrstd, start=True, stop=True)
            m2 = eng.matmul(pb2[:, :], onesf[:], rmr, start=True, stop=True)
            m2.then_inc(s_bc.h, 1)
        kb.on("tensor", fbc)
        s_bc.inc(1)
        bcv = s_bc.n

        for t in range(pt):
            def fxn(eng, t=t, bcv=bcv, xsv=xsv, xnv=xnv, xnfv=xnfv, rv=rv,
                    resv=resv, need_out=need_out, slot=slot):
                if t == 0:
                    eng.wait_ge(s_bc.h, bcv)
                    if need_out:
                        eng.wait_ge(s_out.h, need_out)
                    if rv is not None:
                        eng.wait_ge(s_res.h, resv)
                src = xsv[:, t * n:(t + 1) * n]
                tgt_b = xnv[:, t * n:(t + 1) * n]
                work = xnfv[:, t * n:(t + 1) * n] if xnfv is not None else tmpx[:]
                eng.tensor_tensor(tmpx[:], src, pb1[:, :], ALU.mult)
                if rms:
                    if gamma_sb is not None:
                        ins = eng.tensor_scalar(work, tmpx[:], gamma_sb[:, t:t + 1],
                                                None, ALU.mult)
                    else:
                        ins = eng.tensor_copy(work, tmpx[:])
                else:
                    ins = eng.tensor_tensor(work, tmpx[:], pb2[:, :], ALU.subtract)
                    if gamma_sb is not None:
                        ins = eng.tensor_scalar(work, work, gamma_sb[:, t:t + 1],
                                                beta_sb[:, t:t + 1], ALU.mult, ALU.add)
                if rv is not None:
                    ins = eng.tensor_tensor(tgt_b, work, rv[:, t * n:(t + 1) * n],
                                            ALU.add)
                elif sumb is not None:
                    ins = eng.tensor_copy(tgt_b, work)
                    ins = eng.tensor_tensor(
                        sumb[:, (slot * pt + t) * n:(slot * pt + t + 1) * n],
                        work, src, ALU.add)
                else:
                    ins = eng.tensor_copy(tgt_b, work)
                ins.then_inc(s_xn.h, 1)
            kb.on("vector", fxn)
            s_xn.inc(1)
        xn_done.append(s_xn.n)
        xnn = s_xn.n

        def fst(eng, c=c, xnv=xnv, xnfv=xnfv, xnn=xnn, slot=slot):
            eng.wait_ge(s_xn.h, xnn)
            eng.dma_start(
                out=xn_dram[:, c * nchunk:(c + 1) * nchunk].rearrange(
                    "(t p) n -> p t n", p=128),
                in_=xnv.rearrange("p (t n) -> p t n", t=pt),
            ).then_inc(s_out.h, 16)
            if xnfv is not None:
                eng.dma_start(
                    out=xn_f32_dram[:, c * nchunk:(c + 1) * nchunk].rearrange(
                        "(t p) n -> p t n", p=128),
                    in_=xnfv.rearrange("p (t n) -> p t n", t=pt),
                ).then_inc(s_out.h, 16)
            if sumb is not None:
                eng.dma_start(
                    out=sum_out_dram[:, c * nchunk:(c + 1) * nchunk].rearrange(
                        "(t p) n -> p t n", p=128),
                    in_=sumb[:, slot * pt * nchunk:(slot + 1) * pt * nchunk
                             ].rearrange("p (t n) -> p t n", t=pt),
                ).then_inc(s_out.h, 16)
        kb.on("gpsimd", fst)
        s_out.inc(nst)

    outn = s_out.n

    def fdrain(eng, outn=outn):
        eng.wait_ge(s_out.h, outn)
    kb.on("gpsimd", fdrain)


def emit_proj_phase(kb, bufs, pfx, weights, x_dram, ktiles, out_dram, mtiles,
                    mode="copy_bf16", res_dram=None, w_ready=None,
                    nchunk=NCHUNK, ntok=NTOK, krows=128, bias_sb=None,
                    out2_dram=None, mul_dram=None, act_scale=1.0,
                    res_scale=1.0, res2_dram=None, mask_dram=None,
                    accum_store=False, sems=None):
    nchunks = ntok // nchunk
    n = nchunk
    xb = bufs[pfx + "x"]
    outb = bufs[pfx + "o"]
    tmp = bufs[pfx + "t"]
    resb = bufs.get(pfx + "r")
    pss = bufs[pfx + "ps"]
    nbank = len(pss)
    if sems is None:
        sems = [kb.sem() for _ in range(7)]
    s_ld, s_mm, s_ev, s_st, s_res, s_res2, s_msk = sems
    b_st = s_st.n
    nsets = len(weights)
    bank_free_at = [0] * nbank
    mm_idx = 0
    ev_idx = 0
    ev_ends = []
    mm_ends = []

    for c in range(nchunks):
        slot = c % 2
        xv = xb[:, slot * ktiles * n:(slot + 1) * ktiles * n]
        ov = outb[:, slot * mtiles * n:(slot + 1) * mtiles * n]
        rv = resb[:, slot * mtiles * n:(slot + 1) * mtiles * n] if resb is not None else None

        nst_per = 16 * (2 if out2_dram is not None else 1)
        need_st = b_st + nst_per * (c - 1) if c >= 2 else 0

        def flx(eng, c=c, xv=xv,
                need_mm=(mm_ends[c - 2] if c >= 2 else 0)):
            if need_mm:
                eng.wait_ge(s_mm.h, need_mm)
            if krows == 128:
                eng.dma_start(
                    out=xv.rearrange("p (t n) -> p t n", t=ktiles),
                    in_=x_dram[:, c * n:(c + 1) * n].rearrange("(t p) n -> p t n", p=128),
                ).then_inc(s_ld.h, 16)
            else:
                eng.dma_start(out=xv[0:krows, 0:n],
                              in_=x_dram[0:krows, c * n:(c + 1) * n]).then_inc(s_ld.h, 16)
        kb.on("sync", flx)
        s_ld.inc(16)
        if mul_dram is not None:
            mb = bufs[pfx + "m"]
            mv = mb[:, slot * mtiles * n:(slot + 1) * mtiles * n]

            def flm(eng, c=c, mv=mv,
                    need_ev=(ev_ends[c - 2] if c >= 2 else 0)):
                if need_ev:
                    eng.wait_ge(s_ev.h, need_ev)
                eng.dma_start(
                    out=mv.rearrange("p (t n) -> p t n", t=mtiles),
                    in_=mul_dram[:, c * n:(c + 1) * n].rearrange("(t p) n -> p t n", p=128),
                ).then_inc(s_res.h, 16)
            kb.on("sync", flm)
            s_res.inc(16)
        else:
            mv = None
        if rv is not None:
            def flr(eng, c=c, rv=rv,
                    need_ev=(ev_ends[c - 2] if c >= 2 else 0)):
                if need_ev:
                    eng.wait_ge(s_ev.h, need_ev)
                eng.dma_start(
                    out=rv.rearrange("p (t n) -> p t n", t=mtiles),
                    in_=res_dram[:, c * n:(c + 1) * n].rearrange("(t p) n -> p t n", p=128),
                ).then_inc(s_res.h, 16)
            kb.on("sync", flr)
            s_res.inc(16)
        ldv = s_ld.n
        resv = s_res.n
        if res2_dram is not None:
            r2b = bufs[pfx + "r2"]
            r2v = r2b[:, slot * mtiles * n:(slot + 1) * mtiles * n]

            def flr2(eng, c=c, r2v=r2v,
                     need_ev=(ev_ends[c - 2] if c >= 2 else 0)):
                if need_ev:
                    eng.wait_ge(s_ev.h, need_ev)
                eng.dma_start(
                    out=r2v.rearrange("p (t n) -> p t n", t=mtiles),
                    in_=res2_dram[:, c * n:(c + 1) * n].rearrange("(t p) n -> p t n", p=128),
                ).then_inc(s_res2.h, 16)
            kb.on("sync", flr2)
            s_res2.inc(16)
        else:
            r2v = None
        res2v = s_res2.n
        if mask_dram is not None:
            mrow = bufs[pfx + "mr"]

            def flmk(eng, c=c, slot=slot,
                     need_mm=(mm_ends[c - 2] if c >= 2 else 0)):
                if need_mm:
                    eng.wait_ge(s_mm.h, need_mm)
                eng.dma_start(out=mrow[0:1, slot * n:(slot + 1) * n],
                              in_=mask_dram[0:1, c * n:(c + 1) * n]).then_inc(s_msk.h, 16)
            kb.on("sync", flmk)
            s_msk.inc(16)
        mskv = s_msk.n

        for mt in range(mtiles):
            banks = []
            for si in range(nsets):
                bank = (mm_idx * nsets + si) % nbank
                banks.append(bank)
            need_ev = max(bank_free_at[b] for b in banks)

            def fmm(eng, c=c, mt=mt, banks=banks, need_ev=need_ev, ldv=ldv,
                    xv=xv, mskv=mskv, slot=slot):
                if mt == 0:
                    eng.wait_ge(s_ld.h, ldv)
                    if c == 0 and w_ready is not None:
                        eng.wait_ge(w_ready[0].h, w_ready[1])
                    if mask_dram is not None:
                        eng.wait_ge(s_msk.h, mskv)
                        if c >= 1:
                            eng.wait_ge(s_ev.h, ev_ends[c - 1])
                        mrow = bufs[pfx + "mr"]
                        eng.matmul(bufs[pfx + "psm"][:, :], kb.onesf[:],
                                   mrow[0:1, slot * n:(slot + 1) * n],
                                   start=True, stop=True)
                if need_ev:
                    eng.wait_ge(s_ev.h, need_ev)
                for si in range(nsets):
                    w = weights[si]
                    ps = pss[banks[si]]
                    for kt in range(ktiles):
                        kr = krows if kt == ktiles - 1 else 128
                        ins = eng.matmul(
                            ps[:, :],
                            w[0:kr, (kt * mtiles + mt) * 128:(kt * mtiles + mt) * 128 + 128],
                            xv[0:kr, kt * n:(kt + 1) * n],
                            start=(kt == 0), stop=(kt == ktiles - 1))
                    ins.then_inc(s_mm.h, 1)
            kb.on("tensor", fmm)
            s_mm.inc(nsets)
            mmv = s_mm.n
            mm_idx += 1

            otile = ov[:, mt * n:(mt + 1) * n]
            if mode == "silu_mul":
                tslot = mm_idx % 2
                tv = tmp[:, tslot * n:(tslot + 1) * n]

                def fev1(eng, mmv=mmv, b0=banks[0], tv=tv):
                    eng.wait_ge(s_mm.h, mmv)
                    eng.activation(tv, pss[b0][:, :], AF.Silu,
                                   scale=float(act_scale)).then_inc(s_ev.h, 1)
                kb.on("scalar", fev1)
                s_ev.inc(1)
                bank_free_at[banks[0]] = s_ev.n
                ev1 = s_ev.n

                def fev2(eng, ev1=ev1, b1=banks[1], tv=tv, otile=otile,
                         mt=mt, need_st=need_st):
                    eng.wait_ge(s_ev.h, ev1)
                    if mt == 0 and need_st:
                        eng.wait_ge(s_st.h, need_st)
                    eng.tensor_tensor(otile, tv, pss[b1][:, :], ALU.mult).then_inc(s_ev.h, 1)
                kb.on("vector", fev2)
                s_ev.inc(1)
                bank_free_at[banks[1]] = s_ev.n
            elif mode == "residual2":
                o2b2 = bufs[pfx + "o2"]
                o2t = o2b2[:, slot * mtiles * n:(slot + 1) * mtiles * n][:, mt * n:(mt + 1) * n]

                def fev(eng, mmv=mmv, b0=banks[0], otile=otile, rv=rv, mt=mt,
                        resv=resv, r2v=r2v, res2v=res2v, o2t=o2t, need_st=need_st):
                    eng.wait_ge(s_mm.h, mmv)
                    eng.wait_ge(s_res.h, resv)
                    eng.wait_ge(s_res2.h, res2v)
                    if mt == 0 and need_st:
                        eng.wait_ge(s_st.h, need_st)
                    eng.tensor_tensor(otile, rv[:, mt * n:(mt + 1) * n],
                                      pss[b0][:, :], ALU.add)
                    eng.tensor_tensor(o2t, r2v[:, mt * n:(mt + 1) * n],
                                      pss[b0][:, :], ALU.add).then_inc(s_ev.h, 1)
                kb.on("vector", fev)
                s_ev.inc(1)
                bank_free_at[banks[0]] = s_ev.n
            elif mode == "relu_bias":
                def fev(eng, mmv=mmv, b0=banks[0], otile=otile, mt=mt,
                        need_st=need_st):
                    eng.wait_ge(s_mm.h, mmv)
                    if mt == 0 and need_st:
                        eng.wait_ge(s_st.h, need_st)
                    eng.activation(otile, pss[b0][:, :], AF.Relu,
                                   bias=bias_sb[:, mt:mt + 1]).then_inc(s_ev.h, 1)
                kb.on("scalar", fev)
                s_ev.inc(1)
                bank_free_at[banks[0]] = s_ev.n
            elif mode == "moe_accum":
                fvx = kb.fsem["vector"]
                fz1 = fvx.inc()

                def fev(eng, mmv=mmv, b0=banks[0], otile=otile, mt=mt,
                        need_st=need_st, fz1=fz1):
                    eng.wait_ge(s_mm.h, mmv)
                    if mt == 0 and need_st:
                        eng.wait_ge(s_st.h, need_st)
                    eng.tensor_scalar(otile, pss[b0][:, :], bias_sb[:, mt:mt + 1],
                                      None, ALU.add).then_inc(fvx.h, 1)
                    eng.wait_ge(fvx.h, fz1)
                    eng.tensor_tensor(otile, otile, bufs[pfx + "psm"][:, :],
                                      ALU.mult).then_inc(s_ev.h, 1)
                kb.on("vector", fev)
                s_ev.inc(1)
                bank_free_at[banks[0]] = s_ev.n
            elif mode == "residual":
                def fev(eng, mmv=mmv, b0=banks[0], otile=otile, rv=rv, mt=mt,
                        resv=resv, need_st=need_st):
                    eng.wait_ge(s_mm.h, mmv)
                    eng.wait_ge(s_res.h, resv)
                    if mt == 0 and need_st:
                        eng.wait_ge(s_st.h, need_st)
                    if res_scale != 1.0:
                        eng.scalar_tensor_tensor(otile, pss[b0][:, :],
                                                 float(res_scale),
                                                 rv[:, mt * n:(mt + 1) * n],
                                                 ALU.mult, ALU.add).then_inc(s_ev.h, 1)
                    else:
                        eng.tensor_tensor(otile, rv[:, mt * n:(mt + 1) * n],
                                          pss[b0][:, :], ALU.add).then_inc(s_ev.h, 1)
                kb.on("vector", fev)
                s_ev.inc(1)
                bank_free_at[banks[0]] = s_ev.n
            elif mode == "softplus_bias":
                ltmp = bufs[pfx + "l"]
                tslot = mm_idx % 2
                lv = ltmp[:, tslot * n:(tslot + 1) * n]
                tv = tmp[:, tslot * n:(tslot + 1) * n]
                o2b = bufs.get(pfx + "o2")
                o2tile = (o2b[:, slot * mtiles * n:(slot + 1) * mtiles * n]
                          [:, mt * n:(mt + 1) * n]) if o2b is not None else None

                def fv1(eng, mmv=mmv, b0=banks[0], otile=otile, mt=mt,
                        resv=resv, need_st=need_st):
                    eng.wait_ge(s_mm.h, mmv)
                    if mt == 0 and need_st:
                        eng.wait_ge(s_st.h, need_st)
                    eng.tensor_scalar(otile, pss[b0][:, :], bias_sb[:, mt:mt + 1],
                                      None, ALU.add).then_inc(s_ev.h, 1)
                kb.on("vector", fv1)
                s_ev.inc(1)
                bank_free_at[banks[0]] = s_ev.n
                v1 = s_ev.n
                fsc = kb.fsem["scalar"]
                fa1 = fsc.inc(); fa2 = fsc.inc()

                def fa(eng, v1=v1, otile=otile, lv=lv, fa1=fa1, fa2=fa2):
                    eng.wait_ge(s_ev.h, v1)
                    eng.activation(lv, otile, AF.Abs).then_inc(fsc.h, 1)
                    eng.wait_ge(fsc.h, fa1)
                    eng.activation(lv, lv, AF.Exp, scale=-1.0).then_inc(fsc.h, 1)
                    eng.wait_ge(fsc.h, fa2)
                    eng.activation(lv, lv, AF.Ln, bias=1.0).then_inc(s_ev.h, 1)
                kb.on("scalar", fa)
                s_ev.inc(1)
                a1 = s_ev.n
                fvc = kb.fsem["vector"]
                fb1 = fvc.inc(); fb2 = fvc.inc()

                def fv2(eng, a1=a1, otile=otile, lv=lv, tv=tv, o2tile=o2tile,
                        mv=mv, mt=mt, resv=resv, fb1=fb1, fb2=fb2):
                    eng.wait_ge(s_ev.h, a1)
                    if o2tile is not None and mt == 0:
                        eng.wait_ge(s_res.h, resv)
                    eng.tensor_scalar_max(tv, otile, 0.0).then_inc(fvc.h, 1)
                    eng.wait_ge(fvc.h, fb1)
                    ins = eng.tensor_tensor(otile, tv, lv, ALU.add)
                    if o2tile is not None:
                        ins.then_inc(fvc.h, 1)
                        eng.wait_ge(fvc.h, fb2)
                        ins = eng.tensor_tensor(o2tile, otile,
                                                mv[:, mt * n:(mt + 1) * n], ALU.mult)
                    else:
                        fvc.n -= 0
                    ins.then_inc(s_ev.h, 1)
                kb.on("vector", fv2)
                s_ev.inc(1)
            else:
                func = {"copy_bf16": AF.Copy, "silu_bf16": AF.Silu,
                        "copy_f32": AF.Copy,
                        "sigmoid_bf16": AF.Sigmoid}[mode]

                def fev(eng, mmv=mmv, b0=banks[0], otile=otile, func=func,
                        mt=mt, need_st=need_st):
                    eng.wait_ge(s_mm.h, mmv)
                    if mt == 0 and need_st:
                        eng.wait_ge(s_st.h, need_st)
                    eng.activation(otile, pss[b0][:, :], func).then_inc(s_ev.h, 1)
                kb.on("scalar", fev)
                s_ev.inc(1)
                bank_free_at[banks[0]] = s_ev.n
        evv = s_ev.n
        ev_ends.append(evv)
        mm_ends.append(s_mm.n)

        o2b = bufs.get(pfx + "o2")
        o2v = (o2b[:, slot * mtiles * n:(slot + 1) * mtiles * n]
               if o2b is not None else None)

        def fst(eng, c=c, ov=ov, evv=evv, o2v=o2v):
            eng.wait_ge(s_ev.h, evv)
            if accum_store:
                eng.dma_start(
                    out=out_dram[:, c * n:(c + 1) * n].rearrange("(t p) n -> p t n", p=128),
                    in_=ov.rearrange("p (t n) -> p t n", t=mtiles),
                    accum_op=ALU.add,
                ).then_inc(s_st.h, 16)
            else:
                eng.dma_start(
                    out=out_dram[:, c * n:(c + 1) * n].rearrange("(t p) n -> p t n", p=128),
                    in_=ov.rearrange("p (t n) -> p t n", t=mtiles),
                ).then_inc(s_st.h, 16)
            if o2v is not None:
                eng.dma_start(
                    out=out2_dram[:, c * n:(c + 1) * n].rearrange("(t p) n -> p t n", p=128),
                    in_=o2v.rearrange("p (t n) -> p t n", t=mtiles),
                ).then_inc(s_st.h, 16)
        kb.on("gpsimd", fst)
        s_st.inc(16 if out2_dram is None else 32)

    stn = s_st.n

    def fdrain(eng, stn=stn):
        eng.wait_ge(s_st.h, stn)
    kb.on("gpsimd", fdrain)


DEQ = 1.0 / 127.0


def make_ln_bufs(kb, sc, dim, with_f32=False, nchunk=NCHUNK, pfx="ln",
                 out_dtype=BF16, with_res=False, with_sum=False):
    pt = dim // 128
    b = {
        "lnx": kb.sb(pfx + "x", [128, 2 * pt * nchunk], F32, sc),
        "lnsq": kb.sb(pfx + "sq", [128, 2 * nchunk], F32, sc),
        "lntmp": kb.sb(pfx + "tmp", [128, nchunk], F32, sc),
        "lnxn": kb.sb(pfx + "xn", [128, 2 * pt * nchunk], out_dtype, sc),
        "lnrows": kb.sb(pfx + "rows", [1, 4 * nchunk], F32, sc),
        "onesf": kb.onesf, "onescol": kb.onescol,
        "ps_stat": kb.psum(pfx + "ps_stat", [128, 2 * nchunk], F32, sc),
        "ps_b1": kb.psum(pfx + "ps_b1", [128, nchunk], F32, sc),
        "ps_b2": kb.psum(pfx + "ps_b2", [128, nchunk], F32, sc),
    }
    if with_f32:
        b["lnxnf"] = kb.sb(pfx + "xnf", [128, 2 * pt * nchunk], F32, sc)
    if with_res:
        b["lnres"] = kb.sb(pfx + "res", [128, 2 * pt * nchunk], F32, sc)
    if with_sum:
        b["lnsum"] = kb.sb(pfx + "sum", [128, 2 * pt * nchunk], F32, sc)
    return b


def emit_block_l0ffn(kb, x_in, x_out, wgT, wuT, wdT, xn_bf, h_bf):
    nc = kb.nc
    sc1 = kb.scope()
    wg_sb = kb.sb("l0f_wg", [128, 8 * 24 * 128], BF16, sc1)
    wu_sb = kb.sb("l0f_wu", [128, 8 * 24 * 128], BF16, sc1)
    wstage = kb.sb("l0f_wst", [128, 8 * 128], I32, sc1)
    s_wld = kb.sem(); s_wcvt = kb.sem()
    emit_weight_load_g(kb, wgT, wg_sb, 8, 24, wstage, s_wld, s_wcvt)
    emit_weight_load_g(kb, wuT, wu_sb, 8, 24, wstage, s_wld, s_wcvt)
    w_ready = (s_wcvt, s_wcvt.n)

    sc2 = kb.scope()
    lnb = make_ln_bufs(kb, sc2, D, pfx="l0f_ln")
    emit_ln_phase(kb, lnb, x_in, xn_bf, D, eps=1e-5)
    sc2.close()
    kb.barrier()

    sc3 = kb.scope()
    gub = {
        "gux": kb.sb("l0f_gx", [128, 2 * 8 * NCHUNK], BF16, sc3),
        "guo": kb.sb("l0f_go", [128, 2 * 24 * NCHUNK], BF16, sc3),
        "gut": kb.sb("l0f_gt", [128, 2 * NCHUNK], F32, sc3),
        "gups": [kb.psum(f"l0f_gps{i}", [128, NCHUNK], F32, sc3) for i in range(4)],
    }
    emit_proj_phase(kb, gub, "gu", [wg_sb, wu_sb], xn_bf, 8, h_bf, 24,
                    mode="silu_mul", w_ready=w_ready, act_scale=DEQ)
    sc3.close()
    sc1.close()
    kb.barrier()

    sc4 = kb.scope()
    wd_sb = kb.sb("l0f_wd", [128, 24 * 8 * 128], BF16, sc4)
    wstage2 = kb.sb("l0f_wst2", [128, 8 * 128], I32, sc4)
    s_wld2 = kb.sem(); s_wcvt2 = kb.sem()
    emit_weight_load_g(kb, wdT, wd_sb, 24, 8, wstage2, s_wld2, s_wcvt2)
    dnb = {
        "dnx": kb.sb("l0f_dx", [128, 2 * 24 * NCHUNK], BF16, sc4),
        "dno": kb.sb("l0f_do", [128, 2 * 8 * NCHUNK], F32, sc4),
        "dnt": kb.sb("l0f_dt", [128, 2 * NCHUNK], F32, sc4),
        "dnr": kb.sb("l0f_dr", [128, 2 * 8 * NCHUNK], F32, sc4),
        "dnps": [kb.psum(f"l0f_dps{i}", [128, NCHUNK], F32, sc4) for i in range(2)],
    }
    emit_proj_phase(kb, dnb, "dn", [wd_sb], h_bf, 24, x_out, 8,
                    mode="residual", res_dram=x_in, w_ready=(s_wcvt2, s_wcvt2.n),
                    res_scale=DEQ * DEQ)
    sc4.close()
    kb.barrier()


def emit_weight_load_g(kb, w_dram, wq_sb, ktiles, mtiles, stage_sb, sem_ld,
                       sem_cvt, scale=None, krows=128):
    assert scale is None
    for kt in range(ktiles):
        kr = krows if kt == ktiles - 1 else 128

        def fl(eng, kt=kt, kr=kr):
            eng.dma_start(
                out=wq_sb[0:kr, kt * mtiles * 128:(kt + 1) * mtiles * 128],
                in_=w_dram[kt * 128:kt * 128 + kr, 0:mtiles * 128],
            ).then_inc(sem_cvt.h, 16)
        kb.on("gpsimd", fl)
        sem_cvt.inc(16)


def emit_l0_tokproj(kb, bufs, xn_bf, wv_sb, wk_sb, wb_sb, v_tm, k_tm, beta_tm,
                    w_ready):
    ntiles = NTOK // 128
    xb = bufs["tpx"]
    vo = bufs["tpv"]
    ko = bufs["tpk"]
    kf = bufs["tpkf"]
    ksq = bufs["tpksq"]
    bo = bufs["tpb"]
    rn = bufs["tprn"]
    psv = bufs["tp_psv"]
    psk = bufs["tp_psk"]
    psb = bufs["tp_psb"]
    s_ld = kb.sem(); s_mm = kb.sem(); s_ev = kb.sem(); s_st = kb.sem()
    b_st = s_st.n; b_ev = s_ev.n

    for tt in range(ntiles):
        slot = tt % 2

        def flx(eng, tt=tt, slot=slot):
            if tt >= 2:
                eng.wait_ge(s_st.h, b_st + 48 * (tt - 1))
            eng.dma_start(
                out=xb[:, slot * 1024:(slot + 1) * 1024].rearrange(
                    "p (k c) -> p k c", k=8),
                in_=xn_bf[:, tt * 128:(tt + 1) * 128].rearrange(
                    "(k p) c -> p k c", p=128),
            ).then_inc(s_ld.h, 16)
        kb.on("sync", flx)
        s_ld.inc(16)
        ldv = s_ld.n

        def fmm(eng, tt=tt, slot=slot, ldv=ldv):
            eng.wait_ge(s_ld.h, ldv)
            if tt == 0:
                eng.wait_ge(w_ready[0].h, w_ready[1])
            else:
                eng.wait_ge(s_ev.h, b_ev + 4 * (tt - 1) + 1)
            for kt in range(8):
                st = xb[:, slot * 1024 + kt * 128: slot * 1024 + (kt + 1) * 128]
                b0 = (kt == 0); b1 = (kt == 7)
                eng.matmul(psv[:, 0:512], st, wv_sb[:, kt * 1024:kt * 1024 + 512],
                           start=b0, stop=b1)
                eng.matmul(psv[:, 512:1024], st, wv_sb[:, kt * 1024 + 512:(kt + 1) * 1024],
                           start=b0, stop=b1)
                eng.matmul(psk[:, 0:512], st, wk_sb[:, kt * 1024:kt * 1024 + 512],
                           start=b0, stop=b1)
                eng.matmul(psk[:, 512:1024], st, wk_sb[:, kt * 1024 + 512:(kt + 1) * 1024],
                           start=b0, stop=b1)
                ins = eng.matmul(psb[:, 0:8], st, wb_sb[:, kt * 8:(kt + 1) * 8],
                                 start=b0, stop=b1)
            ins.then_inc(s_mm.h, 1)
        kb.on("tensor", fmm)
        s_mm.inc(1)
        mmv = s_mm.n

        vv = vo[:, slot * 1024:(slot + 1) * 1024]
        kv = ko[:, slot * 1024:(slot + 1) * 1024]
        bv = bo[:, slot * 8:(slot + 1) * 8]
        kfv = kf[:, slot * 1024:(slot + 1) * 1024]
        ksqv = ksq[:, slot * 1024:(slot + 1) * 1024]
        r0 = rn[:, slot * 24:slot * 24 + 8]
        r1 = rn[:, slot * 24 + 8:slot * 24 + 16]
        r2 = rn[:, slot * 24 + 16:slot * 24 + 24]

        def fsc(eng, mmv=mmv, vv=vv, bv=bv, kfv=kfv, ksqv=ksqv, tt=tt):
            eng.wait_ge(s_mm.h, mmv)
            if tt >= 2:
                eng.wait_ge(s_ev.h, b_ev + 4 * (tt - 1))
                eng.wait_ge(s_st.h, b_st + 48 * (tt - 1))
            eng.activation(vv[:, 0:512], psv[:, 0:512], AF.Silu)
            eng.activation(vv[:, 512:1024], psv[:, 512:1024], AF.Silu)
            eng.activation(bv, psb[:, 0:8], AF.Sigmoid)
            eng.copy(kfv, psk[:, :])
            eng.activation(ksqv, psk[:, :], AF.Square).then_inc(s_ev.h, 1)
        kb.on("scalar", fsc)
        s_ev.inc(1)
        evv = s_ev.n

        def fve(eng, evv=evv, ksqv=ksqv, r0=r0):
            eng.wait_ge(s_ev.h, evv)
            eng.tensor_reduce(r0, ksqv.rearrange("p (h c) -> p h c", h=8),
                              AX.X, ALU.add).then_inc(s_ev.h, 1)
        kb.on("vector", fve)
        s_ev.inc(1)
        ev2 = s_ev.n

        def fsc2(eng, ev2=ev2, r0=r0, r1=r1):
            eng.wait_ge(s_ev.h, ev2)
            eng.activation(r1, r0, AF.Sqrt).then_inc(s_ev.h, 1)
        kb.on("scalar", fsc2)
        s_ev.inc(1)
        ev3 = s_ev.n

        fs = kb.fsem["vector"]
        f1 = fs.inc(); f2 = fs.inc()

        def fve2(eng, ev3=ev3, kv=kv, kfv=kfv, r1=r1, r2=r2, f1=f1, f2=f2):
            eng.wait_ge(s_ev.h, ev3)
            eng.tensor_scalar_max(r1, r1, 1e-12).then_inc(fs.h, 1)
            eng.wait_ge(fs.h, f1)
            eng.reciprocal(r2, r1).then_inc(fs.h, 1)
            eng.wait_ge(fs.h, f2)
            for h in range(8):
                ins = eng.tensor_scalar(kv[:, h * 128:(h + 1) * 128],
                                        kfv[:, h * 128:(h + 1) * 128],
                                        r2[:, h:h + 1], None, ALU.mult)
            ins.then_inc(s_ev.h, 1)
        kb.on("vector", fve2)
        s_ev.inc(1)
        ev4 = s_ev.n

        def fst(eng, tt=tt, vv=vv, kv=kv, bv=bv, ev4=ev4):
            eng.wait_ge(s_ev.h, ev4)
            eng.dma_start(out=v_tm[tt * 128:(tt + 1) * 128, :], in_=vv).then_inc(s_st.h, 16)
            eng.dma_start(out=k_tm[tt * 128:(tt + 1) * 128, :], in_=kv).then_inc(s_st.h, 16)
            eng.dma_start(out=beta_tm[tt * 128:(tt + 1) * 128, :], in_=bv).then_inc(s_st.h, 16)
        kb.on("gpsimd", fst)
        s_st.inc(48)

    stn = s_st.n

    def fdrain(eng, stn=stn):
        eng.wait_ge(s_st.h, stn)
    kb.on("gpsimd", fdrain)

def emit_l0_scan(kb, bufs, q_dm, k_tm, v_tm, beta_tm, o_dm):
    H = 8
    W = 1024
    qt = bufs["sqt"]; ktm = bufs["sktm"]; vtm = bufs["svtm"]; bt = bufs["sbt"]
    KTD = bufs["sKTD"]; Bw = bufs["sB"]; BTw = bufs["sBT"]
    Pb = bufs["sP"]; Qb = bufs["sQ"]; MTb = bufs["sMT"]
    RHS = bufs["sRHS"]; Uw = bufs["sU"]; LTw = bufs["sLT"]
    o1w = bufs["so1"]; Bf = bufs["sBf"]
    Sf = bufs["sSf"]; Sb = bufs["sSb"]; Ow = bufs["sO"]
    idb8 = bufs["idb8"]; mneg8 = bufs["mneg8"]; mu8 = bufs["mu8"]; idb = bufs["idb"]
    pT = bufs["s_pT"]; pA = bufs["s_pA"]; pB = bufs["s_pB"]; pC = bufs["s_pC"]
    s_ld = kb.sem(); s_mm = kb.sem(); s_dn = kb.sem(); s_st = kb.sem()
    b_st = s_st.n

    NCH = T_SEQ // 128

    def mmstage(emit_mms, wait_dn):
        def f(eng, wait_dn=wait_dn, emit_mms=emit_mms):
            if wait_dn:
                eng.wait_ge(s_dn.h, wait_dn)
            ins = emit_mms(eng)
            ins.then_inc(s_mm.h, 1)
        kb.on("tensor", f)
        s_mm.inc(1)
        return s_mm.n

    def evstage(engine, emit_ev, wait_mm, extra_dn=0):
        def f(eng, wait_mm=wait_mm, emit_ev=emit_ev, extra_dn=extra_dn):
            if wait_mm:
                eng.wait_ge(s_mm.h, wait_mm)
            if extra_dn:
                eng.wait_ge(s_dn.h, extra_dn)
            ins = emit_ev(eng)
            ins.then_inc(s_dn.h, 1)
        kb.on(engine, f)
        s_dn.inc(1)
        return s_dn.n

    chunk_done = []
    for b in range(B_LOC):
        def fz(eng, b=b):
            eng.memset(Sf[:], 0.0)
            eng.memset(Sb[:], 0.0).then_inc(s_dn.h, 1)
        kb.on("vector", fz)
        s_dn.inc(1)
        s_after_reset = s_dn.n

        for ch in range(NCH):
            ci = b * NCH + ch
            slot = ci % 2
            col0 = b * T_SEQ + ch * 128
            prev_done = s_dn.n

            guard = chunk_done[ci - 2] if ci >= 2 else 0

            def fld(eng, ci=ci, slot=slot, col0=col0, guard=guard):
                if guard:
                    eng.wait_ge(s_dn.h, guard)
                    eng.wait_ge(s_st.h, b_st + 16 * (ci - 1))
                eng.dma_start(out=qt[:, slot * W:(slot + 1) * W].rearrange(
                                  "p (h c) -> p h c", h=H),
                              in_=q_dm[:, col0:col0 + 128].rearrange(
                                  "(h p) c -> p h c", p=128)).then_inc(s_ld.h, 16)
                r0 = col0
                eng.dma_start(out=ktm[:, slot * W:(slot + 1) * W],
                              in_=k_tm[r0:r0 + 128, :]).then_inc(s_ld.h, 16)
                eng.dma_start(out=vtm[:, slot * W:(slot + 1) * W],
                              in_=v_tm[r0:r0 + 128, :]).then_inc(s_ld.h, 16)
                eng.dma_start(out=bt[:, slot * 8:(slot + 1) * 8],
                              in_=beta_tm[r0:r0 + 128, :]).then_inc(s_ld.h, 16)
            kb.on("sync", fld)
            s_ld.inc(64)
            ldv = s_ld.n
            qv = qt[:, slot * W:(slot + 1) * W]
            kv = ktm[:, slot * W:(slot + 1) * W]
            vv = vtm[:, slot * W:(slot + 1) * W]
            bv = bt[:, slot * 8:(slot + 1) * 8]

            def mm1(eng, kv=kv, ldv=ldv):
                eng.wait_ge(s_ld.h, ldv)
                for h in range(H):
                    ins = eng.transpose(pT[:, h * 128:(h + 1) * 128],
                                        kv[:, h * 128:(h + 1) * 128], idb[:])
                return ins
            m = mmstage(mm1, prev_done)
            d_ktd = evstage("scalar", lambda eng: eng.copy(KTD[:], pT[:, :]), m)

            def mm2(eng):
                for h in range(H):
                    ins = eng.matmul(pA[:, h * 128:(h + 1) * 128],
                                     KTD[:, h * 128:(h + 1) * 128],
                                     KTD[:, h * 128:(h + 1) * 128],
                                     start=True, stop=True)
                return ins
            m = mmstage(mm2, d_ktd)

            fs = kb.fsem["vector"]
            fB = fs.inc()

            def evB(eng, bv=bv, fB=fB):
                for h in range(H):
                    ins = eng.tensor_scalar(Bf[:, h * 128:(h + 1) * 128],
                                            pA[:, h * 128:(h + 1) * 128],
                                            bv[:, h:h + 1], None, ALU.mult)
                ins.then_inc(fs.h, 1)
                eng.wait_ge(fs.h, fB)
                return eng.tensor_tensor(Bw[:], Bf[:], mneg8[:], ALU.mult)
            d_B = evstage("vector", evB, m)

            def mm3(eng):
                for h in range(H):
                    ins = eng.transpose(pT[:, h * 128:(h + 1) * 128],
                                        Bw[:, h * 128:(h + 1) * 128], idb[:])
                return ins
            m = mmstage(mm3, d_B)
            d_bt = evstage("scalar", lambda eng: eng.copy(BTw[:], pT[:, :]), m)
            d_mt = evstage("vector",
                           lambda eng: eng.tensor_tensor(MTb[0][:], idb8[:], BTw[:], ALU.add),
                           0, extra_dn=d_bt)

            Pcur, Qcur = Bw, BTw
            d_p, d_q = d_B, d_bt
            mtc = 0
            for j in range(1, 7):
                Pnew = Pb[j % 2]; Qnew = Qb[j % 2]

                def mmP(eng, Pcur=Pcur, Qcur=Qcur):
                    for h in range(H):
                        ins = eng.matmul(pB[:, h * 128:(h + 1) * 128],
                                         Qcur[:, h * 128:(h + 1) * 128],
                                         Pcur[:, h * 128:(h + 1) * 128],
                                         start=True, stop=True)
                    return ins
                m = mmstage(mmP, max(d_p, d_q))
                d_pn = evstage("scalar", lambda eng, Pnew=Pnew: eng.copy(Pnew[:], pB[:, :]), m)
                if j < 6:
                    def mmQ(eng, Pcur=Pcur, Qcur=Qcur):
                        for h in range(H):
                            ins = eng.matmul(pC[:, h * 128:(h + 1) * 128],
                                             Pcur[:, h * 128:(h + 1) * 128],
                                             Qcur[:, h * 128:(h + 1) * 128],
                                             start=True, stop=True)
                        return ins
                    m = mmstage(mmQ, max(d_p, d_q))
                    d_qn = evstage("scalar", lambda eng, Qnew=Qnew: eng.copy(Qnew[:], pC[:, :]), m)
                else:
                    d_qn = d_q

                MTold = MTb[mtc]; MTnew = MTb[1 - mtc]; mtc = 1 - mtc

                def mmM(eng, Pnew=Pnew, MTold=MTold):
                    for h in range(H):
                        ins = eng.matmul(pA[:, h * 128:(h + 1) * 128],
                                         Pnew[:, h * 128:(h + 1) * 128],
                                         MTold[:, h * 128:(h + 1) * 128],
                                         start=True, stop=True)
                    return ins
                m = mmstage(mmM, max(d_pn, d_mt))
                d_mt = evstage("vector",
                               lambda eng, MTnew=MTnew, MTold=MTold:
                               eng.tensor_tensor(MTnew[:], MTold[:], pA[:, :], ALU.add), m)
                Pcur, Qcur = Pnew, Qnew
                d_p, d_q = d_pn, d_qn

            MT = MTb[mtc]

            def mmKS(eng):
                for h in range(H):
                    ins = eng.matmul(pB[:, h * 128:(h + 1) * 128],
                                     KTD[:, h * 128:(h + 1) * 128],
                                     Sb[:, h * 128:(h + 1) * 128],
                                     start=True, stop=True)
                return ins
            state_dep = s_after_reset if ch == 0 else d_state
            m = mmstage(mmKS, max(state_dep, d_p))

            fR = kb.fsem["vector"].inc()

            def evRHS(eng, bv=bv, vv=vv, fR=fR):
                for h in range(H):
                    ins = eng.tensor_scalar(o1w[:, h * 128:(h + 1) * 128],
                                            pB[:, h * 128:(h + 1) * 128],
                                            bv[:, h:h + 1], None, ALU.mult)
                ins.then_inc(kb.fsem["vector"].h, 1)
                eng.wait_ge(kb.fsem["vector"].h, fR)
                for h in range(H):
                    ins = eng.scalar_tensor_tensor(
                        RHS[:, h * 128:(h + 1) * 128],
                        vv[:, h * 128:(h + 1) * 128],
                        bv[:, h:h + 1],
                        o1w[:, h * 128:(h + 1) * 128],
                        ALU.mult, ALU.subtract)
                return ins
            d_rhs = evstage("vector", evRHS, m)

            def mmU(eng, MT=MT):
                for h in range(H):
                    ins = eng.matmul(pC[:, h * 128:(h + 1) * 128],
                                     MT[:, h * 128:(h + 1) * 128],
                                     RHS[:, h * 128:(h + 1) * 128],
                                     start=True, stop=True)
                return ins
            m = mmstage(mmU, max(d_rhs, d_mt))
            d_u = evstage("scalar", lambda eng: eng.copy(Uw[:], pC[:, :]), m)

            def mmLT(eng, qv=qv):
                for h in range(H):
                    ins = eng.matmul(pA[:, h * 128:(h + 1) * 128],
                                     KTD[:, h * 128:(h + 1) * 128],
                                     qv[:, h * 128:(h + 1) * 128],
                                     start=True, stop=True)
                return ins
            m = mmstage(mmLT, d_mt)
            d_lt = evstage("vector",
                           lambda eng: eng.tensor_tensor(LTw[:], pA[:, :], mu8[:], ALU.mult), m)

            def mmO(eng, qv=qv):
                for h in range(H):
                    eng.matmul(pB[:, h * 128:(h + 1) * 128],
                               Sb[:, h * 128:(h + 1) * 128],
                               qv[:, h * 128:(h + 1) * 128],
                               start=True, stop=False)
                    ins = eng.matmul(pB[:, h * 128:(h + 1) * 128],
                                     Uw[:, h * 128:(h + 1) * 128],
                                     LTw[:, h * 128:(h + 1) * 128],
                                     start=False, stop=True)
                return ins
            m = mmstage(mmO, max(d_u, d_lt, d_rhs))
            oslot = ci % 2
            ov = Ow[:, oslot * W:(oslot + 1) * W]

            def evO(eng, ov=ov, ci=ci):
                if ci >= 2:
                    eng.wait_ge(s_st.h, b_st + 16 * (ci - 1))
                return eng.copy(ov, pB[:, :])
            d_o = evstage("scalar", evO, m)

            def fso(eng, ov=ov, col0=col0, d_o=d_o):
                eng.wait_ge(s_dn.h, d_o)
                eng.dma_start(out=o_dm[:, col0:col0 + 128].rearrange(
                                  "(h p) c -> p h c", p=128),
                              in_=ov.rearrange("p (h c) -> p h c", h=H)).then_inc(s_st.h, 16)
            kb.on("gpsimd", fso)
            s_st.inc(16)

            def mmS(eng, kv=kv):
                for h in range(H):
                    ins = eng.matmul(pC[:, h * 128:(h + 1) * 128],
                                     kv[:, h * 128:(h + 1) * 128],
                                     Uw[:, h * 128:(h + 1) * 128],
                                     start=True, stop=True)
                return ins
            m = mmstage(mmS, max(d_u, d_o))

            fS = kb.fsem["vector"].inc()

            def evS(eng, fS=fS):
                eng.tensor_tensor(Sf[:], Sf[:], pC[:, :], ALU.add).then_inc(
                    kb.fsem["vector"].h, 1)
                eng.wait_ge(kb.fsem["vector"].h, fS)
                return eng.tensor_copy(Sb[:], Sf[:])
            d_state = evstage("vector", evS, m)
            chunk_done.append(d_state)

    stn = s_st.n

    def fdrain(eng, stn=stn):
        eng.wait_ge(s_st.h, stn)
    kb.on("gpsimd", fdrain)

def emit_l0_wload_tok(kb, sc, WvD, WkD, WbD):
    wv_sb = kb.sb("l0a_wv", [128, 8 * 1024], BF16, sc)
    wk_sb = kb.sb("l0a_wk", [128, 8 * 1024], BF16, sc)
    wb_sb = kb.sb("l0a_wb", [128, 8 * 8], BF16, sc)
    stage = kb.sb("l0a_wst", [128, 2 * 1024], F32, sc)
    s_ld = kb.sem(); s_cvt = kb.sem()
    items = [(WvD, wv_sb, 1024), (WkD, wk_sb, 1024), (WbD, wb_sb, 8)]
    for (wd, wsb, ncol) in items:
        for kt in range(8):
            def fl(eng, wd=wd, wsb=wsb, kt=kt, ncol=ncol):
                eng.dma_start(out=wsb[:, kt * ncol:(kt + 1) * ncol],
                              in_=wd[kt * 128:(kt + 1) * 128, :]).then_inc(s_cvt.h, 16)
            kb.on("gpsimd", fl)
            s_cvt.inc(16)
    return wv_sb, wk_sb, wb_sb, (s_cvt, s_cvt.n)


def emit_const_load(kb, sc, cdram, shapes):
    out = {}
    s_ld = kb.sem("cld")
    off = 0
    tiles = {}
    for name, cols in shapes:
        t = kb.sb("c_" + name, [128, cols], BF16, sc)
        tiles[name] = (t, off, cols)
        off += cols
    for name, (t, o, cols) in tiles.items():
        def fl(eng, t=t, o=o, cols=cols):
            eng.dma_start(out=t[:], in_=cdram[:, o:o + cols]).then_inc(s_ld.h, 16)
        kb.on("sync", fl)
        s_ld.inc(16)
        out[name] = t
    return out, (s_ld, s_ld.n)


def emit_block_l0attn(kb, x_in, x_out, WqD, WkD, WvD, WbD, lnwD, lnbD, cdram,
                      xn_bf, q_dm, k_tm, v_tm, beta_tm, o_dm):
    scC = kb.scope()
    consts, c_ready = emit_const_load(kb, scC, cdram,
                                      [("idb", 128), ("idb8", 1024),
                                       ("mneg8", 1024), ("mu8", 1024)])
    lnw_sb = kb.sb("l0a_lnw", [128, 8], F32, scC)
    lnb_sb = kb.sb("l0a_lnb", [128, 8], F32, scC)
    s_p = kb.sem("l0a_sp")

    def flp(eng, cw=c_ready):
        eng.dma_start(out=lnw_sb[:], in_=lnwD[:, :]).then_inc(s_p.h, 16)
        eng.dma_start(out=lnb_sb[:], in_=lnbD[:, :]).then_inc(s_p.h, 16)
        eng.wait_ge(s_p.h, 32)
        eng.wait_ge(cw[0].h, cw[1])
    kb.on("sync", flp)
    s_p.inc(32)

    sc1 = kb.scope()
    lnb1 = make_ln_bufs(kb, sc1, D, pfx="l0a_ln")
    emit_ln_phase(kb, lnb1, x_in, xn_bf, D, eps=1e-5)
    sc1.close()
    kb.barrier()

    sc2 = kb.scope()
    wq_sb = kb.sb("l0a_wq", [128, 8 * 8 * 128], BF16, sc2)
    wstage = kb.sb("l0a_wqst", [128, 8 * 128], F32, sc2)
    s_wld = kb.sem(); s_wcvt = kb.sem()
    emit_weight_load_g(kb, WqD, wq_sb, 8, 8, wstage, s_wld, s_wcvt)
    qb = {
        "qx": kb.sb("l0a_qx", [128, 2 * 8 * NCHUNK], BF16, sc2),
        "qo": kb.sb("l0a_qo", [128, 2 * 8 * NCHUNK], BF16, sc2),
        "qt": kb.sb("l0a_qt", [128, 2 * NCHUNK], F32, sc2),
        "qps": [kb.psum(f"l0a_qps{i}", [128, NCHUNK], F32, sc2) for i in range(2)],
    }
    emit_proj_phase(kb, qb, "q", [wq_sb], xn_bf, 8, q_dm, 8,
                    mode="copy_bf16", w_ready=(s_wcvt, s_wcvt.n))
    sc2.close()
    kb.barrier()

    sc3 = kb.scope()
    wv_sb, wk_sb, wb_sb, w_ready = emit_l0_wload_tok(kb, sc3, WvD, WkD, WbD)
    tpb = {
        "tpx": kb.sb("l0a_tpx", [128, 2 * 1024], BF16, sc3),
        "tpv": kb.sb("l0a_tpv", [128, 2 * 1024], BF16, sc3),
        "tpk": kb.sb("l0a_tpk", [128, 2 * 1024], BF16, sc3),
        "tpkf": kb.sb("l0a_tpkf", [128, 2 * 1024], F32, sc3),
        "tpksq": kb.sb("l0a_tpksq", [128, 2 * 1024], F32, sc3),
        "tpb": kb.sb("l0a_tpb", [128, 2 * 8], F32, sc3),
        "tprn": kb.sb("l0a_tprn", [128, 2 * 24], F32, sc3),
        "tp_psv": kb.psum("l0a_psv", [128, 1024], F32, sc3),
        "tp_psk": kb.psum("l0a_psk", [128, 1024], F32, sc3),
        "tp_psb": kb.psum("l0a_psb", [128, 512], F32, sc3),
    }
    emit_l0_tokproj(kb, tpb, xn_bf, wv_sb, wk_sb, wb_sb, v_tm, k_tm, beta_tm,
                    w_ready)
    sc3.close()
    kb.barrier()

    sc4 = kb.scope()
    W = 1024
    sbufs = {
        "sqt": kb.sb("l0a_sqt", [128, 2 * W], BF16, sc4),
        "sktm": kb.sb("l0a_sktm", [128, 2 * W], BF16, sc4),
        "svtm": kb.sb("l0a_svtm", [128, 2 * W], BF16, sc4),
        "sbt": kb.sb("l0a_sbt", [128, 2 * 8], F32, sc4),
        "sKTD": kb.sb("l0a_sKTD", [128, W], BF16, sc4),
        "sB": kb.sb("l0a_sB", [128, W], BF16, sc4),
        "sBT": kb.sb("l0a_sBT", [128, W], BF16, sc4),
        "sP": [kb.sb(f"l0a_sP{i}", [128, W], BF16, sc4) for i in range(2)],
        "sQ": [kb.sb(f"l0a_sQ{i}", [128, W], BF16, sc4) for i in range(2)],
        "sMT": [kb.sb(f"l0a_sMT{i}", [128, W], BF16, sc4) for i in range(2)],
        "sRHS": kb.sb("l0a_sRHS", [128, W], BF16, sc4),
        "sU": kb.sb("l0a_sU", [128, W], BF16, sc4),
        "sLT": kb.sb("l0a_sLT", [128, W], BF16, sc4),
        "so1": kb.sb("l0a_so1", [128, W], F32, sc4),
        "sBf": kb.sb("l0a_sBf", [128, W], F32, sc4),
        "sSf": kb.sb("l0a_sSf", [128, W], F32, sc4),
        "sSb": kb.sb("l0a_sSb", [128, W], BF16, sc4),
        "sO": kb.sb("l0a_sO", [128, 2 * W], F32, sc4),
        "idb8": consts["idb8"], "mneg8": consts["mneg8"],
        "mu8": consts["mu8"], "idb": consts["idb"],
        "s_pT": kb.psum("l0a_pT", [128, W], BF16, sc4),
        "s_pA": kb.psum("l0a_pA", [128, W], F32, sc4),
        "s_pB": kb.psum("l0a_pB", [128, W], F32, sc4),
        "s_pC": kb.psum("l0a_pC", [128, W], F32, sc4),
    }
    emit_l0_scan(kb, sbufs, q_dm, k_tm, v_tm, beta_tm, o_dm)
    sc4.close()
    kb.barrier()

    sc5 = kb.scope()
    lnb2 = make_ln_bufs(kb, sc5, D, pfx="l0a_oln", out_dtype=F32, with_res=True)
    emit_ln_phase(kb, lnb2, o_dm, x_out, D, gamma_sb=lnw_sb, beta_sb=lnb_sb,
                  eps=1e-5, res_dram=x_in)
    sc5.close()
    scC.close()
    kb.barrier()


def l0attn_consts():
    idb = np.eye(128, dtype=np.float32)
    ml = np.tril(np.full((128, 128), -1.0, np.float32), -1)
    mu = np.triu(np.ones((128, 128), np.float32), 0)
    c = np.concatenate([idb] + [idb] * 8 + [ml] * 8 + [mu] * 8, axis=1)
    return c.astype(ml_dtypes.bfloat16)


def emit_mamba_conv(kb, bufs, xz_dm, xc_dm, convw_sb, convb_sb, w_ready):
    xs = bufs["cvx"]
    acc = bufs["cva"]
    oc = bufs["cvo"]
    s_ld = kb.sem(); s_ev = kb.sem(); s_ac = kb.sem(); s_st = kb.sem()
    b_st = s_st.n; b_ac = s_ac.n
    T = T_SEQ
    fvc = kb.fsem["vector"]
    idx = 0
    for dt in range(16):
        for b in range(B_LOC):
            slot = idx % 2
            xv = xs[:, slot * (3 + T):(slot + 1) * (3 + T)]
            av = acc[:, slot * T:(slot + 1) * T]
            ov = oc[:, slot * T:(slot + 1) * T]

            def fl(eng, xv=xv, dt=dt, b=b, idx=idx):
                if idx >= 2:
                    eng.wait_ge(s_st.h, b_st + 16 * (idx - 1))
                eng.dma_start(out=xv[:, 3:3 + T],
                              in_=xz_dm[dt * 128:(dt + 1) * 128,
                                        b * T:(b + 1) * T]).then_inc(s_ld.h, 16)
            kb.on("sync", fl)
            s_ld.inc(16)
            ldv = s_ld.n

            f1 = fvc.inc(); f2 = fvc.inc(); f3 = fvc.inc()

            def fc(eng, xv=xv, av=av, dt=dt, ldv=ldv, f1=f1, f2=f2, f3=f3,
                   idx=idx):
                eng.memset(xv[:, 0:3], 0.0)
                eng.wait_ge(s_ld.h, ldv)
                if idx == 0:
                    eng.wait_ge(w_ready[0].h, w_ready[1])
                if idx >= 2:
                    eng.wait_ge(s_ac.h, b_ac + idx - 1)
                eng.tensor_scalar(av, xv[:, 0:T], convw_sb[:, dt * 4:dt * 4 + 1],
                                  None, ALU.mult).then_inc(fvc.h, 1)
                eng.wait_ge(fvc.h, f1)
                eng.scalar_tensor_tensor(av, xv[:, 1:1 + T],
                                         convw_sb[:, dt * 4 + 1:dt * 4 + 2], av,
                                         ALU.mult, ALU.add).then_inc(fvc.h, 1)
                eng.wait_ge(fvc.h, f2)
                eng.scalar_tensor_tensor(av, xv[:, 2:2 + T],
                                         convw_sb[:, dt * 4 + 2:dt * 4 + 3], av,
                                         ALU.mult, ALU.add).then_inc(fvc.h, 1)
                eng.wait_ge(fvc.h, f3)
                eng.scalar_tensor_tensor(av, xv[:, 3:3 + T],
                                         convw_sb[:, dt * 4 + 3:dt * 4 + 4], av,
                                         ALU.mult, ALU.add).then_inc(s_ev.h, 1)
            kb.on("vector", fc)
            s_ev.inc(1)
            evv = s_ev.n

            def fs(eng, av=av, ov=ov, dt=dt, evv=evv, idx=idx):
                eng.wait_ge(s_ev.h, evv)
                if idx >= 2:
                    eng.wait_ge(s_st.h, b_st + 16 * (idx - 1))
                eng.activation(ov, av, AF.Silu,
                               bias=convb_sb[:, dt:dt + 1]).then_inc(s_ac.h, 1)
            kb.on("scalar", fs)
            s_ac.inc(1)
            acv = s_ac.n

            def fst(eng, ov=ov, dt=dt, b=b, acv=acv):
                eng.wait_ge(s_ac.h, acv)
                eng.dma_start(out=xc_dm[dt * 128:(dt + 1) * 128, b * T:(b + 1) * T],
                              in_=ov).then_inc(s_st.h, 16)
            kb.on("gpsimd", fst)
            s_st.inc(16)
            idx += 1
    stn = s_st.n

    def fdrain(eng, stn=stn):
        eng.wait_ge(s_st.h, stn)
    kb.on("gpsimd", fdrain)


def emit_mamba_scan(kb, bufs, xdbl_dm, dt_dm, dtx_dm, ys_dm, A_sb, a_ready):
    T = T_SEQ
    dtb = bufs["scdt"]
    dxb = bufs["scdx"]
    Bbc = bufs["scB"]
    Cbc = bufs["scC"]
    rowb = bufs["scrow"]
    dAb = bufs["scdA"]
    hb = bufs["sch"]
    bbf = bufs["scb"]
    yb = bufs["scy"]
    psb = bufs["sc_ps"]
    s_ld = kb.sem(); s_row = kb.sem(); s_bc = kb.sem(); s_bce = kb.sem()
    s_ev = kb.sem(); s_ac = kb.sem(); s_st = kb.sem()
    b_st = s_st.n; b_bce = s_bce.n
    fvc = kb.fsem["vector"]
    onesf = kb.onesf
    NC2 = T // 512
    idx = 0
    iter_done = []
    for b in range(B_LOC):
        bbc_consumed = iter_done[-1] if iter_done else 0
        bc_base = s_bc.n
        bce_base = s_bce.n
        for j in range(32):
            tgt = Bbc if j < 16 else Cbc
            nn = j % 16
            rslot = j % 2

            def flrow(eng, j=j, b=b, rslot=rslot, bc_base=bc_base):
                if j >= 2:
                    eng.wait_ge(s_bc.h, bc_base + (j - 1) * NC2)
                eng.dma_start(out=rowb[0:1, rslot * T:(rslot + 1) * T],
                              in_=xdbl_dm[64 + j:65 + j, b * T:(b + 1) * T]
                              ).then_inc(s_row.h, 16)
            kb.on("sync", flrow)
            s_row.inc(16)
            rowv = s_row.n

            row_free = (iter_done[(b * 16 - 1) * 16 + (j % 16)]
                        if b > 0 else 0)

            def fbc(eng, j=j, rslot=rslot, rowv=rowv, b=b,
                    row_free=row_free):
                eng.wait_ge(s_row.h, rowv)
                if j == 0 and b == 0:
                    eng.wait_ge(a_ready[0].h, a_ready[1])
                if row_free:
                    eng.wait_ge(s_ev.h, row_free)
                for cc in range(NC2):
                    if j * NC2 + cc >= 2:
                        eng.wait_ge(s_bce.h, b_bce + (32 * NC2) * b + j * NC2 + cc - 1)
                    eng.matmul(psb[cc % 2][:, :], kb.onesb[:],
                               rowb[0:1, rslot * T + cc * 512:rslot * T + (cc + 1) * 512],
                               start=True, stop=True).then_inc(s_bc.h, 1)
            kb.on("tensor", fbc)
            s_bc.inc(NC2)
            bcv = s_bc.n

            def fbe(eng, j=j, nn=nn, tgt=tgt, bcv=bcv):
                for cc in range(NC2):
                    eng.wait_ge(s_bc.h, bcv - NC2 + cc + 1)
                    eng.copy(tgt[:, nn * T + cc * 512:nn * T + (cc + 1) * 512],
                             psb[cc % 2][:, :]).then_inc(s_bce.h, 1)
            kb.on("scalar", fbe)
            s_bce.inc(NC2)
        bc_done = s_bce.n

        for dt in range(16):
            slot = idx % 2
            dtv = dtb[:, slot * T:(slot + 1) * T]
            dxv = dxb[:, slot * T:(slot + 1) * T]
            yv = yb[:, slot * T:(slot + 1) * T]

            def fld(eng, dt=dt, b=b, slot=slot, idx=idx,
                    need_ev=(iter_done[idx * 16 - 17] if idx >= 2 else 0)):
                if need_ev:
                    eng.wait_ge(s_ev.h, need_ev)
                eng.dma_start(out=dtb[:, slot * T:(slot + 1) * T],
                              in_=dt_dm[dt * 128:(dt + 1) * 128, b * T:(b + 1) * T]
                              ).then_inc(s_ld.h, 16)
                eng.dma_start(out=dxb[:, slot * T:(slot + 1) * T],
                              in_=dtx_dm[dt * 128:(dt + 1) * 128, b * T:(b + 1) * T]
                              ).then_inc(s_ld.h, 16)
            kb.on("sync", fld)
            s_ld.inc(32)
            ldv = s_ld.n

            for nn in range(16):
                g = idx * 16 + nn
                nslot = g % 2
                dAv = dAb[:, nslot * T:(nslot + 1) * T]
                bv = bbf[:, nslot * T:(nslot + 1) * T]
                hv = hb[:, nslot * T:(nslot + 1) * T]
                g_done = iter_done[g - 2] if g >= 2 else 0

                def fda(eng, dAv=dAv, dtv=dtv, dt=dt, nn=nn, ldv=ldv,
                        g_done=g_done):
                    eng.wait_ge(s_ld.h, ldv)
                    if g_done:
                        eng.wait_ge(s_ev.h, g_done)
                    eng.activation(dAv, dtv, AF.Exp,
                                   scale=A_sb[:, dt * 16 + nn:dt * 16 + nn + 1]
                                   ).then_inc(s_ac.h, 1)
                kb.on("scalar", fda)
                s_ac.inc(1)
                acv = s_ac.n

                def fsc(eng, nn=nn, dt=dt, dAv=dAv, bv=bv, hv=hv, dxv=dxv,
                        yv=yv, acv=acv, g_done=g_done, bce_base=bce_base,
                        idx=idx):
                    eng.wait_ge(s_ac.h, acv)
                    if dt == 0:
                        eng.wait_ge(s_bce.h, bce_base + (17 + nn) * NC2)
                    if nn == 0:
                        if idx >= 2:
                            eng.wait_ge(s_st.h, b_st + 16 * (idx - 1))
                    if g_done:
                        eng.wait_ge(s_ev.h, g_done)
                    eng.tensor_tensor(bv, dxv, Bbc[:, nn * T:(nn + 1) * T],
                                      ALU.mult)
                    eng.tensor_tensor_scan(hv, dAv, bv, 0.0, ALU.mult,
                                           ALU.add)
                    eng.tensor_tensor(hv, hv, Cbc[:, nn * T:(nn + 1) * T],
                                      ALU.mult)
                    if nn == 0:
                        ins = eng.tensor_copy(yv, hv)
                    else:
                        ins = eng.tensor_tensor(yv, yv, hv, ALU.add)
                    ins.then_inc(s_ev.h, 1)
                kb.on("vector", fsc)
                s_ev.inc(1)
                iter_done.append(s_ev.n)
            yv_done = iter_done[-1]

            def fst(eng, dt=dt, b=b, yv=yv, yv_done=yv_done):
                eng.wait_ge(s_ev.h, yv_done)
                eng.dma_start(out=ys_dm[dt * 128:(dt + 1) * 128, b * T:(b + 1) * T],
                              in_=yv).then_inc(s_st.h, 16)
            kb.on("gpsimd", fst)
            s_st.inc(16)
            idx += 1
    stn = s_st.n

    def fdrain(eng, stn=stn):
        eng.wait_ge(s_st.h, stn)
    kb.on("gpsimd", fdrain)

def emit_mamba_gate(kb, bufs, ys_dm, xc_dm, xz_dm, y2_dm, D_sb, d_ready):
    ysb = bufs["mgy"]; xcb = bufs["mgx"]; zb = bufs["mgz"]; szb = bufs["mgs"]
    ob = bufs["mgo"]
    n = NCHUNK
    s_ld = kb.sem(); s_ac = kb.sem(); s_ev = kb.sem(); s_st = kb.sem()
    b_st = s_st.n; b_ev = s_ev.n
    fvc = kb.fsem["vector"]
    nch = NTOK // n
    for c in range(nch):
        slot = c % 2
        ysv = ysb[:, slot * 16 * n:(slot + 1) * 16 * n]
        xcv = xcb[:, slot * 16 * n:(slot + 1) * 16 * n]
        zv = zb[:, slot * 16 * n:(slot + 1) * 16 * n]
        szv = szb[:, slot * 16 * n:(slot + 1) * 16 * n]
        ov = ob[:, slot * 16 * n:(slot + 1) * 16 * n]

        def fl(eng, c=c, ysv=ysv, xcv=xcv, zv=zv):
            if c >= 2:
                eng.wait_ge(s_st.h, b_st + 16 * (c - 1))
            eng.dma_start(out=ysv.rearrange("p (t n) -> p t n", t=16),
                          in_=ys_dm[:, c * n:(c + 1) * n].rearrange(
                              "(t p) n -> p t n", p=128)).then_inc(s_ld.h, 16)
            eng.dma_start(out=xcv.rearrange("p (t n) -> p t n", t=16),
                          in_=xc_dm[:, c * n:(c + 1) * n].rearrange(
                              "(t p) n -> p t n", p=128)).then_inc(s_ld.h, 16)
            eng.dma_start(out=zv.rearrange("p (t n) -> p t n", t=16),
                          in_=xz_dm[2048:4096, c * n:(c + 1) * n].rearrange(
                              "(t p) n -> p t n", p=128)).then_inc(s_ld.h, 16)
        kb.on("sync", fl)
        s_ld.inc(48)
        ldv = s_ld.n

        def fs(eng, c=c, zv=zv, szv=szv, ldv=ldv):
            eng.wait_ge(s_ld.h, ldv)
            if c == 0:
                eng.wait_ge(d_ready[0].h, d_ready[1])
            if c >= 2:
                eng.wait_ge(s_ev.h, b_ev + c - 1)
            eng.activation(szv, zv, AF.Silu).then_inc(s_ac.h, 1)
        kb.on("scalar", fs)
        s_ac.inc(1)
        acv = s_ac.n

        fg = [fvc.inc() for _ in range(16)]

        def fv(eng, c=c, ysv=ysv, xcv=xcv, szv=szv, ov=ov, acv=acv, fg=fg):
            eng.wait_ge(s_ac.h, acv)
            if c >= 2:
                eng.wait_ge(s_st.h, b_st + 16 * (c - 1))
            for t in range(16):
                eng.scalar_tensor_tensor(
                    ov[:, t * n:(t + 1) * n], xcv[:, t * n:(t + 1) * n],
                    D_sb[:, t:t + 1], ysv[:, t * n:(t + 1) * n],
                    ALU.mult, ALU.add).then_inc(fvc.h, 1)
                eng.wait_ge(fvc.h, fg[t])
                ins = eng.tensor_tensor(ov[:, t * n:(t + 1) * n],
                                        ov[:, t * n:(t + 1) * n],
                                        szv[:, t * n:(t + 1) * n], ALU.mult)
            ins.then_inc(s_ev.h, 1)
        kb.on("vector", fv)
        s_ev.inc(1)
        evv = s_ev.n

        def fst(eng, c=c, ov=ov, evv=evv):
            eng.wait_ge(s_ev.h, evv)
            eng.dma_start(out=y2_dm[:, c * n:(c + 1) * n].rearrange(
                              "(t p) n -> p t n", p=128),
                          in_=ov.rearrange("p (t n) -> p t n", t=16)
                          ).then_inc(s_st.h, 16)
        kb.on("gpsimd", fst)
        s_st.inc(16)
    stn = s_st.n

    def fdrain(eng, stn=stn):
        eng.wait_ge(s_st.h, stn)
    kb.on("gpsimd", fdrain)


def emit_block_l1attn(kb, x_in, x_out, inpT, xprojT, dtprojT, outprojT,
                      lng_D, lnb_D, convw_D, convb_D, dtb_D, Alog_D, Dp_D,
                      xn_bf, xz_dm, xc_dm, xdbl_dm, dt_dm, dtx_dm, ys_dm, y2_dm,
                      xn1_f32, xres):
    nc = kb.nc
    scP = kb.scope()
    lng_sb = kb.sb("m_lng", [128, 8], F32, scP)
    lnbt_sb = kb.sb("m_lnb", [128, 8], F32, scP)
    convw_sb = kb.sb("m_cw", [128, 64], F32, scP)
    convb_sb = kb.sb("m_cb", [128, 16], F32, scP)
    dtb_sb = kb.sb("m_dtb", [128, 16], F32, scP)
    Alog_sb = kb.sb("m_alog", [128, 256], F32, scP)
    A_sb = kb.sb("m_A", [128, 256], F32, scP)
    D_sb = kb.sb("m_D", [128, 16], F32, scP)
    s_p = kb.sem("m_sp")

    def flp(eng):
        eng.dma_start(out=lng_sb[:], in_=lng_D[:, :]).then_inc(s_p.h, 16)
        eng.dma_start(out=lnbt_sb[:], in_=lnb_D[:, :]).then_inc(s_p.h, 16)
        eng.dma_start(out=convw_sb[:], in_=convw_D[:, :]).then_inc(s_p.h, 16)
        eng.dma_start(out=convb_sb[:], in_=convb_D[:, :]).then_inc(s_p.h, 16)
        eng.dma_start(out=dtb_sb[:], in_=dtb_D[:, :]).then_inc(s_p.h, 16)
        eng.dma_start(out=Alog_sb[:], in_=Alog_D[:, :]).then_inc(s_p.h, 16)
        eng.dma_start(out=D_sb[:], in_=Dp_D[:, :]).then_inc(s_p.h, 16)
        eng.wait_ge(s_p.h, 112)
    kb.on("sync", flp)
    s_p.inc(112)
    pv = s_p.n
    fsc = kb.fsem["scalar"]
    fA = fsc.inc()

    def fprep(eng, pv=pv, fA=fA):
        eng.wait_ge(s_p.h, pv)
        eng.activation(A_sb[:], Alog_sb[:], AF.Exp).then_inc(fsc.h, 1)
        eng.wait_ge(fsc.h, fA)
        eng.mul(A_sb[:], A_sb[:], -1.0).then_inc(s_p.h, 1)
    kb.on("scalar", fprep)
    s_p.inc(1)
    a_ready = (s_p, s_p.n)

    sc1 = kb.scope()
    lnb1 = make_ln_bufs(kb, sc1, D, pfx="m_ln1", with_f32=True, with_sum=True)
    emit_ln_phase(kb, lnb1, x_in, xn_bf, D, eps=1e-5, xn_f32_dram=xn1_f32,
                  sum_out_dram=xres)
    sc1.close()
    kb.barrier()
    sc1b = kb.scope()
    lnb1b = make_ln_bufs(kb, sc1b, D, pfx="m_ln2")
    emit_ln_phase(kb, lnb1b, xn1_f32, xn_bf, D, gamma_sb=lng_sb, beta_sb=lnbt_sb,
                  eps=1e-5)
    sc1b.close()
    kb.barrier()

    sc2 = kb.scope()
    wi_sb = kb.sb("m_wi", [128, 8 * 32 * 128], BF16, sc2)
    wst = kb.sb("m_wist", [128, 8 * 128], F32, sc2)
    s_wld = kb.sem(); s_wcvt = kb.sem()
    emit_weight_load_g(kb, inpT, wi_sb, 8, 32, wst, s_wld, s_wcvt)
    ipb = {
        "ipx": kb.sb("m_ipx", [128, 2 * 8 * NCHUNK], BF16, sc2),
        "ipo": kb.sb("m_ipo", [128, 2 * 32 * NCHUNK], BF16, sc2),
        "ipt": kb.sb("m_ipt", [128, 2 * NCHUNK], F32, sc2),
        "ipps": [kb.psum(f"m_ipps{i}", [128, NCHUNK], F32, sc2) for i in range(2)],
    }
    emit_proj_phase(kb, ipb, "ip", [wi_sb], xn_bf, 8, xz_dm, 32,
                    mode="copy_bf16", w_ready=(s_wcvt, s_wcvt.n))
    sc2.close()
    kb.barrier()

    sc3 = kb.scope()
    cvb = {
        "cvx": kb.sb("m_cvx", [128, 2 * (3 + T_SEQ)], BF16, sc3),
        "cva": kb.sb("m_cva", [128, 2 * T_SEQ], F32, sc3),
        "cvo": kb.sb("m_cvo", [128, 2 * T_SEQ], BF16, sc3),
    }
    emit_mamba_conv(kb, cvb, xz_dm, xc_dm, convw_sb, convb_sb, (s_p, pv))
    sc3.close()
    kb.barrier()

    sc4 = kb.scope()
    wx_sb = kb.sb("m_wx", [128, 16 * 1 * 128], BF16, sc4)
    wst2 = kb.sb("m_wxst", [128, 4 * 128], F32, sc4)
    s_w2 = kb.sem(); s_wc2 = kb.sem()
    emit_weight_load_g(kb, xprojT, wx_sb, 16, 1, wst2, s_w2, s_wc2)
    xdb = {
        "xdx": kb.sb("m_xdx", [128, 2 * 16 * NCHUNK], BF16, sc4),
        "xdo": kb.sb("m_xdo", [128, 2 * 1 * NCHUNK], BF16, sc4),
        "xdt": kb.sb("m_xdt", [128, 2 * NCHUNK], F32, sc4),
        "xdps": [kb.psum(f"m_xdps{i}", [128, NCHUNK], F32, sc4) for i in range(2)],
    }
    emit_proj_phase(kb, xdb, "xd", [wx_sb], xc_dm, 16, xdbl_dm, 1,
                    mode="copy_bf16", w_ready=(s_wc2, s_wc2.n))
    sc4.close()
    kb.barrier()

    sc5 = kb.scope()
    wdt_sb = kb.sb("m_wdt", [128, 1 * 16 * 128], BF16, sc5)
    wst3 = kb.sb("m_wdtst", [128, 2 * 128], F32, sc5)
    s_w3 = kb.sem(); s_wc3 = kb.sem()
    emit_weight_load_g(kb, dtprojT, wdt_sb, 1, 16, wst3, s_w3, s_wc3, krows=64)
    dtbf = {
        "dtx": kb.sb("m_dtx", [128, 2 * 1 * NCHUNK], BF16, sc5),
        "dto": kb.sb("m_dto", [128, 2 * 16 * NCHUNK], F32, sc5),
        "dtt": kb.sb("m_dtt", [128, 2 * NCHUNK], F32, sc5),
        "dtl": kb.sb("m_dtl", [128, 2 * NCHUNK], F32, sc5),
        "dtm": kb.sb("m_dtm", [128, 2 * 16 * NCHUNK], BF16, sc5),
        "dto2": kb.sb("m_dto2", [128, 2 * 16 * NCHUNK], BF16, sc5),
        "dtps": [kb.psum(f"m_dtps{i}", [128, NCHUNK], F32, sc5) for i in range(4)],
    }
    emit_proj_phase(kb, dtbf, "dt", [wdt_sb], xdbl_dm, 1, dt_dm, 16,
                    mode="softplus_bias", w_ready=(s_wc3, s_wc3.n), krows=64,
                    bias_sb=dtb_sb, out2_dram=dtx_dm, mul_dram=xc_dm)
    sc5.close()
    kb.barrier()

    sc6 = kb.scope()
    scb = {
        "scdt": kb.sb("m_scdt", [128, 2 * T_SEQ], F32, sc6),
        "scdx": kb.sb("m_scdx", [128, 2 * T_SEQ], BF16, sc6),
        "scB": kb.sb("m_scB", [128, 16 * T_SEQ], BF16, sc6),
        "scC": kb.sb("m_scC", [128, 16 * T_SEQ], BF16, sc6),
        "scrow": kb.sb("m_scrow", [1, 2 * T_SEQ], BF16, sc6),
        "scdA": kb.sb("m_scdA", [128, 2 * T_SEQ], BF16, sc6),
        "sch": kb.sb("m_sch", [128, 2 * T_SEQ], BF16, sc6),
        "scb": kb.sb("m_scb", [128, 2 * T_SEQ], BF16, sc6),
        "scy": kb.sb("m_scy", [128, 2 * T_SEQ], BF16, sc6),
        "sc_ps": [kb.psum(f"m_scps{i}", [128, 512], F32, sc6) for i in range(2)],
    }
    emit_mamba_scan(kb, scb, xdbl_dm, dt_dm, dtx_dm, ys_dm, A_sb, a_ready)
    sc6.close()
    kb.barrier()

    sc7 = kb.scope()
    mgb = {
        "mgy": kb.sb("m_mgy", [128, 2 * 16 * NCHUNK], BF16, sc7),
        "mgx": kb.sb("m_mgx", [128, 2 * 16 * NCHUNK], BF16, sc7),
        "mgz": kb.sb("m_mgz", [128, 2 * 16 * NCHUNK], BF16, sc7),
        "mgs": kb.sb("m_mgs", [128, 2 * 16 * NCHUNK], BF16, sc7),
        "mgo": kb.sb("m_mgo", [128, 2 * 16 * NCHUNK], BF16, sc7),
    }
    emit_mamba_gate(kb, mgb, ys_dm, xc_dm, xz_dm, y2_dm, D_sb, (s_p, pv))
    sc7.close()
    kb.barrier()

    sc8 = kb.scope()
    wo_sb = kb.sb("m_wo", [128, 16 * 8 * 128], BF16, sc8)
    wst4 = kb.sb("m_wost", [128, 4 * 128], F32, sc8)
    s_w4 = kb.sem(); s_wc4 = kb.sem()
    emit_weight_load_g(kb, outprojT, wo_sb, 16, 8, wst4, s_w4, s_wc4)
    opb = {
        "opx": kb.sb("m_opx", [128, 2 * 16 * NCHUNK], BF16, sc8),
        "opo": kb.sb("m_opo", [128, 2 * 8 * NCHUNK], F32, sc8),
        "opt": kb.sb("m_opt", [128, 2 * NCHUNK], F32, sc8),
        "opr": kb.sb("m_opr", [128, 2 * 8 * NCHUNK], F32, sc8),
        "opps": [kb.psum(f"m_opps{i}", [128, NCHUNK], F32, sc8) for i in range(2)],
    }
    emit_proj_phase(kb, opb, "op", [wo_sb], y2_dm, 16, x_out, 8,
                    mode="residual", res_dram=xres, w_ready=(s_wc4, s_wc4.n))
    sc8.close()
    scP.close()
    kb.barrier()


def emit_rope_phase(kb, bufs, x_dm, cos_sb, sin_sb, psw_sb, w_ready):
    T = T_SEQ
    xb = bufs["rpx"]; t1 = bufs["rpa"]; t2 = bufs["rps"]; ob = bufs["rpo"]
    psw = bufs["rp_ps"]
    s_ld = kb.sem(); s_mm = kb.sem(); s_ev = kb.sem(); s_st = kb.sem()
    b_st = s_st.n
    fvc = kb.fsem["vector"]
    idx = 0
    NC2 = T // 512
    for dt in range(8):
        for b in range(B_LOC):
            slot = idx % 2
            xv = xb[:, slot * T:(slot + 1) * T]
            ov = ob[:, slot * T:(slot + 1) * T]

            def fl(eng, dt=dt, b=b, xv=xv, idx=idx):
                if idx >= 2:
                    eng.wait_ge(s_st.h, b_st + 16 * (idx - 1))
                eng.dma_start(out=xv, in_=x_dm[dt * 128:(dt + 1) * 128,
                                               b * T:(b + 1) * T]).then_inc(s_ld.h, 16)
            kb.on("sync", fl)
            s_ld.inc(16)
            ldv = s_ld.n

            for cc in range(NC2):
                bank = cc % 2
                wdn = ev_hist[-2] if False else 0

                def fmm(eng, xv=xv, cc=cc, bank=bank, ldv=ldv, idx=idx,
                        wdn=(s_ev.n - 1 if (idx * NC2 + cc) >= 2 else 0)):
                    if cc == 0:
                        eng.wait_ge(s_ld.h, ldv)
                        if idx == 0:
                            eng.wait_ge(w_ready[0].h, w_ready[1])
                    if wdn:
                        eng.wait_ge(s_ev.h, wdn)
                    return eng.matmul(psw[bank][:, :], psw_sb[:],
                                      xv[:, cc * 512:(cc + 1) * 512],
                                      start=True, stop=True).then_inc(s_mm.h, 1)
                kb.on("tensor", fmm)
                s_mm.inc(1)
                mmv = s_mm.n

                f1 = fvc.inc(); f2 = fvc.inc()

                def fv(eng, xv=xv, ov=ov, cc=cc, bank=bank, b=b, mmv=mmv,
                       idx=idx, f1=f1, f2=f2):
                    eng.wait_ge(s_mm.h, mmv)
                    if idx >= 2 and cc == 0:
                        eng.wait_ge(s_st.h, b_st + 16 * (idx - 1))
                    c0 = cc * 512
                    eng.tensor_tensor(t1[:, 0:512], xv[:, c0:c0 + 512],
                                      cos_sb[:, c0:c0 + 512], ALU.mult).then_inc(fvc.h, 1)
                    eng.tensor_tensor(t2[:, 0:512], psw[bank][:, :],
                                      sin_sb[:, c0:c0 + 512], ALU.mult).then_inc(fvc.h, 1)
                    eng.wait_ge(fvc.h, f2)
                    return eng.tensor_tensor(ov[:, c0:c0 + 512], t1[:, 0:512],
                                             t2[:, 0:512], ALU.add).then_inc(s_ev.h, 1)
                kb.on("vector", fv)
                s_ev.inc(1)
            evv = s_ev.n

            def fst(eng, dt=dt, b=b, ov=ov, evv=evv):
                eng.wait_ge(s_ev.h, evv)
                eng.dma_start(out=x_dm[dt * 128:(dt + 1) * 128, b * T:(b + 1) * T],
                              in_=ov).then_inc(s_st.h, 16)
            kb.on("gpsimd", fst)
            s_st.inc(16)
            idx += 1
    stn = s_st.n

    def fdrain(eng, stn=stn):
        eng.wait_ge(s_st.h, stn)
    kb.on("gpsimd", fdrain)


def emit_mla_attn(kb, bufs, q_dm, k_dm, vt_tm, attn_dm):
    T = T_SEQ
    H = 16
    qb = bufs["maq"]; kbuf = bufs["mak"]; vb = bufs["mav"]
    expb = bufs["maxp"]
    osb = bufs["mao"]
    ones64 = bufs["ones64"]
    outb = bufs["maob"]
    stat = bufs["mast"]
    recb = bufs["marc"]
    rec0 = bufs["marc0"]
    onescolb = bufs["onescolb"]
    cmask_s = bufs["cmask_s"]; cmask_b = bufs["cmask_b"]
    psS = bufs["ma_psS"]
    psO = bufs["ma_psO"]
    psB = bufs["ma_psB"]
    idq = bufs["idq"]
    s_ld = kb.sem(); s_mm = kb.sem(); s_st = kb.sem(); s_lv = kb.sem()
    s_dv = kb.sem(); s_ds = kb.sem()
    fvc = kb.fsem["vector"]
    NQT = T // 128
    inst = 0
    inst_done = []
    st_count = []
    ZERO = ()
    last_psS = [ZERO, ZERO, ZERO, ZERO]
    last_psO = [ZERO, ZERO]
    last_psB = [ZERO, ZERO]
    last_osb = [ZERO, ZERO]
    pending = [None]
    last_stat = [ZERO, ZERO]

    def wdeps(eng, deps):
        dv = 0; ds = 0
        for (tag, val) in deps:
            if tag == "v":
                dv = max(dv, val)
            else:
                ds = max(ds, val)
        if dv:
            eng.wait_ge(s_dv.h, dv)
        if ds:
            eng.wait_ge(s_ds.h, ds)

    def mmstage(emit_mms, deps=(), wait_ld=0, wait_lv=0):
        deps = tuple(deps)

        def f(eng, deps=deps, wait_ld=wait_ld, wait_lv=wait_lv,
              emit_mms=emit_mms):
            if wait_ld:
                eng.wait_ge(s_ld.h, wait_ld)
            if wait_lv:
                eng.wait_ge(s_lv.h, wait_lv)
            wdeps(eng, deps)
            emit_mms(eng).then_inc(s_mm.h, 1)
        kb.on("tensor", f)
        s_mm.inc(1)
        return s_mm.n

    def evstage(engine, emit_ev, wait_mm=0, deps=(), wait_st=0):
        deps = tuple(deps)
        sem = s_dv if engine == "vector" else s_ds

        def f(eng, wait_mm=wait_mm, emit_ev=emit_ev, deps=deps,
              wait_st=wait_st, sem=sem):
            if wait_mm:
                eng.wait_ge(s_mm.h, wait_mm)
            wdeps(eng, deps)
            if wait_st:
                eng.wait_ge(s_st.h, wait_st)
            emit_ev(eng).then_inc(sem.h, 1)
        kb.on(engine, f)
        sem.inc(1)
        return (("v" if engine == "vector" else "s"), sem.n)

    def fsetup(eng):
        eng.memset(ones64[:], 1.0)
        eng.memset(onescolb[:], 1.0).then_inc(s_dv.h, 1)
    kb.on("vector", fsetup)
    s_dv.inc(1)

    for b in range(B_LOC):
        for h in range(H):
            slot = inst % 2
            qv = qb[0:64, slot * T:(slot + 1) * T]
            kv = kbuf[0:64, slot * T:(slot + 1) * T]
            vv = vb[:, slot * 1072:slot * 1072 + 1072]
            ov = outb[:, slot * T:(slot + 1) * T]
            in_guard = inst_done[inst - 2] if inst >= 2 else ()
            r0 = h * 64

            def fld(eng, b=b, r0=r0, qv=qv, kv=kv, in_guard=in_guard):
                wdeps(eng, in_guard)
                eng.dma_start(out=qv, in_=q_dm[r0:r0 + 64, b * T:(b + 1) * T]
                              ).then_inc(s_ld.h, 16)
                eng.dma_start(out=kv, in_=k_dm[r0:r0 + 64, b * T:(b + 1) * T]
                              ).then_inc(s_ld.h, 16)
            kb.on("sync", fld)
            s_ld.inc(32)

            def fldv(eng, b=b, r0=r0, vv=vv, in_guard=in_guard):
                wdeps(eng, in_guard)
                eng.memset(vv, 1.0).then_inc(s_lv.h, 1)
                eng.dma_start(
                    out=vv.rearrange("p (kt c) -> p kt c", kt=16)[:, :, 0:64],
                    in_=vt_tm[b * T:(b + 1) * T, r0:r0 + 64].rearrange(
                        "(kt p) c -> p kt c", p=128),
                ).then_inc(s_lv.h, 16)
            kb.on("gpsimd", fldv)
            s_lv.inc(17)
            lvv = s_lv.n
            ldv = s_ld.n


            for qc in range(4):
                nkt = (qc + 1) * 4
                ob_ = qc % 2
                psOc = psO[ob_]
                exp_deps = []
                kt = 0
                while kt < nkt:
                    pair = [kt, kt + 1]
                    bp = (0, 1) if (kt // 2) % 2 == 0 else (2, 3)

                    def mmS(eng, qc=qc, pair=pair, bp=bp, qv=qv, kv=kv):
                        for j, k in enumerate(pair):
                            ins = eng.matmul(psS[bp[j]][:, :],
                                             kv[:, k * 128:(k + 1) * 128],
                                             qv[:, qc * 512:(qc + 1) * 512],
                                             start=True, stop=True)
                        return ins
                    dd = list(last_psS[bp[0]]) + list(last_psS[bp[1]])
                    if qc == 0 and kt == 0:
                        dd += list(in_guard)
                    m = mmstage(mmS, deps=dd,
                                wait_ld=(ldv if qc == 0 and kt == 0 else 0))

                    diag0 = pair[0] >= qc * 4
                    if diag0:
                        def evmask(eng, qc=qc, pair=pair, bp=bp):
                            for j, k in enumerate(pair):
                                soff = 128 + (k % 2) * 512
                                v = k - qc * 4
                                ins = eng.tensor_tensor(
                                    stat[:, soff:soff + 512], psS[bp[j]][:, :],
                                    cmask_b[:, v * 512:(v + 1) * 512], ALU.add)
                            return eng.tensor_copy(stat[:, 1665:1666],
                                                   stat[:, 128:129])
                        d_msk = evstage("vector", evmask, wait_mm=m,
                                        deps=list(last_stat[0]) + list(last_stat[1]))

                        def evexp(eng, pair=pair):
                            for k in pair:
                                soff = 128 + (k % 2) * 512
                                ins = eng.activation(
                                    expb[:, k * 512:(k + 1) * 512],
                                    stat[:, soff:soff + 512], AF.Exp, scale=0.125)
                            return ins
                        d_e = evstage("scalar", evexp, deps=(d_msk,))
                        last_stat[0] = (d_e,)
                        last_stat[1] = (d_e,)
                        last_psS[bp[0]] = (d_msk,)
                        last_psS[bp[1]] = (d_msk,)
                        exp_deps.append(d_e)
                    else:
                        def evexp(eng, pair=pair, bp=bp):
                            for j, k in enumerate(pair):
                                ins = eng.activation(
                                    expb[:, k * 512:(k + 1) * 512],
                                    psS[bp[j]][:, :], AF.Exp, scale=0.125)
                            return ins
                        d_e = evstage("scalar", evexp, wait_mm=m)
                        last_psS[bp[0]] = (d_e,)
                        last_psS[bp[1]] = (d_e,)
                        exp_deps.append(d_e)
                    kt += 2

                if pending[0] is not None:
                    pending[0]()
                    pending[0] = None

                def mmV(eng, nkt=nkt, vv=vv, psOc=psOc):
                    for kt in range(nkt):
                        ins = eng.matmul(psOc[0:65, :], vv[:, kt * 67:kt * 67 + 65],
                                         expb[:, kt * 512:(kt + 1) * 512],
                                         start=(kt == 0), stop=(kt == nkt - 1))
                    return ins
                m = mmstage(mmV, deps=list(exp_deps) + list(last_psO[ob_]),
                            wait_lv=lvv)

                def evrec(eng, psOc=psOc):
                    eng.reciprocal(rec0[0:1, :], psOc[64:65, :])
                    return eng.memset(kb.junk[0:1, 12:13], 0.0)
                d_rec = evstage("vector", evrec, wait_mm=m)
                osv = osb[:, ob_ * 512:(ob_ + 1) * 512]
                d_oc = evstage("scalar",
                               lambda eng, psOc=psOc, osv=osv: eng.copy(
                                   osv[0:64, :], psOc[0:64, :]),
                               wait_mm=m, deps=last_osb[ob_])
                last_psO[ob_] = (d_oc, d_rec)

                def mktail(qc=qc, ob_=ob_, d_rec=d_rec, d_oc=d_oc, osv=osv,
                           ov=ov, inst=inst, b=b, r0=r0):
                    def flush():
                        m2 = mmstage(
                            lambda eng, ob_=ob_: eng.matmul(
                                psB[ob_][0:64, :], kb.onesf[0:1, 0:64],
                                rec0[0:1, :], start=True, stop=True),
                            deps=list(last_psB[ob_]) + [d_rec])

                        def evO(eng, qc=qc, ov=ov, inst=inst, ob_=ob_,
                                osv=osv):
                            if inst >= 2 and qc == 0:
                                eng.wait_ge(s_st.h, st_count[inst - 2])
                            return eng.tensor_tensor(
                                ov[0:64, qc * 512:(qc + 1) * 512],
                                osv[0:64, :], psB[ob_][0:64, :], ALU.mult)
                        dO = evstage("vector", evO, wait_mm=m2, deps=(d_oc,))
                        last_psB[ob_] = (dO,)
                        last_osb[ob_] = (dO,)
                        if qc == 3:
                            dnv = (dO, d_oc)
                            inst_done.append(dnv)

                            def fso(eng, b=b, r0=r0, ov=ov, dnv=dnv):
                                wdeps(eng, dnv)
                                eng.dma_start(
                                    out=attn_dm[r0:r0 + 64, b * T:(b + 1) * T],
                                    in_=ov[0:64, :]).then_inc(s_st.h, 16)
                            kb.on("gpsimd", fso)
                            s_st.inc(16)
                            st_count.append(s_st.n)
                    return flush
                pending[0] = mktail()

            inst += 1
    if pending[0] is not None:
        pending[0]()
        pending[0] = None
    stn = s_st.n

    def fdrain(eng, stn=stn):
        eng.wait_ge(s_st.h, stn)
    kb.on("gpsimd", fdrain)


def emit_tokmajor_proj(kb, bufs, pfx, x_dm, w_sb, ktiles, mcols, out_tm,
                       w_ready):
    ntiles = NTOK // 128
    xb = bufs[pfx + "x"]
    ob = bufs[pfx + "o"]
    pss = bufs[pfx + "ps"]
    s_ld = kb.sem(); s_mm = kb.sem(); s_ev = kb.sem(); s_st = kb.sem()
    b_st = s_st.n; b_ev = s_ev.n
    nc2 = ceil_div(mcols, 512)
    for tt in range(ntiles):
        slot = tt % 2

        def flx(eng, tt=tt, slot=slot):
            if tt >= 2:
                eng.wait_ge(s_st.h, b_st + 16 * (tt - 1))
            eng.dma_start(
                out=xb[:, slot * ktiles * 128:(slot + 1) * ktiles * 128
                       ].rearrange("p (k c) -> p k c", k=ktiles),
                in_=x_dm[:, tt * 128:(tt + 1) * 128].rearrange(
                    "(k p) c -> p k c", p=128),
            ).then_inc(s_ld.h, 16)
        kb.on("sync", flx)
        s_ld.inc(16)
        ldv = s_ld.n

        def fmm(eng, tt=tt, slot=slot, ldv=ldv):
            eng.wait_ge(s_ld.h, ldv)
            if tt == 0:
                eng.wait_ge(w_ready[0].h, w_ready[1])
            else:
                eng.wait_ge(s_ev.h, b_ev + tt)
            for kt in range(ktiles):
                st = xb[:, slot * ktiles * 128 + kt * 128:
                        slot * ktiles * 128 + (kt + 1) * 128]
                for cc in range(nc2):
                    c0 = cc * 512; c1 = min(mcols, c0 + 512)
                    ins = eng.matmul(pss[:, c0:c1], st,
                                     w_sb[:, kt * mcols + c0:kt * mcols + c1],
                                     start=(kt == 0), stop=(kt == ktiles - 1))
            ins.then_inc(s_mm.h, 1)
        kb.on("tensor", fmm)
        s_mm.inc(1)
        mmv = s_mm.n
        ov = ob[:, slot * mcols:(slot + 1) * mcols]

        def fev(eng, mmv=mmv, ov=ov, tt=tt):
            eng.wait_ge(s_mm.h, mmv)
            if tt >= 2:
                eng.wait_ge(s_st.h, b_st + 16 * (tt - 1))
            eng.activation(ov, pss[:, :], AF.Copy).then_inc(s_ev.h, 1)
        kb.on("scalar", fev)
        s_ev.inc(1)
        evv = s_ev.n

        def fst(eng, tt=tt, ov=ov, evv=evv):
            eng.wait_ge(s_ev.h, evv)
            eng.dma_start(out=out_tm[tt * 128:(tt + 1) * 128, :],
                          in_=ov).then_inc(s_st.h, 16)
        kb.on("gpsimd", fst)
        s_st.inc(16)
    stn = s_st.n

    def fdrain(eng, stn=stn):
        eng.wait_ge(s_st.h, stn)
    kb.on("gpsimd", fdrain)


def emit_gating(kb, bufs, xn2_f32, gwT_sb, gb_row, mask_dm, idq, w_ready):
    ntiles = NTOK // 128
    xb = bufs["gtx"]
    lg = bufs["gtl"]
    mro = bufs["gtm"]
    psL = bufs["gt_psL"]
    psT = bufs["gt_psT"]
    s_ld = kb.sem(); s_mm = kb.sem(); s_dn = kb.sem(); s_st = kb.sem()
    b_st = s_st.n
    fvc = kb.fsem["vector"]
    for tt in range(ntiles):
        slot = tt % 2

        def flx(eng, tt=tt, slot=slot):
            if tt >= 2:
                eng.wait_ge(s_st.h, b_st + 64 * (tt - 1))
            eng.dma_start(
                out=xb[:, slot * 1024:(slot + 1) * 1024].rearrange(
                    "p (k c) -> p k c", k=8),
                in_=xn2_f32[:, tt * 128:(tt + 1) * 128].rearrange(
                    "(k p) c -> p k c", p=128),
            ).then_inc(s_ld.h, 16)
        kb.on("sync", flx)
        s_ld.inc(16)
        ldv = s_ld.n

        def fmm(eng, tt=tt, slot=slot, ldv=ldv, wdn=(s_dn.n)):
            eng.wait_ge(s_ld.h, ldv)
            if tt == 0:
                eng.wait_ge(w_ready[0].h, w_ready[1])
            if wdn:
                eng.wait_ge(s_dn.h, wdn)
            for kt in range(8):
                st = xb[:, slot * 1024 + kt * 128:slot * 1024 + (kt + 1) * 128]
                eng.matmul(psL[:, 0:4], st, gwT_sb[:, kt * 4:(kt + 1) * 4],
                           start=(kt == 0), stop=False)
            ins = eng.matmul(psL[:, 0:4], kb.onesf[:], gb_row[:, :],
                             start=False, stop=True)
            ins.then_inc(s_mm.h, 1)
        kb.on("tensor", fmm)
        s_mm.inc(1)
        mmv = s_mm.n

        fs = [fvc.inc() for _ in range(6)]

        def fdv(eng, mmv=mmv, fs=fs):
            eng.wait_ge(s_mm.h, mmv)
            eng.tensor_copy(lg[:, 0:4], psL[:, 0:4]).then_inc(fvc.h, 1)
            eng.wait_ge(fvc.h, fs[0])
            eng.tensor_reduce(lg[:, 4:5], lg[:, 0:4], AX.X, ALU.max).then_inc(fvc.h, 1)
            eng.wait_ge(fvc.h, fs[1])
            eng.tensor_tensor(lg[:, 5:9], lg[:, 0:4],
                              lg[:, 4:5].broadcast_to([128, 4]) if False else
                              lg[:, 4:5], ALU.is_ge) if False else None
            for e in range(4):
                eng.tensor_tensor(lg[:, 5 + e:6 + e], lg[:, e:e + 1],
                                  lg[:, 4:5], ALU.is_ge)
            ins = eng.tensor_copy(lg[:, 9:10], lg[:, 5:6])
            ins.then_inc(fvc.h, 1)
            eng.wait_ge(fvc.h, fs[2])
            eng.tensor_scalar(lg[:, 13:14], lg[:, 9:10], -1.0, 1.0,
                              ALU.mult, ALU.add).then_inc(fvc.h, 1)
            eng.wait_ge(fvc.h, fs[3])
            eng.tensor_tensor(lg[:, 10:11], lg[:, 6:7], lg[:, 13:14], ALU.mult)
            eng.tensor_tensor(lg[:, 14:15], lg[:, 9:10], lg[:, 10:11], ALU.add).then_inc(fvc.h, 1)
            eng.wait_ge(fvc.h, fs[4])
            eng.tensor_scalar(lg[:, 13:14], lg[:, 14:15], -1.0, 1.0,
                              ALU.mult, ALU.add)
            eng.tensor_tensor(lg[:, 11:12], lg[:, 7:8], lg[:, 13:14], ALU.mult)
            eng.tensor_tensor(lg[:, 14:15], lg[:, 14:15], lg[:, 11:12], ALU.add).then_inc(fvc.h, 1)
            eng.wait_ge(fvc.h, fs[5])
            eng.tensor_scalar(lg[:, 13:14], lg[:, 14:15], -1.0, 1.0,
                              ALU.mult, ALU.add)
            ins = eng.tensor_tensor(lg[:, 12:13], lg[:, 8:9], lg[:, 13:14],
                                    ALU.mult)
            return ins
        d_m = None
        def fdvw(eng, fdv=fdv):
            fdv(eng).then_inc(s_dn.h, 1)
        kb.on("vector", fdvw)
        s_dn.inc(1)
        d_m = s_dn.n

        def fmt(eng, d_m=d_m, wst=(s_mm.n)):
            eng.wait_ge(s_dn.h, d_m)
            for e in range(4):
                eng.transpose(psT[0:1, e * 128:(e + 1) * 128],
                              lg[:, 9 + e:10 + e], idq[:]).then_inc(s_mm.h, 1)
        kb.on("tensor", fmt)
        s_mm.inc(4)
        mtv = s_mm.n

        def fme(eng, mtv=mtv, slot=slot, tt=tt):
            eng.wait_ge(s_mm.h, mtv)
            if tt >= 2:
                eng.wait_ge(s_st.h, b_st + 64 * (tt - 1))
            ins = eng.copy(mro[0:1, slot * 512:slot * 512 + 512],
                           psT[0:1, 0:512])
            ins.then_inc(s_dn.h, 1)
        kb.on("scalar", fme)
        s_dn.inc(1)
        dv2 = s_dn.n

        def fms(eng, tt=tt, slot=slot, dv2=dv2):
            eng.wait_ge(s_dn.h, dv2)
            for e in range(4):
                eng.dma_start(out=mask_dm[e:e + 1, tt * 128:(tt + 1) * 128],
                              in_=mro[0:1, slot * 512 + e * 128:
                                      slot * 512 + (e + 1) * 128]
                              ).then_inc(s_st.h, 16)
        kb.on("gpsimd", fms)
        s_st.inc(64)
    stn = s_st.n

    def fdrain(eng, stn=stn):
        eng.wait_ge(s_st.h, stn)
    kb.on("gpsimd", fdrain)

def emit_block_l1ffn(kb, x_in, x_out, WqT, WdkvT, WukT, WuvT, WoT, gwT_D,
                     gb_D, W1T, b1_D, W2T, b2_D, n1_D, n2_D, lng_D, lnb_D,
                     ropec_D, ropes_D, cmask_D, idq_D, psw_D,
                     xn_bf, xp_f32, xres0, q_dm, k_dm, lat_f32, latn_bf,
                     vt_tm, attn_dm, xc2_f32, xn2_bf, xn2_f32, mask_dm,
                     he_bf, stop_after=99):
    scP = kb.scope()
    n1_sb = kb.sb("f_n1", [128, 8], F32, scP)
    n2_sb = kb.sb("f_n2", [128, 8], F32, scP)
    lng_sb = kb.sb("f_lng", [128, 4], F32, scP)
    lnb_sb = kb.sb("f_lnb", [128, 4], F32, scP)
    gwT_sb = kb.sb("f_gw", [128, 32], F32, scP)
    gb_sb = kb.sb("f_gb", [1, 4], F32, scP)
    b1_sb = kb.sb("f_b1", [128, 128], F32, scP)
    b2_sb = kb.sb("f_b2", [128, 32], F32, scP)
    cosb = kb.sb("f_cos", [128, T_SEQ], BF16, scP)
    sinb = kb.sb("f_sin", [128, T_SEQ], BF16, scP)
    pswb = kb.sb("f_psw", [128, 128], BF16, scP)
    cmsk = kb.sb("f_cm", [128, 2 * 2048], F32, scP)
    idq = kb.sb("f_idq", [128, 128], F32, scP)
    s_p = kb.sem("f_sp")

    def flp(eng):
        eng.dma_start(out=n1_sb[:], in_=n1_D[:, :]).then_inc(s_p.h, 16)
        eng.dma_start(out=n2_sb[:], in_=n2_D[:, :]).then_inc(s_p.h, 16)
        eng.dma_start(out=lng_sb[:], in_=lng_D[:, :]).then_inc(s_p.h, 16)
        eng.dma_start(out=lnb_sb[:], in_=lnb_D[:, :]).then_inc(s_p.h, 16)
        eng.dma_start(out=gwT_sb[:], in_=gwT_D[:, :]).then_inc(s_p.h, 16)
        eng.dma_start(out=gb_sb[:], in_=gb_D[:, :]).then_inc(s_p.h, 16)
        eng.dma_start(out=b1_sb[:], in_=b1_D[:, :]).then_inc(s_p.h, 16)
        eng.dma_start(out=b2_sb[:], in_=b2_D[:, :]).then_inc(s_p.h, 16)
        eng.dma_start(out=cosb[:], in_=ropec_D[:, :]).then_inc(s_p.h, 16)
        eng.dma_start(out=sinb[:], in_=ropes_D[:, :]).then_inc(s_p.h, 16)
        eng.dma_start(out=cmsk[:], in_=cmask_D[:, :]).then_inc(s_p.h, 16)
        eng.dma_start(out=idq[:], in_=idq_D[:, :]).then_inc(s_p.h, 16)
        eng.dma_start(out=pswb[:], in_=psw_D[:, :]).then_inc(s_p.h, 16)
        eng.wait_ge(s_p.h, 16 * 13)
    kb.on("sync", flp)
    s_p.inc(16 * 13)
    p_ready = (s_p, s_p.n)

    sc = kb.scope()
    lnb1 = make_ln_bufs(kb, sc, D, pfx="f_ln1", with_f32=True, with_sum=True)
    emit_ln_phase(kb, lnb1, x_in, xn_bf, D, eps=1e-5, xn_f32_dram=xp_f32,
                  sum_out_dram=xres0)
    sc.close()
    kb.barrier()
    if stop_after <= 1:
        return

    sc = kb.scope()
    lnr = make_ln_bufs(kb, sc, D, pfx="f_rms1")
    emit_ln_phase(kb, lnr, xp_f32, xn_bf, D, gamma_sb=n1_sb, rms=True, eps=1e-6)
    sc.close()
    kb.barrier()
    if stop_after <= 2:
        return

    sc = kb.scope()
    wq_sb = kb.sb("f_wq", [128, 8 * 8 * 128], BF16, sc)
    wl_sb = kb.sb("f_wl", [128, 8 * 4 * 128], BF16, sc)
    sA = kb.sem(); sB = kb.sem()
    emit_weight_load_g(kb, WqT, wq_sb, 8, 8, None, sA, sA)
    emit_weight_load_g(kb, WdkvT, wl_sb, 8, 4, None, sA, sA)
    qb = {
        "qx": kb.sb("f_qx", [128, 2 * 8 * NCHUNK], BF16, sc),
        "qo": kb.sb("f_qo", [128, 2 * 8 * NCHUNK], BF16, sc),
        "qt": kb.sb("f_qt", [128, 2 * NCHUNK], F32, sc),
        "qps": [kb.psum(f"f_qps{i}", [128, NCHUNK], F32, sc) for i in range(2)],
    }
    emit_proj_phase(kb, qb, "q", [wq_sb], xn_bf, 8, q_dm, 8,
                    mode="copy_bf16", w_ready=(sA, sA.n))
    lb = {
        "lx": kb.sb("f_lx", [128, 2 * 8 * NCHUNK], BF16, sc),
        "lo": kb.sb("f_lo", [128, 2 * 4 * NCHUNK], F32, sc),
        "lt": kb.sb("f_lt", [128, 2 * NCHUNK], F32, sc),
        "lps": [kb.psum(f"f_lps{i}", [128, NCHUNK], F32, sc) for i in range(2)],
    }
    emit_proj_phase(kb, lb, "l", [wl_sb], xn_bf, 8, lat_f32, 4,
                    mode="copy_f32", w_ready=(sA, sA.n))
    sc.close()
    kb.barrier()
    if stop_after <= 3:
        return

    sc = kb.scope()
    lnl = make_ln_bufs(kb, sc, 512, pfx="f_lnl")
    emit_ln_phase(kb, lnl, lat_f32, latn_bf, 512, gamma_sb=lng_sb,
                  beta_sb=lnb_sb, eps=1e-5)
    sc.close()
    kb.barrier()
    if stop_after <= 4:
        return

    sc = kb.scope()
    wk_sb = kb.sb("f_wk", [128, 4 * 8 * 128], BF16, sc)
    wv_sb = kb.sb("f_wv", [128, 4 * 1024], BF16, sc)
    sC = kb.sem()
    emit_weight_load_g(kb, WukT, wk_sb, 4, 8, None, sC, sC)
    for kt in range(4):
        def flw(eng, kt=kt):
            eng.dma_start(out=wv_sb[:, kt * 1024:(kt + 1) * 1024],
                          in_=WuvT[kt * 128:(kt + 1) * 128, :]).then_inc(sC.h, 16)
        kb.on("gpsimd", flw)
        sC.inc(16)
    kpb = {
        "kx": kb.sb("f_kx", [128, 2 * 4 * NCHUNK], BF16, sc),
        "ko": kb.sb("f_ko", [128, 2 * 8 * NCHUNK], BF16, sc),
        "kt": kb.sb("f_kt", [128, 2 * NCHUNK], F32, sc),
        "kps": [kb.psum(f"f_kps{i}", [128, NCHUNK], F32, sc) for i in range(2)],
    }
    emit_proj_phase(kb, kpb, "k", [wk_sb], latn_bf, 4, k_dm, 8,
                    mode="copy_bf16", w_ready=(sC, sC.n))
    vtb = {
        "vtx": kb.sb("f_vtx", [128, 2 * 4 * 128], BF16, sc),
        "vto": kb.sb("f_vto", [128, 2 * 1024], BF16, sc),
        "vtps": kb.psum("f_vtps", [128, 1024], F32, sc),
    }
    emit_tokmajor_proj(kb, vtb, "vt", latn_bf, wv_sb, 4, 1024, vt_tm,
                       (sC, sC.n))
    sc.close()
    kb.barrier()
    if stop_after <= 5:
        return

    sc = kb.scope()
    rpb = {
        "rpx": kb.sb("f_rpx", [128, 2 * T_SEQ], BF16, sc),
        "rpa": kb.sb("f_rpa", [128, 512], F32, sc),
        "rps": kb.sb("f_rps", [128, 512], F32, sc),
        "rpo": kb.sb("f_rpo", [128, 2 * T_SEQ], BF16, sc),
        "rp_ps": [kb.psum(f"f_rpps{i}", [128, 512], F32, sc) for i in range(2)],
    }
    emit_rope_phase(kb, rpb, q_dm, cosb, sinb, pswb, p_ready)
    kb.barrier()
    if stop_after <= 6:
        return
    emit_rope_phase(kb, rpb, k_dm, cosb, sinb, pswb, p_ready)
    sc.close()
    kb.barrier()
    if stop_after <= 7:
        return

    sc = kb.scope()
    mab = {
        "maq": kb.sb("f_maq", [64, 2 * T_SEQ], BF16, sc),
        "mak": kb.sb("f_mak", [64, 2 * T_SEQ], BF16, sc),
        "mav": kb.sb("f_mav", [128, 2 * 1072], BF16, sc),
        "mam1": kb.sb("f_mam1", [1, T_SEQ], BF16, sc),
        "maxp": kb.sb("f_maxp", [128, 16 * 512], BF16, sc),
        "mao": kb.sb("f_mao", [128, 1024], F32, sc),
        "maob": kb.sb("f_maob", [64, 2 * T_SEQ], BF16, sc),
        "mast": kb.sb("f_mast", [128, 1824], F32, sc),
        "marc": kb.sb("f_marc", [128, 512], F32, sc),
        "marc0": kb.sb("f_marc0", [1, 512], F32, sc),
        "onescolb": kb.sb("f_onescolb", [128, 1], BF16, sc),
        "ones64": kb.sb("f_ones64", [128, 64], F32, sc),
        "cmask_s": cmsk[:, 0:2048],
        "cmask_b": cmsk[:, 2048:4096],
        "ma_psS": [kb.psum(f"f_psS{i}", [128, 512], F32, sc) for i in range(4)],
        "ma_psO": [kb.psum(f"f_psO{i}", [128, 512], F32, sc) for i in range(2)],
        "ma_psB": [kb.psum(f"f_psB{i}", [128, 512], F32, sc) for i in range(2)],
        "idq": idq,
    }
    emit_mla_attn(kb, mab, q_dm, k_dm, vt_tm, attn_dm)
    sc.close()
    kb.barrier()
    if stop_after <= 8:
        return

    sc = kb.scope()
    wo_sb = kb.sb("f_wo", [128, 8 * 8 * 128], BF16, sc)
    sD = kb.sem()
    emit_weight_load_g(kb, WoT, wo_sb, 8, 8, None, sD, sD)
    opb = {
        "wox": kb.sb("f_wox", [128, 2 * 8 * NCHUNK], BF16, sc),
        "woo": kb.sb("f_woo", [128, 2 * 8 * NCHUNK], F32, sc),
        "wot": kb.sb("f_wot", [128, 2 * NCHUNK], F32, sc),
        "wor": kb.sb("f_wor", [128, 2 * 8 * NCHUNK], F32, sc),
        "wor2": kb.sb("f_wor2", [128, 2 * 8 * NCHUNK], F32, sc),
        "woo2": kb.sb("f_woo2", [128, 2 * 8 * NCHUNK], F32, sc),
        "wops": [kb.psum(f"f_wops{i}", [128, NCHUNK], F32, sc) for i in range(2)],
    }
    emit_proj_phase(kb, opb, "wo", [wo_sb], attn_dm, 8, xc2_f32, 8,
                    mode="residual2", res_dram=xp_f32, res2_dram=xres0,
                    out2_dram=x_out, w_ready=(sD, sD.n))
    sc.close()
    kb.barrier()
    if stop_after <= 9:
        return

    sc = kb.scope()
    lnr2 = make_ln_bufs(kb, sc, D, pfx="f_rms2", with_f32=True)
    emit_ln_phase(kb, lnr2, xc2_f32, xn2_bf, D, gamma_sb=n2_sb, rms=True,
                  eps=1e-6, xn_f32_dram=xn2_f32)
    sc.close()
    kb.barrier()
    if stop_after <= 10:
        return

    sc = kb.scope()
    gtb = {
        "gtx": kb.sb("f_gtx", [128, 2 * 1024], F32, sc),
        "gtl": kb.sb("f_gtl", [128, 16], F32, sc),
        "gtm": kb.sb("f_gtm", [1, 2 * 512], F32, sc),
        "gt_psL": kb.psum("f_gpsL", [128, 512], F32, sc),
        "gt_psT": kb.psum("f_gpsT", [128, 512], F32, sc),
    }
    emit_gating(kb, gtb, xn2_f32, gwT_sb, gb_sb, mask_dm, idq, p_ready)
    sc.close()
    kb.barrier()
    if stop_after <= 11:
        return

    esems1 = [kb.sem() for _ in range(7)]
    esems2 = [kb.sem() for _ in range(7)]
    for e in range(4):
        sc = kb.scope()
        w1_sb = kb.sb(f"f_w1_{e}", [128, 8 * 32 * 128], BF16, sc)
        sE = kb.sem()
        emit_weight_load_g(kb, W1T[e], w1_sb, 8, 32, None, sE, sE)
        heb = {
            "hex": kb.sb(f"f_hex{e}", [128, 2 * 8 * NCHUNK], BF16, sc),
            "heo": kb.sb(f"f_heo{e}", [128, 2 * 32 * NCHUNK], BF16, sc),
            "het": kb.sb(f"f_het{e}", [128, 2 * NCHUNK], F32, sc),
            "heps": [kb.psum(f"f_heps{e}_{i}", [128, NCHUNK], F32, sc) for i in range(2)],
        }
        emit_proj_phase(kb, heb, "he", [w1_sb], xn2_bf, 8, he_bf, 32,
                        mode="relu_bias", bias_sb=b1_sb[:, e * 32:(e + 1) * 32],
                        w_ready=(sE, sE.n), sems=esems1)
        sc.close()
        kb.barrier()
        if stop_after <= 12:
            return
        sc = kb.scope()
        w2_sb = kb.sb(f"f_w2_{e}", [128, 32 * 8 * 128], BF16, sc)
        sF = kb.sem()
        emit_weight_load_g(kb, W2T[e], w2_sb, 32, 8, None, sF, sF)
        zb = {
            "zx": kb.sb(f"f_zx{e}", [128, 2 * 32 * NCHUNK], BF16, sc),
            "zo": kb.sb(f"f_zo{e}", [128, 2 * 8 * NCHUNK], F32, sc),
            "zt": kb.sb(f"f_zt{e}", [128, 2 * NCHUNK], F32, sc),
            "zmr": kb.sb(f"f_zmr{e}", [1, 2 * NCHUNK], F32, sc),
            "zpsm": kb.psum(f"f_zpsm{e}", [128, NCHUNK], F32, sc),
            "zps": [kb.psum(f"f_zps{e}_{i}", [128, NCHUNK], F32, sc) for i in range(2)],
        }
        emit_proj_phase(kb, zb, "z", [w2_sb], he_bf, 32, x_out, 8,
                        mode="moe_accum", bias_sb=b2_sb[:, e * 8:(e + 1) * 8],
                        mask_dram=mask_dm[e:e + 1, :], accum_store=True,
                        w_ready=(sF, sF.n), sems=esems2)
        sc.close()
        kb.barrier()
        if stop_after <= 13:
            return
    scP.close()


def l1ffn_consts():
    t = np.arange(T_SEQ, dtype=np.float64)
    inv = 1.0 / 10000.0 ** (np.arange(32, dtype=np.float64) / 32.0)
    ang = inv[:, None] * t[None, :]
    cos = np.tile(np.cos(ang), (4, 1)).astype(ml_dtypes.bfloat16)
    sgn = np.tile(np.sin(ang), (4, 1))
    sin = np.concatenate([np.concatenate([-sgn[0:32], sgn[32:64]], 0)] * 2,
                         0).astype(ml_dtypes.bfloat16)
    psw = np.zeros((128, 128), np.float32)
    for i in range(128):
        j = i + 32 if (i % 64) < 32 else i - 32
        psw[j, i] = 1.0
    psw = psw.astype(ml_dtypes.bfloat16)
    r = np.arange(128)[:, None]
    j = np.arange(512)[None, :]
    cs = []
    cb = []
    for v in range(4):
        cs.append(np.where(j <= v * 128 + r, 0.0, -1e9).astype(np.float32))
        cb.append(np.where(j >= r + v * 128, 0.0, -1e9).astype(np.float32))
    cmask = np.concatenate(cs + cb, axis=1)
    idq = np.eye(128, dtype=np.float32)
    return cos, sin, cmask, idq, psw


def _decl_x(nc, name, kind=None):
    if kind:
        return nc.dram_tensor(name, [1024, NTOK], F32, kind=kind)
    return nc.dram_tensor(name, [1024, NTOK], F32)


def build_l0attn():
    nc = bass.Bass("TRN2", target_bir_lowering=False)
    DT = nc.dram_tensor
    x_in = DT("x_in", [1024, NTOK], F32, kind="ExternalInput")
    WqD = DT("WqD", [1024, 1024], F32, kind="ExternalInput")
    WkD = DT("WkD", [1024, 1024], F32, kind="ExternalInput")
    WvD = DT("WvD", [1024, 1024], F32, kind="ExternalInput")
    WbD = DT("WbD", [1024, 8], F32, kind="ExternalInput")
    lnwD = DT("lnwD", [128, 8], F32, kind="ExternalInput")
    lnbD = DT("lnbD", [128, 8], F32, kind="ExternalInput")
    cD = DT("cD", [128, 128 + 3 * 1024], BF16, kind="ExternalInput")
    x_out = DT("x_out", [1024, NTOK], F32, kind="ExternalOutput")
    xn_bf = DT("xn_bf", [1024, NTOK], BF16)
    q_dm = DT("q_dm", [1024, NTOK], BF16)
    k_tm = DT("k_tm", [NTOK, 1024], BF16)
    v_tm = DT("v_tm", [NTOK, 1024], BF16)
    beta_tm = DT("beta_tm", [NTOK, 8], F32)
    o_dm = DT("o_dm", [1024, NTOK], F32)
    es = ExitStack()
    kb = KB(nc, es)
    kb.emit_setup()
    emit_block_l0attn(kb, x_in, x_out, WqD, WkD, WvD, WbD, lnwD, lnbD, cD,
                      xn_bf, q_dm, k_tm, v_tm, beta_tm, o_dm)
    kb.finalize()
    es.close()
    return nc


def build_l0ffn():
    nc = bass.Bass("TRN2", target_bir_lowering=False)
    DT = nc.dram_tensor
    x_in = DT("x_in", [1024, NTOK], F32, kind="ExternalInput")
    wgT = DT("wgT", [1024, 3072], I32, kind="ExternalInput")
    wuT = DT("wuT", [1024, 3072], I32, kind="ExternalInput")
    wdT = DT("wdT", [3072, 1024], I32, kind="ExternalInput")
    x_out = DT("x_out", [1024, NTOK], F32, kind="ExternalOutput")
    xn_bf = DT("xn_bf", [1024, NTOK], BF16)
    h_bf = DT("h_bf", [3072, NTOK], BF16)
    es = ExitStack()
    kb = KB(nc, es)
    kb.emit_setup()
    emit_block_l0ffn(kb, x_in, x_out, wgT, wuT, wdT, xn_bf, h_bf)
    kb.finalize()
    es.close()
    return nc


def build_l1attn():
    nc = bass.Bass("TRN2", target_bir_lowering=False)
    DT = nc.dram_tensor
    x_in = DT("x_in", [1024, NTOK], F32, kind="ExternalInput")
    inpT = DT("inpT", [1024, 4096], F32, kind="ExternalInput")
    xprojT = DT("xprojT", [2048, 128], F32, kind="ExternalInput")
    dtprojT = DT("dtprojT", [64, 2048], F32, kind="ExternalInput")
    outprojT = DT("outprojT", [2048, 1024], F32, kind="ExternalInput")
    lng_D = DT("lng_D", [128, 8], F32, kind="ExternalInput")
    lnb_D = DT("lnb_D", [128, 8], F32, kind="ExternalInput")
    convw_D = DT("convw_D", [128, 64], F32, kind="ExternalInput")
    convb_D = DT("convb_D", [128, 16], F32, kind="ExternalInput")
    dtb_D = DT("dtb_D", [128, 16], F32, kind="ExternalInput")
    Alog_D = DT("Alog_D", [128, 256], F32, kind="ExternalInput")
    Dp_D = DT("Dp_D", [128, 16], F32, kind="ExternalInput")
    x_out = DT("x_out", [1024, NTOK], F32, kind="ExternalOutput")
    xn_bf = DT("xn_bf", [1024, NTOK], BF16)
    xz_dm = DT("xz_dm", [4096, NTOK], BF16)
    xc_dm = DT("xc_dm", [2048, NTOK], BF16)
    xdbl_dm = DT("xdbl_dm", [128, NTOK], BF16)
    dt_dm = DT("dt_dm", [2048, NTOK], F32)
    dtx_dm = DT("dtx_dm", [2048, NTOK], BF16)
    ys_dm = DT("ys_dm", [2048, NTOK], BF16)
    y2_dm = DT("y2_dm", [2048, NTOK], BF16)
    xn1_f32 = DT("xn1_f32", [1024, NTOK], F32)
    xres = DT("xres", [1024, NTOK], F32)
    es = ExitStack()
    kb = KB(nc, es)
    kb.emit_setup()
    emit_block_l1attn(kb, x_in, x_out, inpT, xprojT, dtprojT, outprojT,
                      lng_D, lnb_D, convw_D, convb_D, dtb_D, Alog_D, Dp_D,
                      xn_bf, xz_dm, xc_dm, xdbl_dm, dt_dm, dtx_dm, ys_dm,
                      y2_dm, xn1_f32, xres)
    kb.finalize()
    es.close()
    return nc


def build_l1ffn():
    nc = bass.Bass("TRN2", target_bir_lowering=False)
    DT = nc.dram_tensor
    x_in = DT("x_in", [1024, NTOK], F32, kind="ExternalInput")
    WqT_D = DT("WqT", [1024, 1024], F32, kind="ExternalInput")
    WdkvT_D = DT("WdkvT", [1024, 512], F32, kind="ExternalInput")
    WukT_D = DT("WukT", [512, 1024], F32, kind="ExternalInput")
    WuvT_D = DT("WuvT", [512, 1024], F32, kind="ExternalInput")
    WoT_D = DT("WoT", [1024, 1024], F32, kind="ExternalInput")
    gwT_D = DT("gwT", [128, 32], F32, kind="ExternalInput")
    gb_D = DT("gb", [1, 4], F32, kind="ExternalInput")
    W1T_D = DT("W1T", [4, 1024, 4096], F32, kind="ExternalInput")
    b1_D = DT("b1D", [128, 128], F32, kind="ExternalInput")
    W2T_D = DT("W2T", [4, 4096, 1024], F32, kind="ExternalInput")
    b2_D = DT("b2D", [128, 32], F32, kind="ExternalInput")
    n1_D = DT("n1D", [128, 8], F32, kind="ExternalInput")
    n2_D = DT("n2D", [128, 8], F32, kind="ExternalInput")
    lng_D = DT("lngD", [128, 4], F32, kind="ExternalInput")
    lnb_D = DT("lnbD", [128, 4], F32, kind="ExternalInput")
    ropec_D = DT("ropecD", [128, 2048], BF16, kind="ExternalInput")
    ropes_D = DT("ropesD", [128, 2048], BF16, kind="ExternalInput")
    psw_D = DT("pswD", [128, 128], BF16, kind="ExternalInput")
    cmask_D = DT("cmaskD", [128, 4096], F32, kind="ExternalInput")
    idq_D = DT("idqD", [128, 128], F32, kind="ExternalInput")
    x_out = DT("x_out", [1024, NTOK], F32, kind="ExternalOutput")
    xn_bf = DT("xn_bf", [1024, NTOK], BF16)
    xp_f32 = DT("xp_f32", [1024, NTOK], F32)
    xres0 = DT("xres0", [1024, NTOK], F32)
    q_dm = DT("q_dm", [1024, NTOK], BF16)
    k_dm = DT("k_dm", [1024, NTOK], BF16)
    lat_f32 = DT("lat_f32", [512, NTOK], F32)
    latn_bf = DT("latn_bf", [512, NTOK], BF16)
    vt_tm = DT("vt_tm", [NTOK, 1024], BF16)
    attn_dm = DT("attn_dm", [1024, NTOK], BF16)
    xc2_f32 = DT("xc2_f32", [1024, NTOK], F32)
    xn2_bf = DT("xn2_bf", [1024, NTOK], BF16)
    xn2_f32 = DT("xn2_f32", [1024, NTOK], F32)
    mask_dm = DT("mask_dm", [4, NTOK], F32)
    he_bf = DT("he_bf", [4096, NTOK], BF16)
    es = ExitStack()
    kb = KB(nc, es)
    kb.emit_setup()
    emit_block_l1ffn(kb, x_in, x_out, WqT_D, WdkvT_D, WukT_D, WuvT_D, WoT_D,
                     gwT_D, gb_D, W1T_D, b1_D, W2T_D, b2_D, n1_D, n2_D,
                     lng_D, lnb_D, ropec_D, ropes_D, cmask_D, idq_D, psw_D,
                     xn_bf, xp_f32, xres0, q_dm, k_dm, lat_f32, latn_bf,
                     vt_tm, attn_dm, xc2_f32, xn2_bf, xn2_f32, mask_dm, he_bf)
    kb.finalize()
    es.close()
    return nc


def _colmaj(a, tiles):
    return np.ascontiguousarray(np.asarray(a, np.float32).reshape(tiles, 128).T)


FUSED = True


def _common_dicts(L0_attn_Wq, L0_attn_Wk, L0_attn_Wv, L0_attn_Wb,
                  L0_attn_ln_w, L0_attn_ln_b, L0_ffn_w_gate, L0_ffn_w_up,
                  L0_ffn_w_down, L1_attn_ln_gamma, L1_attn_ln_beta,
                  L1_attn_in_proj_w, L1_attn_conv_w, L1_attn_conv_b,
                  L1_attn_x_proj_w, L1_attn_dt_proj_w, L1_attn_dt_proj_b,
                  L1_attn_A_log, L1_attn_D_param, L1_attn_out_proj_w,
                  L1_ffn_norm1_w, L1_ffn_norm2_w, L1_ffn_Wq, L1_ffn_Wdkv,
                  L1_ffn_Wuk, L1_ffn_Wuv, L1_ffn_Wo, L1_ffn_ln_g,
                  L1_ffn_ln_b, L1_ffn_gate_w, L1_ffn_gate_b, L1_ffn_W1,
                  L1_ffn_b1, L1_ffn_W2, L1_ffn_b2):
    f32 = lambda a: np.ascontiguousarray(np.asarray(a, np.float32))
    c1 = {"WqD": f32(L0_attn_Wq), "WkD": f32(L0_attn_Wk),
          "WvD": f32(L0_attn_Wv), "WbD": f32(L0_attn_Wb),
          "lnwD": _colmaj(L0_attn_ln_w, 8), "lnbD": _colmaj(L0_attn_ln_b, 8),
          "cD": l0attn_consts()}
    c2 = {"wgT": np.ascontiguousarray(np.asarray(L0_ffn_w_gate).T),
          "wuT": np.ascontiguousarray(np.asarray(L0_ffn_w_up).T),
          "wdT": np.ascontiguousarray(np.asarray(L0_ffn_w_down).T)}
    xpp = np.zeros((2048, 128), np.float32)
    xpp[:, 0:96] = np.asarray(L1_attn_x_proj_w, np.float32).T
    c3 = {
        "inpT": np.ascontiguousarray(np.asarray(L1_attn_in_proj_w, np.float32).T),
        "xprojT": xpp,
        "dtprojT": np.ascontiguousarray(np.asarray(L1_attn_dt_proj_w, np.float32).T),
        "outprojT": np.ascontiguousarray(np.asarray(L1_attn_out_proj_w, np.float32).T),
        "lng_D": _colmaj(L1_attn_ln_gamma, 8), "lnb_D": _colmaj(L1_attn_ln_beta, 8),
        "convw_D": np.ascontiguousarray(np.asarray(L1_attn_conv_w, np.float32)
                                        .reshape(16, 128, 4).transpose(1, 0, 2)
                                        .reshape(128, 64)),
        "convb_D": _colmaj(L1_attn_conv_b, 16),
        "dtb_D": _colmaj(L1_attn_dt_proj_b, 16),
        "Alog_D": np.ascontiguousarray(np.asarray(L1_attn_A_log, np.float32)
                                       .reshape(16, 128, 16).transpose(1, 0, 2)
                                       .reshape(128, 256)),
        "Dp_D": _colmaj(L1_attn_D_param, 16),
    }
    cos, sin, cmask, idq, psw = l1ffn_consts()
    c4 = {
        "WqT": np.ascontiguousarray(np.asarray(L1_ffn_Wq, np.float32).T),
        "WdkvT": np.ascontiguousarray(np.asarray(L1_ffn_Wdkv, np.float32).T),
        "WukT": np.ascontiguousarray(np.asarray(L1_ffn_Wuk, np.float32).T),
        "WuvT": np.ascontiguousarray(np.asarray(L1_ffn_Wuv, np.float32).T),
        "WoT": np.ascontiguousarray(np.asarray(L1_ffn_Wo, np.float32).T),
        "gwT": np.ascontiguousarray(np.asarray(L1_ffn_gate_w, np.float32).T
                                    .reshape(8, 128, 4).transpose(1, 0, 2)
                                    .reshape(128, 32)),
        "gb": np.asarray(L1_ffn_gate_b, np.float32).reshape(1, 4),
        "W1T": np.ascontiguousarray(np.asarray(L1_ffn_W1, np.float32)
                                    .transpose(0, 2, 1)),
        "b1D": np.ascontiguousarray(np.concatenate(
            [np.asarray(L1_ffn_b1, np.float32)[e].reshape(32, 128).T
             for e in range(4)], 1)),
        "W2T": np.ascontiguousarray(np.asarray(L1_ffn_W2, np.float32)
                                    .transpose(0, 2, 1)),
        "b2D": np.ascontiguousarray(np.concatenate(
            [np.asarray(L1_ffn_b2, np.float32)[e].reshape(8, 128).T
             for e in range(4)], 1)),
        "n1D": _colmaj(L1_ffn_norm1_w, 8), "n2D": _colmaj(L1_ffn_norm2_w, 8),
        "lngD": _colmaj(L1_ffn_ln_g, 4), "lnbD": _colmaj(L1_ffn_ln_b, 4),
        "ropecD": cos, "ropesD": sin, "cmaskD": cmask, "idqD": idq, "pswD": psw,
    }
    return c1, c2, c3, c4


def kernel(L0_attn_x, L0_attn_Wq, L0_attn_Wk, L0_attn_Wv, L0_attn_Wb,
           L0_attn_ln_w, L0_attn_ln_b, L0_ffn_w_gate, L0_ffn_w_up,
           L0_ffn_w_down, L1_attn_ln_gamma, L1_attn_ln_beta,
           L1_attn_in_proj_w, L1_attn_conv_w, L1_attn_conv_b,
           L1_attn_x_proj_w, L1_attn_dt_proj_w, L1_attn_dt_proj_b,
           L1_attn_A_log, L1_attn_D_param, L1_attn_out_proj_w,
           L1_ffn_norm1_w, L1_ffn_norm2_w, L1_ffn_Wq, L1_ffn_Wdkv,
           L1_ffn_Wuk, L1_ffn_Wuv, L1_ffn_Wo, L1_ffn_ln_g, L1_ffn_ln_b,
           L1_ffn_gate_w, L1_ffn_gate_b, L1_ffn_W1, L1_ffn_b1, L1_ffn_W2,
           L1_ffn_b2):
    shards = shard_x(np.asarray(L0_attn_x, np.float32))
    c1, c2, c3, c4 = _common_dicts(
        L0_attn_Wq, L0_attn_Wk, L0_attn_Wv, L0_attn_Wb, L0_attn_ln_w,
        L0_attn_ln_b, L0_ffn_w_gate, L0_ffn_w_up, L0_ffn_w_down,
        L1_attn_ln_gamma, L1_attn_ln_beta, L1_attn_in_proj_w, L1_attn_conv_w,
        L1_attn_conv_b, L1_attn_x_proj_w, L1_attn_dt_proj_w, L1_attn_dt_proj_b,
        L1_attn_A_log, L1_attn_D_param, L1_attn_out_proj_w, L1_ffn_norm1_w,
        L1_ffn_norm2_w, L1_ffn_Wq, L1_ffn_Wdkv, L1_ffn_Wuk, L1_ffn_Wuv,
        L1_ffn_Wo, L1_ffn_ln_g, L1_ffn_ln_b, L1_ffn_gate_w, L1_ffn_gate_b,
        L1_ffn_W1, L1_ffn_b1, L1_ffn_W2, L1_ffn_b2)
    if FUSED:
        c4f = dict(c4)
        c4f["flngD"] = c4f.pop("lngD")
        c4f["flnbD"] = c4f.pop("lnbD")
        merged = {}
        for c in (c1, c2, c3, c4f):
            merged.update(c)
        nc = build_fused()
        in_maps = [dict(merged, x_in=shards[ci]) for ci in range(NCORES)]
        res = run_bass_kernel_spmd(nc, in_maps, core_ids=list(range(NCORES)))
        shards = [r["x_out"] for r in res.results]
        return unshard_x(shards)
    nc = build_l0attn()
    in_maps = [dict(c1, x_in=shards[c]) for c in range(NCORES)]
    res = run_bass_kernel_spmd(nc, in_maps, core_ids=list(range(NCORES)))
    shards = [r["x_out"] for r in res.results]
    nc = build_l0ffn()
    in_maps = [dict(c2, x_in=shards[c]) for c in range(NCORES)]
    res = run_bass_kernel_spmd(nc, in_maps, core_ids=list(range(NCORES)))
    shards = [r["x_out"] for r in res.results]
    nc = build_l1attn()
    in_maps = [dict(c3, x_in=shards[c]) for c in range(NCORES)]
    res = run_bass_kernel_spmd(nc, in_maps, core_ids=list(range(NCORES)))
    shards = [r["x_out"] for r in res.results]
    nc = build_l1ffn()
    in_maps = [dict(c4, x_in=shards[c]) for c in range(NCORES)]
    res = run_bass_kernel_spmd(nc, in_maps, core_ids=list(range(NCORES)))
    shards = [r["x_out"] for r in res.results]
    return unshard_x(shards)



def build_fused():
    nc = bass.Bass("TRN2", target_bir_lowering=False)
    DT = nc.dram_tensor
    x_in = DT("x_in", [1024, NTOK], F32, kind="ExternalInput")
    x_out = DT("x_out", [1024, NTOK], F32, kind="ExternalOutput")
    x1 = DT("x1", [1024, NTOK], F32)
    x2 = DT("x2", [1024, NTOK], F32)
    x3 = DT("x3", [1024, NTOK], F32)
    WqD = DT("WqD", [1024, 1024], F32, kind="ExternalInput")
    WkD = DT("WkD", [1024, 1024], F32, kind="ExternalInput")
    WvD = DT("WvD", [1024, 1024], F32, kind="ExternalInput")
    WbD = DT("WbD", [1024, 8], F32, kind="ExternalInput")
    lnwD = DT("lnwD", [128, 8], F32, kind="ExternalInput")
    lnbD = DT("lnbD", [128, 8], F32, kind="ExternalInput")
    cD = DT("cD", [128, 128 + 3 * 1024], BF16, kind="ExternalInput")
    a_xn = DT("dr_a_xn", [1024, NTOK], BF16)
    a_q = DT("dr_a_q", [1024, NTOK], BF16)
    a_ktm = DT("dr_a_ktm", [NTOK, 1024], BF16)
    a_vtm = DT("dr_a_vtm", [NTOK, 1024], BF16)
    a_beta = DT("dr_a_beta", [NTOK, 8], F32)
    a_o = DT("dr_a_o", [1024, NTOK], F32)
    wgT = DT("wgT", [1024, 3072], I32, kind="ExternalInput")
    wuT = DT("wuT", [1024, 3072], I32, kind="ExternalInput")
    wdT = DT("wdT", [3072, 1024], I32, kind="ExternalInput")
    f_xn = DT("dr_f_xn", [1024, NTOK], BF16)
    f_h = DT("dr_f_h", [3072, NTOK], BF16)
    inpT = DT("inpT", [1024, 4096], F32, kind="ExternalInput")
    xprojT = DT("xprojT", [2048, 128], F32, kind="ExternalInput")
    dtprojT = DT("dtprojT", [64, 2048], F32, kind="ExternalInput")
    outprojT = DT("outprojT", [2048, 1024], F32, kind="ExternalInput")
    lng_D = DT("lng_D", [128, 8], F32, kind="ExternalInput")
    lnb_D = DT("lnb_D", [128, 8], F32, kind="ExternalInput")
    convw_D = DT("convw_D", [128, 64], F32, kind="ExternalInput")
    convb_D = DT("convb_D", [128, 16], F32, kind="ExternalInput")
    dtb_D = DT("dtb_D", [128, 16], F32, kind="ExternalInput")
    Alog_D = DT("Alog_D", [128, 256], F32, kind="ExternalInput")
    Dp_D = DT("Dp_D", [128, 16], F32, kind="ExternalInput")
    m_xn = DT("dr_m_xn", [1024, NTOK], BF16)
    m_xz = DT("dr_m_xz", [4096, NTOK], BF16)
    m_xc = DT("dr_m_xc", [2048, NTOK], BF16)
    m_xdbl = DT("dr_m_xdbl", [128, NTOK], BF16)
    m_dt = DT("dr_m_dt", [2048, NTOK], F32)
    m_dtx = DT("dr_m_dtx", [2048, NTOK], BF16)
    m_ys = DT("dr_m_ys", [2048, NTOK], BF16)
    m_y2 = DT("dr_m_y2", [2048, NTOK], BF16)
    m_xn1 = DT("dr_m_xn1", [1024, NTOK], F32)
    m_xres = DT("dr_m_xres", [1024, NTOK], F32)
    WqT_D = DT("WqT", [1024, 1024], F32, kind="ExternalInput")
    WdkvT_D = DT("WdkvT", [1024, 512], F32, kind="ExternalInput")
    WukT_D = DT("WukT", [512, 1024], F32, kind="ExternalInput")
    WuvT_D = DT("WuvT", [512, 1024], F32, kind="ExternalInput")
    WoT_D = DT("WoT", [1024, 1024], F32, kind="ExternalInput")
    gwT_D = DT("gwT", [128, 32], F32, kind="ExternalInput")
    gb_D = DT("gb", [1, 4], F32, kind="ExternalInput")
    W1T_D = DT("W1T", [4, 1024, 4096], F32, kind="ExternalInput")
    b1_D = DT("b1D", [128, 128], F32, kind="ExternalInput")
    W2T_D = DT("W2T", [4, 4096, 1024], F32, kind="ExternalInput")
    b2_D = DT("b2D", [128, 32], F32, kind="ExternalInput")
    n1_D = DT("n1D", [128, 8], F32, kind="ExternalInput")
    n2_D = DT("n2D", [128, 8], F32, kind="ExternalInput")
    flng_D = DT("flngD", [128, 4], F32, kind="ExternalInput")
    flnb_D = DT("flnbD", [128, 4], F32, kind="ExternalInput")
    ropec_D = DT("ropecD", [128, 2048], BF16, kind="ExternalInput")
    ropes_D = DT("ropesD", [128, 2048], BF16, kind="ExternalInput")
    psw_D = DT("pswD", [128, 128], BF16, kind="ExternalInput")
    cmask_D = DT("cmaskD", [128, 4096], F32, kind="ExternalInput")
    idq_D = DT("idqD", [128, 128], F32, kind="ExternalInput")
    g_xn = DT("dr_g_xn", [1024, NTOK], BF16)
    g_xp = DT("dr_g_xp", [1024, NTOK], F32)
    g_xres0 = DT("dr_g_xres0", [1024, NTOK], F32)
    g_q = DT("dr_g_q", [1024, NTOK], BF16)
    g_k = DT("dr_g_k", [1024, NTOK], BF16)
    g_lat = DT("dr_g_lat", [512, NTOK], F32)
    g_latn = DT("dr_g_latn", [512, NTOK], BF16)
    g_vt = DT("dr_g_vt", [NTOK, 1024], BF16)
    g_attn = DT("dr_g_attn", [1024, NTOK], BF16)
    g_xc2 = DT("dr_g_xc2", [1024, NTOK], F32)
    g_xn2 = DT("dr_g_xn2", [1024, NTOK], BF16)
    g_xn2f = DT("dr_g_xn2f", [1024, NTOK], F32)
    g_mask = DT("dr_g_mask", [4, NTOK], F32)
    g_he = DT("dr_g_he", [4096, NTOK], BF16)

    es = ExitStack()
    kb = KB(nc, es)
    kb.emit_setup()
    emit_block_l0attn(kb, x_in, x1, WqD, WkD, WvD, WbD, lnwD, lnbD, cD,
                      a_xn, a_q, a_ktm, a_vtm, a_beta, a_o)
    kb.barrier()
    emit_block_l0ffn(kb, x1, x2, wgT, wuT, wdT, f_xn, f_h)
    kb.barrier()
    emit_block_l1attn(kb, x2, x3, inpT, xprojT, dtprojT, outprojT,
                      lng_D, lnb_D, convw_D, convb_D, dtb_D, Alog_D, Dp_D,
                      m_xn, m_xz, m_xc, m_xdbl, m_dt, m_dtx, m_ys,
                      m_y2, m_xn1, m_xres)
    kb.barrier()
    emit_block_l1ffn(kb, x3, x_out, WqT_D, WdkvT_D, WukT_D, WuvT_D, WoT_D,
                     gwT_D, gb_D, W1T_D, b1_D, W2T_D, b2_D, n1_D, n2_D,
                     flng_D, flnb_D, ropec_D, ropes_D, cmask_D, idq_D, psw_D,
                     g_xn, g_xp, g_xres0, g_q, g_k, g_lat, g_latn,
                     g_vt, g_attn, g_xc2, g_xn2, g_xn2f, g_mask, g_he)
    kb.finalize()
    es.close()
    return nc


def kernel_fused(inputs_common_by_launch, shards):
    nc = build_fused()
    c1, c2, c3, c4 = inputs_common_by_launch
    common = {}
    for c in (c1, c2, c3, c4):
        common.update(c)
    in_maps = [dict(common, x_in=shards[cix]) for cix in range(NCORES)]
    res = run_bass_kernel_spmd(nc, in_maps, core_ids=list(range(NCORES)))
    return [r["x_out"] for r in res.results]
